# version 06
# baseline (speedup 1.0000x reference)
import math

import numpy as np

import concourse.bass as bass
import concourse.mybir as mybir
from concourse.bass_utils import run_bass_kernel_spmd
from concourse.aot_env import allow_ici_all_to_all

F32 = mybir.dt.float32
BF16 = mybir.dt.bfloat16
I32 = mybir.dt.int32
F8 = mybir.dt.float8e4
U8 = mybir.dt.uint8
AF = mybir.ActivationFunctionType
ALU = mybir.AluOpType

NCORES = 8
S = 8192
D = 2048
T = S // NCORES
H = 16
HL = H // NCORES
Q_LORA = 768
KV_LORA = 512
D_NOPE = 128
D_ROPE = 64
D_V = 128
D_QK = D_NOPE + D_ROPE
I_FFN = 6144
QK_SCALE = 1.0 / math.sqrt(D_QK)
RMS_EPS = 1e-6
LN_EPS = 1e-5
THETA = 10000.0
AG_COLS = Q_LORA + KV_LORA + D_ROPE

C1 = float(np.float32(2 * np.pi))
C2 = float(2 * np.pi - np.float64(np.float32(2 * np.pi)))

_DEBUG = False
LAST_DEBUG = None
_PCFG = {"mm": True, "evac": True, "split": True, "evac_sq": True, "evac_cp": True, "evac_mm": True}


def ts(i, n):
    return slice(i * n, (i + 1) * n)


class Prog:

    ENGINES = ("sync", "gpsimd", "tensor", "vector", "scalar")

    def __init__(self, nc):
        self.nc = nc
        self.ops = {e: [] for e in self.ENGINES}
        self.sems = {}
        self.cnt = {}

    def sem(self, name):
        if name not in self.sems:
            self.sems[name] = self.nc.alloc_semaphore(name)
            self.cnt[name] = 0
        return self.sems[name]

    def inst(self, eng, make, sem=None, n=1):
        if sem is not None:
            self.sem(sem)
            self.cnt[sem] += n
            self.ops[eng].append(
                lambda e, make=make, s=sem, n=n: make(e).then_inc(self.sems[s], n))
            return self.cnt[sem]
        self.ops[eng].append(lambda e, make=make: make(e))
        return None

    def dma(self, eng, out, in_, sem="ld", slow=False):
        return self.inst(
            eng,
            lambda e, o=out, i=in_, s=slow: e.dma_start(
                out=o, in_=i, allow_slow_non_contiguous=s),
            sem=sem, n=16)

    def wait(self, eng, sem, val=None):
        self.sem(sem)
        v = self.cnt[sem] if val is None else val
        if v > 0:
            self.ops[eng].append(lambda e, s=sem, v=v: e.wait_ge(self.sems[s], v))

    def val(self, sem):
        self.sem(sem)
        return self.cnt[sem]

    def qbrk(self, eng, sem):
        self.wait(eng, sem)

    def chain(self, eng, sem, makes):
        for i, mk in enumerate(makes):
            if i > 0:
                self.wait(eng, sem)
            self.inst(eng, mk, sem=sem)

    def barrier(self, engines=None):
        engines = engines or self.ENGINES
        snap = dict(self.cnt)
        for e in engines:
            for s, v in snap.items():
                if v > 0:
                    self.ops[e].append(
                        lambda eng, s=s, v=v: eng.wait_ge(self.sems[s], v))

    def build(self):
        with self.nc.Block() as block:
            for name in self.ENGINES:
                getattr(block, name)(
                    lambda eng, name=name: [fn(eng) for fn in self.ops[name]])



def build_nc(stop_after=None):
    nc = bass.Bass("TRN2", target_bir_lowering=False, debug=False)
    p = Prog(nc)

    dt_ = nc.dram_tensor
    x_t = dt_("x_t", [D, T], F32, kind="ExternalInput").ap()
    w_qa = dt_("w_qa", [D, Q_LORA], F32, kind="ExternalInput").ap()
    w_qb_h = dt_("w_qb_h", [Q_LORA, HL * D_QK], F32, kind="ExternalInput").ap()
    w_kva = dt_("w_kva", [D, KV_LORA + D_ROPE], F32, kind="ExternalInput").ap()
    w_kvb_h = dt_("w_kvb_h", [KV_LORA, HL * 256], F32, kind="ExternalInput").ap()
    w_o = dt_("w_o", [D, D], F32, kind="ExternalInput").ap()
    gate_w = dt_("gate_w", [D, I_FFN], F32, kind="ExternalInput").ap()
    up_w = dt_("up_w", [D, I_FFN], F32, kind="ExternalInput").ap()
    down_w = dt_("down_w", [I_FFN, D], F32, kind="ExternalInput").ap()
    g_q = dt_("g_q", [128, Q_LORA // 128], F32, kind="ExternalInput").ap()
    g_kv = dt_("g_kv", [128, KV_LORA // 128], F32, kind="ExternalInput").ap()
    gate_s = dt_("gate_s", [128, I_FFN // 128], F32, kind="ExternalInput").ap()
    up_s = dt_("up_s", [128, I_FFN // 128], F32, kind="ExternalInput").ap()
    down_s = dt_("down_s", [128, D // 128], F32, kind="ExternalInput").ap()
    tok_base = dt_("tok_base", [1, 1], F32, kind="ExternalInput").ap()
    out_t = dt_("out_t", [D, T], F32, kind="ExternalOutput").ap()

    ag_in = dt_("ag_in", [T, AG_COLS], F32).ap()
    ag_out = dt_("ag_out", [S, AG_COLS], F32).ap()
    a2a_in = dt_("a2a_in", [S, HL * D_V], F32).ap()
    a2a_out = dt_("a2a_out", [S, HL * D_V], F32).ap()
    wq_gate = dt_("wq_gate", [D, I_FFN], BF16).ap()
    wq_up = dt_("wq_up", [D, I_FFN], BF16).ap()
    wq_down = dt_("wq_down", [I_FFN, D], BF16).ap()
    rope_t = dt_("rope_t", [2, 64, S], F32).ap()

    dbg = {}
    if _DEBUG:
        dbg["d_ag"] = dt_("d_ag", [T, AG_COLS], F32, kind="ExternalOutput").ap()
        dbg["d_q"] = dt_("d_q", [4 * 128, S], BF16, kind="ExternalOutput").ap()
        dbg["d_k"] = dt_("d_k", [3 * 128, S], BF16, kind="ExternalOutput").ap()
        dbg["d_v"] = dt_("d_v", [2 * 128, S], BF16, kind="ExternalOutput").ap()
        dbg["d_attn"] = dt_("d_attn", [S, HL * D_V], F32, kind="ExternalOutput").ap()
        dbg["d_x1"] = dt_("d_x1", [D, T], F32, kind="ExternalOutput").ap()
        dbg["d_m"] = dt_("d_m", [1, 4], F32, kind="ExternalOutput").ap()

    A = nc.alloc_sbuf_tensor
    ident = A("ident", [128, 128], BF16).ap()
    identf = A("identf", [128, 128], F32).ap()
    rot = A("rot", [64, 64], BF16).ap()
    masks = A("masks", [128, 4 * 512], BF16).ap()
    ones_b = A("ones_b", [128, 1], BF16).ap()
    ones_row = A("ones_row", [1, 128], F32).ap()
    ones_f = A("ones_f", [128, 1], F32).ap()
    c_eps_rms = A("c_eps_rms", [1, 1], F32).ap()
    c_eps_ln = A("c_eps_ln", [1, 1], F32).ap()
    neg_m = A("neg_m", [128, 1], F32).ap()
    m_tmp = A("m_tmp", [1, 4], F32).ap()
    qn2max = A("qn2max", [1, 1], F32).ap()
    kn2max = A("kn2max", [1, 1], F32).ap()
    tokb = A("tokb", [1, 1], F32).ap()
    gq_c = A("gq_c", [128, Q_LORA // 128], F32).ap()
    gkv_c = A("gkv_c", [128, KV_LORA // 128], F32).ap()
    gs_c = A("gs_c", [128, I_FFN // 128], F32).ap()
    us_c = A("us_c", [128, I_FFN // 128], F32).ap()
    ds_c = A("ds_c", [128, D // 128], F32).ap()
    invf = A("invf", [64, 1], F32).ap()
    invf_i = A("invf_i", [64, 1], I32).ap()
    alpha = A("alpha", [1, T], F32).ap()
    stat1 = A("stat1", [1, T], F32).ap()
    stat2 = A("stat2", [1, T], F32).ap()
    recip_sb = A("recip_sb", [1, 512], F32).ap()
    scr_a = A("scr_a", [128, 1024], F32).ap()
    scr_b = A("scr_b", [128, 1024], F32).ap()
    scr_c = A("scr_c", [128, 1024], F32).ap()
    lnhi = A("lnhi", [128, 512], BF16).ap()
    lnlo = A("lnlo", [128, 512], BF16).ap()

    ARENA = 172 * 1024
    arena_t = A("arena", [128, ARENA], U8)
    views = {F32: arena_t.bitcast(F32).ap(), BF16: arena_t.bitcast(BF16).ap(),
             I32: arena_t.bitcast(I32).ap(), U8: arena_t.ap()}

    def av(off, rows, cols, dt):
        sz = {F32: 4, BF16: 2, I32: 4, U8: 1}[dt]
        assert off % 4 == 0 and off + cols * sz <= ARENA, (off, cols, dt)
        return views[dt][0:rows, off // sz: off // sz + cols]

    PS = nc.alloc_psum_tensor
    ps_s = [PS(f"ps_s{i}", [128, 512], F32).ap() for i in range(2)]
    ps_e = [PS(f"ps_e{i}", [128, 512], F32).ap() for i in range(2)]
    ps_av = [PS(f"ps_av{i}", [128, 512], F32).ap() for i in range(2)]
    ps_d = PS("ps_d", [128, 512], F32).ap()
    ps_x = PS("ps_x", [128, 512], F32).ap()

    p.chain("gpsimd", "g", [
        lambda e: e.memset(ident[:, :], 0.0),
        lambda e: e.affine_select(out=ident[:, :], in_=ident[:, :],
                                  compare_op=ALU.not_equal, fill=1.0, base=0,
                                  pattern=[[-1, 128]], channel_multiplier=1),
        lambda e: e.memset(identf[:, :], 0.0),
        lambda e: e.affine_select(out=identf[:, :], in_=identf[:, :],
                                  compare_op=ALU.not_equal, fill=1.0, base=0,
                                  pattern=[[-1, 128]], channel_multiplier=1),
    ])
    for i in range(4):
        p.chain("gpsimd", "g", [
            lambda e, i=i: e.memset(masks[:, ts(i, 512)], 1.0),
            lambda e, i=i: e.affine_select(
                out=masks[:, ts(i, 512)], in_=masks[:, ts(i, 512)],
                compare_op=ALU.is_ge, fill=0.0, base=-128 * i,
                pattern=[[1, 512]], channel_multiplier=-1),
        ])
    p.chain("gpsimd", "g", [
        lambda e: e.memset(rot[:, :], 0.0),
        lambda e: e.affine_select(
            out=rot[:, 1:64:2], in_=rot[:, 1:64:2], compare_op=ALU.not_equal,
            fill=1.0, base=0, pattern=[[-2, 32]], channel_multiplier=1),
        lambda e: e.affine_select(
            out=rot[:, 0:64:2], in_=rot[:, 0:64:2], compare_op=ALU.not_equal,
            fill=-1.0, base=-1, pattern=[[-2, 32]], channel_multiplier=1),
        lambda e: e.iota(invf_i[:, :], pattern=[[0, 1]], base=0,
                         channel_multiplier=1),
    ])
    for mk in (lambda e: e.memset(ones_b[:, :], 1.0),
               lambda e: e.memset(ones_row[:, :], 1.0),
               lambda e: e.memset(ones_f[:, :], 1.0),
               lambda e: e.memset(c_eps_rms[:, :], RMS_EPS),
               lambda e: e.memset(c_eps_ln[:, :], LN_EPS)):
        p.inst("vector", mk, sem="v")
    p.dma("sync", gq_c[:, :], g_q[:, :])
    p.dma("sync", gkv_c[:, :], g_kv[:, :])
    p.dma("sync", gs_c[:, :], gate_s[:, :])
    p.dma("sync", us_c[:, :], up_s[:, :])
    p.dma("sync", ds_c[:, :], down_s[:, :])
    p.dma("sync", tokb[:, :], tok_base[:, :])
    p.qbrk("sync", "ld")
    p.wait("vector", "g")
    p.wait("vector", "v")
    p.chain("vector", "v", [
        lambda e: e.tensor_copy(out=invf[:, :], in_=invf_i[:, :]),
        lambda e: e.tensor_scalar(out=scr_a[0:64, 0:1], in0=invf[:, :],
                                  scalar1=32.0, scalar2=-32.0,
                                  op0=ALU.is_ge, op1=ALU.mult),
        lambda e: e.tensor_add(invf[:, :], invf[:, :], scr_a[0:64, 0:1]),
        lambda e: e.tensor_scalar(out=invf[:, :], in0=invf[:, :],
                                  scalar1=-math.log(THETA) / 32.0, scalar2=None,
                                  op0=ALU.mult),
    ])
    p.wait("scalar", "v")
    p.inst("scalar", lambda e: e.activation(out=invf[:, :], in_=invf[:, :],
                                            func=AF.Exp), sem="s")

    def build_tables(pos_row, w, cos_o, sin_o, frq, rrf, rri, ps_bc):
        for i in range(max(1, w // 512)):
            wslc = ts(i, 512) if w >= 512 else slice(0, w)
            n = 512 if w >= 512 else w
            p.wait("tensor", "v")
            p.wait("tensor", "s")
            p.inst("tensor", lambda e, wslc=wslc, n=n: e.matmul(
                ps_bc[0:64, 0:n], ones_row[0:1, 0:64], pos_row[0:1, wslc],
                start=True, stop=True), sem="t")
            p.wait("vector", "t")
            p.inst("vector", lambda e, wslc=wslc, n=n: e.tensor_scalar(
                out=frq[0:64, wslc], in0=ps_bc[0:64, 0:n],
                scalar1=invf[0:64, 0:1], scalar2=None, op0=ALU.mult), sem="v")

        def reduce_and_sin(out_ap):
            p.wait("vector", "v")
            p.chain("vector", "v", [
                lambda e: e.tensor_scalar(out=rrf[0:64, 0:w], in0=frq[0:64, 0:w],
                                          scalar1=1.0 / (2 * math.pi),
                                          scalar2=None, op0=ALU.mult),
                lambda e: e.tensor_copy(out=rri[0:64, 0:w], in_=rrf[0:64, 0:w]),
                lambda e: e.tensor_copy(out=rrf[0:64, 0:w], in_=rri[0:64, 0:w]),
                lambda e: e.scalar_tensor_tensor(
                    out=out_ap, in0=rrf[0:64, 0:w], scalar=-C1,
                    in1=frq[0:64, 0:w], op0=ALU.mult, op1=ALU.add),
                lambda e: e.scalar_tensor_tensor(
                    out=out_ap, in0=rrf[0:64, 0:w], scalar=-C2,
                    in1=out_ap, op0=ALU.mult, op1=ALU.add),
                lambda e: e.tensor_scalar(out=rrf[0:64, 0:w], in0=out_ap,
                                          scalar1=math.pi, scalar2=None,
                                          op0=ALU.is_gt),
                lambda e: e.scalar_tensor_tensor(
                    out=out_ap, in0=rrf[0:64, 0:w], scalar=-C1,
                    in1=out_ap, op0=ALU.mult, op1=ALU.add),
                lambda e: e.scalar_tensor_tensor(
                    out=out_ap, in0=rrf[0:64, 0:w], scalar=-C2,
                    in1=out_ap, op0=ALU.mult, op1=ALU.add),
                lambda e: e.tensor_scalar(out=rrf[0:64, 0:w], in0=out_ap,
                                          scalar1=-math.pi, scalar2=None,
                                          op0=ALU.is_lt),
                lambda e: e.scalar_tensor_tensor(
                    out=out_ap, in0=rrf[0:64, 0:w], scalar=C1,
                    in1=out_ap, op0=ALU.mult, op1=ALU.add),
                lambda e: e.scalar_tensor_tensor(
                    out=out_ap, in0=rrf[0:64, 0:w], scalar=C2,
                    in1=out_ap, op0=ALU.mult, op1=ALU.add),
            ])
            p.wait("scalar", "v")
            p.inst("scalar", lambda e: e.activation(
                out=out_ap, in_=out_ap, func=AF.Sin), sem="s")

        reduce_and_sin(sin_o[0:64, 0:w])
        p.wait("vector", "s")
        p.chain("vector", "v", [
            lambda e: e.tensor_scalar(out=frq[0:64, 0:w], in0=frq[0:64, 0:w],
                                      scalar1=math.pi / 2, scalar2=None,
                                      op0=ALU.add),
        ])
        reduce_and_sin(cos_o[0:64, 0:w])

    a_xstage = [av(0, 128, 1024, F32), av(4096, 128, 1024, F32)]
    a_xhi = av(8 * 1024, 128, 16 * 1024, BF16)
    a_xlo = av(40 * 1024, 128, 16 * 1024, BF16)
    a_wf = [av(72 * 1024, 128, 2048, F32), av(80 * 1024, 128, 2048, F32)]
    a_whi = [av(88 * 1024, 128, 2048, BF16), av(92 * 1024, 128, 2048, BF16)]
    a_wlo = [av(96 * 1024, 128, 2048, BF16), av(100 * 1024, 128, 2048, BF16)]
    a_qan = av(104 * 1024, 128, 6 * 1024, F32)
    a_ckvn = av(128 * 1024, 128, 4 * 1024, F32)
    a_krope = av(144 * 1024, 64, 1024, F32)
    a_pos = av(148 * 1024, 1, 1024, F32)
    a_posi = av(152 * 1024, 1, 1024, I32)
    a_cos = av(156 * 1024, 64, 1024, F32)
    a_sin = av(160 * 1024, 64, 1024, F32)
    a_frq = av(164 * 1024, 64, 1024, F32)
    a_rri = av(168 * 1024, 64, 1024, I32)
    a_krf = av(152 * 1024, 64, 1024, F32)
    a_krhi = av(0, 64, 1024, BF16)
    a_krlo = av(2048, 64, 1024, BF16)

    def _finish_early():
        p.barrier()
        p.dma("sync", out_t[0:128, :], scr_a[:, :], sem="fin")
        p.wait("gpsimd", "fin")
        with allow_ici_all_to_all():
            p.build()

    if stop_after == "consts":
        _finish_early()
        return nc, dbg

    p.wait("gpsimd", "g")
    p.inst("gpsimd", lambda e: e.iota(a_posi[:, :], pattern=[[1, T]], base=0,
                                      channel_multiplier=0), sem="g")
    p.wait("vector", "g")
    p.wait("vector", "ld")
    p.chain("vector", "v", [
        lambda e: e.tensor_copy(out=a_pos[:, :], in_=a_posi[:, :]),
        lambda e: e.tensor_scalar(out=a_pos[:, :], in0=a_pos[:, :],
                                  scalar1=tokb[0:1, 0:1], scalar2=None,
                                  op0=ALU.add),
    ])
    build_tables(a_pos, T, a_cos, a_sin, a_frq, scr_c, a_rri, ps_x)

    if stop_after == "tables":
        _finish_early()
        return nc, dbg

    xsplit_done = {}
    for k in range(16):
        b = k % 2
        if k >= 2:
            p.wait("sync", "v", xsplit_done[k - 2])
        p.dma("sync", a_xstage[b][:, :], x_t[ts(k, 128), :])
        p.qbrk("sync", "ld")
        p.wait("vector", "ld")
        p.chain("vector", "v", [
            lambda e, k=k, b=b: e.tensor_copy(out=a_xhi[:, ts(k, 1024)],
                                              in_=a_xstage[b][:, :]),
            lambda e, k=k, b=b: e.tensor_sub(a_xlo[:, ts(k, 1024)],
                                             a_xstage[b][:, :],
                                             a_xhi[:, ts(k, 1024)]),
        ])
        xsplit_done[k] = p.val("v")

    if stop_after == "xload":
        _finish_early()
        return nc, dbg

    wstate = {"i": 0, "mm": {}, "sp": {}}

    def proj_cols(w_dram, col0, ncols, ps_pair, evac_cb):
        i = wstate["i"]
        b = i % 2
        if i >= 2:
            p.wait("sync", "v", wstate["sp"][i - 2])
        p.dma("sync", a_wf[b][:, 0:16 * ncols],
              w_dram[:, col0:col0 + ncols].rearrange("(a p) c -> p a c", p=128))
        p.qbrk("sync", "ld")
        p.wait("vector", "ld")
        if i >= 2:
            p.wait("vector", "t", wstate["mm"][i - 2])
        if _PCFG["split"]:
            for k in range(16):
                sl = ts(k, ncols)
                p.chain("vector", "v", [
                    lambda e, b=b, sl=sl: e.tensor_copy(out=a_whi[b][:, sl],
                                                        in_=a_wf[b][:, sl]),
                    lambda e, b=b, sl=sl: e.tensor_sub(a_wlo[b][:, sl],
                                                       a_wf[b][:, sl],
                                                       a_whi[b][:, sl]),
                ])
        wstate["sp"][i] = p.val("v")
        p.wait("tensor", "v", wstate["sp"][i])
        if not _PCFG["mm"]:
            wstate["mm"][i] = p.val("t")
            wstate["i"] = i + 1
            return
        for k in range(16):
            sl = ts(k, ncols)
            for (wt, xt_) in ((a_whi[b], a_xhi), (a_whi[b], a_xlo),
                              (a_wlo[b], a_xhi)):
                for nt in range(2):
                    xsl = slice(k * 1024 + nt * 512, k * 1024 + nt * 512 + 512)
                    p.inst("tensor", lambda e, wt=wt, xt_=xt_, sl=sl, xsl=xsl,
                           nt=nt,
                           st=(k == 0 and wt is a_whi[b] and xt_ is a_xhi),
                           sp=(k == 15 and wt is a_wlo[b]),
                           nc_=ncols:
                           e.matmul(ps_pair[nt][0:nc_, :], wt[:, sl],
                                    xt_[:, xsl], start=st, stop=sp), sem="t")
        wstate["mm"][i] = p.val("t")
        wstate["i"] = i + 1
        if _PCFG["evac"]:
            evac_cb()

    def make_rms_evac(dst, cc, nchunks, ps_pair):
        def cb():
            tdone = p.val("t")
            if _PCFG["evac_sq"]:
                p.wait("scalar", "t", tdone)
                for nt in range(2):
                    p.inst("scalar", lambda e, nt=nt: e.activation(
                        out=scr_b[:, ts(nt, 512)], in_=ps_pair[nt][:, :],
                        func=AF.Square), sem="s")
            if _PCFG["evac_cp"]:
                p.wait("vector", "t", tdone)
                p.wait("vector", "s")
                for nt in range(2):
                    dsl = slice(cc * 1024 + nt * 512,
                                cc * 1024 + nt * 512 + 512)
                    p.inst("vector", lambda e, nt=nt, dsl=dsl: e.tensor_copy(
                        out=dst[:, dsl], in_=ps_pair[nt][:, :]), sem="v")
            if _PCFG["evac_sq"] and _PCFG["evac_mm"]:
                sq_done = p.val("s")
                p.wait("tensor", "s", sq_done)
                for nt in range(2):
                    tgt = ps_d if nt == 0 else ps_x
                    p.inst("tensor", lambda e, nt=nt, cc=cc, tgt=tgt: e.matmul(
                        tgt[0:1, :], ones_f[:, :], scr_b[:, ts(nt, 512)],
                        start=(cc == 0), stop=(cc == nchunks - 1)), sem="t")
        return cb

    def alpha_and_scale(dst, nchunks, denom):
        ssd = p.val("t")
        p.wait("vector", "t", ssd)
        for nt in range(2):
            src = ps_d if nt == 0 else ps_x
            p.chain("vector", "v", [
                lambda e, nt=nt, src=src: e.tensor_scalar(
                    out=stat1[0:1, ts(nt, 512)], in0=src[0:1, :],
                    scalar1=1.0 / denom, scalar2=RMS_EPS,
                    op0=ALU.mult, op1=ALU.add),
                lambda e, nt=nt: e.reciprocal(stat2[0:1, ts(nt, 512)],
                                              stat1[0:1, ts(nt, 512)]),
            ])
        p.wait("scalar", "v")
        p.inst("scalar", lambda e: e.activation(
            out=alpha[0:1, :], in_=stat2[0:1, :], func=AF.Sqrt), sem="s")
        ad = p.val("s")
        p.wait("tensor", "s", ad)
        for nt in range(2):
            tgt = ps_d if nt == 0 else ps_x
            p.inst("tensor", lambda e, nt=nt, tgt=tgt: e.matmul(
                tgt[:, :], ones_row[0:1, :], alpha[0:1, ts(nt, 512)],
                start=True, stop=True), sem="t")
        bd = p.val("t")
        p.wait("vector", "t", bd)
        for cc in range(nchunks):
            for nt in range(2):
                sl = slice(cc * 1024 + nt * 512, cc * 1024 + nt * 512 + 512)
                src = ps_d if nt == 0 else ps_x
                p.inst("vector", lambda e, sl=sl, src=src: e.tensor_mul(
                    dst[:, sl], dst[:, sl], src[:, :]), sem="v")

    for cc in range(6):
        pair = ps_s if cc % 2 == 0 else ps_e
        proj_cols(w_qa, cc * 128, 128, pair,
                  make_rms_evac(a_qan, cc, 6, pair))
        if stop_after == f"proj_cc{cc}":
            _finish_early()
            return nc, dbg
    if stop_after == "proj_qa_mm":
        _finish_early()
        return nc, dbg
    alpha_and_scale(a_qan, 6, Q_LORA)
    if stop_after == "proj_qa":
        _finish_early()
        return nc, dbg
    qan_done = p.val("v")

    for cc in range(4):
        pair = ps_s if cc % 2 == 0 else ps_e
        proj_cols(w_kva, cc * 128, 128, pair, make_rms_evac(a_ckvn, cc, 4, pair))
    alpha_and_scale(a_ckvn, 4, KV_LORA)
    ckvn_done = p.val("v")

    def krope_evac():
        tdone = p.val("t")
        p.wait("vector", "t", tdone)
        for nt in range(2):
            p.inst("vector", lambda e, nt=nt: e.tensor_copy(
                out=a_krf[:, ts(nt, 512)], in_=ps_av[nt][0:64, :]), sem="v")
        p.wait("vector", "v")
        p.chain("vector", "v", [
            lambda e: e.tensor_copy(out=a_krhi[:, :], in_=a_krf[:, :]),
            lambda e: e.tensor_sub(a_krlo[:, :], a_krf[:, :], a_krhi[:, :]),
        ])
        sp = p.val("v")
        p.wait("tensor", "v", sp)
        for nt in range(2):
            tgt = ps_s[nt]
            p.inst("tensor", lambda e, nt=nt, tgt=tgt: e.matmul(
                tgt[0:64, :], rot[:, :], a_krhi[:, ts(nt, 512)],
                start=True, stop=False), sem="t")
            p.inst("tensor", lambda e, nt=nt, tgt=tgt: e.matmul(
                tgt[0:64, :], rot[:, :], a_krlo[:, ts(nt, 512)],
                start=False, stop=True), sem="t")
        rd = p.val("t")
        p.wait("vector", "t", rd)
        for nt in range(2):
            sl = ts(nt, 512)
            p.chain("vector", "v", [
                lambda e, sl=sl: e.tensor_mul(scr_a[0:64, sl], a_krf[:, sl],
                                              a_cos[0:64, sl]),
                lambda e, sl=sl, nt=nt: e.tensor_mul(
                    scr_b[0:64, sl], ps_s[nt][0:64, :], a_sin[0:64, sl]),
                lambda e, sl=sl: e.tensor_add(a_krope[:, sl], scr_a[0:64, sl],
                                              scr_b[0:64, sl]),
            ])

    if stop_after == "proj_kva":
        _finish_early()
        return nc, dbg
    proj_cols(w_kva, 512, 64, ps_av, krope_evac)

    if stop_after == "proj":
        _finish_early()
        return nc, dbg

    def store_ag(src_fm, nchunks, col_base, rows=128):
        grp_w = 4 * rows
        for cc in range(nchunks):
            for grp in range(2):
                pre = p.val("v")
                p.wait("tensor", "v", pre)
                for q in range(4):
                    tt = grp * 4 + q
                    p.inst("tensor", lambda e, cc=cc, tt=tt, q=q: e.transpose(
                        ps_x[:, q * rows:(q + 1) * rows],
                        src_fm[0:rows, cc * 1024 + tt * 128:
                               cc * 1024 + (tt + 1) * 128],
                        identf[0:rows, 0:rows]), sem="t")
                trd = p.val("t")
                p.wait("vector", "t", trd)
                p.wait("vector", "ag")
                p.inst("vector", lambda e, grp_w=grp_w: e.tensor_copy(
                    out=scr_c[:, 0:grp_w], in_=ps_x[:, 0:grp_w]), sem="v")
                evd = p.val("v")
                p.wait("sync", "v", evd)
                for q in range(4):
                    tt = grp * 4 + q
                    p.dma("sync", ag_in[ts(tt, 128),
                                        col_base + cc * rows:
                                        col_base + (cc + 1) * rows],
                          scr_c[:, q * rows:(q + 1) * rows], sem="ag")

    store_ag(a_qan, 6, 0)
    store_ag(a_ckvn, 4, Q_LORA)
    store_ag(a_krope, 1, Q_LORA + KV_LORA, rows=64)

    if _DEBUG:
        p.wait("sync", "ag")
        p.dma("sync", dbg["d_ag"][:, :], ag_in[:, :], sem="st")

    if stop_after == "p1":
        _finish_early()
        return nc, dbg

    for cblk in range(S // 512):
        p.wait("gpsimd", "v")
        p.inst("gpsimd", lambda e, cblk=cblk: e.iota(
            a_rri[0:1, 0:512], pattern=[[1, 512]], base=cblk * 512,
            channel_multiplier=0), sem="g")
        p.wait("vector", "g")
        p.wait("vector", "st")
        p.inst("vector", lambda e: e.tensor_copy(
            out=a_pos[0:1, 0:512], in_=a_rri[0:1, 0:512]), sem="v")
        p.wait("vector", "v")
        build_tables(a_pos[0:1, 0:512], 512, a_cos, a_sin, a_frq, scr_c,
                     a_rri, ps_x)
        tb = p.val("s")
        p.wait("sync", "s", tb)
        p.dma("sync", rope_t[1, :, cblk * 512:(cblk + 1) * 512],
              a_sin[0:64, 0:512], sem="st")
        p.dma("sync", rope_t[0, :, cblk * 512:(cblk + 1) * 512],
              a_cos[0:64, 0:512], sem="st")
        p.qbrk("sync", "st")

    p.wait("gpsimd", "ag")
    p.inst("gpsimd", lambda e: e.collective_compute(
        "AllGather", ALU.bypass, replica_groups=[list(range(NCORES))],
        ins=[ag_in[:, :]], outs=[ag_out[:, :]]), sem="cc")

    if stop_after == "ag":
        _finish_early()
        return nc, dbg

    qz_f = [av(0, 128, 6144, F32), av(24 * 1024, 128, 6144, F32)]
    qz_u = [av(48 * 1024, 128, 6144, U8), av(54 * 1024, 128, 6144, U8)]
    qz_b = [av(60 * 1024, 128, 6144, BF16), av(72 * 1024, 128, 6144, BF16)]
    qz_done = {}
    qi = 0
    p.wait("sync", "ag")
    p.wait("sync", "v")
    for (wsrc, wdst, ncols) in ((gate_w, wq_gate, I_FFN), (up_w, wq_up, I_FFN),
                                (down_w, wq_down, D)):
        for ch in range(wsrc.shape[0] // 128):
            b = qi % 2
            if qi >= 2:
                p.wait("sync", "st", qz_done[qi - 2])
            p.dma("sync", qz_f[b][:, 0:ncols], wsrc[ts(ch, 128), :])
            p.qbrk("sync", "ld")
            p.wait("vector", "ld")
            p.chain("vector", "v", [
                lambda e, b=b, n=ncols: e.tensor_copy(
                    out=qz_u[b].bitcast(F8)[:, 0:n], in_=qz_f[b][:, 0:n]),
                lambda e, b=b, n=ncols: e.tensor_copy(
                    out=qz_b[b][:, 0:n], in_=qz_u[b].bitcast(F8)[:, 0:n]),
            ])
            p.wait("sync", "v")
            qz_done[qi] = p.dma("sync", wdst[ts(ch, 128), :],
                                qz_b[b][:, 0:ncols], sem="st")
            p.qbrk("sync", "st")
            qi += 1

    p.barrier()

    if stop_after == "qz":
        _finish_early()
        return nc, dbg

    o = 0
    a_qn_hi = av(o, 128, S, BF16); o += 16 * 1024
    a_qn_lo = av(o, 128, S, BF16); o += 16 * 1024
    a_qr_A = av(o, 128, S, BF16); o += 16 * 1024
    a_qr_B = av(o, 128, S, BF16); o += 16 * 1024
    a_kn_hi = av(o, 128, S, BF16); o += 16 * 1024
    a_kn_lo = av(o, 128, S, BF16); o += 16 * 1024
    a_kr_P = av(o, 128, S, BF16); o += 16 * 1024
    a_v_hi = av(o, 128, S, BF16); o += 16 * 1024
    a_v_lo = av(o, 128, S, BF16); o += 16 * 1024
    O_W = o
    a_wb_hi = av(O_W, 128, 6 * 384, BF16)
    a_wb_lo = av(O_W + 5 * 1024, 128, 6 * 384, BF16)
    a_wvb_hi = av(O_W, 128, 4 * 512, BF16)
    a_wvb_lo = av(O_W + 5 * 1024, 128, 4 * 512, BF16)
    O_T = O_W + 10 * 1024
    a_stg = av(O_T, 128, 768, F32)
    a_qaT_hi = av(O_T + 3 * 1024, 128, 6 * 512, BF16)
    a_qaT_lo = av(O_T + 9 * 1024, 128, 6 * 512, BF16)
    a_tb_cos = av(O_T + 3 * 1024, 64, 512, F32)
    a_tb_sin = av(O_T + 5 * 1024, 64, 512, F32)
    a_tb_frq = av(O_T + 7 * 1024, 64, 512, F32)
    a_tb_rri = av(O_T + 9 * 1024, 64, 512, I32)
    a_ckvT_hi = av(O_T, 128, 4 * 512, BF16)
    a_ckvT_lo = av(O_T + 4 * 1024, 128, 4 * 512, BF16)
    a_kstg = av(O_T + 8 * 1024, 128, 512, F32)
    a_krT = av(O_T + 10 * 1024, 64, 512, BF16)


    a_ehi = [av(O_T, 128, 512, BF16), av(O_T + 1024, 128, 512, BF16)]
    a_elo = [av(O_T + 2048, 128, 512, BF16), av(O_T + 3072, 128, 512, BF16)]
    a_bc = av(O_T + 4096, 128, 512, F32)
    a_fin = av(O_T + 6144, 128, 512, F32)
    a_ast = av(O_T + 8192, 128, 512, F32)
    a_em = av(O_T + 10240, 128, 512, F32)

    def head_phase(h):
        p.wait("sync", "t")
        for k in range(6):
            p.wait("sync", "v")
            p.dma("sync", scr_a[:, 0:384], w_qb_h[ts(k, 128), :])
            p.wait("vector", "ld")
            p.chain("vector", "v", [
                lambda e, k=k: e.tensor_scalar(
                    out=scr_a[:, 0:384], in0=scr_a[:, 0:384],
                    scalar1=gq_c[:, k:k + 1], scalar2=None, op0=ALU.mult),
                lambda e, k=k: e.tensor_copy(out=a_wb_hi[:, ts(k, 384)],
                                             in_=scr_a[:, 0:384]),
                lambda e, k=k: e.tensor_sub(a_wb_lo[:, ts(k, 384)],
                                            scr_a[:, 0:384],
                                            a_wb_hi[:, ts(k, 384)]),
            ])
        p.inst("vector", lambda e: e.memset(qn2max[:, :], 0.0), sem="v")
        p.inst("vector", lambda e: e.memset(kn2max[:, :], 0.0), sem="v")
        p.wait("tensor", "v")

        for B in range(16):
            for tt in range(4):
                p.wait("gpsimd", "t")
                p.dma("gpsimd", a_stg[:, :],
                      ag_out[B * 512 + tt * 128: B * 512 + (tt + 1) * 128,
                             0:Q_LORA], sem="gd")
                p.wait("tensor", "gd")
                p.wait("tensor", "v")
                for k in range(6):
                    tgt = ps_x[:, ts(k, 128)] if k < 4 else \
                        ps_av[0][:, ts(k - 4, 128)]
                    p.inst("tensor", lambda e, k=k, tgt=tgt: e.transpose(
                        tgt, a_stg[:, ts(k, 128)], identf[:, :]), sem="t")
                trd = p.val("t")
                p.wait("vector", "t", trd)
                for k in range(6):
                    src = ps_x[:, ts(k, 128)] if k < 4 else \
                        ps_av[0][:, ts(k - 4, 128)]
                    dsl = slice(k * 512 + tt * 128, k * 512 + (tt + 1) * 128)
                    p.chain("vector", "v", [
                        lambda e, src=src, dsl=dsl: e.tensor_copy(
                            out=a_qaT_hi[:, dsl], in_=src),
                        lambda e, src=src, dsl=dsl: e.tensor_sub(
                            a_qaT_lo[:, dsl], src, a_qaT_hi[:, dsl]),
                    ])
            qaT_done = p.val("v")
            p.wait("tensor", "v", qaT_done)
            first_n = True
            first_r = True
            for k in range(6):
                wn = slice(k * 384 + h * 192, k * 384 + h * 192 + 128)
                wr = slice(k * 384 + h * 192 + 128, k * 384 + h * 192 + 192)
                qsl = ts(k, 512)
                for (wt, qt) in ((a_wb_hi, a_qaT_hi), (a_wb_hi, a_qaT_lo),
                                 (a_wb_lo, a_qaT_hi)):
                    p.inst("tensor", lambda e, wt=wt, qt=qt, wn=wn, qsl=qsl,
                           st=first_n, sp=(k == 5 and wt is a_wb_lo):
                           e.matmul(ps_s[0][:, :], wt[:, wn], qt[:, qsl],
                                    start=st, stop=sp), sem="t")
                    first_n = False
                for (wt, qt) in ((a_wb_hi, a_qaT_hi), (a_wb_hi, a_qaT_lo),
                                 (a_wb_lo, a_qaT_hi)):
                    p.inst("tensor", lambda e, wt=wt, qt=qt, wr=wr, qsl=qsl,
                           st=first_r, sp=(k == 5 and wt is a_wb_lo):
                           e.matmul(ps_e[0][0:64, :], wt[:, wr], qt[:, qsl],
                                    start=st, stop=sp), sem="t")
                    first_r = False
            qb_done = p.val("t")
            p.wait("vector", "t", qb_done)
            p.chain("vector", "v", [
                lambda e, B=B: e.tensor_copy(out=a_qn_hi[:, ts(B, 512)],
                                             in_=ps_s[0][:, :]),
                lambda e, B=B: e.tensor_sub(a_qn_lo[:, ts(B, 512)],
                                            ps_s[0][:, :],
                                            a_qn_hi[:, ts(B, 512)]),
            ])
            p.wait("scalar", "t", qb_done)
            p.wait("scalar", "v")
            p.wait("scalar", "t")
            p.inst("scalar", lambda e: e.activation(
                out=scr_b[:, 0:512], in_=ps_s[0][:, :], func=AF.Square), sem="s")
            p.chain("vector", "v", [
                lambda e: e.tensor_copy(out=scr_a[0:64, 0:512],
                                        in_=ps_e[0][0:64, :]),
                lambda e, B=B: e.tensor_copy(out=a_krT[:, :],
                                             in_=scr_a[0:64, 0:512]),
                lambda e, B=B: e.tensor_sub(scr_c[0:64, 0:512],
                                            scr_a[0:64, 0:512], a_krT[:, :]),
                lambda e, B=B: e.tensor_copy(out=a_qr_B[0:64, ts(B, 512)],
                                             in_=scr_c[0:64, 0:512]),
            ])
            rsp = p.val("v")
            p.wait("tensor", "v", rsp)
            p.inst("tensor", lambda e: e.matmul(
                ps_e[1][0:64, :], rot[:, :], a_krT[:, :],
                start=True, stop=False), sem="t")
            p.inst("tensor", lambda e, B=B: e.matmul(
                ps_e[1][0:64, :], rot[:, :], a_qr_B[0:64, ts(B, 512)],
                start=False, stop=True), sem="t")
            rot_done = p.val("t")
            p.wait("gpsimd", "v")
            p.wait("gpsimd", "t")
            p.dma("gpsimd", a_tb_cos[:, :],
                  rope_t[0, :, B * 512:(B + 1) * 512], sem="gd")
            p.dma("gpsimd", a_tb_sin[:, :],
                  rope_t[1, :, B * 512:(B + 1) * 512], sem="gd")
            p.qbrk("gpsimd", "gd")
            p.wait("vector", "gd")
            p.wait("vector", "t", rot_done)
            p.wait("vector", "s")
            p.chain("vector", "v", [
                lambda e: e.tensor_mul(scr_b[0:64, 512:1024],
                                       scr_a[0:64, 0:512], a_tb_cos[:, :]),
                lambda e: e.tensor_mul(scr_c[0:64, 0:512], ps_e[1][0:64, :],
                                       a_tb_sin[:, :]),
                lambda e: e.tensor_add(scr_a[0:64, 0:512],
                                       scr_b[0:64, 512:1024],
                                       scr_c[0:64, 0:512]),
                lambda e, B=B: e.tensor_copy(out=a_qr_A[0:64, ts(B, 512)],
                                             in_=scr_a[0:64, 0:512]),
                lambda e, B=B: e.tensor_sub(scr_c[0:64, 0:512],
                                            scr_a[0:64, 0:512],
                                            a_qr_A[0:64, ts(B, 512)]),
                lambda e, B=B: e.tensor_copy(out=a_qr_B[0:64, ts(B, 512)],
                                             in_=scr_c[0:64, 0:512]),
            ])
            p.wait("scalar", "v")
            p.inst("scalar", lambda e: e.activation(
                out=scr_b[0:64, 512:1024], in_=scr_a[0:64, 0:512],
                func=AF.Square), sem="s")
            sq2 = p.val("s")
            p.wait("tensor", "s", sq2)
            p.inst("tensor", lambda e: e.matmul(
                ps_d[0:1, :], ones_f[:, :], scr_b[:, 0:512],
                start=True, stop=False), sem="t")
            p.inst("tensor", lambda e: e.matmul(
                ps_d[0:1, :], ones_f[0:64, :], scr_b[0:64, 512:1024],
                start=False, stop=True), sem="t")
            nrm = p.val("t")
            p.wait("vector", "t", nrm)
            p.chain("vector", "v", [
                lambda e: e.tensor_reduce(m_tmp[0:1, 0:1], ps_d[0:1, :],
                                          axis=mybir.AxisListType.X,
                                          op=ALU.max),
                lambda e: e.tensor_max(qn2max[:, :], qn2max[:, :],
                                       m_tmp[0:1, 0:1]),
            ])
        qfin = p.val("v")
        p.wait("gpsimd", "v", qfin)
        p.dma("gpsimd", a_qr_A[64:128, :], a_qr_A[0:64, :], sem="gd")
        p.qbrk("gpsimd", "gd")
        p.dma("gpsimd", a_qr_B[64:128, :], a_qr_B[0:64, :], sem="gd")
        p.qbrk("gpsimd", "gd")

        p.wait("sync", "t")
        for k in range(4):
            p.wait("sync", "v")
            p.dma("sync", scr_a[:, 0:512], w_kvb_h[ts(k, 128), :])
            p.wait("vector", "ld")
            p.chain("vector", "v", [
                lambda e, k=k: e.tensor_scalar(
                    out=scr_a[:, 0:512], in0=scr_a[:, 0:512],
                    scalar1=gkv_c[:, k:k + 1], scalar2=None, op0=ALU.mult),
                lambda e, k=k: e.tensor_copy(out=a_wvb_hi[:, ts(k, 512)],
                                             in_=scr_a[:, 0:512]),
                lambda e, k=k: e.tensor_sub(a_wvb_lo[:, ts(k, 512)],
                                            scr_a[:, 0:512],
                                            a_wvb_hi[:, ts(k, 512)]),
            ])
        p.wait("tensor", "v")

        for B in range(16):
            for tt in range(4):
                p.wait("gpsimd", "t")
                p.dma("gpsimd", a_kstg[:, :],
                      ag_out[B * 512 + tt * 128: B * 512 + (tt + 1) * 128,
                             Q_LORA:Q_LORA + KV_LORA], sem="gd")
                p.wait("tensor", "gd")
                p.wait("tensor", "v")
                for k in range(4):
                    p.inst("tensor", lambda e, k=k: e.transpose(
                        ps_x[:, ts(k, 128)], a_kstg[:, ts(k, 128)],
                        identf[:, :]), sem="t")
                trd = p.val("t")
                p.wait("vector", "t", trd)
                for k in range(4):
                    dsl = slice(k * 512 + tt * 128, k * 512 + (tt + 1) * 128)
                    p.chain("vector", "v", [
                        lambda e, k=k, dsl=dsl: e.tensor_copy(
                            out=a_ckvT_hi[:, dsl], in_=ps_x[:, ts(k, 128)]),
                        lambda e, k=k, dsl=dsl: e.tensor_sub(
                            a_ckvT_lo[:, dsl], ps_x[:, ts(k, 128)],
                            a_ckvT_hi[:, dsl]),
                    ])
            ckvT_done = p.val("v")
            p.wait("tensor", "v", ckvT_done)
            first = True
            for k in range(4):
                wsl = slice(k * 512 + h * 256, k * 512 + h * 256 + 128)
                csl = ts(k, 512)
                for (wt, ct) in ((a_wvb_hi, a_ckvT_hi), (a_wvb_hi, a_ckvT_lo),
                                 (a_wvb_lo, a_ckvT_hi)):
                    p.inst("tensor", lambda e, wt=wt, ct=ct, wsl=wsl, csl=csl,
                           st=first, sp=(k == 3 and wt is a_wvb_lo):
                           e.matmul(ps_s[0][:, :], wt[:, wsl], ct[:, csl],
                                    start=st, stop=sp), sem="t")
                    first = False
            knd = p.val("t")
            p.wait("vector", "t", knd)
            p.chain("vector", "v", [
                lambda e, B=B: e.tensor_copy(out=a_kn_hi[:, ts(B, 512)],
                                             in_=ps_s[0][:, :]),
                lambda e, B=B: e.tensor_sub(a_kn_lo[:, ts(B, 512)],
                                            ps_s[0][:, :],
                                            a_kn_hi[:, ts(B, 512)]),
            ])
            p.wait("scalar", "t", knd)
            p.wait("scalar", "v")
            p.inst("scalar", lambda e: e.activation(
                out=scr_b[:, 0:512], in_=ps_s[0][:, :], func=AF.Square), sem="s")
            for tt in range(4):
                p.wait("gpsimd", "t")
                p.dma("gpsimd", a_kstg[:, 0:64],
                      ag_out[B * 512 + tt * 128: B * 512 + (tt + 1) * 128,
                             Q_LORA + KV_LORA:AG_COLS], sem="gd")
                p.wait("tensor", "gd")
                p.wait("tensor", "v")
                p.wait("tensor", "s")
                p.inst("tensor", lambda e: e.transpose(
                    ps_e[1][0:64, 0:128], a_kstg[:, 0:64], identf[:, :]),
                    sem="t")
                trd = p.val("t")
                p.wait("vector", "t", trd)
                ksl = slice(B * 512 + tt * 128, B * 512 + (tt + 1) * 128)
                p.chain("vector", "v", [
                    lambda e, ksl=ksl: e.tensor_copy(
                        out=a_kr_P[0:64, ksl], in_=ps_e[1][0:64, 0:128]),
                    lambda e, ksl=ksl: e.tensor_sub(
                        scr_c[0:64, 0:128], ps_e[1][0:64, 0:128],
                        a_kr_P[0:64, ksl]),
                    lambda e, tt=tt: e.tensor_copy(
                        out=a_krT[:, ts(tt, 128)], in_=scr_c[0:64, 0:128]),
                ])
                p.wait("scalar", "t", trd)
                p.wait("scalar", "v")
                p.inst("scalar", lambda e, tt=tt: e.activation(
                    out=scr_b[0:64, 512 + tt * 128: 512 + (tt + 1) * 128],
                    in_=ps_e[1][0:64, 0:128], func=AF.Square), sem="s")
            sqd = p.val("s")
            p.wait("tensor", "s", sqd)
            p.inst("tensor", lambda e: e.matmul(
                ps_d[0:1, :], ones_f[:, :], scr_b[:, 0:512],
                start=True, stop=False), sem="t")
            p.inst("tensor", lambda e: e.matmul(
                ps_d[0:1, :], ones_f[0:64, :], scr_b[0:64, 512:1024],
                start=False, stop=True), sem="t")
            nrm = p.val("t")
            p.wait("vector", "t", nrm)
            p.chain("vector", "v", [
                lambda e: e.tensor_reduce(m_tmp[0:1, 1:2], ps_d[0:1, :],
                                          axis=mybir.AxisListType.X,
                                          op=ALU.max),
                lambda e: e.tensor_max(kn2max[:, :], kn2max[:, :],
                                       m_tmp[0:1, 1:2]),
            ])
            krtd = p.val("v")
            p.wait("gpsimd", "v", krtd)
            p.dma("gpsimd", a_kr_P[64:128, ts(B, 512)], a_krT[:, :], sem="gd")
            for tt in range(4):
                p.wait("tensor", "v")
                first = True
                for k in range(4):
                    csl = slice(k * 512 + tt * 128, k * 512 + (tt + 1) * 128)
                    wsl = slice(k * 512 + h * 256 + 128,
                                k * 512 + h * 256 + 256)
                    for (ct, wt) in ((a_ckvT_hi, a_wvb_hi),
                                     (a_ckvT_hi, a_wvb_lo),
                                     (a_ckvT_lo, a_wvb_hi)):
                        p.inst("tensor", lambda e, ct=ct, wt=wt, csl=csl,
                               wsl=wsl, st=first,
                               sp=(k == 3 and ct is a_ckvT_lo):
                               e.matmul(ps_e[0][:, 0:128], ct[:, csl],
                                        wt[:, wsl], start=st, stop=sp),
                               sem="t")
                        first = False
                vd = p.val("t")
                p.wait("vector", "t", vd)
                vsl = slice((B * 4 + tt) * 128, (B * 4 + tt + 1) * 128)
                p.chain("vector", "v", [
                    lambda e, vsl=vsl: e.tensor_copy(out=a_v_hi[:, vsl],
                                                     in_=ps_e[0][:, 0:128]),
                    lambda e, vsl=vsl: e.tensor_sub(a_v_lo[:, vsl],
                                                    ps_e[0][:, 0:128],
                                                    a_v_hi[:, vsl]),
                ])

        p.wait("vector", "v")
        p.chain("vector", "v", [
            lambda e: e.tensor_mul(m_tmp[0:1, 2:3], qn2max[:, :], kn2max[:, :]),
        ])
        p.wait("scalar", "v")
        p.inst("scalar", lambda e: e.activation(
            out=m_tmp[0:1, 3:4], in_=m_tmp[0:1, 2:3], func=AF.Sqrt), sem="s")
        p.wait("vector", "s")
        p.chain("vector", "v", [
            lambda e: e.tensor_scalar(out=m_tmp[0:1, 0:1], in0=m_tmp[0:1, 3:4],
                                      scalar1=-QK_SCALE, scalar2=None,
                                      op0=ALU.mult),
        ])
        p.wait("tensor", "v")
        p.inst("tensor", lambda e: e.matmul(
            ps_x[:, 0:1], ones_row[0:1, :], m_tmp[0:1, 0:1],
            start=True, stop=True), sem="t")
        p.wait("vector", "t")
        p.inst("vector", lambda e: e.tensor_copy(out=neg_m[:, :],
                                                 in_=ps_x[:, 0:1]), sem="v")
        if _DEBUG:
            p.wait("sync", "v")
            p.dma("sync", dbg["d_m"][:, :], m_tmp[:, :], sem="st")
        p.barrier(("tensor", "vector", "scalar"))

        exp_done = {}
        split_done = {}
        avden_done = {}
        fin_done = {}
        rec_done = {}
        flat = []
        for qc in range(16):
            for kc in range(4 * qc + 4):
                flat.append((qc, kc))
        nj = len(flat)

        def emit_scores(j):
            qc, kc = flat[j]
            jj = j % 2
            qsl = ts(qc, 512)
            ksl = ts(kc, 128)
            if j >= 2:
                p.wait("tensor", "s", exp_done[j - 2])
            for i, (kt, qt) in enumerate((
                    (a_kn_hi, a_qn_hi), (a_kn_hi, a_qn_lo),
                    (a_kn_lo, a_qn_hi))):
                p.inst("tensor", lambda e, kt=kt, qt=qt, ksl=ksl, qsl=qsl,
                       st=(i == 0), jj=jj:
                       e.matmul(ps_s[jj][:, :], kt[:, ksl], qt[:, qsl],
                                start=st, stop=False), sem="t")
            p.inst("tensor", lambda e, ksl=ksl, qsl=qsl, jj=jj:
                   e.matmul(ps_s[jj][:, :], a_kr_P[:, ksl], a_qr_A[:, qsl],
                            start=False, stop=False), sem="t")
            return p.inst("tensor", lambda e, ksl=ksl, qsl=qsl, jj=jj:
                          e.matmul(ps_s[jj][:, :], a_kr_P[:, ksl],
                                   a_qr_B[:, qsl],
                                   start=False, stop=True), sem="t")

        def emit_exp_split(j, sc_done):
            qc, kc = flat[j]
            jj = j % 2
            p.wait("scalar", "t", sc_done)
            if j >= 2:
                p.wait("scalar", "v", split_done[j - 2])
            p.inst("scalar", lambda e, jj=jj: e.activation(
                out=a_ehi[jj][:, :], in_=ps_s[jj][:, :], func=AF.Exp,
                bias=neg_m[:, :], scale=QK_SCALE), sem="s")
            exp_done[j] = p.inst("scalar", lambda e, jj=jj: e.activation(
                out=ps_e[jj][:, :], in_=ps_s[jj][:, :], func=AF.Exp,
                bias=neg_m[:, :], scale=QK_SCALE), sem="s")
            p.wait("vector", "s", exp_done[j])
            if kc >= 4 * qc:
                didx = kc - 4 * qc
                p.chain("vector", "v", [
                    lambda e, jj=jj, didx=didx: e.tensor_mul(
                        a_ehi[jj][:, :], a_ehi[jj][:, :],
                        masks[:, ts(didx, 512)]),
                    lambda e, jj=jj, didx=didx: e.tensor_mul(
                        a_em[:, :], ps_e[jj][:, :], masks[:, ts(didx, 512)]),
                    lambda e, jj=jj: e.tensor_sub(a_elo[jj][:, :],
                                                  a_em[:, :],
                                                  a_ehi[jj][:, :]),
                ])
            else:
                p.inst("vector", lambda e, jj=jj: e.tensor_sub(
                    a_elo[jj][:, :], ps_e[jj][:, :], a_ehi[jj][:, :]), sem="v")
            split_done[j] = p.val("v")

        def emit_avden(j):
            qc, kc = flat[j]
            jj = j % 2
            qq = qc % 2
            nkc = 4 * qc + 4
            ksl = ts(kc, 128)
            p.wait("tensor", "v", split_done[j])
            if kc == 0 and qc >= 2:
                p.wait("tensor", "v", fin_done[qc - 2])
            if kc == 0 and qc >= 1:
                p.wait("tensor", "v", rec_done[qc - 1])
            for i, (vt, et) in enumerate((
                    (a_v_hi, a_ehi[jj]), (a_v_lo, a_ehi[jj]),
                    (a_v_hi, a_elo[jj]))):
                p.inst("tensor", lambda e, vt=vt, et=et, ksl=ksl, qq=qq,
                       st=(kc == 0 and i == 0),
                       sp=(kc == nkc - 1 and i == 2):
                       e.matmul(ps_av[qq][:, :], vt[:, ksl], et[:, :],
                                start=st, stop=sp), sem="t")
            avden_done[j] = p.inst(
                "tensor", lambda e, jj=jj, st=(kc == 0), sp=(kc == nkc - 1):
                e.matmul(ps_d[0:1, :], ones_b[:, :], a_ehi[jj][:, :],
                         start=st, stop=sp), sem="t")

        def emit_finalize(qc):
            qq = qc % 2
            p.wait("vector", "t", avden_done[sum(4 * q + 4 for q in range(qc + 1)) - 1])
            p.inst("vector", lambda e: e.reciprocal(recip_sb[0:1, :],
                                                    ps_d[0:1, :]), sem="v")
            rec_done[qc] = p.val("v")
            p.wait("tensor", "v", rec_done[qc])
            bc_mm = p.inst("tensor", lambda e: e.matmul(
                ps_x[:, :], ones_row[0:1, :], recip_sb[0:1, :],
                start=True, stop=True), sem="t")
            p.wait("vector", "t", bc_mm)
            p.chain("vector", "v", [
                lambda e: e.tensor_copy(out=a_bc[:, :], in_=ps_x[:, :]),
                lambda e, qq=qq: e.tensor_mul(a_fin[:, :], ps_av[qq][:, :],
                                              a_bc[:, :]),
            ])
            find = p.val("v")
            p.wait("tensor", "v", find)
            for q4 in range(4):
                p.inst("tensor", lambda e, q4=q4: e.transpose(
                    ps_x[:, ts(q4, 128)], a_fin[:, ts(q4, 128)], identf[:, :]),
                    sem="t")
            trd = p.val("t")
            p.wait("vector", "t", trd)
            p.wait("vector", "a2")
            p.inst("vector", lambda e: e.tensor_copy(out=a_ast[:, :],
                                                     in_=ps_x[:, :]), sem="v")
            fin_done[qc] = p.val("v")
            p.wait("sync", "v", fin_done[qc])
            for q4 in range(4):
                p.dma("sync", a2a_in[qc * 512 + q4 * 128:
                                     qc * 512 + (q4 + 1) * 128,
                                     h * D_V:(h + 1) * D_V],
                      a_ast[:, ts(q4, 128)], sem="a2")

        sc_done = {}
        sc_done[0] = emit_scores(0)
        for j in range(nj):
            emit_exp_split(j, sc_done[j])
            if j + 1 < nj:
                sc_done[j + 1] = emit_scores(j + 1)
            emit_avden(j)
            qc, kc = flat[j]
            if kc == 4 * qc + 3:
                emit_finalize(qc)

        if _DEBUG and h == 0:
            p.wait("sync", "v")
            p.wait("sync", "t")
            p.dma("sync", dbg["d_q"][0:128, :], a_qn_hi[:, :], sem="st")
            p.dma("sync", dbg["d_q"][128:256, :], a_qn_lo[:, :], sem="st")
            p.dma("sync", dbg["d_q"][256:384, :], a_qr_A[:, :], sem="st")
            p.dma("sync", dbg["d_q"][384:512, :], a_qr_B[:, :], sem="st")
            p.dma("sync", dbg["d_k"][0:128, :], a_kn_hi[:, :], sem="st")
            p.dma("sync", dbg["d_k"][128:256, :], a_kn_lo[:, :], sem="st")
            p.dma("sync", dbg["d_k"][256:384, :], a_kr_P[:, :], sem="st")
            p.dma("sync", dbg["d_v"][0:128, :], a_v_hi[:, :], sem="st")
            p.dma("sync", dbg["d_v"][128:256, :], a_v_lo[:, :], sem="st")
        p.barrier()

    for h in range(HL):
        head_phase(h)
        if stop_after == f"head{h}":
            _finish_early()
            return nc, dbg

    if _DEBUG:
        p.wait("sync", "a2")
        p.dma("sync", dbg["d_attn"][:, :], a2a_in[:, :], sem="st")

    p.wait("gpsimd", "a2")
    p.inst("gpsimd", lambda e: e.collective_compute(
        "AllToAll", ALU.bypass, replica_groups=[list(range(NCORES))],
        ins=[a2a_in[:, :]], outs=[a2a_out[:, :]]), sem="cc")
    p.wait("sync", "cc")
    p.wait("tensor", "cc")
    p.wait("vector", "cc")
    p.wait("gpsimd", "cc")

    if stop_after == "a2a":
        _finish_early()
        return nc, dbg

    a_at_hi = av(0, 128, 16 * 1024, BF16)
    a_at_lo = av(32 * 1024, 128, 16 * 1024, BF16)
    a_xpre = av(64 * 1024, 128, 16 * 1024, F32)
    a_wo_f = [av(128 * 1024, 128, 2048, F32), av(136 * 1024, 128, 2048, F32)]
    a_wo_hi = [av(144 * 1024, 128, 2048, BF16), av(148 * 1024, 128, 2048, BF16)]
    a_wo_lo = [av(152 * 1024, 128, 2048, BF16), av(156 * 1024, 128, 2048, BF16)]
    a_xrs = [av(160 * 1024, 128, 1024, F32), av(164 * 1024, 128, 1024, F32)]
    a_p4st = av(168 * 1024, 128, 128, F32)
    a_x1_hi = av(0, 128, 16 * 1024, BF16)
    a_x1_lo = av(32 * 1024, 128, 16 * 1024, BF16)

    for fc in range(16):
        sblk, hl = fc // 2, fc % 2
        for tt in range(8):
            p.wait("gpsimd", "t")
            p.dma("gpsimd", a_p4st[:, :],
                  a2a_out[sblk * 1024 + tt * 128: sblk * 1024 + (tt + 1) * 128,
                          hl * 128:(hl + 1) * 128], sem="gd")
            p.wait("tensor", "gd")
            p.wait("tensor", "v")
            p.inst("tensor", lambda e: e.transpose(
                ps_x[:, 0:128], a_p4st[:, :], identf[:, :]), sem="t")
            trd = p.val("t")
            p.wait("vector", "t", trd)
            dsl = slice(fc * 1024 + tt * 128, fc * 1024 + (tt + 1) * 128)
            p.chain("vector", "v", [
                lambda e, dsl=dsl: e.tensor_copy(out=a_at_hi[:, dsl],
                                                 in_=ps_x[:, 0:128]),
                lambda e, dsl=dsl: e.tensor_sub(a_at_lo[:, dsl],
                                                ps_x[:, 0:128],
                                                a_at_hi[:, dsl]),
            ])
    at_done = p.val("v")
    p.wait("tensor", "v", at_done)

    wo_done = {}
    for oc in range(16):
        b = oc % 2
        if oc >= 2:
            p.wait("sync", "v", wo_done[oc - 2][1])
        p.dma("sync", a_wo_f[b][:, :],
              w_o[:, ts(oc, 128)].rearrange("(a p) c -> p a c", p=128))
        p.dma("sync", a_xrs[b][:, :], x_t[ts(oc, 128), :])
        p.qbrk("sync", "ld")
        p.wait("vector", "ld")
        if oc >= 2:
            p.wait("vector", "t", wo_done[oc - 2][0])
        for k in range(16):
            sl = ts(k, 128)
            p.chain("vector", "v", [
                lambda e, b=b, sl=sl: e.tensor_copy(out=a_wo_hi[b][:, sl],
                                                    in_=a_wo_f[b][:, sl]),
                lambda e, b=b, sl=sl: e.tensor_sub(a_wo_lo[b][:, sl],
                                                   a_wo_f[b][:, sl],
                                                   a_wo_hi[b][:, sl]),
            ])
        spd = p.val("v")
        p.wait("tensor", "v", spd)
        for k in range(16):
            for (wt, at) in ((a_wo_hi[b], a_at_hi), (a_wo_hi[b], a_at_lo),
                             (a_wo_lo[b], a_at_hi)):
                for nt in range(2):
                    asl = slice(k * 1024 + nt * 512, k * 1024 + nt * 512 + 512)
                    p.inst("tensor", lambda e, wt=wt, at=at, k=k, asl=asl,
                           pst=ps_s[nt],
                           st=(k == 0 and wt is a_wo_hi[b] and at is a_at_hi),
                           sp=(k == 15 and wt is a_wo_lo[b]):
                           e.matmul(pst[:, :], wt[:, ts(k, 128)], at[:, asl],
                                    start=st, stop=sp), sem="t")
        mmd = p.val("t")
        p.wait("vector", "t", mmd)
        for nt in range(2):
            dsl = slice(oc * 1024 + nt * 512, oc * 1024 + nt * 512 + 512)
            p.inst("vector", lambda e, nt=nt, b=b, dsl=dsl: e.tensor_add(
                a_xpre[:, dsl], ps_s[nt][:, :], a_xrs[b][:, ts(nt, 512)]),
                sem="v")
        wo_done[oc] = (mmd, p.val("v"))

    def layernorm(src, dst_hi, dst_lo, out_dma=None):
        for k in range(16):
            for nt in range(2):
                sl = slice(k * 1024 + nt * 512, k * 1024 + nt * 512 + 512)
                p.wait("vector", "t")
                p.chain("vector", "v", [
                    lambda e, sl=sl: e.tensor_copy(out=lnhi[:, :],
                                                   in_=src[:, sl]),
                    lambda e, sl=sl: e.tensor_sub(lnlo[:, :], src[:, sl],
                                                  lnhi[:, :]),
                ])
                sd = p.val("v")
                p.wait("tensor", "v", sd)
                for i, t_ in enumerate((lnhi, lnlo)):
                    tgt = ps_d if nt == 0 else ps_e[0]
                    p.inst("tensor", lambda e, t_=t_, tgt=tgt, k=k, i=i:
                           e.matmul(tgt[0:1, :], ones_b[:, :], t_[:, :],
                                    start=(k == 0 and i == 0),
                                    stop=(k == 15 and i == 1)), sem="t")
                p.wait("scalar", "v", sd)
                p.wait("scalar", "t")
                p.inst("scalar", lambda e, sl=sl, nt=nt: e.activation(
                    out=scr_c[:, ts(nt, 512)], in_=src[:, sl],
                    func=AF.Square), sem="s")
                sqd = p.val("s")
                p.wait("vector", "s", sqd)
                p.wait("vector", "t")
                p.chain("vector", "v", [
                    lambda e, nt=nt: e.tensor_copy(out=lnhi[:, :],
                                                   in_=scr_c[:, ts(nt, 512)]),
                    lambda e, nt=nt: e.tensor_sub(lnlo[:, :],
                                                  scr_c[:, ts(nt, 512)],
                                                  lnhi[:, :]),
                ])
                sd2 = p.val("v")
                p.wait("tensor", "v", sd2)
                for i, t_ in enumerate((lnhi, lnlo)):
                    tgt = ps_s[0] if nt == 0 else ps_e[1]
                    p.inst("tensor", lambda e, t_=t_, tgt=tgt, k=k, i=i:
                           e.matmul(tgt[0:1, :], ones_b[:, :], t_[:, :],
                                    start=(k == 0 and i == 0),
                                    stop=(k == 15 and i == 1)), sem="t")
        std = p.val("t")
        p.wait("vector", "t", std)
        p.chain("vector", "v", [
            lambda e: e.tensor_scalar(out=stat1[0:1, 0:512], in0=ps_d[0:1, :],
                                      scalar1=1.0 / D, scalar2=None,
                                      op0=ALU.mult),
            lambda e: e.tensor_scalar(out=stat1[0:1, 512:1024],
                                      in0=ps_e[0][0:1, :], scalar1=1.0 / D,
                                      scalar2=None, op0=ALU.mult),
            lambda e: e.tensor_scalar(out=stat2[0:1, 0:512],
                                      in0=ps_s[0][0:1, :],
                                      scalar1=1.0 / D, scalar2=None,
                                      op0=ALU.mult),
            lambda e: e.tensor_scalar(out=stat2[0:1, 512:1024],
                                      in0=ps_e[1][0:1, :], scalar1=1.0 / D,
                                      scalar2=None, op0=ALU.mult),
            lambda e: e.tensor_mul(alpha[0:1, :], stat1[0:1, :], stat1[0:1, :]),
            lambda e: e.tensor_sub(stat2[0:1, :], stat2[0:1, :], alpha[0:1, :]),
        ])
        p.wait("vector", "v")
        p.chain("vector", "v", [
            lambda e: e.tensor_scalar(out=stat2[0:1, :], in0=stat2[0:1, :],
                                      scalar1=LN_EPS, scalar2=None,
                                      op0=ALU.add),
            lambda e: e.reciprocal(stat2[0:1, :], stat2[0:1, :]),
        ])
        p.wait("scalar", "v")
        p.inst("scalar", lambda e: e.activation(
            out=stat2[0:1, :], in_=stat2[0:1, :], func=AF.Sqrt), sem="s")
        rsd = p.val("s")
        p.wait("tensor", "s", rsd)
        for nt in range(2):
            mtgt = ps_d if nt == 0 else ps_e[0]
            rtgt = ps_x if nt == 0 else ps_e[1]
            p.inst("tensor", lambda e, nt=nt, mtgt=mtgt: e.matmul(
                mtgt[:, :], ones_row[0:1, :], stat1[0:1, ts(nt, 512)],
                start=True, stop=True), sem="t")
            p.inst("tensor", lambda e, nt=nt, rtgt=rtgt: e.matmul(
                rtgt[:, :], ones_row[0:1, :], stat2[0:1, ts(nt, 512)],
                start=True, stop=True), sem="t")
        bcd = p.val("t")
        p.wait("vector", "t", bcd)
        for k in range(16):
            for nt in range(2):
                sl = slice(k * 1024 + nt * 512, k * 1024 + nt * 512 + 512)
                mean_b = ps_d if nt == 0 else ps_e[0]
                rstd_b = ps_x if nt == 0 else ps_e[1]
                p.wait("vector", "v")
                p.wait("vector", "st")
                ops = [
                    lambda e, sl=sl, mean_b=mean_b: e.tensor_sub(
                        scr_b[:, 0:512], src[:, sl], mean_b[:, :]),
                    lambda e, sl=sl, rstd_b=rstd_b: e.tensor_mul(
                        scr_b[:, 512:1024], scr_b[:, 0:512], rstd_b[:, :]),
                ]
                if dst_hi is not None:
                    ops += [
                        lambda e, sl=sl: e.tensor_copy(out=dst_hi[:, sl],
                                                       in_=scr_b[:, 512:1024]),
                        lambda e, sl=sl: e.tensor_sub(dst_lo[:, sl],
                                                      scr_b[:, 512:1024],
                                                      dst_hi[:, sl]),
                    ]
                p.chain("vector", "v", ops)
                if out_dma is not None:
                    od = p.val("v")
                    p.wait("sync", "v", od)
                    p.dma("sync", out_dma(k, nt), scr_b[:, 512:1024], sem="st")

    if stop_after == "wo":
        _finish_early()
        return nc, dbg

    layernorm(a_xpre, a_x1_hi, a_x1_lo)
    if _DEBUG:
        p.wait("sync", "v")
        for k in range(16):
            p.dma("sync", dbg["d_x1"][ts(k, 128), :], a_xpre[:, ts(k, 1024)],
                  sem="st")
    p.barrier()

    if stop_after == "ln1":
        _finish_early()
        return nc, dbg

    a_fu_hi = av(64 * 1024, 128, 6 * 1024, BF16)
    a_fu_lo = av(76 * 1024, 128, 6 * 1024, BF16)
    a_dacc = av(88 * 1024, 128, 16 * 1024, F32)
    a_gw = [av(152 * 1024, 128, 2048, BF16), av(156 * 1024, 128, 2048, BF16)]
    a_uw = [av(160 * 1024, 128, 2048, BF16), av(164 * 1024, 128, 2048, BF16)]
    a_dw = [av(168 * 1024, 128, 768, BF16), av(170 * 1024, 128, 768, BF16)]

    gu_done = {}
    for e8 in range(8):
        for fc6 in range(6):
            fc = e8 * 6 + fc6
            b = fc % 2
            if fc >= 2:
                p.wait("sync", "t", gu_done[fc - 2])
            p.dma("sync", a_gw[b][:, :],
                  wq_gate[:, ts(fc, 128)].rearrange("(a p) c -> p a c", p=128))
            p.dma("sync", a_uw[b][:, :],
                  wq_up[:, ts(fc, 128)].rearrange("(a p) c -> p a c", p=128))
            p.qbrk("sync", "ld")
            p.wait("tensor", "ld")
            p.wait("tensor", "v")
            p.wait("tensor", "s")
            for k in range(16):
                for (psp, wsrc_b) in ((ps_s, a_gw[b]), (ps_e, a_uw[b])):
                    for nt in range(2):
                        xsl = slice(k * 1024 + nt * 512,
                                    k * 1024 + nt * 512 + 512)
                        for xt_ in (a_x1_hi, a_x1_lo):
                            p.inst("tensor", lambda e, wsrc_b=wsrc_b, k=k,
                                   xt_=xt_, xsl=xsl, pst=psp[nt],
                                   st=(k == 0 and xt_ is a_x1_hi),
                                   sp=(k == 15 and xt_ is a_x1_lo):
                                   e.matmul(pst[:, :], wsrc_b[:, ts(k, 128)],
                                            xt_[:, xsl], start=st, stop=sp),
                                   sem="t")
            gu_done[fc] = p.val("t")
            p.wait("scalar", "t", gu_done[fc])
            p.wait("scalar", "v")
            for nt in range(2):
                p.inst("scalar", lambda e, nt=nt, fc=fc: e.activation(
                    out=scr_b[:, ts(nt, 512)], in_=ps_s[nt][:, :],
                    func=AF.Silu, scale=gs_c[:, fc:fc + 1]), sem="s")
            sid = p.val("s")
            p.wait("vector", "s", sid)
            p.wait("vector", "t", gu_done[fc])
            for nt in range(2):
                fsl = slice(fc6 * 1024 + nt * 512, fc6 * 1024 + nt * 512 + 512)
                p.chain("vector", "v", [
                    lambda e, nt=nt, fc=fc: e.scalar_tensor_tensor(
                        out=scr_c[:, ts(nt, 512)], in0=ps_e[nt][:, :],
                        scalar=us_c[:, fc:fc + 1], in1=scr_b[:, ts(nt, 512)],
                        op0=ALU.mult, op1=ALU.mult),
                    lambda e, nt=nt, fsl=fsl: e.tensor_copy(
                        out=a_fu_hi[:, fsl], in_=scr_c[:, ts(nt, 512)]),
                    lambda e, nt=nt, fsl=fsl: e.tensor_sub(
                        a_fu_lo[:, fsl], scr_c[:, ts(nt, 512)],
                        a_fu_hi[:, fsl]),
                ])
        fu_done = p.val("v")
        p.wait("tensor", "v", fu_done)
        dw_done = {}
        for oc in range(16):
            b = oc % 2
            if oc >= 2:
                p.wait("sync", "t", dw_done[oc - 2])
            p.dma("sync", a_dw[b][:, :],
                  wq_down[e8 * 768:(e8 + 1) * 768, ts(oc, 128)]
                  .rearrange("(a p) c -> p a c", p=128))
            p.qbrk("sync", "ld")
            p.wait("tensor", "ld")
            p.wait("tensor", "v")
            for k in range(6):
                for nt in range(2):
                    fsl = slice(k * 1024 + nt * 512, k * 1024 + nt * 512 + 512)
                    for ft in (a_fu_hi, a_fu_lo):
                        p.inst("tensor", lambda e, b=b, k=k, ft=ft, fsl=fsl,
                               pst=ps_av[nt],
                               st=(k == 0 and ft is a_fu_hi),
                               sp=(k == 5 and ft is a_fu_lo):
                               e.matmul(pst[:, :], a_dw[b][:, ts(k, 128)],
                                        ft[:, fsl], start=st, stop=sp),
                               sem="t")
            dw_done[oc] = p.val("t")
            p.wait("vector", "t", dw_done[oc])
            for nt in range(2):
                dsl = slice(oc * 1024 + nt * 512, oc * 1024 + nt * 512 + 512)
                if e8 == 0:
                    p.inst("vector", lambda e, nt=nt, dsl=dsl: e.tensor_copy(
                        out=a_dacc[:, dsl], in_=ps_av[nt][:, :]), sem="v")
                else:
                    p.inst("vector", lambda e, nt=nt, dsl=dsl: e.tensor_add(
                        a_dacc[:, dsl], a_dacc[:, dsl], ps_av[nt][:, :]),
                        sem="v")
        p.barrier(("tensor", "vector", "scalar", "sync"))

    for k in range(16):
        for nt in range(2):
            sl = slice(k * 1024 + nt * 512, k * 1024 + nt * 512 + 512)
            p.chain("vector", "v", [
                lambda e, sl=sl, k=k: e.scalar_tensor_tensor(
                    out=a_dacc[:, sl], in0=a_dacc[:, sl],
                    scalar=ds_c[:, k:k + 1], in1=a_x1_hi[:, sl],
                    op0=ALU.mult, op1=ALU.add),
                lambda e, sl=sl: e.tensor_add(a_dacc[:, sl], a_dacc[:, sl],
                                              a_x1_lo[:, sl]),
            ])
    p.barrier(("vector", "tensor", "scalar"))

    def out_dma(k, nt):
        return out_t[ts(k, 128), nt * 512:(nt + 1) * 512]

    layernorm(a_dacc, None, None, out_dma=out_dma)

    p.wait("gpsimd", "st")
    with allow_ici_all_to_all():
        p.build()
    return nc, dbg



_BUILT = None


def kernel(L0_attn_x, L0_attn_w_qa, L0_attn_q_a_norm_w, L0_attn_w_qb,
           L0_attn_w_kva, L0_attn_kv_a_norm_w, L0_attn_w_kvb, L0_attn_w_o,
           L0_ffn_gate_w, L0_ffn_up_w, L0_ffn_down_w, L0_ffn_gate_scale,
           L0_ffn_up_scale, L0_ffn_down_scale):
    global _BUILT, LAST_DEBUG
    if _BUILT is None:
        _BUILT = build_nc()
    nc, dbg = _BUILT

    x = np.ascontiguousarray(np.asarray(L0_attn_x, dtype=np.float32)[0])
    com = {
        "w_qa": np.ascontiguousarray(L0_attn_w_qa, dtype=np.float32),
        "w_kva": np.ascontiguousarray(L0_attn_w_kva, dtype=np.float32),
        "w_o": np.ascontiguousarray(L0_attn_w_o, dtype=np.float32),
        "gate_w": np.ascontiguousarray(L0_ffn_gate_w, dtype=np.float32),
        "up_w": np.ascontiguousarray(L0_ffn_up_w, dtype=np.float32),
        "down_w": np.ascontiguousarray(L0_ffn_down_w, dtype=np.float32),
        "g_q": np.ascontiguousarray(
            np.asarray(L0_attn_q_a_norm_w, dtype=np.float32).reshape(-1, 128).T),
        "g_kv": np.ascontiguousarray(
            np.asarray(L0_attn_kv_a_norm_w, dtype=np.float32).reshape(-1, 128).T),
        "gate_s": np.ascontiguousarray(
            np.asarray(L0_ffn_gate_scale, dtype=np.float32).reshape(-1, 128).T),
        "up_s": np.ascontiguousarray(
            np.asarray(L0_ffn_up_scale, dtype=np.float32).reshape(-1, 128).T),
        "down_s": np.ascontiguousarray(
            np.asarray(L0_ffn_down_scale, dtype=np.float32).reshape(-1, 128).T),
    }
    wqb = np.asarray(L0_attn_w_qb, dtype=np.float32)
    wkvb = np.asarray(L0_attn_w_kvb, dtype=np.float32)
    in_maps = []
    for c in range(NCORES):
        m = dict(com)
        m["x_t"] = np.ascontiguousarray(x[c * T:(c + 1) * T, :].T)
        m["w_qb_h"] = np.ascontiguousarray(
            wqb[:, c * HL * D_QK:(c + 1) * HL * D_QK])
        m["w_kvb_h"] = np.ascontiguousarray(
            wkvb[:, c * HL * 256:(c + 1) * HL * 256])
        m["tok_base"] = np.full((1, 1), c * T, dtype=np.float32)
        in_maps.append(m)

    res = run_bass_kernel_spmd(nc, in_maps, core_ids=list(range(NCORES)))
    out = np.empty((S, D), dtype=np.float32)
    for c in range(NCORES):
        out[c * T:(c + 1) * T, :] = res.results[c]["out_t"].T
    if _DEBUG:
        LAST_DEBUG = list(res.results)
    return out[None, :, :]


# version 07
# speedup vs baseline: 1.0532x; 1.0532x over previous
import math

import numpy as np

import concourse.bass as bass
import concourse.mybir as mybir
from concourse.bass_utils import run_bass_kernel_spmd
from concourse.aot_env import allow_ici_all_to_all

F32 = mybir.dt.float32
BF16 = mybir.dt.bfloat16
I32 = mybir.dt.int32
F8 = mybir.dt.float8e4
U8 = mybir.dt.uint8
AF = mybir.ActivationFunctionType
ALU = mybir.AluOpType

NCORES = 8
S = 8192
D = 2048
T = S // NCORES
H = 16
HL = H // NCORES
Q_LORA = 768
KV_LORA = 512
D_NOPE = 128
D_ROPE = 64
D_V = 128
D_QK = D_NOPE + D_ROPE
I_FFN = 6144
QK_SCALE = 1.0 / math.sqrt(D_QK)
RMS_EPS = 1e-6
LN_EPS = 1e-5
THETA = 10000.0
AG_COLS = Q_LORA + KV_LORA + D_ROPE

C1 = float(np.float32(2 * np.pi))
C2 = float(2 * np.pi - np.float64(np.float32(2 * np.pi)))

_DEBUG = False
LAST_DEBUG = None
_PCFG = {"mm": True, "evac": True, "split": True, "evac_sq": True, "evac_cp": True, "evac_mm": True}


def ts(i, n):
    return slice(i * n, (i + 1) * n)


class Prog:

    ENGINES = ("sync", "gpsimd", "tensor", "vector", "scalar")

    def __init__(self, nc):
        self.nc = nc
        self.ops = {e: [] for e in self.ENGINES}
        self.sems = {}
        self.cnt = {}

    def sem(self, name):
        if name not in self.sems:
            self.sems[name] = self.nc.alloc_semaphore(name)
            self.cnt[name] = 0
        return self.sems[name]

    def inst(self, eng, make, sem=None, n=1):
        if sem is not None:
            self.sem(sem)
            self.cnt[sem] += n
            self.ops[eng].append(
                lambda e, make=make, s=sem, n=n: make(e).then_inc(self.sems[s], n))
            return self.cnt[sem]
        self.ops[eng].append(lambda e, make=make: make(e))
        return None

    def dma(self, eng, out, in_, sem="ld", slow=False):
        return self.inst(
            eng,
            lambda e, o=out, i=in_, s=slow: e.dma_start(
                out=o, in_=i, allow_slow_non_contiguous=s),
            sem=sem, n=16)

    def wait(self, eng, sem, val=None):
        self.sem(sem)
        v = self.cnt[sem] if val is None else val
        if v > 0:
            self.ops[eng].append(lambda e, s=sem, v=v: e.wait_ge(self.sems[s], v))

    def val(self, sem):
        self.sem(sem)
        return self.cnt[sem]

    def qbrk(self, eng, sem):
        self.wait(eng, sem)

    def chain(self, eng, sem, makes):
        for i, mk in enumerate(makes):
            if i > 0:
                self.wait(eng, sem)
            self.inst(eng, mk, sem=sem)

    def barrier(self, engines=None):
        engines = engines or self.ENGINES
        snap = dict(self.cnt)
        for e in engines:
            for s, v in snap.items():
                if v > 0:
                    self.ops[e].append(
                        lambda eng, s=s, v=v: eng.wait_ge(self.sems[s], v))

    def build(self):
        with self.nc.Block() as block:
            for name in self.ENGINES:
                getattr(block, name)(
                    lambda eng, name=name: [fn(eng) for fn in self.ops[name]])



def build_nc(stop_after=None):
    nc = bass.Bass("TRN2", target_bir_lowering=False, debug=False)
    p = Prog(nc)

    dt_ = nc.dram_tensor
    x_t = dt_("x_t", [D, T], F32, kind="ExternalInput").ap()
    w_qa = dt_("w_qa", [D, Q_LORA], F32, kind="ExternalInput").ap()
    w_qb_h = dt_("w_qb_h", [Q_LORA, HL * D_QK], F32, kind="ExternalInput").ap()
    w_kva = dt_("w_kva", [D, KV_LORA + D_ROPE], F32, kind="ExternalInput").ap()
    w_kvb_h = dt_("w_kvb_h", [KV_LORA, HL * 256], F32, kind="ExternalInput").ap()
    w_o = dt_("w_o", [D, D], F32, kind="ExternalInput").ap()
    gate_w = dt_("gate_w", [D, I_FFN], F32, kind="ExternalInput").ap()
    up_w = dt_("up_w", [D, I_FFN], F32, kind="ExternalInput").ap()
    down_w = dt_("down_w", [I_FFN, D], F32, kind="ExternalInput").ap()
    g_q = dt_("g_q", [128, Q_LORA // 128], F32, kind="ExternalInput").ap()
    g_kv = dt_("g_kv", [128, KV_LORA // 128], F32, kind="ExternalInput").ap()
    gate_s = dt_("gate_s", [128, I_FFN // 128], F32, kind="ExternalInput").ap()
    up_s = dt_("up_s", [128, I_FFN // 128], F32, kind="ExternalInput").ap()
    down_s = dt_("down_s", [128, D // 128], F32, kind="ExternalInput").ap()
    tok_base = dt_("tok_base", [1, 1], F32, kind="ExternalInput").ap()
    out_t = dt_("out_t", [D, T], F32, kind="ExternalOutput").ap()

    ag_in = dt_("ag_in", [T, AG_COLS], F32).ap()
    ag_out = dt_("ag_out", [S, AG_COLS], F32).ap()
    a2a_in = dt_("a2a_in", [S, HL * D_V], F32).ap()
    a2a_out = dt_("a2a_out", [S, HL * D_V], F32).ap()
    wq_gate = dt_("wq_gate", [D, I_FFN], BF16).ap()
    wq_up = dt_("wq_up", [D, I_FFN], BF16).ap()
    wq_down = dt_("wq_down", [I_FFN, D], BF16).ap()
    rope_t = dt_("rope_t", [2, 64, S], F32).ap()

    dbg = {}
    if _DEBUG:
        dbg["d_ag"] = dt_("d_ag", [T, AG_COLS], F32, kind="ExternalOutput").ap()
        dbg["d_q"] = dt_("d_q", [4 * 128, S], BF16, kind="ExternalOutput").ap()
        dbg["d_k"] = dt_("d_k", [3 * 128, S], BF16, kind="ExternalOutput").ap()
        dbg["d_v"] = dt_("d_v", [2 * 128, S], BF16, kind="ExternalOutput").ap()
        dbg["d_attn"] = dt_("d_attn", [S, HL * D_V], F32, kind="ExternalOutput").ap()
        dbg["d_x1"] = dt_("d_x1", [D, T], F32, kind="ExternalOutput").ap()
        dbg["d_m"] = dt_("d_m", [1, 4], F32, kind="ExternalOutput").ap()

    A = nc.alloc_sbuf_tensor
    ident = A("ident", [128, 128], BF16).ap()
    identf = A("identf", [128, 128], F32).ap()
    rot = A("rot", [64, 64], BF16).ap()
    masks = A("masks", [128, 4 * 512], BF16).ap()
    ones_b = A("ones_b", [128, 1], BF16).ap()
    ones_row = A("ones_row", [1, 128], F32).ap()
    ones_f = A("ones_f", [128, 1], F32).ap()
    c_eps_rms = A("c_eps_rms", [1, 1], F32).ap()
    c_eps_ln = A("c_eps_ln", [1, 1], F32).ap()
    neg_m = A("neg_m", [128, 1], F32).ap()
    m_tmp = A("m_tmp", [1, 4], F32).ap()
    qn2max = A("qn2max", [1, 1], F32).ap()
    kn2max = A("kn2max", [1, 1], F32).ap()
    tokb = A("tokb", [1, 1], F32).ap()
    gq_c = A("gq_c", [128, Q_LORA // 128], F32).ap()
    gkv_c = A("gkv_c", [128, KV_LORA // 128], F32).ap()
    gs_c = A("gs_c", [128, I_FFN // 128], F32).ap()
    us_c = A("us_c", [128, I_FFN // 128], F32).ap()
    ds_c = A("ds_c", [128, D // 128], F32).ap()
    invf = A("invf", [64, 1], F32).ap()
    invf_i = A("invf_i", [64, 1], I32).ap()
    alpha = A("alpha", [1, T], F32).ap()
    stat1 = A("stat1", [1, T], F32).ap()
    stat2 = A("stat2", [1, T], F32).ap()
    recip_sb = A("recip_sb", [1, 512], F32).ap()
    scr_a = A("scr_a", [128, 1024], F32).ap()
    scr_b = A("scr_b", [128, 1024], F32).ap()
    scr_c = A("scr_c", [128, 1024], F32).ap()
    lnhi = A("lnhi", [128, 512], BF16).ap()
    lnlo = A("lnlo", [128, 512], BF16).ap()

    ARENA = 172 * 1024
    arena_t = A("arena", [128, ARENA], U8)
    views = {F32: arena_t.bitcast(F32).ap(), BF16: arena_t.bitcast(BF16).ap(),
             I32: arena_t.bitcast(I32).ap(), U8: arena_t.ap()}

    def av(off, rows, cols, dt):
        sz = {F32: 4, BF16: 2, I32: 4, U8: 1}[dt]
        assert off % 4 == 0 and off + cols * sz <= ARENA, (off, cols, dt)
        return views[dt][0:rows, off // sz: off // sz + cols]

    PS = nc.alloc_psum_tensor
    ps_s = [PS(f"ps_s{i}", [128, 512], F32).ap() for i in range(2)]
    ps_e = [PS(f"ps_e{i}", [128, 512], F32).ap() for i in range(2)]
    ps_av = [PS(f"ps_av{i}", [128, 512], F32).ap() for i in range(2)]
    ps_d = PS("ps_d", [128, 512], F32).ap()
    ps_x = PS("ps_x", [128, 512], F32).ap()

    p.chain("gpsimd", "g", [
        lambda e: e.memset(ident[:, :], 0.0),
        lambda e: e.affine_select(out=ident[:, :], in_=ident[:, :],
                                  compare_op=ALU.not_equal, fill=1.0, base=0,
                                  pattern=[[-1, 128]], channel_multiplier=1),
        lambda e: e.memset(identf[:, :], 0.0),
        lambda e: e.affine_select(out=identf[:, :], in_=identf[:, :],
                                  compare_op=ALU.not_equal, fill=1.0, base=0,
                                  pattern=[[-1, 128]], channel_multiplier=1),
    ])
    for i in range(4):
        p.chain("gpsimd", "g", [
            lambda e, i=i: e.memset(masks[:, ts(i, 512)], 1.0),
            lambda e, i=i: e.affine_select(
                out=masks[:, ts(i, 512)], in_=masks[:, ts(i, 512)],
                compare_op=ALU.is_ge, fill=0.0, base=-128 * i,
                pattern=[[1, 512]], channel_multiplier=-1),
        ])
    p.chain("gpsimd", "g", [
        lambda e: e.memset(rot[:, :], 0.0),
        lambda e: e.affine_select(
            out=rot[:, 1:64:2], in_=rot[:, 1:64:2], compare_op=ALU.not_equal,
            fill=1.0, base=0, pattern=[[-2, 32]], channel_multiplier=1),
        lambda e: e.affine_select(
            out=rot[:, 0:64:2], in_=rot[:, 0:64:2], compare_op=ALU.not_equal,
            fill=-1.0, base=-1, pattern=[[-2, 32]], channel_multiplier=1),
        lambda e: e.iota(invf_i[:, :], pattern=[[0, 1]], base=0,
                         channel_multiplier=1),
    ])
    for mk in (lambda e: e.memset(ones_b[:, :], 1.0),
               lambda e: e.memset(ones_row[:, :], 1.0),
               lambda e: e.memset(ones_f[:, :], 1.0),
               lambda e: e.memset(c_eps_rms[:, :], RMS_EPS),
               lambda e: e.memset(c_eps_ln[:, :], LN_EPS)):
        p.inst("vector", mk, sem="v")
    p.dma("sync", gq_c[:, :], g_q[:, :])
    p.dma("sync", gkv_c[:, :], g_kv[:, :])
    p.dma("sync", gs_c[:, :], gate_s[:, :])
    p.dma("sync", us_c[:, :], up_s[:, :])
    p.dma("sync", ds_c[:, :], down_s[:, :])
    p.dma("sync", tokb[:, :], tok_base[:, :])
    p.qbrk("sync", "ld")
    p.wait("vector", "g")
    p.wait("vector", "v")
    p.chain("vector", "v", [
        lambda e: e.tensor_copy(out=invf[:, :], in_=invf_i[:, :]),
        lambda e: e.tensor_scalar(out=scr_a[0:64, 0:1], in0=invf[:, :],
                                  scalar1=32.0, scalar2=-32.0,
                                  op0=ALU.is_ge, op1=ALU.mult),
        lambda e: e.tensor_add(invf[:, :], invf[:, :], scr_a[0:64, 0:1]),
        lambda e: e.tensor_scalar(out=invf[:, :], in0=invf[:, :],
                                  scalar1=-math.log(THETA) / 32.0, scalar2=None,
                                  op0=ALU.mult),
    ])
    p.wait("scalar", "v")
    p.inst("scalar", lambda e: e.activation(out=invf[:, :], in_=invf[:, :],
                                            func=AF.Exp), sem="s")

    def build_tables(pos_row, w, cos_o, sin_o, frq, rrf, rri, ps_bc):
        for i in range(max(1, w // 512)):
            wslc = ts(i, 512) if w >= 512 else slice(0, w)
            n = 512 if w >= 512 else w
            p.wait("tensor", "v")
            p.wait("tensor", "s")
            p.inst("tensor", lambda e, wslc=wslc, n=n: e.matmul(
                ps_bc[0:64, 0:n], ones_row[0:1, 0:64], pos_row[0:1, wslc],
                start=True, stop=True), sem="t")
            p.wait("vector", "t")
            p.inst("vector", lambda e, wslc=wslc, n=n: e.tensor_scalar(
                out=frq[0:64, wslc], in0=ps_bc[0:64, 0:n],
                scalar1=invf[0:64, 0:1], scalar2=None, op0=ALU.mult), sem="v")

        def reduce_and_sin(out_ap):
            p.wait("vector", "v")
            p.chain("vector", "v", [
                lambda e: e.tensor_scalar(out=rrf[0:64, 0:w], in0=frq[0:64, 0:w],
                                          scalar1=1.0 / (2 * math.pi),
                                          scalar2=None, op0=ALU.mult),
                lambda e: e.tensor_copy(out=rri[0:64, 0:w], in_=rrf[0:64, 0:w]),
                lambda e: e.tensor_copy(out=rrf[0:64, 0:w], in_=rri[0:64, 0:w]),
                lambda e: e.scalar_tensor_tensor(
                    out=out_ap, in0=rrf[0:64, 0:w], scalar=-C1,
                    in1=frq[0:64, 0:w], op0=ALU.mult, op1=ALU.add),
                lambda e: e.scalar_tensor_tensor(
                    out=out_ap, in0=rrf[0:64, 0:w], scalar=-C2,
                    in1=out_ap, op0=ALU.mult, op1=ALU.add),
                lambda e: e.tensor_scalar(out=rrf[0:64, 0:w], in0=out_ap,
                                          scalar1=math.pi, scalar2=None,
                                          op0=ALU.is_gt),
                lambda e: e.scalar_tensor_tensor(
                    out=out_ap, in0=rrf[0:64, 0:w], scalar=-C1,
                    in1=out_ap, op0=ALU.mult, op1=ALU.add),
                lambda e: e.scalar_tensor_tensor(
                    out=out_ap, in0=rrf[0:64, 0:w], scalar=-C2,
                    in1=out_ap, op0=ALU.mult, op1=ALU.add),
                lambda e: e.tensor_scalar(out=rrf[0:64, 0:w], in0=out_ap,
                                          scalar1=-math.pi, scalar2=None,
                                          op0=ALU.is_lt),
                lambda e: e.scalar_tensor_tensor(
                    out=out_ap, in0=rrf[0:64, 0:w], scalar=C1,
                    in1=out_ap, op0=ALU.mult, op1=ALU.add),
                lambda e: e.scalar_tensor_tensor(
                    out=out_ap, in0=rrf[0:64, 0:w], scalar=C2,
                    in1=out_ap, op0=ALU.mult, op1=ALU.add),
            ])
            p.wait("scalar", "v")
            p.inst("scalar", lambda e: e.activation(
                out=out_ap, in_=out_ap, func=AF.Sin), sem="s")

        reduce_and_sin(sin_o[0:64, 0:w])
        p.wait("vector", "s")
        p.chain("vector", "v", [
            lambda e: e.tensor_scalar(out=frq[0:64, 0:w], in0=frq[0:64, 0:w],
                                      scalar1=math.pi / 2, scalar2=None,
                                      op0=ALU.add),
        ])
        reduce_and_sin(cos_o[0:64, 0:w])

    a_xstage = [av(0, 128, 1024, F32), av(4096, 128, 1024, F32)]
    a_xhi = av(8 * 1024, 128, 16 * 1024, BF16)
    a_xlo = av(40 * 1024, 128, 16 * 1024, BF16)
    a_wf = [av(72 * 1024, 128, 2048, F32), av(80 * 1024, 128, 2048, F32)]
    a_whi = [av(88 * 1024, 128, 2048, BF16), av(92 * 1024, 128, 2048, BF16)]
    a_wlo = [av(96 * 1024, 128, 2048, BF16), av(100 * 1024, 128, 2048, BF16)]
    a_qan = av(104 * 1024, 128, 6 * 1024, F32)
    a_ckvn = av(128 * 1024, 128, 4 * 1024, F32)
    a_krope = av(144 * 1024, 64, 1024, F32)
    a_pos = av(148 * 1024, 1, 1024, F32)
    a_posi = av(152 * 1024, 1, 1024, I32)
    a_cos = av(156 * 1024, 64, 1024, F32)
    a_sin = av(160 * 1024, 64, 1024, F32)
    a_frq = av(164 * 1024, 64, 1024, F32)
    a_rri = av(168 * 1024, 64, 1024, I32)
    a_krf = av(152 * 1024, 64, 1024, F32)
    a_krhi = av(0, 64, 1024, BF16)
    a_krlo = av(2048, 64, 1024, BF16)

    def _finish_early():
        p.barrier()
        p.dma("sync", out_t[0:128, :], scr_a[:, :], sem="fin")
        p.wait("gpsimd", "fin")
        with allow_ici_all_to_all():
            p.build()

    if stop_after == "consts":
        _finish_early()
        return nc, dbg

    p.wait("gpsimd", "g")
    p.inst("gpsimd", lambda e: e.iota(a_posi[:, :], pattern=[[1, T]], base=0,
                                      channel_multiplier=0), sem="g")
    p.wait("vector", "g")
    p.wait("vector", "ld")
    p.chain("vector", "v", [
        lambda e: e.tensor_copy(out=a_pos[:, :], in_=a_posi[:, :]),
        lambda e: e.tensor_scalar(out=a_pos[:, :], in0=a_pos[:, :],
                                  scalar1=tokb[0:1, 0:1], scalar2=None,
                                  op0=ALU.add),
    ])
    build_tables(a_pos, T, a_cos, a_sin, a_frq, scr_c, a_rri, ps_x)

    if stop_after == "tables":
        _finish_early()
        return nc, dbg

    xsplit_done = {}
    for k in range(16):
        b = k % 2
        if k >= 2:
            p.wait("sync", "v", xsplit_done[k - 2])
        p.dma("sync", a_xstage[b][:, :], x_t[ts(k, 128), :])
        p.qbrk("sync", "ld")
        p.wait("vector", "ld")
        p.chain("vector", "v", [
            lambda e, k=k, b=b: e.tensor_copy(out=a_xhi[:, ts(k, 1024)],
                                              in_=a_xstage[b][:, :]),
            lambda e, k=k, b=b: e.tensor_sub(a_xlo[:, ts(k, 1024)],
                                             a_xstage[b][:, :],
                                             a_xhi[:, ts(k, 1024)]),
        ])
        xsplit_done[k] = p.val("v")

    if stop_after == "xload":
        _finish_early()
        return nc, dbg

    wstate = {"i": 0, "mm": {}, "sp": {}}

    def proj_cols(w_dram, col0, ncols, ps_pair, evac_cb):
        i = wstate["i"]
        b = i % 2
        if i >= 2:
            p.wait("sync", "v", wstate["sp"][i - 2])
        p.dma("sync", a_wf[b][:, 0:16 * ncols],
              w_dram[:, col0:col0 + ncols].rearrange("(a p) c -> p a c", p=128))
        p.qbrk("sync", "ld")
        p.wait("vector", "ld")
        if i >= 2:
            p.wait("vector", "t", wstate["mm"][i - 2])
        if _PCFG["split"]:
            for k in range(16):
                sl = ts(k, ncols)
                p.chain("vector", "v", [
                    lambda e, b=b, sl=sl: e.tensor_copy(out=a_whi[b][:, sl],
                                                        in_=a_wf[b][:, sl]),
                    lambda e, b=b, sl=sl: e.tensor_sub(a_wlo[b][:, sl],
                                                       a_wf[b][:, sl],
                                                       a_whi[b][:, sl]),
                ])
        wstate["sp"][i] = p.val("v")
        p.wait("tensor", "v", wstate["sp"][i])
        if not _PCFG["mm"]:
            wstate["mm"][i] = p.val("t")
            wstate["i"] = i + 1
            return
        for k in range(16):
            sl = ts(k, ncols)
            for (wt, xt_) in ((a_whi[b], a_xhi), (a_whi[b], a_xlo),
                              (a_wlo[b], a_xhi)):
                for nt in range(2):
                    xsl = slice(k * 1024 + nt * 512, k * 1024 + nt * 512 + 512)
                    p.inst("tensor", lambda e, wt=wt, xt_=xt_, sl=sl, xsl=xsl,
                           nt=nt,
                           st=(k == 0 and wt is a_whi[b] and xt_ is a_xhi),
                           sp=(k == 15 and wt is a_wlo[b]),
                           nc_=ncols:
                           e.matmul(ps_pair[nt][0:nc_, :], wt[:, sl],
                                    xt_[:, xsl], start=st, stop=sp), sem="t")
        wstate["mm"][i] = p.val("t")
        wstate["i"] = i + 1
        if _PCFG["evac"]:
            evac_cb()

    def make_rms_evac(dst, cc, nchunks, ps_pair):
        def cb():
            tdone = p.val("t")
            if _PCFG["evac_sq"]:
                p.wait("scalar", "t", tdone)
                for nt in range(2):
                    p.inst("scalar", lambda e, nt=nt: e.activation(
                        out=scr_b[:, ts(nt, 512)], in_=ps_pair[nt][:, :],
                        func=AF.Square), sem="s")
            if _PCFG["evac_cp"]:
                p.wait("vector", "t", tdone)
                p.wait("vector", "s")
                for nt in range(2):
                    dsl = slice(cc * 1024 + nt * 512,
                                cc * 1024 + nt * 512 + 512)
                    p.inst("vector", lambda e, nt=nt, dsl=dsl: e.tensor_copy(
                        out=dst[:, dsl], in_=ps_pair[nt][:, :]), sem="v")
            if _PCFG["evac_sq"] and _PCFG["evac_mm"]:
                sq_done = p.val("s")
                p.wait("tensor", "s", sq_done)
                for nt in range(2):
                    tgt = ps_d if nt == 0 else ps_x
                    p.inst("tensor", lambda e, nt=nt, cc=cc, tgt=tgt: e.matmul(
                        tgt[0:1, :], ones_f[:, :], scr_b[:, ts(nt, 512)],
                        start=(cc == 0), stop=(cc == nchunks - 1)), sem="t")
        return cb

    def alpha_and_scale(dst, nchunks, denom):
        ssd = p.val("t")
        p.wait("vector", "t", ssd)
        for nt in range(2):
            src = ps_d if nt == 0 else ps_x
            p.chain("vector", "v", [
                lambda e, nt=nt, src=src: e.tensor_scalar(
                    out=stat1[0:1, ts(nt, 512)], in0=src[0:1, :],
                    scalar1=1.0 / denom, scalar2=RMS_EPS,
                    op0=ALU.mult, op1=ALU.add),
                lambda e, nt=nt: e.reciprocal(stat2[0:1, ts(nt, 512)],
                                              stat1[0:1, ts(nt, 512)]),
            ])
        p.wait("scalar", "v")
        p.inst("scalar", lambda e: e.activation(
            out=alpha[0:1, :], in_=stat2[0:1, :], func=AF.Sqrt), sem="s")
        ad = p.val("s")
        p.wait("tensor", "s", ad)
        for nt in range(2):
            tgt = ps_d if nt == 0 else ps_x
            p.inst("tensor", lambda e, nt=nt, tgt=tgt: e.matmul(
                tgt[:, :], ones_row[0:1, :], alpha[0:1, ts(nt, 512)],
                start=True, stop=True), sem="t")
        bd = p.val("t")
        p.wait("vector", "t", bd)
        for cc in range(nchunks):
            for nt in range(2):
                sl = slice(cc * 1024 + nt * 512, cc * 1024 + nt * 512 + 512)
                src = ps_d if nt == 0 else ps_x
                p.inst("vector", lambda e, sl=sl, src=src: e.tensor_mul(
                    dst[:, sl], dst[:, sl], src[:, :]), sem="v")

    for cc in range(6):
        pair = ps_s if cc % 2 == 0 else ps_e
        proj_cols(w_qa, cc * 128, 128, pair,
                  make_rms_evac(a_qan, cc, 6, pair))
        if stop_after == f"proj_cc{cc}":
            _finish_early()
            return nc, dbg
    if stop_after == "proj_qa_mm":
        _finish_early()
        return nc, dbg
    alpha_and_scale(a_qan, 6, Q_LORA)
    if stop_after == "proj_qa":
        _finish_early()
        return nc, dbg
    qan_done = p.val("v")

    for cc in range(4):
        pair = ps_s if cc % 2 == 0 else ps_e
        proj_cols(w_kva, cc * 128, 128, pair, make_rms_evac(a_ckvn, cc, 4, pair))
    alpha_and_scale(a_ckvn, 4, KV_LORA)
    ckvn_done = p.val("v")

    def krope_evac():
        tdone = p.val("t")
        p.wait("vector", "t", tdone)
        for nt in range(2):
            p.inst("vector", lambda e, nt=nt: e.tensor_copy(
                out=a_krf[:, ts(nt, 512)], in_=ps_av[nt][0:64, :]), sem="v")
        p.wait("vector", "v")
        p.chain("vector", "v", [
            lambda e: e.tensor_copy(out=a_krhi[:, :], in_=a_krf[:, :]),
            lambda e: e.tensor_sub(a_krlo[:, :], a_krf[:, :], a_krhi[:, :]),
        ])
        sp = p.val("v")
        p.wait("tensor", "v", sp)
        for nt in range(2):
            tgt = ps_s[nt]
            p.inst("tensor", lambda e, nt=nt, tgt=tgt: e.matmul(
                tgt[0:64, :], rot[:, :], a_krhi[:, ts(nt, 512)],
                start=True, stop=False), sem="t")
            p.inst("tensor", lambda e, nt=nt, tgt=tgt: e.matmul(
                tgt[0:64, :], rot[:, :], a_krlo[:, ts(nt, 512)],
                start=False, stop=True), sem="t")
        rd = p.val("t")
        p.wait("vector", "t", rd)
        for nt in range(2):
            sl = ts(nt, 512)
            p.chain("vector", "v", [
                lambda e, sl=sl: e.tensor_mul(scr_a[0:64, sl], a_krf[:, sl],
                                              a_cos[0:64, sl]),
                lambda e, sl=sl, nt=nt: e.tensor_mul(
                    scr_b[0:64, sl], ps_s[nt][0:64, :], a_sin[0:64, sl]),
                lambda e, sl=sl: e.tensor_add(a_krope[:, sl], scr_a[0:64, sl],
                                              scr_b[0:64, sl]),
            ])

    if stop_after == "proj_kva":
        _finish_early()
        return nc, dbg
    proj_cols(w_kva, 512, 64, ps_av, krope_evac)

    if stop_after == "proj":
        _finish_early()
        return nc, dbg

    def store_ag(src_fm, nchunks, col_base, rows=128):
        grp_w = 4 * rows
        for cc in range(nchunks):
            for grp in range(2):
                pre = p.val("v")
                p.wait("tensor", "v", pre)
                for q in range(4):
                    tt = grp * 4 + q
                    p.inst("tensor", lambda e, cc=cc, tt=tt, q=q: e.transpose(
                        ps_x[:, q * rows:(q + 1) * rows],
                        src_fm[0:rows, cc * 1024 + tt * 128:
                               cc * 1024 + (tt + 1) * 128],
                        identf[0:rows, 0:rows]), sem="t")
                trd = p.val("t")
                p.wait("vector", "t", trd)
                p.wait("vector", "ag")
                p.inst("vector", lambda e, grp_w=grp_w: e.tensor_copy(
                    out=scr_c[:, 0:grp_w], in_=ps_x[:, 0:grp_w]), sem="v")
                evd = p.val("v")
                p.wait("sync", "v", evd)
                for q in range(4):
                    tt = grp * 4 + q
                    p.dma("sync", ag_in[ts(tt, 128),
                                        col_base + cc * rows:
                                        col_base + (cc + 1) * rows],
                          scr_c[:, q * rows:(q + 1) * rows], sem="ag")

    store_ag(a_qan, 6, 0)
    store_ag(a_ckvn, 4, Q_LORA)
    store_ag(a_krope, 1, Q_LORA + KV_LORA, rows=64)

    if _DEBUG:
        p.wait("sync", "ag")
        p.dma("sync", dbg["d_ag"][:, :], ag_in[:, :], sem="st")

    if stop_after == "p1":
        _finish_early()
        return nc, dbg

    for cblk in range(S // 512):
        p.wait("gpsimd", "v")
        p.inst("gpsimd", lambda e, cblk=cblk: e.iota(
            a_rri[0:1, 0:512], pattern=[[1, 512]], base=cblk * 512,
            channel_multiplier=0), sem="g")
        p.wait("vector", "g")
        p.wait("vector", "st")
        p.inst("vector", lambda e: e.tensor_copy(
            out=a_pos[0:1, 0:512], in_=a_rri[0:1, 0:512]), sem="v")
        p.wait("vector", "v")
        build_tables(a_pos[0:1, 0:512], 512, a_cos, a_sin, a_frq, scr_c,
                     a_rri, ps_x)
        tb = p.val("s")
        p.wait("sync", "s", tb)
        p.dma("sync", rope_t[1, :, cblk * 512:(cblk + 1) * 512],
              a_sin[0:64, 0:512], sem="st")
        p.dma("sync", rope_t[0, :, cblk * 512:(cblk + 1) * 512],
              a_cos[0:64, 0:512], sem="st")
        p.qbrk("sync", "st")

    p.wait("gpsimd", "ag")
    p.inst("gpsimd", lambda e: e.collective_compute(
        "AllGather", ALU.bypass, replica_groups=[list(range(NCORES))],
        ins=[ag_in[:, :]], outs=[ag_out[:, :]]), sem="cc")

    if stop_after == "ag":
        _finish_early()
        return nc, dbg

    p.barrier()

    if stop_after == "qz":
        _finish_early()
        return nc, dbg

    o = 0
    a_qn_hi = av(o, 128, S, BF16); o += 16 * 1024
    a_qn_lo = av(o, 128, S, BF16); o += 16 * 1024
    a_qr_A = av(o, 128, S, BF16); o += 16 * 1024
    a_qr_B = av(o, 128, S, BF16); o += 16 * 1024
    a_kn_hi = av(o, 128, S, BF16); o += 16 * 1024
    a_kn_lo = av(o, 128, S, BF16); o += 16 * 1024
    a_kr_P = av(o, 128, S, BF16); o += 16 * 1024
    a_v_hi = av(o, 128, S, BF16); o += 16 * 1024
    a_v_lo = av(o, 128, S, BF16); o += 16 * 1024
    O_W = o
    a_wb_hi = av(O_W, 128, 6 * 384, BF16)
    a_wb_lo = av(O_W + 5 * 1024, 128, 6 * 384, BF16)
    a_wvb_hi = av(O_W, 128, 4 * 512, BF16)
    a_wvb_lo = av(O_W + 5 * 1024, 128, 4 * 512, BF16)
    O_T = O_W + 10 * 1024
    a_stg = av(O_T, 128, 768, F32)
    a_qaT_hi = av(O_T + 3 * 1024, 128, 6 * 512, BF16)
    a_qaT_lo = av(O_T + 9 * 1024, 128, 6 * 512, BF16)
    a_tb_cos = av(O_T + 3 * 1024, 64, 512, F32)
    a_tb_sin = av(O_T + 5 * 1024, 64, 512, F32)
    a_tb_frq = av(O_T + 7 * 1024, 64, 512, F32)
    a_tb_rri = av(O_T + 9 * 1024, 64, 512, I32)
    a_ckvT_hi = av(O_T, 128, 4 * 512, BF16)
    a_ckvT_lo = av(O_T + 4 * 1024, 128, 4 * 512, BF16)
    a_kstg = av(O_T + 8 * 1024, 128, 512, F32)
    a_krT = av(O_T + 10 * 1024, 64, 512, BF16)


    qz_u = av(169984, 128, 1024, U8)
    qz_b = av(169984 + 1024, 128, 1024, BF16)
    qz_plan = []
    for (wsrc, wdst, ncols) in ((gate_w, wq_gate, I_FFN), (up_w, wq_up, I_FFN),
                                (down_w, wq_down, D)):
        for rch in range(wsrc.shape[0] // 128):
            for cch in range(ncols // 1024):
                qz_plan.append((wsrc, wdst, rch, cch))
    qz_state = {"i": 0, "cast": {}, "out": {}}

    def emit_qz_step():
        i = qz_state["i"]
        if i >= len(qz_plan):
            return
        wsrc, wdst, rch, cch = qz_plan[i]
        if i >= 1:
            p.wait("gpsimd", "v", qz_state["cast"][i - 1])
        p.dma("gpsimd", scr_a[:, :],
              wsrc[ts(rch, 128), cch * 1024:(cch + 1) * 1024], sem="qd")
        p.qbrk("gpsimd", "qd")
        p.wait("vector", "qd")
        if i >= 1:
            p.wait("vector", "qd", qz_state["out"][i - 1])
        p.chain("vector", "v", [
            lambda e: e.tensor_copy(out=qz_u.bitcast(F8)[:, :],
                                    in_=scr_a[:, :]),
            lambda e: e.tensor_copy(out=qz_b[:, :],
                                    in_=qz_u.bitcast(F8)[:, :]),
        ])
        qz_state["cast"][i] = p.val("v")
        p.wait("gpsimd", "v", qz_state["cast"][i])
        qz_state["out"][i] = p.dma(
            "gpsimd", wdst[ts(rch, 128), cch * 1024:(cch + 1) * 1024],
            qz_b[:, :], sem="qd")
        p.qbrk("gpsimd", "qd")
        qz_state["i"] = i + 1

    a_ehi = [av(O_T, 128, 512, BF16), av(O_T + 1024, 128, 512, BF16)]
    a_elo = [av(O_T + 2048, 128, 512, BF16), av(O_T + 3072, 128, 512, BF16)]
    a_bc = av(O_T + 4096, 128, 512, F32)
    a_fin = av(O_T + 6144, 128, 512, F32)
    a_ast = av(O_T + 8192, 128, 512, F32)
    a_em = av(O_T + 10240, 128, 512, F32)

    def load_wqb():
        p.wait("sync", "t")
        for k in range(6):
            p.wait("sync", "v")
            p.dma("sync", scr_a[:, 0:384], w_qb_h[ts(k, 128), :])
            p.wait("vector", "ld")
            p.chain("vector", "v", [
                lambda e, k=k: e.tensor_scalar(
                    out=scr_a[:, 0:384], in0=scr_a[:, 0:384],
                    scalar1=gq_c[:, k:k + 1], scalar2=None, op0=ALU.mult),
                lambda e, k=k: e.tensor_copy(out=a_wb_hi[:, ts(k, 384)],
                                             in_=scr_a[:, 0:384]),
                lambda e, k=k: e.tensor_sub(a_wb_lo[:, ts(k, 384)],
                                            scr_a[:, 0:384],
                                            a_wb_hi[:, ts(k, 384)]),
            ])

    def head_phase(h, wqb_loaded=False):
        if not wqb_loaded:
            load_wqb()
        p.inst("vector", lambda e: e.memset(qn2max[:, :], 0.0), sem="v")
        p.inst("vector", lambda e: e.memset(kn2max[:, :], 0.0), sem="v")
        p.wait("tensor", "v")

        for B in range(16):
            for tt in range(4):
                p.wait("gpsimd", "t")
                p.dma("gpsimd", a_stg[:, :],
                      ag_out[B * 512 + tt * 128: B * 512 + (tt + 1) * 128,
                             0:Q_LORA], sem="gd")
                p.wait("tensor", "gd")
                p.wait("tensor", "v")
                for k in range(6):
                    tgt = ps_x[:, ts(k, 128)] if k < 4 else \
                        ps_av[0][:, ts(k - 4, 128)]
                    p.inst("tensor", lambda e, k=k, tgt=tgt: e.transpose(
                        tgt, a_stg[:, ts(k, 128)], identf[:, :]), sem="t")
                trd = p.val("t")
                p.wait("vector", "t", trd)
                for k in range(6):
                    src = ps_x[:, ts(k, 128)] if k < 4 else \
                        ps_av[0][:, ts(k - 4, 128)]
                    dsl = slice(k * 512 + tt * 128, k * 512 + (tt + 1) * 128)
                    p.chain("vector", "v", [
                        lambda e, src=src, dsl=dsl: e.tensor_copy(
                            out=a_qaT_hi[:, dsl], in_=src),
                        lambda e, src=src, dsl=dsl: e.tensor_sub(
                            a_qaT_lo[:, dsl], src, a_qaT_hi[:, dsl]),
                    ])
            qaT_done = p.val("v")
            p.wait("tensor", "v", qaT_done)
            first_n = True
            first_r = True
            for k in range(6):
                wn = slice(k * 384 + h * 192, k * 384 + h * 192 + 128)
                wr = slice(k * 384 + h * 192 + 128, k * 384 + h * 192 + 192)
                qsl = ts(k, 512)
                for (wt, qt) in ((a_wb_hi, a_qaT_hi), (a_wb_hi, a_qaT_lo),
                                 (a_wb_lo, a_qaT_hi)):
                    p.inst("tensor", lambda e, wt=wt, qt=qt, wn=wn, qsl=qsl,
                           st=first_n, sp=(k == 5 and wt is a_wb_lo):
                           e.matmul(ps_s[0][:, :], wt[:, wn], qt[:, qsl],
                                    start=st, stop=sp), sem="t")
                    first_n = False
                for (wt, qt) in ((a_wb_hi, a_qaT_hi), (a_wb_hi, a_qaT_lo),
                                 (a_wb_lo, a_qaT_hi)):
                    p.inst("tensor", lambda e, wt=wt, qt=qt, wr=wr, qsl=qsl,
                           st=first_r, sp=(k == 5 and wt is a_wb_lo):
                           e.matmul(ps_e[0][0:64, :], wt[:, wr], qt[:, qsl],
                                    start=st, stop=sp), sem="t")
                    first_r = False
            qb_done = p.val("t")
            p.wait("vector", "t", qb_done)
            p.chain("vector", "v", [
                lambda e, B=B: e.tensor_copy(out=a_qn_hi[:, ts(B, 512)],
                                             in_=ps_s[0][:, :]),
                lambda e, B=B: e.tensor_sub(a_qn_lo[:, ts(B, 512)],
                                            ps_s[0][:, :],
                                            a_qn_hi[:, ts(B, 512)]),
            ])
            p.wait("scalar", "t", qb_done)
            p.wait("scalar", "v")
            p.wait("scalar", "t")
            p.inst("scalar", lambda e: e.activation(
                out=scr_b[:, 0:512], in_=ps_s[0][:, :], func=AF.Square), sem="s")
            p.chain("vector", "v", [
                lambda e: e.tensor_copy(out=scr_a[0:64, 0:512],
                                        in_=ps_e[0][0:64, :]),
                lambda e, B=B: e.tensor_copy(out=a_krT[:, :],
                                             in_=scr_a[0:64, 0:512]),
                lambda e, B=B: e.tensor_sub(scr_c[0:64, 0:512],
                                            scr_a[0:64, 0:512], a_krT[:, :]),
                lambda e, B=B: e.tensor_copy(out=a_qr_B[0:64, ts(B, 512)],
                                             in_=scr_c[0:64, 0:512]),
            ])
            rsp = p.val("v")
            p.wait("tensor", "v", rsp)
            p.inst("tensor", lambda e: e.matmul(
                ps_e[1][0:64, :], rot[:, :], a_krT[:, :],
                start=True, stop=False), sem="t")
            p.inst("tensor", lambda e, B=B: e.matmul(
                ps_e[1][0:64, :], rot[:, :], a_qr_B[0:64, ts(B, 512)],
                start=False, stop=True), sem="t")
            rot_done = p.val("t")
            p.wait("gpsimd", "v")
            p.wait("gpsimd", "t")
            p.dma("gpsimd", a_tb_cos[:, :],
                  rope_t[0, :, B * 512:(B + 1) * 512], sem="gd")
            p.dma("gpsimd", a_tb_sin[:, :],
                  rope_t[1, :, B * 512:(B + 1) * 512], sem="gd")
            p.qbrk("gpsimd", "gd")
            p.wait("vector", "gd")
            p.wait("vector", "t", rot_done)
            p.wait("vector", "s")
            p.chain("vector", "v", [
                lambda e: e.tensor_mul(scr_b[0:64, 512:1024],
                                       scr_a[0:64, 0:512], a_tb_cos[:, :]),
                lambda e: e.tensor_mul(scr_c[0:64, 0:512], ps_e[1][0:64, :],
                                       a_tb_sin[:, :]),
                lambda e: e.tensor_add(scr_a[0:64, 0:512],
                                       scr_b[0:64, 512:1024],
                                       scr_c[0:64, 0:512]),
                lambda e, B=B: e.tensor_copy(out=a_qr_A[0:64, ts(B, 512)],
                                             in_=scr_a[0:64, 0:512]),
                lambda e, B=B: e.tensor_sub(scr_c[0:64, 0:512],
                                            scr_a[0:64, 0:512],
                                            a_qr_A[0:64, ts(B, 512)]),
                lambda e, B=B: e.tensor_copy(out=a_qr_B[0:64, ts(B, 512)],
                                             in_=scr_c[0:64, 0:512]),
            ])
            p.wait("scalar", "v")
            p.inst("scalar", lambda e: e.activation(
                out=scr_b[0:64, 512:1024], in_=scr_a[0:64, 0:512],
                func=AF.Square), sem="s")
            sq2 = p.val("s")
            p.wait("tensor", "s", sq2)
            p.inst("tensor", lambda e: e.matmul(
                ps_d[0:1, :], ones_f[:, :], scr_b[:, 0:512],
                start=True, stop=False), sem="t")
            p.inst("tensor", lambda e: e.matmul(
                ps_d[0:1, :], ones_f[0:64, :], scr_b[0:64, 512:1024],
                start=False, stop=True), sem="t")
            nrm = p.val("t")
            p.wait("vector", "t", nrm)
            p.chain("vector", "v", [
                lambda e: e.tensor_reduce(m_tmp[0:1, 0:1], ps_d[0:1, :],
                                          axis=mybir.AxisListType.X,
                                          op=ALU.max),
                lambda e: e.tensor_max(qn2max[:, :], qn2max[:, :],
                                       m_tmp[0:1, 0:1]),
            ])
        qfin = p.val("v")
        p.wait("gpsimd", "v", qfin)
        p.dma("gpsimd", a_qr_A[64:128, :], a_qr_A[0:64, :], sem="gd")
        p.qbrk("gpsimd", "gd")
        p.dma("gpsimd", a_qr_B[64:128, :], a_qr_B[0:64, :], sem="gd")
        p.qbrk("gpsimd", "gd")

        p.wait("sync", "t")
        for k in range(4):
            p.wait("sync", "v")
            p.dma("sync", scr_a[:, 0:512], w_kvb_h[ts(k, 128), :])
            p.wait("vector", "ld")
            p.chain("vector", "v", [
                lambda e, k=k: e.tensor_scalar(
                    out=scr_a[:, 0:512], in0=scr_a[:, 0:512],
                    scalar1=gkv_c[:, k:k + 1], scalar2=None, op0=ALU.mult),
                lambda e, k=k: e.tensor_copy(out=a_wvb_hi[:, ts(k, 512)],
                                             in_=scr_a[:, 0:512]),
                lambda e, k=k: e.tensor_sub(a_wvb_lo[:, ts(k, 512)],
                                            scr_a[:, 0:512],
                                            a_wvb_hi[:, ts(k, 512)]),
            ])
        p.wait("tensor", "v")

        for B in range(16):
            for tt in range(4):
                p.wait("gpsimd", "t")
                p.dma("gpsimd", a_kstg[:, :],
                      ag_out[B * 512 + tt * 128: B * 512 + (tt + 1) * 128,
                             Q_LORA:Q_LORA + KV_LORA], sem="gd")
                p.wait("tensor", "gd")
                p.wait("tensor", "v")
                for k in range(4):
                    p.inst("tensor", lambda e, k=k: e.transpose(
                        ps_x[:, ts(k, 128)], a_kstg[:, ts(k, 128)],
                        identf[:, :]), sem="t")
                trd = p.val("t")
                p.wait("vector", "t", trd)
                for k in range(4):
                    dsl = slice(k * 512 + tt * 128, k * 512 + (tt + 1) * 128)
                    p.chain("vector", "v", [
                        lambda e, k=k, dsl=dsl: e.tensor_copy(
                            out=a_ckvT_hi[:, dsl], in_=ps_x[:, ts(k, 128)]),
                        lambda e, k=k, dsl=dsl: e.tensor_sub(
                            a_ckvT_lo[:, dsl], ps_x[:, ts(k, 128)],
                            a_ckvT_hi[:, dsl]),
                    ])
            ckvT_done = p.val("v")
            p.wait("tensor", "v", ckvT_done)
            first = True
            for k in range(4):
                wsl = slice(k * 512 + h * 256, k * 512 + h * 256 + 128)
                csl = ts(k, 512)
                for (wt, ct) in ((a_wvb_hi, a_ckvT_hi), (a_wvb_hi, a_ckvT_lo),
                                 (a_wvb_lo, a_ckvT_hi)):
                    p.inst("tensor", lambda e, wt=wt, ct=ct, wsl=wsl, csl=csl,
                           st=first, sp=(k == 3 and wt is a_wvb_lo):
                           e.matmul(ps_s[0][:, :], wt[:, wsl], ct[:, csl],
                                    start=st, stop=sp), sem="t")
                    first = False
            knd = p.val("t")
            p.wait("vector", "t", knd)
            p.chain("vector", "v", [
                lambda e, B=B: e.tensor_copy(out=a_kn_hi[:, ts(B, 512)],
                                             in_=ps_s[0][:, :]),
                lambda e, B=B: e.tensor_sub(a_kn_lo[:, ts(B, 512)],
                                            ps_s[0][:, :],
                                            a_kn_hi[:, ts(B, 512)]),
            ])
            p.wait("scalar", "t", knd)
            p.wait("scalar", "v")
            p.inst("scalar", lambda e: e.activation(
                out=scr_b[:, 0:512], in_=ps_s[0][:, :], func=AF.Square), sem="s")
            for tt in range(4):
                p.wait("gpsimd", "t")
                p.dma("gpsimd", a_kstg[:, 0:64],
                      ag_out[B * 512 + tt * 128: B * 512 + (tt + 1) * 128,
                             Q_LORA + KV_LORA:AG_COLS], sem="gd")
                p.wait("tensor", "gd")
                p.wait("tensor", "v")
                p.wait("tensor", "s")
                p.inst("tensor", lambda e: e.transpose(
                    ps_e[1][0:64, 0:128], a_kstg[:, 0:64], identf[:, :]),
                    sem="t")
                trd = p.val("t")
                p.wait("vector", "t", trd)
                ksl = slice(B * 512 + tt * 128, B * 512 + (tt + 1) * 128)
                p.chain("vector", "v", [
                    lambda e, ksl=ksl: e.tensor_copy(
                        out=a_kr_P[0:64, ksl], in_=ps_e[1][0:64, 0:128]),
                    lambda e, ksl=ksl: e.tensor_sub(
                        scr_c[0:64, 0:128], ps_e[1][0:64, 0:128],
                        a_kr_P[0:64, ksl]),
                    lambda e, tt=tt: e.tensor_copy(
                        out=a_krT[:, ts(tt, 128)], in_=scr_c[0:64, 0:128]),
                ])
                p.wait("scalar", "t", trd)
                p.wait("scalar", "v")
                p.inst("scalar", lambda e, tt=tt: e.activation(
                    out=scr_b[0:64, 512 + tt * 128: 512 + (tt + 1) * 128],
                    in_=ps_e[1][0:64, 0:128], func=AF.Square), sem="s")
            sqd = p.val("s")
            p.wait("tensor", "s", sqd)
            p.inst("tensor", lambda e: e.matmul(
                ps_d[0:1, :], ones_f[:, :], scr_b[:, 0:512],
                start=True, stop=False), sem="t")
            p.inst("tensor", lambda e: e.matmul(
                ps_d[0:1, :], ones_f[0:64, :], scr_b[0:64, 512:1024],
                start=False, stop=True), sem="t")
            nrm = p.val("t")
            p.wait("vector", "t", nrm)
            p.chain("vector", "v", [
                lambda e: e.tensor_reduce(m_tmp[0:1, 1:2], ps_d[0:1, :],
                                          axis=mybir.AxisListType.X,
                                          op=ALU.max),
                lambda e: e.tensor_max(kn2max[:, :], kn2max[:, :],
                                       m_tmp[0:1, 1:2]),
            ])
            krtd = p.val("v")
            p.wait("gpsimd", "v", krtd)
            p.dma("gpsimd", a_kr_P[64:128, ts(B, 512)], a_krT[:, :], sem="gd")
            for tt in range(4):
                p.wait("tensor", "v")
                first = True
                for k in range(4):
                    csl = slice(k * 512 + tt * 128, k * 512 + (tt + 1) * 128)
                    wsl = slice(k * 512 + h * 256 + 128,
                                k * 512 + h * 256 + 256)
                    for (ct, wt) in ((a_ckvT_hi, a_wvb_hi),
                                     (a_ckvT_hi, a_wvb_lo),
                                     (a_ckvT_lo, a_wvb_hi)):
                        p.inst("tensor", lambda e, ct=ct, wt=wt, csl=csl,
                               wsl=wsl, st=first,
                               sp=(k == 3 and ct is a_ckvT_lo):
                               e.matmul(ps_e[0][:, 0:128], ct[:, csl],
                                        wt[:, wsl], start=st, stop=sp),
                               sem="t")
                        first = False
                vd = p.val("t")
                p.wait("vector", "t", vd)
                vsl = slice((B * 4 + tt) * 128, (B * 4 + tt + 1) * 128)
                p.chain("vector", "v", [
                    lambda e, vsl=vsl: e.tensor_copy(out=a_v_hi[:, vsl],
                                                     in_=ps_e[0][:, 0:128]),
                    lambda e, vsl=vsl: e.tensor_sub(a_v_lo[:, vsl],
                                                    ps_e[0][:, 0:128],
                                                    a_v_hi[:, vsl]),
                ])

        p.wait("vector", "v")
        p.chain("vector", "v", [
            lambda e: e.tensor_mul(m_tmp[0:1, 2:3], qn2max[:, :], kn2max[:, :]),
        ])
        p.wait("scalar", "v")
        p.inst("scalar", lambda e: e.activation(
            out=m_tmp[0:1, 3:4], in_=m_tmp[0:1, 2:3], func=AF.Sqrt), sem="s")
        p.wait("vector", "s")
        p.chain("vector", "v", [
            lambda e: e.tensor_scalar(out=m_tmp[0:1, 0:1], in0=m_tmp[0:1, 3:4],
                                      scalar1=-QK_SCALE, scalar2=None,
                                      op0=ALU.mult),
        ])
        p.wait("tensor", "v")
        p.inst("tensor", lambda e: e.matmul(
            ps_x[:, 0:1], ones_row[0:1, :], m_tmp[0:1, 0:1],
            start=True, stop=True), sem="t")
        p.wait("vector", "t")
        p.inst("vector", lambda e: e.tensor_copy(out=neg_m[:, :],
                                                 in_=ps_x[:, 0:1]), sem="v")
        if _DEBUG:
            p.wait("sync", "v")
            p.dma("sync", dbg["d_m"][:, :], m_tmp[:, :], sem="st")
        p.barrier(("tensor", "vector", "scalar"))

        exp_done = {}
        split_done = {}
        avden_done = {}
        fin_done = {}
        rec_done = {}
        flat = []
        for qc in range(16):
            for kc in range(4 * qc + 4):
                flat.append((qc, kc))
        nj = len(flat)

        def emit_scores(j):
            qc, kc = flat[j]
            jj = j % 2
            qsl = ts(qc, 512)
            ksl = ts(kc, 128)
            if j >= 2:
                p.wait("tensor", "s", exp_done[j - 2])
            for i, (kt, qt) in enumerate((
                    (a_kn_hi, a_qn_hi), (a_kn_hi, a_qn_lo),
                    (a_kn_lo, a_qn_hi))):
                p.inst("tensor", lambda e, kt=kt, qt=qt, ksl=ksl, qsl=qsl,
                       st=(i == 0), jj=jj:
                       e.matmul(ps_s[jj][:, :], kt[:, ksl], qt[:, qsl],
                                start=st, stop=False), sem="t")
            p.inst("tensor", lambda e, ksl=ksl, qsl=qsl, jj=jj:
                   e.matmul(ps_s[jj][:, :], a_kr_P[:, ksl], a_qr_A[:, qsl],
                            start=False, stop=False), sem="t")
            return p.inst("tensor", lambda e, ksl=ksl, qsl=qsl, jj=jj:
                          e.matmul(ps_s[jj][:, :], a_kr_P[:, ksl],
                                   a_qr_B[:, qsl],
                                   start=False, stop=True), sem="t")

        def emit_exp_split(j, sc_done):
            qc, kc = flat[j]
            jj = j % 2
            p.wait("scalar", "t", sc_done)
            if j >= 2:
                p.wait("scalar", "v", split_done[j - 2])
            p.inst("scalar", lambda e, jj=jj: e.activation(
                out=a_ehi[jj][:, :], in_=ps_s[jj][:, :], func=AF.Exp,
                bias=neg_m[:, :], scale=QK_SCALE), sem="s")
            exp_done[j] = p.inst("scalar", lambda e, jj=jj: e.activation(
                out=ps_e[jj][:, :], in_=ps_s[jj][:, :], func=AF.Exp,
                bias=neg_m[:, :], scale=QK_SCALE), sem="s")
            p.wait("vector", "s", exp_done[j])
            if kc >= 4 * qc:
                didx = kc - 4 * qc
                p.chain("vector", "v", [
                    lambda e, jj=jj, didx=didx: e.tensor_mul(
                        a_ehi[jj][:, :], a_ehi[jj][:, :],
                        masks[:, ts(didx, 512)]),
                    lambda e, jj=jj, didx=didx: e.tensor_mul(
                        a_em[:, :], ps_e[jj][:, :], masks[:, ts(didx, 512)]),
                    lambda e, jj=jj: e.tensor_sub(a_elo[jj][:, :],
                                                  a_em[:, :],
                                                  a_ehi[jj][:, :]),
                ])
            else:
                p.inst("vector", lambda e, jj=jj: e.tensor_sub(
                    a_elo[jj][:, :], ps_e[jj][:, :], a_ehi[jj][:, :]), sem="v")
            split_done[j] = p.val("v")

        def emit_avden(j):
            qc, kc = flat[j]
            jj = j % 2
            qq = qc % 2
            nkc = 4 * qc + 4
            ksl = ts(kc, 128)
            p.wait("tensor", "v", split_done[j])
            if kc == 0 and qc >= 2:
                p.wait("tensor", "v", fin_done[qc - 2])
            if kc == 0 and qc >= 1:
                p.wait("tensor", "v", rec_done[qc - 1])
            for i, (vt, et) in enumerate((
                    (a_v_hi, a_ehi[jj]), (a_v_lo, a_ehi[jj]),
                    (a_v_hi, a_elo[jj]))):
                p.inst("tensor", lambda e, vt=vt, et=et, ksl=ksl, qq=qq,
                       st=(kc == 0 and i == 0),
                       sp=(kc == nkc - 1 and i == 2):
                       e.matmul(ps_av[qq][:, :], vt[:, ksl], et[:, :],
                                start=st, stop=sp), sem="t")
            avden_done[j] = p.inst(
                "tensor", lambda e, jj=jj, st=(kc == 0), sp=(kc == nkc - 1):
                e.matmul(ps_d[0:1, :], ones_b[:, :], a_ehi[jj][:, :],
                         start=st, stop=sp), sem="t")

        def emit_finalize(qc):
            qq = qc % 2
            p.wait("vector", "t", avden_done[sum(4 * q + 4 for q in range(qc + 1)) - 1])
            p.inst("vector", lambda e: e.reciprocal(recip_sb[0:1, :],
                                                    ps_d[0:1, :]), sem="v")
            rec_done[qc] = p.val("v")
            p.wait("tensor", "v", rec_done[qc])
            bc_mm = p.inst("tensor", lambda e: e.matmul(
                ps_x[:, :], ones_row[0:1, :], recip_sb[0:1, :],
                start=True, stop=True), sem="t")
            p.wait("vector", "t", bc_mm)
            p.chain("vector", "v", [
                lambda e: e.tensor_copy(out=a_bc[:, :], in_=ps_x[:, :]),
                lambda e, qq=qq: e.tensor_mul(a_fin[:, :], ps_av[qq][:, :],
                                              a_bc[:, :]),
            ])
            find = p.val("v")
            p.wait("tensor", "v", find)
            for q4 in range(4):
                p.inst("tensor", lambda e, q4=q4: e.transpose(
                    ps_x[:, ts(q4, 128)], a_fin[:, ts(q4, 128)], identf[:, :]),
                    sem="t")
            trd = p.val("t")
            p.wait("vector", "t", trd)
            p.wait("vector", "a2")
            p.inst("vector", lambda e: e.tensor_copy(out=a_ast[:, :],
                                                     in_=ps_x[:, :]), sem="v")
            fin_done[qc] = p.val("v")
            p.wait("sync", "v", fin_done[qc])
            for q4 in range(4):
                p.dma("sync", a2a_in[qc * 512 + q4 * 128:
                                     qc * 512 + (q4 + 1) * 128,
                                     h * D_V:(h + 1) * D_V],
                      a_ast[:, ts(q4, 128)], sem="a2")

        sc_done = {}
        sc_done[0] = emit_scores(0)
        for j in range(nj):
            emit_exp_split(j, sc_done[j])
            if j + 1 < nj:
                sc_done[j + 1] = emit_scores(j + 1)
            emit_avden(j)
            if j % 3 == 1:
                emit_qz_step()
            qc, kc = flat[j]
            if kc == 4 * qc + 3:
                emit_finalize(qc)

        if _DEBUG and h == 0:
            p.wait("sync", "v")
            p.wait("sync", "t")
            p.dma("sync", dbg["d_q"][0:128, :], a_qn_hi[:, :], sem="st")
            p.dma("sync", dbg["d_q"][128:256, :], a_qn_lo[:, :], sem="st")
            p.dma("sync", dbg["d_q"][256:384, :], a_qr_A[:, :], sem="st")
            p.dma("sync", dbg["d_q"][384:512, :], a_qr_B[:, :], sem="st")
            p.dma("sync", dbg["d_k"][0:128, :], a_kn_hi[:, :], sem="st")
            p.dma("sync", dbg["d_k"][128:256, :], a_kn_lo[:, :], sem="st")
            p.dma("sync", dbg["d_k"][256:384, :], a_kr_P[:, :], sem="st")
            p.dma("sync", dbg["d_v"][0:128, :], a_v_hi[:, :], sem="st")
            p.dma("sync", dbg["d_v"][128:256, :], a_v_lo[:, :], sem="st")
        p.barrier()

    for h in range(HL):
        head_phase(h)
        if stop_after == f"head{h}":
            _finish_early()
            return nc, dbg

    if _DEBUG:
        p.wait("sync", "a2")
        p.dma("sync", dbg["d_attn"][:, :], a2a_in[:, :], sem="st")

    p.wait("gpsimd", "a2")
    p.inst("gpsimd", lambda e: e.collective_compute(
        "AllToAll", ALU.bypass, replica_groups=[list(range(NCORES))],
        ins=[a2a_in[:, :]], outs=[a2a_out[:, :]]), sem="cc")
    p.wait("sync", "cc")
    p.wait("tensor", "cc")
    p.wait("vector", "cc")
    p.wait("gpsimd", "cc")

    if stop_after == "a2a":
        _finish_early()
        return nc, dbg

    a_at_hi = av(0, 128, 16 * 1024, BF16)
    a_at_lo = av(32 * 1024, 128, 16 * 1024, BF16)
    a_xpre = av(64 * 1024, 128, 16 * 1024, F32)
    a_wo_f = [av(128 * 1024, 128, 2048, F32), av(136 * 1024, 128, 2048, F32)]
    a_wo_hi = [av(144 * 1024, 128, 2048, BF16), av(148 * 1024, 128, 2048, BF16)]
    a_wo_lo = [av(152 * 1024, 128, 2048, BF16), av(156 * 1024, 128, 2048, BF16)]
    a_xrs = [av(160 * 1024, 128, 1024, F32), av(164 * 1024, 128, 1024, F32)]
    a_p4st = av(168 * 1024, 128, 128, F32)
    a_x1_hi = av(0, 128, 16 * 1024, BF16)
    a_x1_lo = av(32 * 1024, 128, 16 * 1024, BF16)

    for fc in range(16):
        sblk, hl = fc // 2, fc % 2
        for tt in range(8):
            p.wait("gpsimd", "t")
            p.dma("gpsimd", a_p4st[:, :],
                  a2a_out[sblk * 1024 + tt * 128: sblk * 1024 + (tt + 1) * 128,
                          hl * 128:(hl + 1) * 128], sem="gd")
            p.wait("tensor", "gd")
            p.wait("tensor", "v")
            p.inst("tensor", lambda e: e.transpose(
                ps_x[:, 0:128], a_p4st[:, :], identf[:, :]), sem="t")
            trd = p.val("t")
            p.wait("vector", "t", trd)
            dsl = slice(fc * 1024 + tt * 128, fc * 1024 + (tt + 1) * 128)
            p.chain("vector", "v", [
                lambda e, dsl=dsl: e.tensor_copy(out=a_at_hi[:, dsl],
                                                 in_=ps_x[:, 0:128]),
                lambda e, dsl=dsl: e.tensor_sub(a_at_lo[:, dsl],
                                                ps_x[:, 0:128],
                                                a_at_hi[:, dsl]),
            ])
    at_done = p.val("v")
    p.wait("tensor", "v", at_done)

    wo_done = {}
    for oc in range(16):
        b = oc % 2
        if oc >= 2:
            p.wait("sync", "v", wo_done[oc - 2][1])
        p.dma("sync", a_wo_f[b][:, :],
              w_o[:, ts(oc, 128)].rearrange("(a p) c -> p a c", p=128))
        p.dma("sync", a_xrs[b][:, :], x_t[ts(oc, 128), :])
        p.qbrk("sync", "ld")
        p.wait("vector", "ld")
        if oc >= 2:
            p.wait("vector", "t", wo_done[oc - 2][0])
        for k in range(16):
            sl = ts(k, 128)
            p.chain("vector", "v", [
                lambda e, b=b, sl=sl: e.tensor_copy(out=a_wo_hi[b][:, sl],
                                                    in_=a_wo_f[b][:, sl]),
                lambda e, b=b, sl=sl: e.tensor_sub(a_wo_lo[b][:, sl],
                                                   a_wo_f[b][:, sl],
                                                   a_wo_hi[b][:, sl]),
            ])
        spd = p.val("v")
        p.wait("tensor", "v", spd)
        for k in range(16):
            for (wt, at) in ((a_wo_hi[b], a_at_hi), (a_wo_hi[b], a_at_lo),
                             (a_wo_lo[b], a_at_hi)):
                for nt in range(2):
                    asl = slice(k * 1024 + nt * 512, k * 1024 + nt * 512 + 512)
                    p.inst("tensor", lambda e, wt=wt, at=at, k=k, asl=asl,
                           pst=ps_s[nt],
                           st=(k == 0 and wt is a_wo_hi[b] and at is a_at_hi),
                           sp=(k == 15 and wt is a_wo_lo[b]):
                           e.matmul(pst[:, :], wt[:, ts(k, 128)], at[:, asl],
                                    start=st, stop=sp), sem="t")
        mmd = p.val("t")
        p.wait("vector", "t", mmd)
        for nt in range(2):
            dsl = slice(oc * 1024 + nt * 512, oc * 1024 + nt * 512 + 512)
            p.inst("vector", lambda e, nt=nt, b=b, dsl=dsl: e.tensor_add(
                a_xpre[:, dsl], ps_s[nt][:, :], a_xrs[b][:, ts(nt, 512)]),
                sem="v")
        wo_done[oc] = (mmd, p.val("v"))

    def layernorm(src, dst_hi, dst_lo, out_dma=None):
        for k in range(16):
            for nt in range(2):
                sl = slice(k * 1024 + nt * 512, k * 1024 + nt * 512 + 512)
                p.wait("vector", "t")
                p.chain("vector", "v", [
                    lambda e, sl=sl: e.tensor_copy(out=lnhi[:, :],
                                                   in_=src[:, sl]),
                    lambda e, sl=sl: e.tensor_sub(lnlo[:, :], src[:, sl],
                                                  lnhi[:, :]),
                ])
                sd = p.val("v")
                p.wait("tensor", "v", sd)
                for i, t_ in enumerate((lnhi, lnlo)):
                    tgt = ps_d if nt == 0 else ps_e[0]
                    p.inst("tensor", lambda e, t_=t_, tgt=tgt, k=k, i=i:
                           e.matmul(tgt[0:1, :], ones_b[:, :], t_[:, :],
                                    start=(k == 0 and i == 0),
                                    stop=(k == 15 and i == 1)), sem="t")
                p.wait("scalar", "v", sd)
                p.wait("scalar", "t")
                p.inst("scalar", lambda e, sl=sl, nt=nt: e.activation(
                    out=scr_c[:, ts(nt, 512)], in_=src[:, sl],
                    func=AF.Square), sem="s")
                sqd = p.val("s")
                p.wait("vector", "s", sqd)
                p.wait("vector", "t")
                p.chain("vector", "v", [
                    lambda e, nt=nt: e.tensor_copy(out=lnhi[:, :],
                                                   in_=scr_c[:, ts(nt, 512)]),
                    lambda e, nt=nt: e.tensor_sub(lnlo[:, :],
                                                  scr_c[:, ts(nt, 512)],
                                                  lnhi[:, :]),
                ])
                sd2 = p.val("v")
                p.wait("tensor", "v", sd2)
                for i, t_ in enumerate((lnhi, lnlo)):
                    tgt = ps_s[0] if nt == 0 else ps_e[1]
                    p.inst("tensor", lambda e, t_=t_, tgt=tgt, k=k, i=i:
                           e.matmul(tgt[0:1, :], ones_b[:, :], t_[:, :],
                                    start=(k == 0 and i == 0),
                                    stop=(k == 15 and i == 1)), sem="t")
        std = p.val("t")
        p.wait("vector", "t", std)
        p.chain("vector", "v", [
            lambda e: e.tensor_scalar(out=stat1[0:1, 0:512], in0=ps_d[0:1, :],
                                      scalar1=1.0 / D, scalar2=None,
                                      op0=ALU.mult),
            lambda e: e.tensor_scalar(out=stat1[0:1, 512:1024],
                                      in0=ps_e[0][0:1, :], scalar1=1.0 / D,
                                      scalar2=None, op0=ALU.mult),
            lambda e: e.tensor_scalar(out=stat2[0:1, 0:512],
                                      in0=ps_s[0][0:1, :],
                                      scalar1=1.0 / D, scalar2=None,
                                      op0=ALU.mult),
            lambda e: e.tensor_scalar(out=stat2[0:1, 512:1024],
                                      in0=ps_e[1][0:1, :], scalar1=1.0 / D,
                                      scalar2=None, op0=ALU.mult),
            lambda e: e.tensor_mul(alpha[0:1, :], stat1[0:1, :], stat1[0:1, :]),
            lambda e: e.tensor_sub(stat2[0:1, :], stat2[0:1, :], alpha[0:1, :]),
        ])
        p.wait("vector", "v")
        p.chain("vector", "v", [
            lambda e: e.tensor_scalar(out=stat2[0:1, :], in0=stat2[0:1, :],
                                      scalar1=LN_EPS, scalar2=None,
                                      op0=ALU.add),
            lambda e: e.reciprocal(stat2[0:1, :], stat2[0:1, :]),
        ])
        p.wait("scalar", "v")
        p.inst("scalar", lambda e: e.activation(
            out=stat2[0:1, :], in_=stat2[0:1, :], func=AF.Sqrt), sem="s")
        rsd = p.val("s")
        p.wait("tensor", "s", rsd)
        for nt in range(2):
            mtgt = ps_d if nt == 0 else ps_e[0]
            rtgt = ps_x if nt == 0 else ps_e[1]
            p.inst("tensor", lambda e, nt=nt, mtgt=mtgt: e.matmul(
                mtgt[:, :], ones_row[0:1, :], stat1[0:1, ts(nt, 512)],
                start=True, stop=True), sem="t")
            p.inst("tensor", lambda e, nt=nt, rtgt=rtgt: e.matmul(
                rtgt[:, :], ones_row[0:1, :], stat2[0:1, ts(nt, 512)],
                start=True, stop=True), sem="t")
        bcd = p.val("t")
        p.wait("vector", "t", bcd)
        for k in range(16):
            for nt in range(2):
                sl = slice(k * 1024 + nt * 512, k * 1024 + nt * 512 + 512)
                mean_b = ps_d if nt == 0 else ps_e[0]
                rstd_b = ps_x if nt == 0 else ps_e[1]
                p.wait("vector", "v")
                p.wait("vector", "st")
                ops = [
                    lambda e, sl=sl, mean_b=mean_b: e.tensor_sub(
                        scr_b[:, 0:512], src[:, sl], mean_b[:, :]),
                    lambda e, sl=sl, rstd_b=rstd_b: e.tensor_mul(
                        scr_b[:, 512:1024], scr_b[:, 0:512], rstd_b[:, :]),
                ]
                if dst_hi is not None:
                    ops += [
                        lambda e, sl=sl: e.tensor_copy(out=dst_hi[:, sl],
                                                       in_=scr_b[:, 512:1024]),
                        lambda e, sl=sl: e.tensor_sub(dst_lo[:, sl],
                                                      scr_b[:, 512:1024],
                                                      dst_hi[:, sl]),
                    ]
                p.chain("vector", "v", ops)
                if out_dma is not None:
                    od = p.val("v")
                    p.wait("sync", "v", od)
                    p.dma("sync", out_dma(k, nt), scr_b[:, 512:1024], sem="st")

    if stop_after == "wo":
        _finish_early()
        return nc, dbg

    layernorm(a_xpre, a_x1_hi, a_x1_lo)
    if _DEBUG:
        p.wait("sync", "v")
        for k in range(16):
            p.dma("sync", dbg["d_x1"][ts(k, 128), :], a_xpre[:, ts(k, 1024)],
                  sem="st")
    p.barrier()

    if stop_after == "ln1":
        _finish_early()
        return nc, dbg

    a_fu_hi = av(64 * 1024, 128, 6 * 1024, BF16)
    a_fu_lo = av(76 * 1024, 128, 6 * 1024, BF16)
    a_dacc = av(88 * 1024, 128, 16 * 1024, F32)
    a_gw = [av(152 * 1024, 128, 2048, BF16), av(156 * 1024, 128, 2048, BF16)]
    a_uw = [av(160 * 1024, 128, 2048, BF16), av(164 * 1024, 128, 2048, BF16)]
    a_dw = [av(168 * 1024, 128, 768, BF16), av(170 * 1024, 128, 768, BF16)]

    gu_done = {}
    for e8 in range(8):
        for fc6 in range(6):
            fc = e8 * 6 + fc6
            b = fc % 2
            if fc >= 2:
                p.wait("sync", "t", gu_done[fc - 2])
            p.dma("sync", a_gw[b][:, :],
                  wq_gate[:, ts(fc, 128)].rearrange("(a p) c -> p a c", p=128))
            p.dma("sync", a_uw[b][:, :],
                  wq_up[:, ts(fc, 128)].rearrange("(a p) c -> p a c", p=128))
            p.qbrk("sync", "ld")
            p.wait("tensor", "ld")
            p.wait("tensor", "v")
            p.wait("tensor", "s")
            for k in range(16):
                for (psp, wsrc_b) in ((ps_s, a_gw[b]), (ps_e, a_uw[b])):
                    for nt in range(2):
                        xsl = slice(k * 1024 + nt * 512,
                                    k * 1024 + nt * 512 + 512)
                        for xt_ in (a_x1_hi, a_x1_lo):
                            p.inst("tensor", lambda e, wsrc_b=wsrc_b, k=k,
                                   xt_=xt_, xsl=xsl, pst=psp[nt],
                                   st=(k == 0 and xt_ is a_x1_hi),
                                   sp=(k == 15 and xt_ is a_x1_lo):
                                   e.matmul(pst[:, :], wsrc_b[:, ts(k, 128)],
                                            xt_[:, xsl], start=st, stop=sp),
                                   sem="t")
            gu_done[fc] = p.val("t")
            p.wait("scalar", "t", gu_done[fc])
            p.wait("scalar", "v")
            for nt in range(2):
                p.inst("scalar", lambda e, nt=nt, fc=fc: e.activation(
                    out=scr_b[:, ts(nt, 512)], in_=ps_s[nt][:, :],
                    func=AF.Silu, scale=gs_c[:, fc:fc + 1]), sem="s")
            sid = p.val("s")
            p.wait("vector", "s", sid)
            p.wait("vector", "t", gu_done[fc])
            for nt in range(2):
                fsl = slice(fc6 * 1024 + nt * 512, fc6 * 1024 + nt * 512 + 512)
                p.chain("vector", "v", [
                    lambda e, nt=nt, fc=fc: e.scalar_tensor_tensor(
                        out=scr_c[:, ts(nt, 512)], in0=ps_e[nt][:, :],
                        scalar=us_c[:, fc:fc + 1], in1=scr_b[:, ts(nt, 512)],
                        op0=ALU.mult, op1=ALU.mult),
                    lambda e, nt=nt, fsl=fsl: e.tensor_copy(
                        out=a_fu_hi[:, fsl], in_=scr_c[:, ts(nt, 512)]),
                    lambda e, nt=nt, fsl=fsl: e.tensor_sub(
                        a_fu_lo[:, fsl], scr_c[:, ts(nt, 512)],
                        a_fu_hi[:, fsl]),
                ])
        fu_done = p.val("v")
        p.wait("tensor", "v", fu_done)
        dw_done = {}
        for oc in range(16):
            b = oc % 2
            if oc >= 2:
                p.wait("sync", "t", dw_done[oc - 2])
            p.dma("sync", a_dw[b][:, :],
                  wq_down[e8 * 768:(e8 + 1) * 768, ts(oc, 128)]
                  .rearrange("(a p) c -> p a c", p=128))
            p.qbrk("sync", "ld")
            p.wait("tensor", "ld")
            p.wait("tensor", "v")
            for k in range(6):
                for nt in range(2):
                    fsl = slice(k * 1024 + nt * 512, k * 1024 + nt * 512 + 512)
                    for ft in (a_fu_hi, a_fu_lo):
                        p.inst("tensor", lambda e, b=b, k=k, ft=ft, fsl=fsl,
                               pst=ps_av[nt],
                               st=(k == 0 and ft is a_fu_hi),
                               sp=(k == 5 and ft is a_fu_lo):
                               e.matmul(pst[:, :], a_dw[b][:, ts(k, 128)],
                                        ft[:, fsl], start=st, stop=sp),
                               sem="t")
            dw_done[oc] = p.val("t")
            p.wait("vector", "t", dw_done[oc])
            for nt in range(2):
                dsl = slice(oc * 1024 + nt * 512, oc * 1024 + nt * 512 + 512)
                if e8 == 0:
                    p.inst("vector", lambda e, nt=nt, dsl=dsl: e.tensor_copy(
                        out=a_dacc[:, dsl], in_=ps_av[nt][:, :]), sem="v")
                else:
                    p.inst("vector", lambda e, nt=nt, dsl=dsl: e.tensor_add(
                        a_dacc[:, dsl], a_dacc[:, dsl], ps_av[nt][:, :]),
                        sem="v")
        p.barrier(("tensor", "vector", "scalar", "sync"))

    for k in range(16):
        for nt in range(2):
            sl = slice(k * 1024 + nt * 512, k * 1024 + nt * 512 + 512)
            p.chain("vector", "v", [
                lambda e, sl=sl, k=k: e.scalar_tensor_tensor(
                    out=a_dacc[:, sl], in0=a_dacc[:, sl],
                    scalar=ds_c[:, k:k + 1], in1=a_x1_hi[:, sl],
                    op0=ALU.mult, op1=ALU.add),
                lambda e, sl=sl: e.tensor_add(a_dacc[:, sl], a_dacc[:, sl],
                                              a_x1_lo[:, sl]),
            ])
    p.barrier(("vector", "tensor", "scalar"))

    def out_dma(k, nt):
        return out_t[ts(k, 128), nt * 512:(nt + 1) * 512]

    layernorm(a_dacc, None, None, out_dma=out_dma)

    p.wait("gpsimd", "st")
    with allow_ici_all_to_all():
        p.build()
    return nc, dbg



_BUILT = None


def kernel(L0_attn_x, L0_attn_w_qa, L0_attn_q_a_norm_w, L0_attn_w_qb,
           L0_attn_w_kva, L0_attn_kv_a_norm_w, L0_attn_w_kvb, L0_attn_w_o,
           L0_ffn_gate_w, L0_ffn_up_w, L0_ffn_down_w, L0_ffn_gate_scale,
           L0_ffn_up_scale, L0_ffn_down_scale):
    global _BUILT, LAST_DEBUG
    if _BUILT is None:
        _BUILT = build_nc()
    nc, dbg = _BUILT

    x = np.ascontiguousarray(np.asarray(L0_attn_x, dtype=np.float32)[0])
    com = {
        "w_qa": np.ascontiguousarray(L0_attn_w_qa, dtype=np.float32),
        "w_kva": np.ascontiguousarray(L0_attn_w_kva, dtype=np.float32),
        "w_o": np.ascontiguousarray(L0_attn_w_o, dtype=np.float32),
        "gate_w": np.ascontiguousarray(L0_ffn_gate_w, dtype=np.float32),
        "up_w": np.ascontiguousarray(L0_ffn_up_w, dtype=np.float32),
        "down_w": np.ascontiguousarray(L0_ffn_down_w, dtype=np.float32),
        "g_q": np.ascontiguousarray(
            np.asarray(L0_attn_q_a_norm_w, dtype=np.float32).reshape(-1, 128).T),
        "g_kv": np.ascontiguousarray(
            np.asarray(L0_attn_kv_a_norm_w, dtype=np.float32).reshape(-1, 128).T),
        "gate_s": np.ascontiguousarray(
            np.asarray(L0_ffn_gate_scale, dtype=np.float32).reshape(-1, 128).T),
        "up_s": np.ascontiguousarray(
            np.asarray(L0_ffn_up_scale, dtype=np.float32).reshape(-1, 128).T),
        "down_s": np.ascontiguousarray(
            np.asarray(L0_ffn_down_scale, dtype=np.float32).reshape(-1, 128).T),
    }
    wqb = np.asarray(L0_attn_w_qb, dtype=np.float32)
    wkvb = np.asarray(L0_attn_w_kvb, dtype=np.float32)
    in_maps = []
    for c in range(NCORES):
        m = dict(com)
        m["x_t"] = np.ascontiguousarray(x[c * T:(c + 1) * T, :].T)
        m["w_qb_h"] = np.ascontiguousarray(
            wqb[:, c * HL * D_QK:(c + 1) * HL * D_QK])
        m["w_kvb_h"] = np.ascontiguousarray(
            wkvb[:, c * HL * 256:(c + 1) * HL * 256])
        m["tok_base"] = np.full((1, 1), c * T, dtype=np.float32)
        in_maps.append(m)

    res = run_bass_kernel_spmd(nc, in_maps, core_ids=list(range(NCORES)))
    out = np.empty((S, D), dtype=np.float32)
    for c in range(NCORES):
        out[c * T:(c + 1) * T, :] = res.results[c]["out_t"].T
    if _DEBUG:
        LAST_DEBUG = list(res.results)
    return out[None, :, :]


# version 08
# speedup vs baseline: 1.0765x; 1.0222x over previous
import math

import numpy as np

import concourse.bass as bass
import concourse.mybir as mybir
from concourse.bass_utils import run_bass_kernel_spmd
from concourse.aot_env import allow_ici_all_to_all

F32 = mybir.dt.float32
BF16 = mybir.dt.bfloat16
I32 = mybir.dt.int32
F8 = mybir.dt.float8e4
U8 = mybir.dt.uint8
AF = mybir.ActivationFunctionType
ALU = mybir.AluOpType

NCORES = 8
S = 8192
D = 2048
T = S // NCORES
H = 16
HL = H // NCORES
Q_LORA = 768
KV_LORA = 512
D_NOPE = 128
D_ROPE = 64
D_V = 128
D_QK = D_NOPE + D_ROPE
I_FFN = 6144
QK_SCALE = 1.0 / math.sqrt(D_QK)
RMS_EPS = 1e-6
LN_EPS = 1e-5
THETA = 10000.0
AG_COLS = Q_LORA + KV_LORA + D_ROPE

C1 = float(np.float32(2 * np.pi))
C2 = float(2 * np.pi - np.float64(np.float32(2 * np.pi)))

_DEBUG = False
LAST_DEBUG = None
_PCFG = {"mm": True, "evac": True, "split": True, "evac_sq": True, "evac_cp": True, "evac_mm": True}


def ts(i, n):
    return slice(i * n, (i + 1) * n)


class Prog:

    ENGINES = ("sync", "gpsimd", "tensor", "vector", "scalar")

    def __init__(self, nc):
        self.nc = nc
        self.ops = {e: [] for e in self.ENGINES}
        self.sems = {}
        self.cnt = {}

    def sem(self, name):
        if name not in self.sems:
            self.sems[name] = self.nc.alloc_semaphore(name)
            self.cnt[name] = 0
        return self.sems[name]

    def inst(self, eng, make, sem=None, n=1):
        if sem is not None:
            self.sem(sem)
            self.cnt[sem] += n
            self.ops[eng].append(
                lambda e, make=make, s=sem, n=n: make(e).then_inc(self.sems[s], n))
            return self.cnt[sem]
        self.ops[eng].append(lambda e, make=make: make(e))
        return None

    def dma(self, eng, out, in_, sem="ld", slow=False):
        return self.inst(
            eng,
            lambda e, o=out, i=in_, s=slow: e.dma_start(
                out=o, in_=i, allow_slow_non_contiguous=s),
            sem=sem, n=16)

    def wait(self, eng, sem, val=None):
        self.sem(sem)
        v = self.cnt[sem] if val is None else val
        if v > 0:
            self.ops[eng].append(lambda e, s=sem, v=v: e.wait_ge(self.sems[s], v))

    def val(self, sem):
        self.sem(sem)
        return self.cnt[sem]

    def qbrk(self, eng, sem):
        self.wait(eng, sem)

    def chain(self, eng, sem, makes):
        for i, mk in enumerate(makes):
            if i > 0:
                self.wait(eng, sem)
            self.inst(eng, mk, sem=sem)

    def barrier(self, engines=None):
        engines = engines or self.ENGINES
        snap = dict(self.cnt)
        for e in engines:
            for s, v in snap.items():
                if v > 0:
                    self.ops[e].append(
                        lambda eng, s=s, v=v: eng.wait_ge(self.sems[s], v))

    def build(self):
        with self.nc.Block() as block:
            for name in self.ENGINES:
                getattr(block, name)(
                    lambda eng, name=name: [fn(eng) for fn in self.ops[name]])



def build_nc(stop_after=None):
    nc = bass.Bass("TRN2", target_bir_lowering=False, debug=False)
    p = Prog(nc)

    dt_ = nc.dram_tensor
    x_t = dt_("x_t", [D, T], F32, kind="ExternalInput").ap()
    w_qa = dt_("w_qa", [D, Q_LORA], F32, kind="ExternalInput").ap()
    w_qb_h = dt_("w_qb_h", [Q_LORA, HL * D_QK], F32, kind="ExternalInput").ap()
    w_kva = dt_("w_kva", [D, KV_LORA + D_ROPE], F32, kind="ExternalInput").ap()
    w_kvb_h = dt_("w_kvb_h", [KV_LORA, HL * 256], F32, kind="ExternalInput").ap()
    w_o = dt_("w_o", [D, D], F32, kind="ExternalInput").ap()
    gate_w = dt_("gate_w", [D, I_FFN], F32, kind="ExternalInput").ap()
    up_w = dt_("up_w", [D, I_FFN], F32, kind="ExternalInput").ap()
    down_w = dt_("down_w", [I_FFN, D], F32, kind="ExternalInput").ap()
    g_q = dt_("g_q", [128, Q_LORA // 128], F32, kind="ExternalInput").ap()
    g_kv = dt_("g_kv", [128, KV_LORA // 128], F32, kind="ExternalInput").ap()
    gate_s = dt_("gate_s", [128, I_FFN // 128], F32, kind="ExternalInput").ap()
    up_s = dt_("up_s", [128, I_FFN // 128], F32, kind="ExternalInput").ap()
    down_s = dt_("down_s", [128, D // 128], F32, kind="ExternalInput").ap()
    tok_base = dt_("tok_base", [1, 1], F32, kind="ExternalInput").ap()
    out_t = dt_("out_t", [D, T], F32, kind="ExternalOutput").ap()

    ag_in = dt_("ag_in", [T, AG_COLS], F32).ap()
    ag_out = dt_("ag_out", [S, AG_COLS], F32).ap()
    a2a_in = dt_("a2a_in", [S, HL * D_V], F32).ap()
    a2a_out = dt_("a2a_out", [S, HL * D_V], F32).ap()
    wq_gate = dt_("wq_gate", [D, I_FFN], BF16).ap()
    wq_up = dt_("wq_up", [D, I_FFN], BF16).ap()
    wq_down = dt_("wq_down", [I_FFN, D], BF16).ap()
    rope_t = dt_("rope_t", [2, 64, S], F32).ap()

    dbg = {}
    if _DEBUG:
        dbg["d_ag"] = dt_("d_ag", [T, AG_COLS], F32, kind="ExternalOutput").ap()
        dbg["d_q"] = dt_("d_q", [4 * 128, S], BF16, kind="ExternalOutput").ap()
        dbg["d_k"] = dt_("d_k", [3 * 128, S], BF16, kind="ExternalOutput").ap()
        dbg["d_v"] = dt_("d_v", [2 * 128, S], BF16, kind="ExternalOutput").ap()
        dbg["d_attn"] = dt_("d_attn", [S, HL * D_V], F32, kind="ExternalOutput").ap()
        dbg["d_x1"] = dt_("d_x1", [D, T], F32, kind="ExternalOutput").ap()
        dbg["d_m"] = dt_("d_m", [1, 4], F32, kind="ExternalOutput").ap()

    A = nc.alloc_sbuf_tensor
    ident = A("ident", [128, 128], BF16).ap()
    identf = A("identf", [128, 128], F32).ap()
    rot = A("rot", [64, 64], BF16).ap()
    masks = A("masks", [128, 4 * 512], BF16).ap()
    ones_b = A("ones_b", [128, 1], BF16).ap()
    ones_row = A("ones_row", [1, 128], F32).ap()
    ones_f = A("ones_f", [128, 1], F32).ap()
    c_eps_rms = A("c_eps_rms", [1, 1], F32).ap()
    c_eps_ln = A("c_eps_ln", [1, 1], F32).ap()
    neg_m = A("neg_m", [128, 1], F32).ap()
    m_tmp = A("m_tmp", [1, 4], F32).ap()
    qn2max = A("qn2max", [1, 1], F32).ap()
    kn2max = A("kn2max", [1, 1], F32).ap()
    tokb = A("tokb", [1, 1], F32).ap()
    gq_c = A("gq_c", [128, Q_LORA // 128], F32).ap()
    gkv_c = A("gkv_c", [128, KV_LORA // 128], F32).ap()
    gs_c = A("gs_c", [128, I_FFN // 128], F32).ap()
    us_c = A("us_c", [128, I_FFN // 128], F32).ap()
    ds_c = A("ds_c", [128, D // 128], F32).ap()
    invf = A("invf", [64, 1], F32).ap()
    invf_i = A("invf_i", [64, 1], I32).ap()
    alpha = A("alpha", [1, T], F32).ap()
    stat1 = A("stat1", [1, T], F32).ap()
    stat2 = A("stat2", [1, T], F32).ap()
    recip_sb = A("recip_sb", [1, 512], F32).ap()
    scr_a = A("scr_a", [128, 1024], F32).ap()
    scr_b = A("scr_b", [128, 1024], F32).ap()
    scr_c = A("scr_c", [128, 1024], F32).ap()
    lnhi = A("lnhi", [128, 512], BF16).ap()
    lnlo = A("lnlo", [128, 512], BF16).ap()

    ARENA = 172 * 1024
    arena_t = A("arena", [128, ARENA], U8)
    views = {F32: arena_t.bitcast(F32).ap(), BF16: arena_t.bitcast(BF16).ap(),
             I32: arena_t.bitcast(I32).ap(), U8: arena_t.ap()}

    def av(off, rows, cols, dt):
        sz = {F32: 4, BF16: 2, I32: 4, U8: 1}[dt]
        assert off % 4 == 0 and off + cols * sz <= ARENA, (off, cols, dt)
        return views[dt][0:rows, off // sz: off // sz + cols]

    PS = nc.alloc_psum_tensor
    ps_s = [PS(f"ps_s{i}", [128, 512], F32).ap() for i in range(2)]
    ps_e = [PS(f"ps_e{i}", [128, 512], F32).ap() for i in range(2)]
    ps_av = [PS(f"ps_av{i}", [128, 512], F32).ap() for i in range(2)]
    ps_d = PS("ps_d", [128, 512], F32).ap()
    ps_x = PS("ps_x", [128, 512], F32).ap()

    p.chain("gpsimd", "g", [
        lambda e: e.memset(ident[:, :], 0.0),
        lambda e: e.affine_select(out=ident[:, :], in_=ident[:, :],
                                  compare_op=ALU.not_equal, fill=1.0, base=0,
                                  pattern=[[-1, 128]], channel_multiplier=1),
        lambda e: e.memset(identf[:, :], 0.0),
        lambda e: e.affine_select(out=identf[:, :], in_=identf[:, :],
                                  compare_op=ALU.not_equal, fill=1.0, base=0,
                                  pattern=[[-1, 128]], channel_multiplier=1),
    ])
    for i in range(4):
        p.chain("gpsimd", "g", [
            lambda e, i=i: e.memset(masks[:, ts(i, 512)], 1.0),
            lambda e, i=i: e.affine_select(
                out=masks[:, ts(i, 512)], in_=masks[:, ts(i, 512)],
                compare_op=ALU.is_ge, fill=0.0, base=-128 * i,
                pattern=[[1, 512]], channel_multiplier=-1),
        ])
    p.chain("gpsimd", "g", [
        lambda e: e.memset(rot[:, :], 0.0),
        lambda e: e.affine_select(
            out=rot[:, 1:64:2], in_=rot[:, 1:64:2], compare_op=ALU.not_equal,
            fill=1.0, base=0, pattern=[[-2, 32]], channel_multiplier=1),
        lambda e: e.affine_select(
            out=rot[:, 0:64:2], in_=rot[:, 0:64:2], compare_op=ALU.not_equal,
            fill=-1.0, base=-1, pattern=[[-2, 32]], channel_multiplier=1),
        lambda e: e.iota(invf_i[:, :], pattern=[[0, 1]], base=0,
                         channel_multiplier=1),
    ])
    for mk in (lambda e: e.memset(ones_b[:, :], 1.0),
               lambda e: e.memset(ones_row[:, :], 1.0),
               lambda e: e.memset(ones_f[:, :], 1.0),
               lambda e: e.memset(c_eps_rms[:, :], RMS_EPS),
               lambda e: e.memset(c_eps_ln[:, :], LN_EPS)):
        p.inst("vector", mk, sem="v")
    p.dma("sync", gq_c[:, :], g_q[:, :])
    p.dma("sync", gkv_c[:, :], g_kv[:, :])
    p.dma("sync", gs_c[:, :], gate_s[:, :])
    p.dma("sync", us_c[:, :], up_s[:, :])
    p.dma("sync", ds_c[:, :], down_s[:, :])
    p.dma("sync", tokb[:, :], tok_base[:, :])
    p.qbrk("sync", "ld")
    p.wait("vector", "g")
    p.wait("vector", "v")
    p.chain("vector", "v", [
        lambda e: e.tensor_copy(out=invf[:, :], in_=invf_i[:, :]),
        lambda e: e.tensor_scalar(out=scr_a[0:64, 0:1], in0=invf[:, :],
                                  scalar1=32.0, scalar2=-32.0,
                                  op0=ALU.is_ge, op1=ALU.mult),
        lambda e: e.tensor_add(invf[:, :], invf[:, :], scr_a[0:64, 0:1]),
        lambda e: e.tensor_scalar(out=invf[:, :], in0=invf[:, :],
                                  scalar1=-math.log(THETA) / 32.0, scalar2=None,
                                  op0=ALU.mult),
    ])
    p.wait("scalar", "v")
    p.inst("scalar", lambda e: e.activation(out=invf[:, :], in_=invf[:, :],
                                            func=AF.Exp), sem="s")

    def build_tables(pos_row, w, cos_o, sin_o, frq, rrf, rri, ps_bc):
        for i in range(max(1, w // 512)):
            wslc = ts(i, 512) if w >= 512 else slice(0, w)
            n = 512 if w >= 512 else w
            p.wait("tensor", "v")
            p.wait("tensor", "s")
            p.inst("tensor", lambda e, wslc=wslc, n=n: e.matmul(
                ps_bc[0:64, 0:n], ones_row[0:1, 0:64], pos_row[0:1, wslc],
                start=True, stop=True), sem="t")
            p.wait("vector", "t")
            p.inst("vector", lambda e, wslc=wslc, n=n: e.tensor_scalar(
                out=frq[0:64, wslc], in0=ps_bc[0:64, 0:n],
                scalar1=invf[0:64, 0:1], scalar2=None, op0=ALU.mult), sem="v")

        def reduce_and_sin(out_ap):
            p.wait("vector", "v")
            p.chain("vector", "v", [
                lambda e: e.tensor_scalar(out=rrf[0:64, 0:w], in0=frq[0:64, 0:w],
                                          scalar1=1.0 / (2 * math.pi),
                                          scalar2=None, op0=ALU.mult),
                lambda e: e.tensor_copy(out=rri[0:64, 0:w], in_=rrf[0:64, 0:w]),
                lambda e: e.tensor_copy(out=rrf[0:64, 0:w], in_=rri[0:64, 0:w]),
                lambda e: e.scalar_tensor_tensor(
                    out=out_ap, in0=rrf[0:64, 0:w], scalar=-C1,
                    in1=frq[0:64, 0:w], op0=ALU.mult, op1=ALU.add),
                lambda e: e.scalar_tensor_tensor(
                    out=out_ap, in0=rrf[0:64, 0:w], scalar=-C2,
                    in1=out_ap, op0=ALU.mult, op1=ALU.add),
                lambda e: e.tensor_scalar(out=rrf[0:64, 0:w], in0=out_ap,
                                          scalar1=math.pi, scalar2=None,
                                          op0=ALU.is_gt),
                lambda e: e.scalar_tensor_tensor(
                    out=out_ap, in0=rrf[0:64, 0:w], scalar=-C1,
                    in1=out_ap, op0=ALU.mult, op1=ALU.add),
                lambda e: e.scalar_tensor_tensor(
                    out=out_ap, in0=rrf[0:64, 0:w], scalar=-C2,
                    in1=out_ap, op0=ALU.mult, op1=ALU.add),
                lambda e: e.tensor_scalar(out=rrf[0:64, 0:w], in0=out_ap,
                                          scalar1=-math.pi, scalar2=None,
                                          op0=ALU.is_lt),
                lambda e: e.scalar_tensor_tensor(
                    out=out_ap, in0=rrf[0:64, 0:w], scalar=C1,
                    in1=out_ap, op0=ALU.mult, op1=ALU.add),
                lambda e: e.scalar_tensor_tensor(
                    out=out_ap, in0=rrf[0:64, 0:w], scalar=C2,
                    in1=out_ap, op0=ALU.mult, op1=ALU.add),
            ])
            p.wait("scalar", "v")
            p.inst("scalar", lambda e: e.activation(
                out=out_ap, in_=out_ap, func=AF.Sin), sem="s")

        reduce_and_sin(sin_o[0:64, 0:w])
        p.wait("vector", "s")
        p.chain("vector", "v", [
            lambda e: e.tensor_scalar(out=frq[0:64, 0:w], in0=frq[0:64, 0:w],
                                      scalar1=math.pi / 2, scalar2=None,
                                      op0=ALU.add),
        ])
        reduce_and_sin(cos_o[0:64, 0:w])

    a_xstage = [av(0, 128, 1024, F32), av(4096, 128, 1024, F32)]
    a_xhi = av(8 * 1024, 128, 16 * 1024, BF16)
    a_xlo = av(40 * 1024, 128, 16 * 1024, BF16)
    a_wf = [av(72 * 1024, 128, 2048, F32), av(80 * 1024, 128, 2048, F32)]
    a_whi = [av(88 * 1024, 128, 2048, BF16), av(92 * 1024, 128, 2048, BF16)]
    a_wlo = [av(96 * 1024, 128, 2048, BF16), av(100 * 1024, 128, 2048, BF16)]
    a_qan = av(104 * 1024, 128, 6 * 1024, F32)
    a_ckvn = av(128 * 1024, 128, 4 * 1024, F32)
    a_krope = av(144 * 1024, 64, 1024, F32)
    a_pos = av(148 * 1024, 1, 1024, F32)
    a_posi = av(152 * 1024, 1, 1024, I32)
    a_cos = av(156 * 1024, 64, 1024, F32)
    a_sin = av(160 * 1024, 64, 1024, F32)
    a_frq = av(164 * 1024, 64, 1024, F32)
    a_rri = av(168 * 1024, 64, 1024, I32)
    a_krf = av(152 * 1024, 64, 1024, F32)
    a_krhi = av(0, 64, 1024, BF16)
    a_krlo = av(2048, 64, 1024, BF16)

    def _finish_early():
        p.barrier()
        p.dma("sync", out_t[0:128, :], scr_a[:, :], sem="fin")
        p.wait("gpsimd", "fin")
        with allow_ici_all_to_all():
            p.build()

    if stop_after == "consts":
        _finish_early()
        return nc, dbg

    p.wait("gpsimd", "g")
    p.inst("gpsimd", lambda e: e.iota(a_posi[:, :], pattern=[[1, T]], base=0,
                                      channel_multiplier=0), sem="g")
    p.wait("vector", "g")
    p.wait("vector", "ld")
    p.chain("vector", "v", [
        lambda e: e.tensor_copy(out=a_pos[:, :], in_=a_posi[:, :]),
        lambda e: e.tensor_scalar(out=a_pos[:, :], in0=a_pos[:, :],
                                  scalar1=tokb[0:1, 0:1], scalar2=None,
                                  op0=ALU.add),
    ])
    build_tables(a_pos, T, a_cos, a_sin, a_frq, scr_c, a_rri, ps_x)

    if stop_after == "tables":
        _finish_early()
        return nc, dbg

    xsplit_done = {}
    for k in range(16):
        b = k % 2
        if k >= 2:
            p.wait("sync", "v", xsplit_done[k - 2])
        p.dma("sync", a_xstage[b][:, :], x_t[ts(k, 128), :])
        p.qbrk("sync", "ld")
        p.wait("vector", "ld")
        p.chain("vector", "v", [
            lambda e, k=k, b=b: e.tensor_copy(out=a_xhi[:, ts(k, 1024)],
                                              in_=a_xstage[b][:, :]),
            lambda e, k=k, b=b: e.tensor_sub(a_xlo[:, ts(k, 1024)],
                                             a_xstage[b][:, :],
                                             a_xhi[:, ts(k, 1024)]),
        ])
        xsplit_done[k] = p.val("v")

    if stop_after == "xload":
        _finish_early()
        return nc, dbg

    wstate = {"i": 0, "mm": {}, "sp": {}}

    def proj_cols(w_dram, col0, ncols, ps_pair, evac_cb):
        i = wstate["i"]
        b = i % 2
        if i >= 2:
            p.wait("sync", "v", wstate["sp"][i - 2])
        p.dma("sync", a_wf[b][:, 0:16 * ncols],
              w_dram[:, col0:col0 + ncols].rearrange("(a p) c -> p a c", p=128))
        p.qbrk("sync", "ld")
        p.wait("vector", "ld")
        if i >= 2:
            p.wait("vector", "t", wstate["mm"][i - 2])
        if _PCFG["split"]:
            for k in range(16):
                sl = ts(k, ncols)
                p.chain("vector", "v", [
                    lambda e, b=b, sl=sl: e.tensor_copy(out=a_whi[b][:, sl],
                                                        in_=a_wf[b][:, sl]),
                    lambda e, b=b, sl=sl: e.tensor_sub(a_wlo[b][:, sl],
                                                       a_wf[b][:, sl],
                                                       a_whi[b][:, sl]),
                ])
        wstate["sp"][i] = p.val("v")
        p.wait("tensor", "v", wstate["sp"][i])
        if not _PCFG["mm"]:
            wstate["mm"][i] = p.val("t")
            wstate["i"] = i + 1
            return
        for k in range(16):
            sl = ts(k, ncols)
            for (wt, xt_) in ((a_whi[b], a_xhi), (a_whi[b], a_xlo),
                              (a_wlo[b], a_xhi)):
                for nt in range(2):
                    xsl = slice(k * 1024 + nt * 512, k * 1024 + nt * 512 + 512)
                    p.inst("tensor", lambda e, wt=wt, xt_=xt_, sl=sl, xsl=xsl,
                           nt=nt,
                           st=(k == 0 and wt is a_whi[b] and xt_ is a_xhi),
                           sp=(k == 15 and wt is a_wlo[b]),
                           nc_=ncols:
                           e.matmul(ps_pair[nt][0:nc_, :], wt[:, sl],
                                    xt_[:, xsl], start=st, stop=sp), sem="t")
        wstate["mm"][i] = p.val("t")
        wstate["i"] = i + 1
        if _PCFG["evac"]:
            evac_cb()

    def make_rms_evac(dst, cc, nchunks, ps_pair):
        def cb():
            tdone = p.val("t")
            if _PCFG["evac_sq"]:
                p.wait("scalar", "t", tdone)
                for nt in range(2):
                    p.inst("scalar", lambda e, nt=nt: e.activation(
                        out=scr_b[:, ts(nt, 512)], in_=ps_pair[nt][:, :],
                        func=AF.Square), sem="s")
            if _PCFG["evac_cp"]:
                p.wait("vector", "t", tdone)
                p.wait("vector", "s")
                for nt in range(2):
                    dsl = slice(cc * 1024 + nt * 512,
                                cc * 1024 + nt * 512 + 512)
                    p.inst("vector", lambda e, nt=nt, dsl=dsl: e.tensor_copy(
                        out=dst[:, dsl], in_=ps_pair[nt][:, :]), sem="v")
            if _PCFG["evac_sq"] and _PCFG["evac_mm"]:
                sq_done = p.val("s")
                p.wait("tensor", "s", sq_done)
                for nt in range(2):
                    tgt = ps_d if nt == 0 else ps_x
                    p.inst("tensor", lambda e, nt=nt, cc=cc, tgt=tgt: e.matmul(
                        tgt[0:1, :], ones_f[:, :], scr_b[:, ts(nt, 512)],
                        start=(cc == 0), stop=(cc == nchunks - 1)), sem="t")
        return cb

    def alpha_and_scale(dst, nchunks, denom):
        ssd = p.val("t")
        p.wait("vector", "t", ssd)
        for nt in range(2):
            src = ps_d if nt == 0 else ps_x
            p.chain("vector", "v", [
                lambda e, nt=nt, src=src: e.tensor_scalar(
                    out=stat1[0:1, ts(nt, 512)], in0=src[0:1, :],
                    scalar1=1.0 / denom, scalar2=RMS_EPS,
                    op0=ALU.mult, op1=ALU.add),
                lambda e, nt=nt: e.reciprocal(stat2[0:1, ts(nt, 512)],
                                              stat1[0:1, ts(nt, 512)]),
            ])
        p.wait("scalar", "v")
        p.inst("scalar", lambda e: e.activation(
            out=alpha[0:1, :], in_=stat2[0:1, :], func=AF.Sqrt), sem="s")
        ad = p.val("s")
        p.wait("tensor", "s", ad)
        for nt in range(2):
            tgt = ps_d if nt == 0 else ps_x
            p.inst("tensor", lambda e, nt=nt, tgt=tgt: e.matmul(
                tgt[:, :], ones_row[0:1, :], alpha[0:1, ts(nt, 512)],
                start=True, stop=True), sem="t")
        bd = p.val("t")
        p.wait("vector", "t", bd)
        for cc in range(nchunks):
            for nt in range(2):
                sl = slice(cc * 1024 + nt * 512, cc * 1024 + nt * 512 + 512)
                src = ps_d if nt == 0 else ps_x
                p.inst("vector", lambda e, sl=sl, src=src: e.tensor_mul(
                    dst[:, sl], dst[:, sl], src[:, :]), sem="v")

    for cc in range(6):
        pair = ps_s if cc % 2 == 0 else ps_e
        proj_cols(w_qa, cc * 128, 128, pair,
                  make_rms_evac(a_qan, cc, 6, pair))
        if stop_after == f"proj_cc{cc}":
            _finish_early()
            return nc, dbg
    if stop_after == "proj_qa_mm":
        _finish_early()
        return nc, dbg
    alpha_and_scale(a_qan, 6, Q_LORA)
    if stop_after == "proj_qa":
        _finish_early()
        return nc, dbg
    qan_done = p.val("v")

    for cc in range(4):
        pair = ps_s if cc % 2 == 0 else ps_e
        proj_cols(w_kva, cc * 128, 128, pair, make_rms_evac(a_ckvn, cc, 4, pair))
    alpha_and_scale(a_ckvn, 4, KV_LORA)
    ckvn_done = p.val("v")

    def krope_evac():
        tdone = p.val("t")
        p.wait("vector", "t", tdone)
        for nt in range(2):
            p.inst("vector", lambda e, nt=nt: e.tensor_copy(
                out=a_krf[:, ts(nt, 512)], in_=ps_av[nt][0:64, :]), sem="v")
        p.wait("vector", "v")
        p.chain("vector", "v", [
            lambda e: e.tensor_copy(out=a_krhi[:, :], in_=a_krf[:, :]),
            lambda e: e.tensor_sub(a_krlo[:, :], a_krf[:, :], a_krhi[:, :]),
        ])
        sp = p.val("v")
        p.wait("tensor", "v", sp)
        for nt in range(2):
            tgt = ps_s[nt]
            p.inst("tensor", lambda e, nt=nt, tgt=tgt: e.matmul(
                tgt[0:64, :], rot[:, :], a_krhi[:, ts(nt, 512)],
                start=True, stop=False), sem="t")
            p.inst("tensor", lambda e, nt=nt, tgt=tgt: e.matmul(
                tgt[0:64, :], rot[:, :], a_krlo[:, ts(nt, 512)],
                start=False, stop=True), sem="t")
        rd = p.val("t")
        p.wait("vector", "t", rd)
        for nt in range(2):
            sl = ts(nt, 512)
            p.chain("vector", "v", [
                lambda e, sl=sl: e.tensor_mul(scr_a[0:64, sl], a_krf[:, sl],
                                              a_cos[0:64, sl]),
                lambda e, sl=sl, nt=nt: e.tensor_mul(
                    scr_b[0:64, sl], ps_s[nt][0:64, :], a_sin[0:64, sl]),
                lambda e, sl=sl: e.tensor_add(a_krope[:, sl], scr_a[0:64, sl],
                                              scr_b[0:64, sl]),
            ])

    if stop_after == "proj_kva":
        _finish_early()
        return nc, dbg
    proj_cols(w_kva, 512, 64, ps_av, krope_evac)

    if stop_after == "proj":
        _finish_early()
        return nc, dbg

    def store_ag(src_fm, nchunks, col_base, rows=128):
        grp_w = 4 * rows
        for cc in range(nchunks):
            for grp in range(2):
                pre = p.val("v")
                p.wait("tensor", "v", pre)
                for q in range(4):
                    tt = grp * 4 + q
                    p.inst("tensor", lambda e, cc=cc, tt=tt, q=q: e.transpose(
                        ps_x[:, q * rows:(q + 1) * rows],
                        src_fm[0:rows, cc * 1024 + tt * 128:
                               cc * 1024 + (tt + 1) * 128],
                        identf[0:rows, 0:rows]), sem="t")
                trd = p.val("t")
                p.wait("vector", "t", trd)
                p.wait("vector", "ag")
                p.inst("vector", lambda e, grp_w=grp_w: e.tensor_copy(
                    out=scr_c[:, 0:grp_w], in_=ps_x[:, 0:grp_w]), sem="v")
                evd = p.val("v")
                p.wait("sync", "v", evd)
                for q in range(4):
                    tt = grp * 4 + q
                    p.dma("sync", ag_in[ts(tt, 128),
                                        col_base + cc * rows:
                                        col_base + (cc + 1) * rows],
                          scr_c[:, q * rows:(q + 1) * rows], sem="ag")

    store_ag(a_qan, 6, 0)
    store_ag(a_ckvn, 4, Q_LORA)
    store_ag(a_krope, 1, Q_LORA + KV_LORA, rows=64)

    if _DEBUG:
        p.wait("sync", "ag")
        p.dma("sync", dbg["d_ag"][:, :], ag_in[:, :], sem="st")

    if stop_after == "p1":
        _finish_early()
        return nc, dbg

    for cblk in range(S // 512):
        p.wait("gpsimd", "v")
        p.inst("gpsimd", lambda e, cblk=cblk: e.iota(
            a_rri[0:1, 0:512], pattern=[[1, 512]], base=cblk * 512,
            channel_multiplier=0), sem="g")
        p.wait("vector", "g")
        p.wait("vector", "st")
        p.inst("vector", lambda e: e.tensor_copy(
            out=a_pos[0:1, 0:512], in_=a_rri[0:1, 0:512]), sem="v")
        p.wait("vector", "v")
        build_tables(a_pos[0:1, 0:512], 512, a_cos, a_sin, a_frq, scr_c,
                     a_rri, ps_x)
        tb = p.val("s")
        p.wait("sync", "s", tb)
        p.dma("sync", rope_t[1, :, cblk * 512:(cblk + 1) * 512],
              a_sin[0:64, 0:512], sem="st")
        p.dma("sync", rope_t[0, :, cblk * 512:(cblk + 1) * 512],
              a_cos[0:64, 0:512], sem="st")
        p.qbrk("sync", "st")

    p.wait("gpsimd", "ag")
    p.inst("gpsimd", lambda e: e.collective_compute(
        "AllGather", ALU.bypass, replica_groups=[list(range(NCORES))],
        ins=[ag_in[:, :]], outs=[ag_out[:, :]]), sem="cc")

    if stop_after == "ag":
        _finish_early()
        return nc, dbg

    p.barrier()

    if stop_after == "qz":
        _finish_early()
        return nc, dbg

    o = 0
    a_qn_hi = av(o, 128, S, BF16); o += 16 * 1024
    a_qn_lo = av(o, 128, S, BF16); o += 16 * 1024
    a_qr_A = av(o, 128, S, BF16); o += 16 * 1024
    a_qr_B = av(o, 128, S, BF16); o += 16 * 1024
    a_kn_hi = av(o, 128, S, BF16); o += 16 * 1024
    a_kn_lo = av(o, 128, S, BF16); o += 16 * 1024
    a_kr_P = av(o, 128, S, BF16); o += 16 * 1024
    a_v_hi = av(o, 128, S, BF16); o += 16 * 1024
    a_v_lo = av(o, 128, S, BF16); o += 16 * 1024
    O_W = o
    a_wb_hi = av(O_W, 128, 6 * 384, BF16)
    a_wb_lo = av(O_W + 5 * 1024, 128, 6 * 384, BF16)
    a_wvb_hi = av(O_W, 128, 4 * 512, BF16)
    a_wvb_lo = av(O_W + 5 * 1024, 128, 4 * 512, BF16)
    O_T = O_W + 10 * 1024
    a_stg = av(O_T, 128, 768, F32)
    a_qaT_hi = av(O_T + 3 * 1024, 128, 6 * 512, BF16)
    a_qaT_lo = av(O_T + 9 * 1024, 128, 6 * 512, BF16)
    a_tb_cos = av(O_T + 3 * 1024, 64, 512, F32)
    a_tb_sin = av(O_T + 5 * 1024, 64, 512, F32)
    a_tb_frq = av(O_T + 7 * 1024, 64, 512, F32)
    a_tb_rri = av(O_T + 9 * 1024, 64, 512, I32)
    a_ckvT_hi = av(O_T, 128, 4 * 512, BF16)
    a_ckvT_lo = av(O_T + 4 * 1024, 128, 4 * 512, BF16)
    a_kstg = av(O_T + 8 * 1024, 128, 512, F32)
    a_krT = av(O_T + 10 * 1024, 64, 512, BF16)


    qz_u = av(169984, 128, 1024, U8)
    qz_b = av(169984 + 1024, 128, 1024, BF16)
    qz_plan = []
    for (wsrc, wdst, ncols) in ((gate_w, wq_gate, I_FFN), (up_w, wq_up, I_FFN),
                                (down_w, wq_down, D)):
        for rch in range(wsrc.shape[0] // 128):
            for cch in range(ncols // 1024):
                qz_plan.append((wsrc, wdst, rch, cch))
    qz_state = {"i": 0, "cast": {}, "out": {}}

    def emit_qz_step():
        i = qz_state["i"]
        if i >= len(qz_plan):
            return
        wsrc, wdst, rch, cch = qz_plan[i]
        if i >= 1:
            p.wait("gpsimd", "v", qz_state["cast"][i - 1])
        p.dma("gpsimd", scr_a[:, :],
              wsrc[ts(rch, 128), cch * 1024:(cch + 1) * 1024], sem="qd")
        p.qbrk("gpsimd", "qd")
        p.wait("vector", "qd")
        if i >= 1:
            p.wait("vector", "qd", qz_state["out"][i - 1])
        p.chain("vector", "v", [
            lambda e: e.tensor_copy(out=qz_u.bitcast(F8)[:, :],
                                    in_=scr_a[:, :]),
            lambda e: e.tensor_copy(out=qz_b[:, :],
                                    in_=qz_u.bitcast(F8)[:, :]),
        ])
        qz_state["cast"][i] = p.val("v")
        p.wait("gpsimd", "v", qz_state["cast"][i])
        qz_state["out"][i] = p.dma(
            "gpsimd", wdst[ts(rch, 128), cch * 1024:(cch + 1) * 1024],
            qz_b[:, :], sem="qd")
        p.qbrk("gpsimd", "qd")
        qz_state["i"] = i + 1

    a_ehi = [av(O_T, 128, 512, BF16), av(O_T + 1024, 128, 512, BF16)]
    a_elo = [av(O_T + 2048, 128, 512, BF16), av(O_T + 3072, 128, 512, BF16)]
    a_bc = av(O_T + 4096, 128, 512, F32)
    a_fin = av(O_T + 6144, 128, 512, F32)
    a_ast = av(O_T + 8192, 128, 512, F32)
    a_em = av(O_T + 10240, 128, 512, F32)

    def load_wqb():
        p.wait("sync", "t")
        for k in range(6):
            p.wait("sync", "v")
            p.dma("sync", scr_a[:, 0:384], w_qb_h[ts(k, 128), :])
            p.wait("vector", "ld")
            p.chain("vector", "v", [
                lambda e, k=k: e.tensor_scalar(
                    out=scr_a[:, 0:384], in0=scr_a[:, 0:384],
                    scalar1=gq_c[:, k:k + 1], scalar2=None, op0=ALU.mult),
                lambda e, k=k: e.tensor_copy(out=a_wb_hi[:, ts(k, 384)],
                                             in_=scr_a[:, 0:384]),
                lambda e, k=k: e.tensor_sub(a_wb_lo[:, ts(k, 384)],
                                            scr_a[:, 0:384],
                                            a_wb_hi[:, ts(k, 384)]),
            ])

    def head_phase(h, wqb_loaded=False):
        if not wqb_loaded:
            load_wqb()
        p.inst("vector", lambda e: e.memset(qn2max[:, :], 0.0), sem="v")
        p.inst("vector", lambda e: e.memset(kn2max[:, :], 0.0), sem="v")
        p.wait("tensor", "v")

        for B in range(16):
            for tt in range(4):
                p.wait("gpsimd", "t")
                p.dma("gpsimd", a_stg[:, :],
                      ag_out[B * 512 + tt * 128: B * 512 + (tt + 1) * 128,
                             0:Q_LORA], sem="gd")
                p.wait("tensor", "gd")
                p.wait("tensor", "v")
                for k in range(6):
                    tgt = ps_x[:, ts(k, 128)] if k < 4 else \
                        ps_av[0][:, ts(k - 4, 128)]
                    p.inst("tensor", lambda e, k=k, tgt=tgt: e.transpose(
                        tgt, a_stg[:, ts(k, 128)], identf[:, :]), sem="t")
                trd = p.val("t")
                p.wait("vector", "t", trd)
                for k in range(6):
                    src = ps_x[:, ts(k, 128)] if k < 4 else \
                        ps_av[0][:, ts(k - 4, 128)]
                    dsl = slice(k * 512 + tt * 128, k * 512 + (tt + 1) * 128)
                    p.chain("vector", "v", [
                        lambda e, src=src, dsl=dsl: e.tensor_copy(
                            out=a_qaT_hi[:, dsl], in_=src),
                        lambda e, src=src, dsl=dsl: e.tensor_sub(
                            a_qaT_lo[:, dsl], src, a_qaT_hi[:, dsl]),
                    ])
            qaT_done = p.val("v")
            p.wait("tensor", "v", qaT_done)
            first_n = True
            first_r = True
            for k in range(6):
                wn = slice(k * 384 + h * 192, k * 384 + h * 192 + 128)
                wr = slice(k * 384 + h * 192 + 128, k * 384 + h * 192 + 192)
                qsl = ts(k, 512)
                for (wt, qt) in ((a_wb_hi, a_qaT_hi), (a_wb_hi, a_qaT_lo),
                                 (a_wb_lo, a_qaT_hi)):
                    p.inst("tensor", lambda e, wt=wt, qt=qt, wn=wn, qsl=qsl,
                           st=first_n, sp=(k == 5 and wt is a_wb_lo):
                           e.matmul(ps_s[0][:, :], wt[:, wn], qt[:, qsl],
                                    start=st, stop=sp), sem="t")
                    first_n = False
                for (wt, qt) in ((a_wb_hi, a_qaT_hi), (a_wb_hi, a_qaT_lo),
                                 (a_wb_lo, a_qaT_hi)):
                    p.inst("tensor", lambda e, wt=wt, qt=qt, wr=wr, qsl=qsl,
                           st=first_r, sp=(k == 5 and wt is a_wb_lo):
                           e.matmul(ps_e[0][0:64, :], wt[:, wr], qt[:, qsl],
                                    start=st, stop=sp), sem="t")
                    first_r = False
            qb_done = p.val("t")
            p.wait("vector", "t", qb_done)
            p.chain("vector", "v", [
                lambda e, B=B: e.tensor_copy(out=a_qn_hi[:, ts(B, 512)],
                                             in_=ps_s[0][:, :]),
                lambda e, B=B: e.tensor_sub(a_qn_lo[:, ts(B, 512)],
                                            ps_s[0][:, :],
                                            a_qn_hi[:, ts(B, 512)]),
            ])
            p.wait("scalar", "t", qb_done)
            p.wait("scalar", "v")
            p.wait("scalar", "t")
            p.inst("scalar", lambda e: e.activation(
                out=scr_b[:, 0:512], in_=ps_s[0][:, :], func=AF.Square), sem="s")
            p.chain("vector", "v", [
                lambda e: e.tensor_copy(out=scr_a[0:64, 0:512],
                                        in_=ps_e[0][0:64, :]),
                lambda e, B=B: e.tensor_copy(out=a_krT[:, :],
                                             in_=scr_a[0:64, 0:512]),
                lambda e, B=B: e.tensor_sub(scr_c[0:64, 0:512],
                                            scr_a[0:64, 0:512], a_krT[:, :]),
                lambda e, B=B: e.tensor_copy(out=a_qr_B[0:64, ts(B, 512)],
                                             in_=scr_c[0:64, 0:512]),
            ])
            rsp = p.val("v")
            p.wait("tensor", "v", rsp)
            p.inst("tensor", lambda e: e.matmul(
                ps_e[1][0:64, :], rot[:, :], a_krT[:, :],
                start=True, stop=False), sem="t")
            p.inst("tensor", lambda e, B=B: e.matmul(
                ps_e[1][0:64, :], rot[:, :], a_qr_B[0:64, ts(B, 512)],
                start=False, stop=True), sem="t")
            rot_done = p.val("t")
            p.wait("gpsimd", "v")
            p.wait("gpsimd", "t")
            p.dma("gpsimd", a_tb_cos[:, :],
                  rope_t[0, :, B * 512:(B + 1) * 512], sem="gd")
            p.dma("gpsimd", a_tb_sin[:, :],
                  rope_t[1, :, B * 512:(B + 1) * 512], sem="gd")
            p.qbrk("gpsimd", "gd")
            p.wait("vector", "gd")
            p.wait("vector", "t", rot_done)
            p.wait("vector", "s")
            p.chain("vector", "v", [
                lambda e: e.tensor_mul(scr_b[0:64, 512:1024],
                                       scr_a[0:64, 0:512], a_tb_cos[:, :]),
                lambda e: e.tensor_mul(scr_c[0:64, 0:512], ps_e[1][0:64, :],
                                       a_tb_sin[:, :]),
                lambda e: e.tensor_add(scr_a[0:64, 0:512],
                                       scr_b[0:64, 512:1024],
                                       scr_c[0:64, 0:512]),
                lambda e, B=B: e.tensor_copy(out=a_qr_A[0:64, ts(B, 512)],
                                             in_=scr_a[0:64, 0:512]),
                lambda e, B=B: e.tensor_sub(scr_c[0:64, 0:512],
                                            scr_a[0:64, 0:512],
                                            a_qr_A[0:64, ts(B, 512)]),
                lambda e, B=B: e.tensor_copy(out=a_qr_B[0:64, ts(B, 512)],
                                             in_=scr_c[0:64, 0:512]),
            ])
            p.wait("scalar", "v")
            p.inst("scalar", lambda e: e.activation(
                out=scr_b[0:64, 512:1024], in_=scr_a[0:64, 0:512],
                func=AF.Square), sem="s")
            sq2 = p.val("s")
            p.wait("tensor", "s", sq2)
            p.inst("tensor", lambda e: e.matmul(
                ps_d[0:1, :], ones_f[:, :], scr_b[:, 0:512],
                start=True, stop=False), sem="t")
            p.inst("tensor", lambda e: e.matmul(
                ps_d[0:1, :], ones_f[0:64, :], scr_b[0:64, 512:1024],
                start=False, stop=True), sem="t")
            nrm = p.val("t")
            p.wait("vector", "t", nrm)
            p.chain("vector", "v", [
                lambda e: e.tensor_reduce(m_tmp[0:1, 0:1], ps_d[0:1, :],
                                          axis=mybir.AxisListType.X,
                                          op=ALU.max),
                lambda e: e.tensor_max(qn2max[:, :], qn2max[:, :],
                                       m_tmp[0:1, 0:1]),
            ])
        qfin = p.val("v")
        p.wait("gpsimd", "v", qfin)
        p.dma("gpsimd", a_qr_A[64:128, :], a_qr_A[0:64, :], sem="gd")
        p.qbrk("gpsimd", "gd")
        p.dma("gpsimd", a_qr_B[64:128, :], a_qr_B[0:64, :], sem="gd")
        p.qbrk("gpsimd", "gd")

        p.wait("sync", "t")
        for k in range(4):
            p.wait("sync", "v")
            p.dma("sync", scr_a[:, 0:512], w_kvb_h[ts(k, 128), :])
            p.wait("vector", "ld")
            p.chain("vector", "v", [
                lambda e, k=k: e.tensor_scalar(
                    out=scr_a[:, 0:512], in0=scr_a[:, 0:512],
                    scalar1=gkv_c[:, k:k + 1], scalar2=None, op0=ALU.mult),
                lambda e, k=k: e.tensor_copy(out=a_wvb_hi[:, ts(k, 512)],
                                             in_=scr_a[:, 0:512]),
                lambda e, k=k: e.tensor_sub(a_wvb_lo[:, ts(k, 512)],
                                            scr_a[:, 0:512],
                                            a_wvb_hi[:, ts(k, 512)]),
            ])
        p.wait("tensor", "v")

        for B in range(16):
            for tt in range(4):
                p.wait("gpsimd", "t")
                p.dma("gpsimd", a_kstg[:, :],
                      ag_out[B * 512 + tt * 128: B * 512 + (tt + 1) * 128,
                             Q_LORA:Q_LORA + KV_LORA], sem="gd")
                p.wait("tensor", "gd")
                p.wait("tensor", "v")
                for k in range(4):
                    p.inst("tensor", lambda e, k=k: e.transpose(
                        ps_x[:, ts(k, 128)], a_kstg[:, ts(k, 128)],
                        identf[:, :]), sem="t")
                trd = p.val("t")
                p.wait("vector", "t", trd)
                for k in range(4):
                    dsl = slice(k * 512 + tt * 128, k * 512 + (tt + 1) * 128)
                    p.chain("vector", "v", [
                        lambda e, k=k, dsl=dsl: e.tensor_copy(
                            out=a_ckvT_hi[:, dsl], in_=ps_x[:, ts(k, 128)]),
                        lambda e, k=k, dsl=dsl: e.tensor_sub(
                            a_ckvT_lo[:, dsl], ps_x[:, ts(k, 128)],
                            a_ckvT_hi[:, dsl]),
                    ])
            ckvT_done = p.val("v")
            p.wait("tensor", "v", ckvT_done)
            first = True
            for k in range(4):
                wsl = slice(k * 512 + h * 256, k * 512 + h * 256 + 128)
                csl = ts(k, 512)
                for (wt, ct) in ((a_wvb_hi, a_ckvT_hi), (a_wvb_hi, a_ckvT_lo),
                                 (a_wvb_lo, a_ckvT_hi)):
                    p.inst("tensor", lambda e, wt=wt, ct=ct, wsl=wsl, csl=csl,
                           st=first, sp=(k == 3 and wt is a_wvb_lo):
                           e.matmul(ps_s[0][:, :], wt[:, wsl], ct[:, csl],
                                    start=st, stop=sp), sem="t")
                    first = False
            knd = p.val("t")
            p.wait("vector", "t", knd)
            p.chain("vector", "v", [
                lambda e, B=B: e.tensor_copy(out=a_kn_hi[:, ts(B, 512)],
                                             in_=ps_s[0][:, :]),
                lambda e, B=B: e.tensor_sub(a_kn_lo[:, ts(B, 512)],
                                            ps_s[0][:, :],
                                            a_kn_hi[:, ts(B, 512)]),
            ])
            p.wait("scalar", "t", knd)
            p.wait("scalar", "v")
            p.inst("scalar", lambda e: e.activation(
                out=scr_b[:, 0:512], in_=ps_s[0][:, :], func=AF.Square), sem="s")
            for tt in range(4):
                p.wait("gpsimd", "t")
                p.dma("gpsimd", a_kstg[:, 0:64],
                      ag_out[B * 512 + tt * 128: B * 512 + (tt + 1) * 128,
                             Q_LORA + KV_LORA:AG_COLS], sem="gd")
                p.wait("tensor", "gd")
                p.wait("tensor", "v")
                p.wait("tensor", "s")
                p.inst("tensor", lambda e: e.transpose(
                    ps_e[1][0:64, 0:128], a_kstg[:, 0:64], identf[:, :]),
                    sem="t")
                trd = p.val("t")
                p.wait("vector", "t", trd)
                ksl = slice(B * 512 + tt * 128, B * 512 + (tt + 1) * 128)
                p.chain("vector", "v", [
                    lambda e, ksl=ksl: e.tensor_copy(
                        out=a_kr_P[0:64, ksl], in_=ps_e[1][0:64, 0:128]),
                    lambda e, ksl=ksl: e.tensor_sub(
                        scr_c[0:64, 0:128], ps_e[1][0:64, 0:128],
                        a_kr_P[0:64, ksl]),
                    lambda e, tt=tt: e.tensor_copy(
                        out=a_krT[:, ts(tt, 128)], in_=scr_c[0:64, 0:128]),
                ])
                p.wait("scalar", "t", trd)
                p.wait("scalar", "v")
                p.inst("scalar", lambda e, tt=tt: e.activation(
                    out=scr_b[0:64, 512 + tt * 128: 512 + (tt + 1) * 128],
                    in_=ps_e[1][0:64, 0:128], func=AF.Square), sem="s")
            sqd = p.val("s")
            p.wait("tensor", "s", sqd)
            p.inst("tensor", lambda e: e.matmul(
                ps_d[0:1, :], ones_f[:, :], scr_b[:, 0:512],
                start=True, stop=False), sem="t")
            p.inst("tensor", lambda e: e.matmul(
                ps_d[0:1, :], ones_f[0:64, :], scr_b[0:64, 512:1024],
                start=False, stop=True), sem="t")
            nrm = p.val("t")
            p.wait("vector", "t", nrm)
            p.chain("vector", "v", [
                lambda e: e.tensor_reduce(m_tmp[0:1, 1:2], ps_d[0:1, :],
                                          axis=mybir.AxisListType.X,
                                          op=ALU.max),
                lambda e: e.tensor_max(kn2max[:, :], kn2max[:, :],
                                       m_tmp[0:1, 1:2]),
            ])
            krtd = p.val("v")
            p.wait("gpsimd", "v", krtd)
            p.dma("gpsimd", a_kr_P[64:128, ts(B, 512)], a_krT[:, :], sem="gd")
            for tt in range(4):
                p.wait("tensor", "v")
                first = True
                for k in range(4):
                    csl = slice(k * 512 + tt * 128, k * 512 + (tt + 1) * 128)
                    wsl = slice(k * 512 + h * 256 + 128,
                                k * 512 + h * 256 + 256)
                    for (ct, wt) in ((a_ckvT_hi, a_wvb_hi),
                                     (a_ckvT_hi, a_wvb_lo),
                                     (a_ckvT_lo, a_wvb_hi)):
                        p.inst("tensor", lambda e, ct=ct, wt=wt, csl=csl,
                               wsl=wsl, st=first,
                               sp=(k == 3 and ct is a_ckvT_lo):
                               e.matmul(ps_e[0][:, 0:128], ct[:, csl],
                                        wt[:, wsl], start=st, stop=sp),
                               sem="t")
                        first = False
                vd = p.val("t")
                p.wait("vector", "t", vd)
                vsl = slice((B * 4 + tt) * 128, (B * 4 + tt + 1) * 128)
                p.chain("vector", "v", [
                    lambda e, vsl=vsl: e.tensor_copy(out=a_v_hi[:, vsl],
                                                     in_=ps_e[0][:, 0:128]),
                    lambda e, vsl=vsl: e.tensor_sub(a_v_lo[:, vsl],
                                                    ps_e[0][:, 0:128],
                                                    a_v_hi[:, vsl]),
                ])

        p.wait("vector", "v")
        p.chain("vector", "v", [
            lambda e: e.tensor_mul(m_tmp[0:1, 2:3], qn2max[:, :], kn2max[:, :]),
        ])
        p.wait("scalar", "v")
        p.inst("scalar", lambda e: e.activation(
            out=m_tmp[0:1, 3:4], in_=m_tmp[0:1, 2:3], func=AF.Sqrt), sem="s")
        p.wait("vector", "s")
        p.chain("vector", "v", [
            lambda e: e.tensor_scalar(out=m_tmp[0:1, 0:1], in0=m_tmp[0:1, 3:4],
                                      scalar1=-QK_SCALE, scalar2=None,
                                      op0=ALU.mult),
        ])
        p.wait("tensor", "v")
        p.inst("tensor", lambda e: e.matmul(
            ps_x[:, 0:1], ones_row[0:1, :], m_tmp[0:1, 0:1],
            start=True, stop=True), sem="t")
        p.wait("vector", "t")
        p.inst("vector", lambda e: e.tensor_copy(out=neg_m[:, :],
                                                 in_=ps_x[:, 0:1]), sem="v")
        if _DEBUG:
            p.wait("sync", "v")
            p.dma("sync", dbg["d_m"][:, :], m_tmp[:, :], sem="st")
        p.barrier(("tensor", "vector", "scalar"))

        exp_done = {}
        split_done = {}
        avden_done = {}
        fin_done = {}
        rec_done = {}
        flat = []
        for qc in range(16):
            for kc in range(4 * qc + 4):
                flat.append((qc, kc))
        nj = len(flat)

        def emit_scores(j):
            qc, kc = flat[j]
            jj = j % 2
            qsl = ts(qc, 512)
            ksl = ts(kc, 128)
            if j >= 2:
                p.wait("tensor", "s", exp_done[j - 2])
            for i, (kt, qt) in enumerate((
                    (a_kn_hi, a_qn_hi), (a_kn_hi, a_qn_lo),
                    (a_kn_lo, a_qn_hi))):
                p.inst("tensor", lambda e, kt=kt, qt=qt, ksl=ksl, qsl=qsl,
                       st=(i == 0), jj=jj:
                       e.matmul(ps_s[jj][:, :], kt[:, ksl], qt[:, qsl],
                                start=st, stop=False), sem="t")
            p.inst("tensor", lambda e, ksl=ksl, qsl=qsl, jj=jj:
                   e.matmul(ps_s[jj][:, :], a_kr_P[:, ksl], a_qr_A[:, qsl],
                            start=False, stop=False), sem="t")
            return p.inst("tensor", lambda e, ksl=ksl, qsl=qsl, jj=jj:
                          e.matmul(ps_s[jj][:, :], a_kr_P[:, ksl],
                                   a_qr_B[:, qsl],
                                   start=False, stop=True), sem="t")

        def emit_exp_split(j, sc_done):
            qc, kc = flat[j]
            jj = j % 2
            p.wait("scalar", "t", sc_done)
            if j >= 2:
                p.wait("scalar", "v", split_done[j - 2])
            p.inst("scalar", lambda e, jj=jj: e.activation(
                out=a_ehi[jj][:, :], in_=ps_s[jj][:, :], func=AF.Exp,
                bias=neg_m[:, :], scale=QK_SCALE), sem="s")
            exp_done[j] = p.inst("scalar", lambda e, jj=jj: e.activation(
                out=ps_e[jj][:, :], in_=ps_s[jj][:, :], func=AF.Exp,
                bias=neg_m[:, :], scale=QK_SCALE), sem="s")
            p.wait("vector", "s", exp_done[j])
            if kc >= 4 * qc:
                didx = kc - 4 * qc
                p.chain("vector", "v", [
                    lambda e, jj=jj, didx=didx: e.tensor_mul(
                        a_ehi[jj][:, :], a_ehi[jj][:, :],
                        masks[:, ts(didx, 512)]),
                    lambda e, jj=jj, didx=didx: e.tensor_mul(
                        a_em[:, :], ps_e[jj][:, :], masks[:, ts(didx, 512)]),
                    lambda e, jj=jj: e.tensor_sub(a_elo[jj][:, :],
                                                  a_em[:, :],
                                                  a_ehi[jj][:, :]),
                ])
            else:
                p.inst("vector", lambda e, jj=jj: e.tensor_sub(
                    a_elo[jj][:, :], ps_e[jj][:, :], a_ehi[jj][:, :]), sem="v")
            split_done[j] = p.val("v")

        def emit_avden(j):
            qc, kc = flat[j]
            jj = j % 2
            qq = qc % 2
            nkc = 4 * qc + 4
            ksl = ts(kc, 128)
            p.wait("tensor", "v", split_done[j])
            if kc == 0 and qc >= 2:
                p.wait("tensor", "v", fin_done[qc - 2])
            if kc == 0 and qc >= 1:
                p.wait("tensor", "v", rec_done[qc - 1])
            for i, (vt, et) in enumerate((
                    (a_v_hi, a_ehi[jj]), (a_v_lo, a_ehi[jj]),
                    (a_v_hi, a_elo[jj]))):
                p.inst("tensor", lambda e, vt=vt, et=et, ksl=ksl, qq=qq,
                       st=(kc == 0 and i == 0),
                       sp=(kc == nkc - 1 and i == 2):
                       e.matmul(ps_av[qq][:, :], vt[:, ksl], et[:, :],
                                start=st, stop=sp), sem="t")
            avden_done[j] = p.inst(
                "tensor", lambda e, jj=jj, st=(kc == 0), sp=(kc == nkc - 1):
                e.matmul(ps_d[0:1, :], ones_b[:, :], a_ehi[jj][:, :],
                         start=st, stop=sp), sem="t")

        def emit_finalize(qc):
            qq = qc % 2
            p.wait("vector", "t", avden_done[sum(4 * q + 4 for q in range(qc + 1)) - 1])
            p.inst("vector", lambda e: e.reciprocal(recip_sb[0:1, :],
                                                    ps_d[0:1, :]), sem="v")
            rec_done[qc] = p.val("v")
            p.wait("tensor", "v", rec_done[qc])
            bc_mm = p.inst("tensor", lambda e: e.matmul(
                ps_x[:, :], ones_row[0:1, :], recip_sb[0:1, :],
                start=True, stop=True), sem="t")
            p.wait("vector", "t", bc_mm)
            p.chain("vector", "v", [
                lambda e: e.tensor_copy(out=a_bc[:, :], in_=ps_x[:, :]),
                lambda e, qq=qq: e.tensor_mul(a_fin[:, :], ps_av[qq][:, :],
                                              a_bc[:, :]),
            ])
            find = p.val("v")
            p.wait("tensor", "v", find)
            for q4 in range(4):
                p.inst("tensor", lambda e, q4=q4: e.transpose(
                    ps_x[:, ts(q4, 128)], a_fin[:, ts(q4, 128)], identf[:, :]),
                    sem="t")
            trd = p.val("t")
            p.wait("vector", "t", trd)
            p.wait("vector", "a2")
            p.inst("vector", lambda e: e.tensor_copy(out=a_ast[:, :],
                                                     in_=ps_x[:, :]), sem="v")
            fin_done[qc] = p.val("v")
            p.wait("sync", "v", fin_done[qc])
            for q4 in range(4):
                p.dma("sync", a2a_in[qc * 512 + q4 * 128:
                                     qc * 512 + (q4 + 1) * 128,
                                     h * D_V:(h + 1) * D_V],
                      a_ast[:, ts(q4, 128)], sem="a2")

        sc_done = {}
        sc_done[0] = emit_scores(0)
        for j in range(nj):
            emit_exp_split(j, sc_done[j])
            if j + 1 < nj:
                sc_done[j + 1] = emit_scores(j + 1)
            emit_avden(j)
            if j % 3 == 1:
                emit_qz_step()
            qc, kc = flat[j]
            if kc == 4 * qc + 3:
                emit_finalize(qc)

        if _DEBUG and h == 0:
            p.wait("sync", "v")
            p.wait("sync", "t")
            p.dma("sync", dbg["d_q"][0:128, :], a_qn_hi[:, :], sem="st")
            p.dma("sync", dbg["d_q"][128:256, :], a_qn_lo[:, :], sem="st")
            p.dma("sync", dbg["d_q"][256:384, :], a_qr_A[:, :], sem="st")
            p.dma("sync", dbg["d_q"][384:512, :], a_qr_B[:, :], sem="st")
            p.dma("sync", dbg["d_k"][0:128, :], a_kn_hi[:, :], sem="st")
            p.dma("sync", dbg["d_k"][128:256, :], a_kn_lo[:, :], sem="st")
            p.dma("sync", dbg["d_k"][256:384, :], a_kr_P[:, :], sem="st")
            p.dma("sync", dbg["d_v"][0:128, :], a_v_hi[:, :], sem="st")
            p.dma("sync", dbg["d_v"][128:256, :], a_v_lo[:, :], sem="st")
        p.barrier()

    for h in range(HL):
        head_phase(h)
        if stop_after == f"head{h}":
            _finish_early()
            return nc, dbg

    if _DEBUG:
        p.wait("sync", "a2")
        p.dma("sync", dbg["d_attn"][:, :], a2a_in[:, :], sem="st")

    p.wait("gpsimd", "a2")
    p.inst("gpsimd", lambda e: e.collective_compute(
        "AllToAll", ALU.bypass, replica_groups=[list(range(NCORES))],
        ins=[a2a_in[:, :]], outs=[a2a_out[:, :]]), sem="cc")

    a_at_hi = av(0, 128, 16 * 1024, BF16)
    a_at_lo = av(32 * 1024, 128, 16 * 1024, BF16)
    a_xpre = av(64 * 1024, 128, 16 * 1024, F32)
    a_wo_f = [av(128 * 1024, 128, 2048, F32), av(136 * 1024, 128, 2048, F32)]
    a_wo_hi = [av(144 * 1024, 128, 2048, BF16), av(148 * 1024, 128, 2048, BF16)]
    a_wo_lo = [av(152 * 1024, 128, 2048, BF16), av(156 * 1024, 128, 2048, BF16)]
    a_xrs = [av(160 * 1024, 128, 1024, F32), av(164 * 1024, 128, 1024, F32)]
    a_p4st = av(168 * 1024, 128, 512, F32)
    a_x1_hi = av(0, 128, 16 * 1024, BF16)
    a_x1_lo = av(32 * 1024, 128, 16 * 1024, BF16)

    wo_pre = {}
    for oc in range(2):
        b = oc
        p.dma("sync", a_wo_f[b][:, :],
              w_o[:, ts(oc, 128)].rearrange("(a p) c -> p a c", p=128))
        p.dma("sync", a_xrs[b][:, :], x_t[ts(oc, 128), :])
        p.qbrk("sync", "ld")
        p.wait("vector", "ld")
        for k in range(16):
            sl = ts(k, 128)
            p.chain("vector", "v", [
                lambda e, b=b, sl=sl: e.tensor_copy(out=a_wo_hi[b][:, sl],
                                                    in_=a_wo_f[b][:, sl]),
                lambda e, b=b, sl=sl: e.tensor_sub(a_wo_lo[b][:, sl],
                                                   a_wo_f[b][:, sl],
                                                   a_wo_hi[b][:, sl]),
            ])
        wo_pre[oc] = p.val("v")

    p.wait("sync", "cc")
    p.wait("tensor", "cc")
    p.wait("vector", "cc")
    p.wait("gpsimd", "cc")

    for fc in range(16):
        sblk, hl = fc // 2, fc % 2
        for ttg in range(2):
            pst = ps_x if ttg == 0 else ps_av[0]
            p.wait("gpsimd", "t")
            p.dma("gpsimd", a_p4st[:, :],
                  a2a_out[sblk * 1024 + ttg * 512: sblk * 1024 + (ttg + 1) * 512,
                          hl * 128:(hl + 1) * 128]
                  .rearrange("(a p) c -> p a c", p=128), sem="gd")
            p.wait("tensor", "gd")
            p.wait("tensor", "v")
            for q4 in range(4):
                p.inst("tensor", lambda e, q4=q4, pst=pst: e.transpose(
                    pst[:, ts(q4, 128)], a_p4st[:, ts(q4, 128)], identf[:, :]),
                    sem="t")
            trd = p.val("t")
            p.wait("vector", "t", trd)
            dsl = slice(fc * 1024 + ttg * 512, fc * 1024 + (ttg + 1) * 512)
            p.chain("vector", "v", [
                lambda e, dsl=dsl, pst=pst: e.tensor_copy(out=a_at_hi[:, dsl],
                                                          in_=pst[:, :]),
                lambda e, dsl=dsl, pst=pst: e.tensor_sub(a_at_lo[:, dsl],
                                                         pst[:, :],
                                                         a_at_hi[:, dsl]),
            ])
    at_done = p.val("v")
    p.wait("tensor", "v", at_done)

    wo_done = {}
    for oc in range(16):
        b = oc % 2
        if oc >= 2:
            p.wait("sync", "v", wo_done[oc - 2][1])
            p.dma("sync", a_wo_f[b][:, :],
                  w_o[:, ts(oc, 128)].rearrange("(a p) c -> p a c", p=128))
            p.dma("sync", a_xrs[b][:, :], x_t[ts(oc, 128), :])
            p.qbrk("sync", "ld")
            p.wait("vector", "ld")
            p.wait("vector", "t", wo_done[oc - 2][0])
            for k in range(16):
                sl = ts(k, 128)
                p.chain("vector", "v", [
                    lambda e, b=b, sl=sl: e.tensor_copy(out=a_wo_hi[b][:, sl],
                                                        in_=a_wo_f[b][:, sl]),
                    lambda e, b=b, sl=sl: e.tensor_sub(a_wo_lo[b][:, sl],
                                                       a_wo_f[b][:, sl],
                                                       a_wo_hi[b][:, sl]),
                ])
            spd = p.val("v")
        else:
            spd = wo_pre[oc]
        p.wait("tensor", "v", spd)
        if oc >= 1:
            p.wait("tensor", "v", wo_done[oc - 1][1])
        for k in range(16):
            for (wt, at) in ((a_wo_hi[b], a_at_hi), (a_wo_hi[b], a_at_lo),
                             (a_wo_lo[b], a_at_hi)):
                for nt in range(2):
                    asl = slice(k * 1024 + nt * 512, k * 1024 + nt * 512 + 512)
                    p.inst("tensor", lambda e, wt=wt, at=at, k=k, asl=asl,
                           pst=ps_s[nt],
                           st=(k == 0 and wt is a_wo_hi[b] and at is a_at_hi),
                           sp=(k == 15 and wt is a_wo_lo[b]):
                           e.matmul(pst[:, :], wt[:, ts(k, 128)], at[:, asl],
                                    start=st, stop=sp), sem="t")
        mmd = p.val("t")
        p.wait("vector", "t", mmd)
        for nt in range(2):
            dsl = slice(oc * 1024 + nt * 512, oc * 1024 + nt * 512 + 512)
            p.inst("vector", lambda e, nt=nt, b=b, dsl=dsl: e.tensor_add(
                a_xpre[:, dsl], ps_s[nt][:, :], a_xrs[b][:, ts(nt, 512)]),
                sem="v")
        wo_done[oc] = (mmd, p.val("v"))

    def layernorm(src, dst_hi, dst_lo, out_dma=None):
        for k in range(16):
            for nt in range(2):
                sl = slice(k * 1024 + nt * 512, k * 1024 + nt * 512 + 512)
                p.wait("vector", "t")
                p.chain("vector", "v", [
                    lambda e, sl=sl: e.tensor_copy(out=lnhi[:, :],
                                                   in_=src[:, sl]),
                    lambda e, sl=sl: e.tensor_sub(lnlo[:, :], src[:, sl],
                                                  lnhi[:, :]),
                ])
                sd = p.val("v")
                p.wait("tensor", "v", sd)
                for i, t_ in enumerate((lnhi, lnlo)):
                    tgt = ps_d if nt == 0 else ps_e[0]
                    p.inst("tensor", lambda e, t_=t_, tgt=tgt, k=k, i=i:
                           e.matmul(tgt[0:1, :], ones_b[:, :], t_[:, :],
                                    start=(k == 0 and i == 0),
                                    stop=(k == 15 and i == 1)), sem="t")
                p.wait("scalar", "v", sd)
                p.wait("scalar", "t")
                p.inst("scalar", lambda e, sl=sl, nt=nt: e.activation(
                    out=scr_c[:, ts(nt, 512)], in_=src[:, sl],
                    func=AF.Square), sem="s")
                sqd = p.val("s")
                p.wait("vector", "s", sqd)
                p.wait("vector", "t")
                p.chain("vector", "v", [
                    lambda e, nt=nt: e.tensor_copy(out=lnhi[:, :],
                                                   in_=scr_c[:, ts(nt, 512)]),
                    lambda e, nt=nt: e.tensor_sub(lnlo[:, :],
                                                  scr_c[:, ts(nt, 512)],
                                                  lnhi[:, :]),
                ])
                sd2 = p.val("v")
                p.wait("tensor", "v", sd2)
                for i, t_ in enumerate((lnhi, lnlo)):
                    tgt = ps_s[0] if nt == 0 else ps_e[1]
                    p.inst("tensor", lambda e, t_=t_, tgt=tgt, k=k, i=i:
                           e.matmul(tgt[0:1, :], ones_b[:, :], t_[:, :],
                                    start=(k == 0 and i == 0),
                                    stop=(k == 15 and i == 1)), sem="t")
        std = p.val("t")
        p.wait("vector", "t", std)
        p.chain("vector", "v", [
            lambda e: e.tensor_scalar(out=stat1[0:1, 0:512], in0=ps_d[0:1, :],
                                      scalar1=1.0 / D, scalar2=None,
                                      op0=ALU.mult),
            lambda e: e.tensor_scalar(out=stat1[0:1, 512:1024],
                                      in0=ps_e[0][0:1, :], scalar1=1.0 / D,
                                      scalar2=None, op0=ALU.mult),
            lambda e: e.tensor_scalar(out=stat2[0:1, 0:512],
                                      in0=ps_s[0][0:1, :],
                                      scalar1=1.0 / D, scalar2=None,
                                      op0=ALU.mult),
            lambda e: e.tensor_scalar(out=stat2[0:1, 512:1024],
                                      in0=ps_e[1][0:1, :], scalar1=1.0 / D,
                                      scalar2=None, op0=ALU.mult),
            lambda e: e.tensor_mul(alpha[0:1, :], stat1[0:1, :], stat1[0:1, :]),
            lambda e: e.tensor_sub(stat2[0:1, :], stat2[0:1, :], alpha[0:1, :]),
        ])
        p.wait("vector", "v")
        p.chain("vector", "v", [
            lambda e: e.tensor_scalar(out=stat2[0:1, :], in0=stat2[0:1, :],
                                      scalar1=LN_EPS, scalar2=None,
                                      op0=ALU.add),
            lambda e: e.reciprocal(stat2[0:1, :], stat2[0:1, :]),
        ])
        p.wait("scalar", "v")
        p.inst("scalar", lambda e: e.activation(
            out=stat2[0:1, :], in_=stat2[0:1, :], func=AF.Sqrt), sem="s")
        rsd = p.val("s")
        p.wait("tensor", "s", rsd)
        for nt in range(2):
            mtgt = ps_d if nt == 0 else ps_e[0]
            rtgt = ps_x if nt == 0 else ps_e[1]
            p.inst("tensor", lambda e, nt=nt, mtgt=mtgt: e.matmul(
                mtgt[:, :], ones_row[0:1, :], stat1[0:1, ts(nt, 512)],
                start=True, stop=True), sem="t")
            p.inst("tensor", lambda e, nt=nt, rtgt=rtgt: e.matmul(
                rtgt[:, :], ones_row[0:1, :], stat2[0:1, ts(nt, 512)],
                start=True, stop=True), sem="t")
        bcd = p.val("t")
        p.wait("vector", "t", bcd)
        for k in range(16):
            for nt in range(2):
                sl = slice(k * 1024 + nt * 512, k * 1024 + nt * 512 + 512)
                mean_b = ps_d if nt == 0 else ps_e[0]
                rstd_b = ps_x if nt == 0 else ps_e[1]
                p.wait("vector", "v")
                p.wait("vector", "st")
                ops = [
                    lambda e, sl=sl, mean_b=mean_b: e.tensor_sub(
                        scr_b[:, 0:512], src[:, sl], mean_b[:, :]),
                    lambda e, sl=sl, rstd_b=rstd_b: e.tensor_mul(
                        scr_b[:, 512:1024], scr_b[:, 0:512], rstd_b[:, :]),
                ]
                if dst_hi is not None:
                    ops += [
                        lambda e, sl=sl: e.tensor_copy(out=dst_hi[:, sl],
                                                       in_=scr_b[:, 512:1024]),
                        lambda e, sl=sl: e.tensor_sub(dst_lo[:, sl],
                                                      scr_b[:, 512:1024],
                                                      dst_hi[:, sl]),
                    ]
                p.chain("vector", "v", ops)
                if out_dma is not None:
                    od = p.val("v")
                    p.wait("sync", "v", od)
                    p.dma("sync", out_dma(k, nt), scr_b[:, 512:1024], sem="st")

    if stop_after == "wo":
        _finish_early()
        return nc, dbg

    layernorm(a_xpre, a_x1_hi, a_x1_lo)
    if _DEBUG:
        p.wait("sync", "v")
        for k in range(16):
            p.dma("sync", dbg["d_x1"][ts(k, 128), :], a_xpre[:, ts(k, 1024)],
                  sem="st")
    p.barrier()

    if stop_after == "ln1":
        _finish_early()
        return nc, dbg

    a_fu_hi = av(64 * 1024, 128, 6 * 1024, BF16)
    a_fu_lo = av(76 * 1024, 128, 6 * 1024, BF16)
    a_dacc = av(88 * 1024, 128, 16 * 1024, F32)
    a_gw = [av(152 * 1024, 128, 2048, BF16), av(156 * 1024, 128, 2048, BF16)]
    a_uw = [av(160 * 1024, 128, 2048, BF16), av(164 * 1024, 128, 2048, BF16)]
    a_dw = [av(168 * 1024, 128, 768, BF16), av(170 * 1024, 128, 768, BF16)]

    gu_done = {}
    for e8 in range(8):
        for fc6 in range(6):
            fc = e8 * 6 + fc6
            b = fc % 2
            if fc >= 2:
                p.wait("sync", "t", gu_done[fc - 2])
            p.dma("sync", a_gw[b][:, :],
                  wq_gate[:, ts(fc, 128)].rearrange("(a p) c -> p a c", p=128))
            p.dma("sync", a_uw[b][:, :],
                  wq_up[:, ts(fc, 128)].rearrange("(a p) c -> p a c", p=128))
            p.qbrk("sync", "ld")
            p.wait("tensor", "ld")
            p.wait("tensor", "v")
            p.wait("tensor", "s")
            for k in range(16):
                for (psp, wsrc_b) in ((ps_s, a_gw[b]), (ps_e, a_uw[b])):
                    for nt in range(2):
                        xsl = slice(k * 1024 + nt * 512,
                                    k * 1024 + nt * 512 + 512)
                        for xt_ in (a_x1_hi, a_x1_lo):
                            p.inst("tensor", lambda e, wsrc_b=wsrc_b, k=k,
                                   xt_=xt_, xsl=xsl, pst=psp[nt],
                                   st=(k == 0 and xt_ is a_x1_hi),
                                   sp=(k == 15 and xt_ is a_x1_lo):
                                   e.matmul(pst[:, :], wsrc_b[:, ts(k, 128)],
                                            xt_[:, xsl], start=st, stop=sp),
                                   sem="t")
            gu_done[fc] = p.val("t")
            p.wait("scalar", "t", gu_done[fc])
            p.wait("scalar", "v")
            for nt in range(2):
                p.inst("scalar", lambda e, nt=nt, fc=fc: e.activation(
                    out=scr_b[:, ts(nt, 512)], in_=ps_s[nt][:, :],
                    func=AF.Silu, scale=gs_c[:, fc:fc + 1]), sem="s")
            sid = p.val("s")
            p.wait("vector", "s", sid)
            p.wait("vector", "t", gu_done[fc])
            for nt in range(2):
                fsl = slice(fc6 * 1024 + nt * 512, fc6 * 1024 + nt * 512 + 512)
                p.chain("vector", "v", [
                    lambda e, nt=nt, fc=fc: e.scalar_tensor_tensor(
                        out=scr_c[:, ts(nt, 512)], in0=ps_e[nt][:, :],
                        scalar=us_c[:, fc:fc + 1], in1=scr_b[:, ts(nt, 512)],
                        op0=ALU.mult, op1=ALU.mult),
                    lambda e, nt=nt, fsl=fsl: e.tensor_copy(
                        out=a_fu_hi[:, fsl], in_=scr_c[:, ts(nt, 512)]),
                    lambda e, nt=nt, fsl=fsl: e.tensor_sub(
                        a_fu_lo[:, fsl], scr_c[:, ts(nt, 512)],
                        a_fu_hi[:, fsl]),
                ])
        fu_done = p.val("v")
        p.wait("tensor", "v", fu_done)
        dw_done = {}
        for oc in range(16):
            b = oc % 2
            if oc >= 2:
                p.wait("sync", "t", dw_done[oc - 2])
            p.dma("sync", a_dw[b][:, :],
                  wq_down[e8 * 768:(e8 + 1) * 768, ts(oc, 128)]
                  .rearrange("(a p) c -> p a c", p=128))
            p.qbrk("sync", "ld")
            p.wait("tensor", "ld")
            p.wait("tensor", "v")
            for k in range(6):
                for nt in range(2):
                    fsl = slice(k * 1024 + nt * 512, k * 1024 + nt * 512 + 512)
                    for ft in (a_fu_hi, a_fu_lo):
                        p.inst("tensor", lambda e, b=b, k=k, ft=ft, fsl=fsl,
                               pst=ps_av[nt],
                               st=(k == 0 and ft is a_fu_hi),
                               sp=(k == 5 and ft is a_fu_lo):
                               e.matmul(pst[:, :], a_dw[b][:, ts(k, 128)],
                                        ft[:, fsl], start=st, stop=sp),
                               sem="t")
            dw_done[oc] = p.val("t")
            p.wait("vector", "t", dw_done[oc])
            for nt in range(2):
                dsl = slice(oc * 1024 + nt * 512, oc * 1024 + nt * 512 + 512)
                if e8 == 0:
                    p.inst("vector", lambda e, nt=nt, dsl=dsl: e.tensor_copy(
                        out=a_dacc[:, dsl], in_=ps_av[nt][:, :]), sem="v")
                else:
                    p.inst("vector", lambda e, nt=nt, dsl=dsl: e.tensor_add(
                        a_dacc[:, dsl], a_dacc[:, dsl], ps_av[nt][:, :]),
                        sem="v")
        p.barrier(("tensor", "vector", "scalar", "sync"))

    for k in range(16):
        for nt in range(2):
            sl = slice(k * 1024 + nt * 512, k * 1024 + nt * 512 + 512)
            p.chain("vector", "v", [
                lambda e, sl=sl, k=k: e.scalar_tensor_tensor(
                    out=a_dacc[:, sl], in0=a_dacc[:, sl],
                    scalar=ds_c[:, k:k + 1], in1=a_x1_hi[:, sl],
                    op0=ALU.mult, op1=ALU.add),
                lambda e, sl=sl: e.tensor_add(a_dacc[:, sl], a_dacc[:, sl],
                                              a_x1_lo[:, sl]),
            ])
    p.barrier(("vector", "tensor", "scalar"))

    def out_dma(k, nt):
        return out_t[ts(k, 128), nt * 512:(nt + 1) * 512]

    layernorm(a_dacc, None, None, out_dma=out_dma)

    p.wait("gpsimd", "st")
    with allow_ici_all_to_all():
        p.build()
    return nc, dbg



_BUILT = None


def kernel(L0_attn_x, L0_attn_w_qa, L0_attn_q_a_norm_w, L0_attn_w_qb,
           L0_attn_w_kva, L0_attn_kv_a_norm_w, L0_attn_w_kvb, L0_attn_w_o,
           L0_ffn_gate_w, L0_ffn_up_w, L0_ffn_down_w, L0_ffn_gate_scale,
           L0_ffn_up_scale, L0_ffn_down_scale):
    global _BUILT, LAST_DEBUG
    if _BUILT is None:
        _BUILT = build_nc()
    nc, dbg = _BUILT

    x = np.ascontiguousarray(np.asarray(L0_attn_x, dtype=np.float32)[0])
    com = {
        "w_qa": np.ascontiguousarray(L0_attn_w_qa, dtype=np.float32),
        "w_kva": np.ascontiguousarray(L0_attn_w_kva, dtype=np.float32),
        "w_o": np.ascontiguousarray(L0_attn_w_o, dtype=np.float32),
        "gate_w": np.ascontiguousarray(L0_ffn_gate_w, dtype=np.float32),
        "up_w": np.ascontiguousarray(L0_ffn_up_w, dtype=np.float32),
        "down_w": np.ascontiguousarray(L0_ffn_down_w, dtype=np.float32),
        "g_q": np.ascontiguousarray(
            np.asarray(L0_attn_q_a_norm_w, dtype=np.float32).reshape(-1, 128).T),
        "g_kv": np.ascontiguousarray(
            np.asarray(L0_attn_kv_a_norm_w, dtype=np.float32).reshape(-1, 128).T),
        "gate_s": np.ascontiguousarray(
            np.asarray(L0_ffn_gate_scale, dtype=np.float32).reshape(-1, 128).T),
        "up_s": np.ascontiguousarray(
            np.asarray(L0_ffn_up_scale, dtype=np.float32).reshape(-1, 128).T),
        "down_s": np.ascontiguousarray(
            np.asarray(L0_ffn_down_scale, dtype=np.float32).reshape(-1, 128).T),
    }
    wqb = np.asarray(L0_attn_w_qb, dtype=np.float32)
    wkvb = np.asarray(L0_attn_w_kvb, dtype=np.float32)
    in_maps = []
    for c in range(NCORES):
        m = dict(com)
        m["x_t"] = np.ascontiguousarray(x[c * T:(c + 1) * T, :].T)
        m["w_qb_h"] = np.ascontiguousarray(
            wqb[:, c * HL * D_QK:(c + 1) * HL * D_QK])
        m["w_kvb_h"] = np.ascontiguousarray(
            wkvb[:, c * HL * 256:(c + 1) * HL * 256])
        m["tok_base"] = np.full((1, 1), c * T, dtype=np.float32)
        in_maps.append(m)

    res = run_bass_kernel_spmd(nc, in_maps, core_ids=list(range(NCORES)))
    out = np.empty((S, D), dtype=np.float32)
    for c in range(NCORES):
        out[c * T:(c + 1) * T, :] = res.results[c]["out_t"].T
    if _DEBUG:
        LAST_DEBUG = list(res.results)
    return out[None, :, :]


# version 09
# speedup vs baseline: 1.0794x; 1.0026x over previous
import math

import numpy as np

import concourse.bass as bass
import concourse.mybir as mybir
from concourse.bass_utils import run_bass_kernel_spmd
from concourse.aot_env import allow_ici_all_to_all

F32 = mybir.dt.float32
BF16 = mybir.dt.bfloat16
I32 = mybir.dt.int32
F8 = mybir.dt.float8e4
U8 = mybir.dt.uint8
AF = mybir.ActivationFunctionType
ALU = mybir.AluOpType

NCORES = 8
S = 8192
D = 2048
T = S // NCORES
H = 16
HL = H // NCORES
Q_LORA = 768
KV_LORA = 512
D_NOPE = 128
D_ROPE = 64
D_V = 128
D_QK = D_NOPE + D_ROPE
I_FFN = 6144
QK_SCALE = 1.0 / math.sqrt(D_QK)
RMS_EPS = 1e-6
LN_EPS = 1e-5
THETA = 10000.0
AG_COLS = Q_LORA + KV_LORA + D_ROPE

C1 = float(np.float32(2 * np.pi))
C2 = float(2 * np.pi - np.float64(np.float32(2 * np.pi)))

_DEBUG = False
LAST_DEBUG = None
_PCFG = {"mm": True, "evac": True, "split": True, "evac_sq": True, "evac_cp": True, "evac_mm": True}


def ts(i, n):
    return slice(i * n, (i + 1) * n)


class Prog:

    ENGINES = ("sync", "gpsimd", "tensor", "vector", "scalar")

    def __init__(self, nc):
        self.nc = nc
        self.ops = {e: [] for e in self.ENGINES}
        self.sems = {}
        self.cnt = {}

    def sem(self, name):
        if name not in self.sems:
            self.sems[name] = self.nc.alloc_semaphore(name)
            self.cnt[name] = 0
        return self.sems[name]

    def inst(self, eng, make, sem=None, n=1):
        if sem is not None:
            self.sem(sem)
            self.cnt[sem] += n
            self.ops[eng].append(
                lambda e, make=make, s=sem, n=n: make(e).then_inc(self.sems[s], n))
            return self.cnt[sem]
        self.ops[eng].append(lambda e, make=make: make(e))
        return None

    def dma(self, eng, out, in_, sem="ld", slow=False):
        return self.inst(
            eng,
            lambda e, o=out, i=in_, s=slow: e.dma_start(
                out=o, in_=i, allow_slow_non_contiguous=s),
            sem=sem, n=16)

    def wait(self, eng, sem, val=None):
        self.sem(sem)
        v = self.cnt[sem] if val is None else val
        if v > 0:
            self.ops[eng].append(lambda e, s=sem, v=v: e.wait_ge(self.sems[s], v))

    def val(self, sem):
        self.sem(sem)
        return self.cnt[sem]

    def qbrk(self, eng, sem):
        self.wait(eng, sem)

    def chain(self, eng, sem, makes):
        for i, mk in enumerate(makes):
            if i > 0:
                self.wait(eng, sem)
            self.inst(eng, mk, sem=sem)

    def barrier(self, engines=None):
        engines = engines or self.ENGINES
        snap = dict(self.cnt)
        for e in engines:
            for s, v in snap.items():
                if v > 0:
                    self.ops[e].append(
                        lambda eng, s=s, v=v: eng.wait_ge(self.sems[s], v))

    def build(self):
        with self.nc.Block() as block:
            for name in self.ENGINES:
                getattr(block, name)(
                    lambda eng, name=name: [fn(eng) for fn in self.ops[name]])



def build_nc(stop_after=None):
    nc = bass.Bass("TRN2", target_bir_lowering=False, debug=False)
    p = Prog(nc)

    dt_ = nc.dram_tensor
    x_t = dt_("x_t", [D, T], F32, kind="ExternalInput").ap()
    w_qa = dt_("w_qa", [D, Q_LORA], F32, kind="ExternalInput").ap()
    w_qb_h = dt_("w_qb_h", [Q_LORA, HL * D_QK], F32, kind="ExternalInput").ap()
    w_kva = dt_("w_kva", [D, KV_LORA + D_ROPE], F32, kind="ExternalInput").ap()
    w_kvb_h = dt_("w_kvb_h", [KV_LORA, HL * 256], F32, kind="ExternalInput").ap()
    w_o = dt_("w_o", [D, D], F32, kind="ExternalInput").ap()
    gate_w = dt_("gate_w", [D, I_FFN], F32, kind="ExternalInput").ap()
    up_w = dt_("up_w", [D, I_FFN], F32, kind="ExternalInput").ap()
    down_w = dt_("down_w", [I_FFN, D], F32, kind="ExternalInput").ap()
    g_q = dt_("g_q", [128, Q_LORA // 128], F32, kind="ExternalInput").ap()
    g_kv = dt_("g_kv", [128, KV_LORA // 128], F32, kind="ExternalInput").ap()
    gate_s = dt_("gate_s", [128, I_FFN // 128], F32, kind="ExternalInput").ap()
    up_s = dt_("up_s", [128, I_FFN // 128], F32, kind="ExternalInput").ap()
    down_s = dt_("down_s", [128, D // 128], F32, kind="ExternalInput").ap()
    tok_base = dt_("tok_base", [1, 1], F32, kind="ExternalInput").ap()
    out_t = dt_("out_t", [D, T], F32, kind="ExternalOutput").ap()

    ag_in = dt_("ag_in", [T, AG_COLS], F32).ap()
    ag_out = dt_("ag_out", [S, AG_COLS], F32).ap()
    a2a_in = dt_("a2a_in", [S, HL * D_V], F32).ap()
    a2a_out = dt_("a2a_out", [S, HL * D_V], F32).ap()
    wq_gate = dt_("wq_gate", [D, I_FFN], BF16).ap()
    wq_up = dt_("wq_up", [D, I_FFN], BF16).ap()
    wq_down = dt_("wq_down", [I_FFN, D], BF16).ap()
    rope_t = dt_("rope_t", [2, 64, S], F32).ap()

    dbg = {}
    if _DEBUG:
        dbg["d_ag"] = dt_("d_ag", [T, AG_COLS], F32, kind="ExternalOutput").ap()
        dbg["d_q"] = dt_("d_q", [4 * 128, S], BF16, kind="ExternalOutput").ap()
        dbg["d_k"] = dt_("d_k", [3 * 128, S], BF16, kind="ExternalOutput").ap()
        dbg["d_v"] = dt_("d_v", [2 * 128, S], BF16, kind="ExternalOutput").ap()
        dbg["d_attn"] = dt_("d_attn", [S, HL * D_V], F32, kind="ExternalOutput").ap()
        dbg["d_x1"] = dt_("d_x1", [D, T], F32, kind="ExternalOutput").ap()
        dbg["d_m"] = dt_("d_m", [1, 4], F32, kind="ExternalOutput").ap()

    A = nc.alloc_sbuf_tensor
    ident = A("ident", [128, 128], BF16).ap()
    identf = A("identf", [128, 128], F32).ap()
    rot = A("rot", [64, 64], BF16).ap()
    masks = A("masks", [128, 4 * 512], BF16).ap()
    ones_b = A("ones_b", [128, 1], BF16).ap()
    ones_row = A("ones_row", [1, 128], F32).ap()
    ones_f = A("ones_f", [128, 1], F32).ap()
    c_eps_rms = A("c_eps_rms", [1, 1], F32).ap()
    c_eps_ln = A("c_eps_ln", [1, 1], F32).ap()
    neg_m = A("neg_m", [128, 1], F32).ap()
    m_tmp = A("m_tmp", [1, 4], F32).ap()
    qn2max = A("qn2max", [1, 1], F32).ap()
    kn2max = A("kn2max", [1, 1], F32).ap()
    tokb = A("tokb", [1, 1], F32).ap()
    gq_c = A("gq_c", [128, Q_LORA // 128], F32).ap()
    gkv_c = A("gkv_c", [128, KV_LORA // 128], F32).ap()
    gs_c = A("gs_c", [128, I_FFN // 128], F32).ap()
    us_c = A("us_c", [128, I_FFN // 128], F32).ap()
    ds_c = A("ds_c", [128, D // 128], F32).ap()
    invf = A("invf", [64, 1], F32).ap()
    invf_i = A("invf_i", [64, 1], I32).ap()
    alpha = A("alpha", [1, T], F32).ap()
    stat1 = A("stat1", [1, T], F32).ap()
    stat2 = A("stat2", [1, T], F32).ap()
    recip_sb = A("recip_sb", [1, 512], F32).ap()
    scr_a = A("scr_a", [128, 1024], F32).ap()
    scr_b = A("scr_b", [128, 1024], F32).ap()
    scr_c = A("scr_c", [128, 1024], F32).ap()
    lnhi = A("lnhi", [128, 512], BF16).ap()
    lnlo = A("lnlo", [128, 512], BF16).ap()

    ARENA = 172 * 1024
    arena_t = A("arena", [128, ARENA], U8)
    views = {F32: arena_t.bitcast(F32).ap(), BF16: arena_t.bitcast(BF16).ap(),
             I32: arena_t.bitcast(I32).ap(), U8: arena_t.ap()}

    def av(off, rows, cols, dt):
        sz = {F32: 4, BF16: 2, I32: 4, U8: 1}[dt]
        assert off % 4 == 0 and off + cols * sz <= ARENA, (off, cols, dt)
        return views[dt][0:rows, off // sz: off // sz + cols]

    PS = nc.alloc_psum_tensor
    ps_s = [PS(f"ps_s{i}", [128, 512], F32).ap() for i in range(2)]
    ps_e = [PS(f"ps_e{i}", [128, 512], F32).ap() for i in range(2)]
    ps_av = [PS(f"ps_av{i}", [128, 512], F32).ap() for i in range(2)]
    ps_d = PS("ps_d", [128, 512], F32).ap()
    ps_x = PS("ps_x", [128, 512], F32).ap()

    p.chain("gpsimd", "g", [
        lambda e: e.memset(ident[:, :], 0.0),
        lambda e: e.affine_select(out=ident[:, :], in_=ident[:, :],
                                  compare_op=ALU.not_equal, fill=1.0, base=0,
                                  pattern=[[-1, 128]], channel_multiplier=1),
        lambda e: e.memset(identf[:, :], 0.0),
        lambda e: e.affine_select(out=identf[:, :], in_=identf[:, :],
                                  compare_op=ALU.not_equal, fill=1.0, base=0,
                                  pattern=[[-1, 128]], channel_multiplier=1),
    ])
    for i in range(4):
        p.chain("gpsimd", "g", [
            lambda e, i=i: e.memset(masks[:, ts(i, 512)], 1.0),
            lambda e, i=i: e.affine_select(
                out=masks[:, ts(i, 512)], in_=masks[:, ts(i, 512)],
                compare_op=ALU.is_ge, fill=0.0, base=-128 * i,
                pattern=[[1, 512]], channel_multiplier=-1),
        ])
    p.chain("gpsimd", "g", [
        lambda e: e.memset(rot[:, :], 0.0),
        lambda e: e.affine_select(
            out=rot[:, 1:64:2], in_=rot[:, 1:64:2], compare_op=ALU.not_equal,
            fill=1.0, base=0, pattern=[[-2, 32]], channel_multiplier=1),
        lambda e: e.affine_select(
            out=rot[:, 0:64:2], in_=rot[:, 0:64:2], compare_op=ALU.not_equal,
            fill=-1.0, base=-1, pattern=[[-2, 32]], channel_multiplier=1),
        lambda e: e.iota(invf_i[:, :], pattern=[[0, 1]], base=0,
                         channel_multiplier=1),
    ])
    for mk in (lambda e: e.memset(ones_b[:, :], 1.0),
               lambda e: e.memset(ones_row[:, :], 1.0),
               lambda e: e.memset(ones_f[:, :], 1.0),
               lambda e: e.memset(c_eps_rms[:, :], RMS_EPS),
               lambda e: e.memset(c_eps_ln[:, :], LN_EPS)):
        p.inst("vector", mk, sem="v")
    p.dma("sync", gq_c[:, :], g_q[:, :])
    p.dma("sync", gkv_c[:, :], g_kv[:, :])
    p.dma("sync", gs_c[:, :], gate_s[:, :])
    p.dma("sync", us_c[:, :], up_s[:, :])
    p.dma("sync", ds_c[:, :], down_s[:, :])
    p.dma("sync", tokb[:, :], tok_base[:, :])
    p.qbrk("sync", "ld")
    p.wait("vector", "g")
    p.wait("vector", "v")
    p.chain("vector", "v", [
        lambda e: e.tensor_copy(out=invf[:, :], in_=invf_i[:, :]),
        lambda e: e.tensor_scalar(out=scr_a[0:64, 0:1], in0=invf[:, :],
                                  scalar1=32.0, scalar2=-32.0,
                                  op0=ALU.is_ge, op1=ALU.mult),
        lambda e: e.tensor_add(invf[:, :], invf[:, :], scr_a[0:64, 0:1]),
        lambda e: e.tensor_scalar(out=invf[:, :], in0=invf[:, :],
                                  scalar1=-math.log(THETA) / 32.0, scalar2=None,
                                  op0=ALU.mult),
    ])
    p.wait("scalar", "v")
    p.inst("scalar", lambda e: e.activation(out=invf[:, :], in_=invf[:, :],
                                            func=AF.Exp), sem="s")

    def build_tables(pos_row, w, cos_o, sin_o, frq, rrf, rri, ps_bc):
        for i in range(max(1, w // 512)):
            wslc = ts(i, 512) if w >= 512 else slice(0, w)
            n = 512 if w >= 512 else w
            p.wait("tensor", "v")
            p.wait("tensor", "s")
            p.inst("tensor", lambda e, wslc=wslc, n=n: e.matmul(
                ps_bc[0:64, 0:n], ones_row[0:1, 0:64], pos_row[0:1, wslc],
                start=True, stop=True), sem="t")
            p.wait("vector", "t")
            p.inst("vector", lambda e, wslc=wslc, n=n: e.tensor_scalar(
                out=frq[0:64, wslc], in0=ps_bc[0:64, 0:n],
                scalar1=invf[0:64, 0:1], scalar2=None, op0=ALU.mult), sem="v")

        def reduce_and_sin(out_ap):
            p.wait("vector", "v")
            p.chain("vector", "v", [
                lambda e: e.tensor_scalar(out=rrf[0:64, 0:w], in0=frq[0:64, 0:w],
                                          scalar1=1.0 / (2 * math.pi),
                                          scalar2=None, op0=ALU.mult),
                lambda e: e.tensor_copy(out=rri[0:64, 0:w], in_=rrf[0:64, 0:w]),
                lambda e: e.tensor_copy(out=rrf[0:64, 0:w], in_=rri[0:64, 0:w]),
                lambda e: e.scalar_tensor_tensor(
                    out=out_ap, in0=rrf[0:64, 0:w], scalar=-C1,
                    in1=frq[0:64, 0:w], op0=ALU.mult, op1=ALU.add),
                lambda e: e.scalar_tensor_tensor(
                    out=out_ap, in0=rrf[0:64, 0:w], scalar=-C2,
                    in1=out_ap, op0=ALU.mult, op1=ALU.add),
                lambda e: e.tensor_scalar(out=rrf[0:64, 0:w], in0=out_ap,
                                          scalar1=math.pi, scalar2=None,
                                          op0=ALU.is_gt),
                lambda e: e.scalar_tensor_tensor(
                    out=out_ap, in0=rrf[0:64, 0:w], scalar=-C1,
                    in1=out_ap, op0=ALU.mult, op1=ALU.add),
                lambda e: e.scalar_tensor_tensor(
                    out=out_ap, in0=rrf[0:64, 0:w], scalar=-C2,
                    in1=out_ap, op0=ALU.mult, op1=ALU.add),
                lambda e: e.tensor_scalar(out=rrf[0:64, 0:w], in0=out_ap,
                                          scalar1=-math.pi, scalar2=None,
                                          op0=ALU.is_lt),
                lambda e: e.scalar_tensor_tensor(
                    out=out_ap, in0=rrf[0:64, 0:w], scalar=C1,
                    in1=out_ap, op0=ALU.mult, op1=ALU.add),
                lambda e: e.scalar_tensor_tensor(
                    out=out_ap, in0=rrf[0:64, 0:w], scalar=C2,
                    in1=out_ap, op0=ALU.mult, op1=ALU.add),
            ])
            p.wait("scalar", "v")
            p.inst("scalar", lambda e: e.activation(
                out=out_ap, in_=out_ap, func=AF.Sin), sem="s")

        reduce_and_sin(sin_o[0:64, 0:w])
        p.wait("vector", "s")
        p.chain("vector", "v", [
            lambda e: e.tensor_scalar(out=frq[0:64, 0:w], in0=frq[0:64, 0:w],
                                      scalar1=math.pi / 2, scalar2=None,
                                      op0=ALU.add),
        ])
        reduce_and_sin(cos_o[0:64, 0:w])

    a_xstage = [av(0, 128, 1024, F32), av(4096, 128, 1024, F32)]
    a_xhi = av(8 * 1024, 128, 16 * 1024, BF16)
    a_xlo = av(40 * 1024, 128, 16 * 1024, BF16)
    a_wf = [av(72 * 1024, 128, 2048, F32), av(80 * 1024, 128, 2048, F32)]
    a_whi = [av(88 * 1024, 128, 2048, BF16), av(92 * 1024, 128, 2048, BF16)]
    a_wlo = [av(96 * 1024, 128, 2048, BF16), av(100 * 1024, 128, 2048, BF16)]
    a_qan = av(104 * 1024, 128, 6 * 1024, F32)
    a_ckvn = av(128 * 1024, 128, 4 * 1024, F32)
    a_krope = av(144 * 1024, 64, 1024, F32)
    a_pos = av(148 * 1024, 1, 1024, F32)
    a_posi = av(152 * 1024, 1, 1024, I32)
    a_cos = av(156 * 1024, 64, 1024, F32)
    a_sin = av(160 * 1024, 64, 1024, F32)
    a_frq = av(164 * 1024, 64, 1024, F32)
    a_rri = av(168 * 1024, 64, 1024, I32)
    a_krf = av(152 * 1024, 64, 1024, F32)
    a_krhi = av(0, 64, 1024, BF16)
    a_krlo = av(2048, 64, 1024, BF16)

    def _finish_early():
        p.barrier()
        p.dma("sync", out_t[0:128, :], scr_a[:, :], sem="fin")
        p.wait("gpsimd", "fin")
        with allow_ici_all_to_all():
            p.build()

    if stop_after == "consts":
        _finish_early()
        return nc, dbg

    p.wait("gpsimd", "g")
    p.inst("gpsimd", lambda e: e.iota(a_posi[:, :], pattern=[[1, T]], base=0,
                                      channel_multiplier=0), sem="g")
    p.wait("vector", "g")
    p.wait("vector", "ld")
    p.chain("vector", "v", [
        lambda e: e.tensor_copy(out=a_pos[:, :], in_=a_posi[:, :]),
        lambda e: e.tensor_scalar(out=a_pos[:, :], in0=a_pos[:, :],
                                  scalar1=tokb[0:1, 0:1], scalar2=None,
                                  op0=ALU.add),
    ])
    build_tables(a_pos, T, a_cos, a_sin, a_frq, scr_c, a_rri, ps_x)

    if stop_after == "tables":
        _finish_early()
        return nc, dbg

    xsplit_done = {}
    for k in range(16):
        b = k % 2
        if k >= 2:
            p.wait("sync", "v", xsplit_done[k - 2])
        p.dma("sync", a_xstage[b][:, :], x_t[ts(k, 128), :])
        p.qbrk("sync", "ld")
        p.wait("vector", "ld")
        p.chain("vector", "v", [
            lambda e, k=k, b=b: e.tensor_copy(out=a_xhi[:, ts(k, 1024)],
                                              in_=a_xstage[b][:, :]),
            lambda e, k=k, b=b: e.tensor_sub(a_xlo[:, ts(k, 1024)],
                                             a_xstage[b][:, :],
                                             a_xhi[:, ts(k, 1024)]),
        ])
        xsplit_done[k] = p.val("v")

    if stop_after == "xload":
        _finish_early()
        return nc, dbg

    wstate = {"i": 0, "mm": {}, "sp": {}}

    def proj_cols(w_dram, col0, ncols, ps_pair, evac_cb):
        i = wstate["i"]
        b = i % 2
        if i >= 2:
            p.wait("sync", "v", wstate["sp"][i - 2])
        p.dma("sync", a_wf[b][:, 0:16 * ncols],
              w_dram[:, col0:col0 + ncols].rearrange("(a p) c -> p a c", p=128))
        p.qbrk("sync", "ld")
        p.wait("vector", "ld")
        if i >= 2:
            p.wait("vector", "t", wstate["mm"][i - 2])
        if _PCFG["split"]:
            for k in range(16):
                sl = ts(k, ncols)
                p.chain("vector", "v", [
                    lambda e, b=b, sl=sl: e.tensor_copy(out=a_whi[b][:, sl],
                                                        in_=a_wf[b][:, sl]),
                    lambda e, b=b, sl=sl: e.tensor_sub(a_wlo[b][:, sl],
                                                       a_wf[b][:, sl],
                                                       a_whi[b][:, sl]),
                ])
        wstate["sp"][i] = p.val("v")
        p.wait("tensor", "v", wstate["sp"][i])
        if not _PCFG["mm"]:
            wstate["mm"][i] = p.val("t")
            wstate["i"] = i + 1
            return
        for k in range(16):
            sl = ts(k, ncols)
            for (wt, xt_) in ((a_whi[b], a_xhi), (a_whi[b], a_xlo),
                              (a_wlo[b], a_xhi)):
                for nt in range(2):
                    xsl = slice(k * 1024 + nt * 512, k * 1024 + nt * 512 + 512)
                    p.inst("tensor", lambda e, wt=wt, xt_=xt_, sl=sl, xsl=xsl,
                           nt=nt,
                           st=(k == 0 and wt is a_whi[b] and xt_ is a_xhi),
                           sp=(k == 15 and wt is a_wlo[b]),
                           nc_=ncols:
                           e.matmul(ps_pair[nt][0:nc_, :], wt[:, sl],
                                    xt_[:, xsl], start=st, stop=sp), sem="t")
        wstate["mm"][i] = p.val("t")
        wstate["i"] = i + 1
        if _PCFG["evac"]:
            evac_cb()

    def make_rms_evac(dst, cc, nchunks, ps_pair):
        def cb():
            tdone = p.val("t")
            if _PCFG["evac_sq"]:
                p.wait("scalar", "t", tdone)
                for nt in range(2):
                    p.inst("scalar", lambda e, nt=nt: e.activation(
                        out=scr_b[:, ts(nt, 512)], in_=ps_pair[nt][:, :],
                        func=AF.Square), sem="s")
            if _PCFG["evac_cp"]:
                p.wait("vector", "t", tdone)
                p.wait("vector", "s")
                for nt in range(2):
                    dsl = slice(cc * 1024 + nt * 512,
                                cc * 1024 + nt * 512 + 512)
                    p.inst("vector", lambda e, nt=nt, dsl=dsl: e.tensor_copy(
                        out=dst[:, dsl], in_=ps_pair[nt][:, :]), sem="v")
            if _PCFG["evac_sq"] and _PCFG["evac_mm"]:
                sq_done = p.val("s")
                p.wait("tensor", "s", sq_done)
                for nt in range(2):
                    tgt = ps_d if nt == 0 else ps_x
                    p.inst("tensor", lambda e, nt=nt, cc=cc, tgt=tgt: e.matmul(
                        tgt[0:1, :], ones_f[:, :], scr_b[:, ts(nt, 512)],
                        start=(cc == 0), stop=(cc == nchunks - 1)), sem="t")
        return cb

    def alpha_and_scale(dst, nchunks, denom):
        ssd = p.val("t")
        p.wait("vector", "t", ssd)
        for nt in range(2):
            src = ps_d if nt == 0 else ps_x
            p.chain("vector", "v", [
                lambda e, nt=nt, src=src: e.tensor_scalar(
                    out=stat1[0:1, ts(nt, 512)], in0=src[0:1, :],
                    scalar1=1.0 / denom, scalar2=RMS_EPS,
                    op0=ALU.mult, op1=ALU.add),
                lambda e, nt=nt: e.reciprocal(stat2[0:1, ts(nt, 512)],
                                              stat1[0:1, ts(nt, 512)]),
            ])
        p.wait("scalar", "v")
        p.inst("scalar", lambda e: e.activation(
            out=alpha[0:1, :], in_=stat2[0:1, :], func=AF.Sqrt), sem="s")
        ad = p.val("s")
        p.wait("tensor", "s", ad)
        for nt in range(2):
            tgt = ps_d if nt == 0 else ps_x
            p.inst("tensor", lambda e, nt=nt, tgt=tgt: e.matmul(
                tgt[:, :], ones_row[0:1, :], alpha[0:1, ts(nt, 512)],
                start=True, stop=True), sem="t")
        bd = p.val("t")
        p.wait("vector", "t", bd)
        for cc in range(nchunks):
            for nt in range(2):
                sl = slice(cc * 1024 + nt * 512, cc * 1024 + nt * 512 + 512)
                src = ps_d if nt == 0 else ps_x
                p.inst("vector", lambda e, sl=sl, src=src: e.tensor_mul(
                    dst[:, sl], dst[:, sl], src[:, :]), sem="v")

    for cc in range(6):
        pair = ps_s if cc % 2 == 0 else ps_e
        proj_cols(w_qa, cc * 128, 128, pair,
                  make_rms_evac(a_qan, cc, 6, pair))
        if stop_after == f"proj_cc{cc}":
            _finish_early()
            return nc, dbg
    if stop_after == "proj_qa_mm":
        _finish_early()
        return nc, dbg
    alpha_and_scale(a_qan, 6, Q_LORA)
    if stop_after == "proj_qa":
        _finish_early()
        return nc, dbg
    qan_done = p.val("v")

    for cc in range(4):
        pair = ps_s if cc % 2 == 0 else ps_e
        proj_cols(w_kva, cc * 128, 128, pair, make_rms_evac(a_ckvn, cc, 4, pair))
    alpha_and_scale(a_ckvn, 4, KV_LORA)
    ckvn_done = p.val("v")

    def krope_evac():
        tdone = p.val("t")
        p.wait("vector", "t", tdone)
        for nt in range(2):
            p.inst("vector", lambda e, nt=nt: e.tensor_copy(
                out=a_krf[:, ts(nt, 512)], in_=ps_av[nt][0:64, :]), sem="v")
        p.wait("vector", "v")
        p.chain("vector", "v", [
            lambda e: e.tensor_copy(out=a_krhi[:, :], in_=a_krf[:, :]),
            lambda e: e.tensor_sub(a_krlo[:, :], a_krf[:, :], a_krhi[:, :]),
        ])
        sp = p.val("v")
        p.wait("tensor", "v", sp)
        for nt in range(2):
            tgt = ps_s[nt]
            p.inst("tensor", lambda e, nt=nt, tgt=tgt: e.matmul(
                tgt[0:64, :], rot[:, :], a_krhi[:, ts(nt, 512)],
                start=True, stop=False), sem="t")
            p.inst("tensor", lambda e, nt=nt, tgt=tgt: e.matmul(
                tgt[0:64, :], rot[:, :], a_krlo[:, ts(nt, 512)],
                start=False, stop=True), sem="t")
        rd = p.val("t")
        p.wait("vector", "t", rd)
        for nt in range(2):
            sl = ts(nt, 512)
            p.chain("vector", "v", [
                lambda e, sl=sl: e.tensor_mul(scr_a[0:64, sl], a_krf[:, sl],
                                              a_cos[0:64, sl]),
                lambda e, sl=sl, nt=nt: e.tensor_mul(
                    scr_b[0:64, sl], ps_s[nt][0:64, :], a_sin[0:64, sl]),
                lambda e, sl=sl: e.tensor_add(a_krope[:, sl], scr_a[0:64, sl],
                                              scr_b[0:64, sl]),
            ])

    if stop_after == "proj_kva":
        _finish_early()
        return nc, dbg
    proj_cols(w_kva, 512, 64, ps_av, krope_evac)

    if stop_after == "proj":
        _finish_early()
        return nc, dbg

    def store_ag(src_fm, nchunks, col_base, rows=128):
        grp_w = 4 * rows
        for cc in range(nchunks):
            for grp in range(2):
                pre = p.val("v")
                p.wait("tensor", "v", pre)
                for q in range(4):
                    tt = grp * 4 + q
                    p.inst("tensor", lambda e, cc=cc, tt=tt, q=q: e.transpose(
                        ps_x[:, q * rows:(q + 1) * rows],
                        src_fm[0:rows, cc * 1024 + tt * 128:
                               cc * 1024 + (tt + 1) * 128],
                        identf[0:rows, 0:rows]), sem="t")
                trd = p.val("t")
                p.wait("vector", "t", trd)
                p.wait("vector", "ag")
                p.inst("vector", lambda e, grp_w=grp_w: e.tensor_copy(
                    out=scr_c[:, 0:grp_w], in_=ps_x[:, 0:grp_w]), sem="v")
                evd = p.val("v")
                p.wait("sync", "v", evd)
                for q in range(4):
                    tt = grp * 4 + q
                    p.dma("sync", ag_in[ts(tt, 128),
                                        col_base + cc * rows:
                                        col_base + (cc + 1) * rows],
                          scr_c[:, q * rows:(q + 1) * rows], sem="ag")

    store_ag(a_qan, 6, 0)
    store_ag(a_ckvn, 4, Q_LORA)
    store_ag(a_krope, 1, Q_LORA + KV_LORA, rows=64)

    if _DEBUG:
        p.wait("sync", "ag")
        p.dma("sync", dbg["d_ag"][:, :], ag_in[:, :], sem="st")

    if stop_after == "p1":
        _finish_early()
        return nc, dbg

    for cblk in range(S // 512):
        p.wait("gpsimd", "v")
        p.inst("gpsimd", lambda e, cblk=cblk: e.iota(
            a_rri[0:1, 0:512], pattern=[[1, 512]], base=cblk * 512,
            channel_multiplier=0), sem="g")
        p.wait("vector", "g")
        p.wait("vector", "st")
        p.inst("vector", lambda e: e.tensor_copy(
            out=a_pos[0:1, 0:512], in_=a_rri[0:1, 0:512]), sem="v")
        p.wait("vector", "v")
        build_tables(a_pos[0:1, 0:512], 512, a_cos, a_sin, a_frq, scr_c,
                     a_rri, ps_x)
        tb = p.val("s")
        p.wait("sync", "s", tb)
        p.dma("sync", rope_t[1, :, cblk * 512:(cblk + 1) * 512],
              a_sin[0:64, 0:512], sem="st")
        p.dma("sync", rope_t[0, :, cblk * 512:(cblk + 1) * 512],
              a_cos[0:64, 0:512], sem="st")
        p.qbrk("sync", "st")

    p.wait("gpsimd", "ag")
    p.inst("gpsimd", lambda e: e.collective_compute(
        "AllGather", ALU.bypass, replica_groups=[list(range(NCORES))],
        ins=[ag_in[:, :]], outs=[ag_out[:, :]]), sem="cc")

    if stop_after == "ag":
        _finish_early()
        return nc, dbg

    p.barrier()

    if stop_after == "qz":
        _finish_early()
        return nc, dbg

    o = 0
    a_qn_hi = av(o, 128, S, BF16); o += 16 * 1024
    a_qn_lo = av(o, 128, S, BF16); o += 16 * 1024
    a_qr_A = av(o, 128, S, BF16); o += 16 * 1024
    a_qr_B = av(o, 128, S, BF16); o += 16 * 1024
    a_kn_hi = av(o, 128, S, BF16); o += 16 * 1024
    a_kn_lo = av(o, 128, S, BF16); o += 16 * 1024
    a_kr_P = av(o, 128, S, BF16); o += 16 * 1024
    a_v_hi = av(o, 128, S, BF16); o += 16 * 1024
    a_v_lo = av(o, 128, S, BF16); o += 16 * 1024
    O_W = o
    a_wb_hi = av(O_W, 128, 6 * 384, BF16)
    a_wb_lo = av(O_W + 5 * 1024, 128, 6 * 384, BF16)
    a_wvb_hi = av(O_W, 128, 4 * 512, BF16)
    a_wvb_lo = av(O_W + 5 * 1024, 128, 4 * 512, BF16)
    O_T = O_W + 10 * 1024
    a_stg2 = av(O_T, 128, 2 * 768, F32)
    a_qaT_hi = av(O_T + 6 * 1024, 128, 6 * 512, BF16)
    a_qaT_lo = av(O_T + 12 * 1024, 128, 6 * 512, BF16)
    a_tb_cos = av(O_T, 64, 512, F32)
    a_tb_sin = av(O_T + 2 * 1024, 64, 512, F32)
    a_qrh = av(O_T + 4 * 1024, 64, 512, BF16)
    a_kstg4 = av(O_T, 128, 4 * 512, F32)
    a_ckvT_hi = av(O_T + 8 * 1024, 128, 4 * 512, BF16)
    a_ckvT_lo = av(O_T + 12 * 1024, 128, 4 * 512, BF16)
    a_kstg = av(O_T, 128, 64, F32)
    a_krT = av(O_T + 16 * 1024, 64, 512, BF16)


    qz_u = av(169984, 128, 1024, U8)
    qz_b = av(169984 + 1024, 128, 1024, BF16)
    qz_plan = []
    for (wsrc, wdst, ncols) in ((gate_w, wq_gate, I_FFN), (up_w, wq_up, I_FFN),
                                (down_w, wq_down, D)):
        for rch in range(wsrc.shape[0] // 128):
            for cch in range(ncols // 1024):
                qz_plan.append((wsrc, wdst, rch, cch))
    qz_state = {"i": 0, "cast": {}, "out": {}}

    def emit_qz_step():
        i = qz_state["i"]
        if i >= len(qz_plan):
            return
        wsrc, wdst, rch, cch = qz_plan[i]
        if i >= 1:
            p.wait("gpsimd", "v", qz_state["cast"][i - 1])
        p.dma("gpsimd", scr_a[:, :],
              wsrc[ts(rch, 128), cch * 1024:(cch + 1) * 1024], sem="qd")
        p.qbrk("gpsimd", "qd")
        p.wait("vector", "qd")
        if i >= 1:
            p.wait("vector", "qd", qz_state["out"][i - 1])
        p.chain("vector", "v", [
            lambda e: e.tensor_copy(out=qz_u.bitcast(F8)[:, :],
                                    in_=scr_a[:, :]),
            lambda e: e.tensor_copy(out=qz_b[:, :],
                                    in_=qz_u.bitcast(F8)[:, :]),
        ])
        qz_state["cast"][i] = p.val("v")
        p.wait("gpsimd", "v", qz_state["cast"][i])
        qz_state["out"][i] = p.dma(
            "gpsimd", wdst[ts(rch, 128), cch * 1024:(cch + 1) * 1024],
            qz_b[:, :], sem="qd")
        p.qbrk("gpsimd", "qd")
        qz_state["i"] = i + 1

    a_ehi = [av(O_T, 128, 512, BF16), av(O_T + 1024, 128, 512, BF16)]
    a_elo = [av(O_T + 2048, 128, 512, BF16), av(O_T + 3072, 128, 512, BF16)]
    a_bc = av(O_T + 4096, 128, 512, F32)
    a_fin = av(O_T + 6144, 128, 512, F32)
    a_ast = av(O_T + 8192, 128, 512, F32)
    a_em = av(O_T + 10240, 128, 512, F32)

    def load_wqb():
        p.wait("sync", "t")
        for k in range(6):
            p.wait("sync", "v")
            p.dma("sync", scr_a[:, 0:384], w_qb_h[ts(k, 128), :])
            p.wait("vector", "ld")
            p.chain("vector", "v", [
                lambda e, k=k: e.tensor_scalar(
                    out=scr_a[:, 0:384], in0=scr_a[:, 0:384],
                    scalar1=gq_c[:, k:k + 1], scalar2=None, op0=ALU.mult),
                lambda e, k=k: e.tensor_copy(out=a_wb_hi[:, ts(k, 384)],
                                             in_=scr_a[:, 0:384]),
                lambda e, k=k: e.tensor_sub(a_wb_lo[:, ts(k, 384)],
                                            scr_a[:, 0:384],
                                            a_wb_hi[:, ts(k, 384)]),
            ])

    def head_phase(h, wqb_loaded=False):
        if not wqb_loaded:
            load_wqb()
        p.inst("vector", lambda e: e.memset(qn2max[:, :], 0.0), sem="v")
        p.inst("vector", lambda e: e.memset(kn2max[:, :], 0.0), sem="v")
        p.wait("tensor", "v")

        for B in range(16):
            for hb in range(2):
                p.wait("gpsimd", "t")
                p.dma("gpsimd", a_stg2[:, :],
                      ag_out[B * 512 + hb * 256: B * 512 + (hb + 1) * 256,
                             0:Q_LORA]
                      .rearrange("(a p) c -> p a c", p=128), sem="gd")
                p.qbrk("gpsimd", "gd")
                p.wait("tensor", "gd")
                for k in range(6):
                    pst = ps_x if k % 2 == 0 else ps_av[0]
                    p.wait("tensor", "v")
                    for tt in range(2):
                        p.inst("tensor", lambda e, k=k, tt=tt, pst=pst:
                               e.transpose(pst[:, ts(tt, 128)],
                                           a_stg2[:, tt * 768 + k * 128:
                                                  tt * 768 + (k + 1) * 128],
                                           identf[:, :]), sem="t")
                    trd = p.val("t")
                    p.wait("vector", "t", trd)
                    dsl = slice(k * 512 + hb * 256, k * 512 + hb * 256 + 256)
                    p.chain("vector", "v", [
                        lambda e, dsl=dsl, pst=pst: e.tensor_copy(
                            out=a_qaT_hi[:, dsl], in_=pst[:, 0:256]),
                        lambda e, dsl=dsl, pst=pst: e.tensor_sub(
                            a_qaT_lo[:, dsl], pst[:, 0:256],
                            a_qaT_hi[:, dsl]),
                    ])
            qaT_done = p.val("v")
            p.wait("tensor", "v", qaT_done)
            first_n = True
            first_r = True
            for k in range(6):
                wn = slice(k * 384 + h * 192, k * 384 + h * 192 + 128)
                wr = slice(k * 384 + h * 192 + 128, k * 384 + h * 192 + 192)
                qsl = ts(k, 512)
                for (wt, qt) in ((a_wb_hi, a_qaT_hi), (a_wb_hi, a_qaT_lo),
                                 (a_wb_lo, a_qaT_hi)):
                    p.inst("tensor", lambda e, wt=wt, qt=qt, wn=wn, qsl=qsl,
                           st=first_n, sp=(k == 5 and wt is a_wb_lo):
                           e.matmul(ps_s[0][:, :], wt[:, wn], qt[:, qsl],
                                    start=st, stop=sp), sem="t")
                    first_n = False
                for (wt, qt) in ((a_wb_hi, a_qaT_hi), (a_wb_hi, a_qaT_lo),
                                 (a_wb_lo, a_qaT_hi)):
                    p.inst("tensor", lambda e, wt=wt, qt=qt, wr=wr, qsl=qsl,
                           st=first_r, sp=(k == 5 and wt is a_wb_lo):
                           e.matmul(ps_e[0][0:64, :], wt[:, wr], qt[:, qsl],
                                    start=st, stop=sp), sem="t")
                    first_r = False
            qb_done = p.val("t")
            p.wait("vector", "t", qb_done)
            p.chain("vector", "v", [
                lambda e, B=B: e.tensor_copy(out=a_qn_hi[:, ts(B, 512)],
                                             in_=ps_s[0][:, :]),
                lambda e, B=B: e.tensor_sub(a_qn_lo[:, ts(B, 512)],
                                            ps_s[0][:, :],
                                            a_qn_hi[:, ts(B, 512)]),
            ])
            p.wait("scalar", "t", qb_done)
            p.wait("scalar", "v")
            p.wait("scalar", "t")
            p.inst("scalar", lambda e: e.activation(
                out=scr_b[:, 0:512], in_=ps_s[0][:, :], func=AF.Square), sem="s")
            p.chain("vector", "v", [
                lambda e: e.tensor_copy(out=scr_a[0:64, 0:512],
                                        in_=ps_e[0][0:64, :]),
                lambda e, B=B: e.tensor_copy(out=a_qrh[:, :],
                                             in_=scr_a[0:64, 0:512]),
                lambda e, B=B: e.tensor_sub(scr_c[0:64, 0:512],
                                            scr_a[0:64, 0:512], a_qrh[:, :]),
                lambda e, B=B: e.tensor_copy(out=a_qr_B[0:64, ts(B, 512)],
                                             in_=scr_c[0:64, 0:512]),
            ])
            rsp = p.val("v")
            p.wait("tensor", "v", rsp)
            p.inst("tensor", lambda e: e.matmul(
                ps_e[1][0:64, :], rot[:, :], a_qrh[:, :],
                start=True, stop=False), sem="t")
            p.inst("tensor", lambda e, B=B: e.matmul(
                ps_e[1][0:64, :], rot[:, :], a_qr_B[0:64, ts(B, 512)],
                start=False, stop=True), sem="t")
            rot_done = p.val("t")
            p.wait("gpsimd", "v")
            p.wait("gpsimd", "t")
            p.dma("gpsimd", a_tb_cos[:, :],
                  rope_t[0, :, B * 512:(B + 1) * 512], sem="gd")
            p.dma("gpsimd", a_tb_sin[:, :],
                  rope_t[1, :, B * 512:(B + 1) * 512], sem="gd")
            p.qbrk("gpsimd", "gd")
            p.wait("vector", "gd")
            p.wait("vector", "t", rot_done)
            p.wait("vector", "s")
            p.chain("vector", "v", [
                lambda e: e.tensor_mul(scr_b[0:64, 512:1024],
                                       scr_a[0:64, 0:512], a_tb_cos[:, :]),
                lambda e: e.tensor_mul(scr_c[0:64, 0:512], ps_e[1][0:64, :],
                                       a_tb_sin[:, :]),
                lambda e: e.tensor_add(scr_a[0:64, 0:512],
                                       scr_b[0:64, 512:1024],
                                       scr_c[0:64, 0:512]),
                lambda e, B=B: e.tensor_copy(out=a_qr_A[0:64, ts(B, 512)],
                                             in_=scr_a[0:64, 0:512]),
                lambda e, B=B: e.tensor_sub(scr_c[0:64, 0:512],
                                            scr_a[0:64, 0:512],
                                            a_qr_A[0:64, ts(B, 512)]),
                lambda e, B=B: e.tensor_copy(out=a_qr_B[0:64, ts(B, 512)],
                                             in_=scr_c[0:64, 0:512]),
            ])
            p.wait("scalar", "v")
            p.inst("scalar", lambda e: e.activation(
                out=scr_b[0:64, 512:1024], in_=scr_a[0:64, 0:512],
                func=AF.Square), sem="s")
            sq2 = p.val("s")
            p.wait("tensor", "s", sq2)
            p.inst("tensor", lambda e: e.matmul(
                ps_d[0:1, :], ones_f[:, :], scr_b[:, 0:512],
                start=True, stop=False), sem="t")
            p.inst("tensor", lambda e: e.matmul(
                ps_d[0:1, :], ones_f[0:64, :], scr_b[0:64, 512:1024],
                start=False, stop=True), sem="t")
            nrm = p.val("t")
            p.wait("vector", "t", nrm)
            p.chain("vector", "v", [
                lambda e: e.tensor_reduce(m_tmp[0:1, 0:1], ps_d[0:1, :],
                                          axis=mybir.AxisListType.X,
                                          op=ALU.max),
                lambda e: e.tensor_max(qn2max[:, :], qn2max[:, :],
                                       m_tmp[0:1, 0:1]),
            ])
        qfin = p.val("v")
        p.wait("gpsimd", "v", qfin)
        p.dma("gpsimd", a_qr_A[64:128, :], a_qr_A[0:64, :], sem="gd")
        p.qbrk("gpsimd", "gd")
        p.dma("gpsimd", a_qr_B[64:128, :], a_qr_B[0:64, :], sem="gd")
        p.qbrk("gpsimd", "gd")

        p.wait("sync", "t")
        for k in range(4):
            p.wait("sync", "v")
            p.dma("sync", scr_a[:, 0:512], w_kvb_h[ts(k, 128), :])
            p.wait("vector", "ld")
            p.chain("vector", "v", [
                lambda e, k=k: e.tensor_scalar(
                    out=scr_a[:, 0:512], in0=scr_a[:, 0:512],
                    scalar1=gkv_c[:, k:k + 1], scalar2=None, op0=ALU.mult),
                lambda e, k=k: e.tensor_copy(out=a_wvb_hi[:, ts(k, 512)],
                                             in_=scr_a[:, 0:512]),
                lambda e, k=k: e.tensor_sub(a_wvb_lo[:, ts(k, 512)],
                                            scr_a[:, 0:512],
                                            a_wvb_hi[:, ts(k, 512)]),
            ])
        p.wait("tensor", "v")

        for B in range(16):
            p.wait("gpsimd", "t")
            p.dma("gpsimd", a_kstg4[:, :],
                  ag_out[B * 512:(B + 1) * 512, Q_LORA:Q_LORA + KV_LORA]
                  .rearrange("(a p) c -> p a c", p=128), sem="gd")
            p.qbrk("gpsimd", "gd")
            p.wait("tensor", "gd")
            for k in range(4):
                pst = ps_x if k % 2 == 0 else ps_av[0]
                p.wait("tensor", "v")
                for tt in range(4):
                    p.inst("tensor", lambda e, k=k, tt=tt, pst=pst:
                           e.transpose(pst[:, ts(tt, 128)],
                                       a_kstg4[:, tt * 512 + k * 128:
                                               tt * 512 + (k + 1) * 128],
                                       identf[:, :]), sem="t")
                trd = p.val("t")
                p.wait("vector", "t", trd)
                p.chain("vector", "v", [
                    lambda e, k=k, pst=pst: e.tensor_copy(
                        out=a_ckvT_hi[:, ts(k, 512)], in_=pst[:, :]),
                    lambda e, k=k, pst=pst: e.tensor_sub(
                        a_ckvT_lo[:, ts(k, 512)], pst[:, :],
                        a_ckvT_hi[:, ts(k, 512)]),
                ])
            ckvT_done = p.val("v")
            p.wait("tensor", "v", ckvT_done)
            first = True
            for k in range(4):
                wsl = slice(k * 512 + h * 256, k * 512 + h * 256 + 128)
                csl = ts(k, 512)
                for (wt, ct) in ((a_wvb_hi, a_ckvT_hi), (a_wvb_hi, a_ckvT_lo),
                                 (a_wvb_lo, a_ckvT_hi)):
                    p.inst("tensor", lambda e, wt=wt, ct=ct, wsl=wsl, csl=csl,
                           st=first, sp=(k == 3 and wt is a_wvb_lo):
                           e.matmul(ps_s[0][:, :], wt[:, wsl], ct[:, csl],
                                    start=st, stop=sp), sem="t")
                    first = False
            knd = p.val("t")
            p.wait("vector", "t", knd)
            p.chain("vector", "v", [
                lambda e, B=B: e.tensor_copy(out=a_kn_hi[:, ts(B, 512)],
                                             in_=ps_s[0][:, :]),
                lambda e, B=B: e.tensor_sub(a_kn_lo[:, ts(B, 512)],
                                            ps_s[0][:, :],
                                            a_kn_hi[:, ts(B, 512)]),
            ])
            p.wait("scalar", "t", knd)
            p.wait("scalar", "v")
            p.inst("scalar", lambda e: e.activation(
                out=scr_b[:, 0:512], in_=ps_s[0][:, :], func=AF.Square), sem="s")
            for tt in range(4):
                p.wait("gpsimd", "t")
                p.dma("gpsimd", a_kstg[:, 0:64],
                      ag_out[B * 512 + tt * 128: B * 512 + (tt + 1) * 128,
                             Q_LORA + KV_LORA:AG_COLS], sem="gd")
                p.wait("tensor", "gd")
                p.wait("tensor", "v")
                p.wait("tensor", "s")
                p.inst("tensor", lambda e: e.transpose(
                    ps_e[1][0:64, 0:128], a_kstg[:, 0:64], identf[:, :]),
                    sem="t")
                trd = p.val("t")
                p.wait("vector", "t", trd)
                ksl = slice(B * 512 + tt * 128, B * 512 + (tt + 1) * 128)
                p.chain("vector", "v", [
                    lambda e, ksl=ksl: e.tensor_copy(
                        out=a_kr_P[0:64, ksl], in_=ps_e[1][0:64, 0:128]),
                    lambda e, ksl=ksl: e.tensor_sub(
                        scr_c[0:64, 0:128], ps_e[1][0:64, 0:128],
                        a_kr_P[0:64, ksl]),
                    lambda e, tt=tt: e.tensor_copy(
                        out=a_krT[:, ts(tt, 128)], in_=scr_c[0:64, 0:128]),
                ])
                p.wait("scalar", "t", trd)
                p.wait("scalar", "v")
                p.inst("scalar", lambda e, tt=tt: e.activation(
                    out=scr_b[0:64, 512 + tt * 128: 512 + (tt + 1) * 128],
                    in_=ps_e[1][0:64, 0:128], func=AF.Square), sem="s")
            sqd = p.val("s")
            p.wait("tensor", "s", sqd)
            p.inst("tensor", lambda e: e.matmul(
                ps_d[0:1, :], ones_f[:, :], scr_b[:, 0:512],
                start=True, stop=False), sem="t")
            p.inst("tensor", lambda e: e.matmul(
                ps_d[0:1, :], ones_f[0:64, :], scr_b[0:64, 512:1024],
                start=False, stop=True), sem="t")
            nrm = p.val("t")
            p.wait("vector", "t", nrm)
            p.chain("vector", "v", [
                lambda e: e.tensor_reduce(m_tmp[0:1, 1:2], ps_d[0:1, :],
                                          axis=mybir.AxisListType.X,
                                          op=ALU.max),
                lambda e: e.tensor_max(kn2max[:, :], kn2max[:, :],
                                       m_tmp[0:1, 1:2]),
            ])
            krtd = p.val("v")
            p.wait("gpsimd", "v", krtd)
            p.dma("gpsimd", a_kr_P[64:128, ts(B, 512)], a_krT[:, :], sem="gd")
            for tt in range(4):
                p.wait("tensor", "v")
                first = True
                for k in range(4):
                    csl = slice(k * 512 + tt * 128, k * 512 + (tt + 1) * 128)
                    wsl = slice(k * 512 + h * 256 + 128,
                                k * 512 + h * 256 + 256)
                    for (ct, wt) in ((a_ckvT_hi, a_wvb_hi),
                                     (a_ckvT_hi, a_wvb_lo),
                                     (a_ckvT_lo, a_wvb_hi)):
                        p.inst("tensor", lambda e, ct=ct, wt=wt, csl=csl,
                               wsl=wsl, st=first,
                               sp=(k == 3 and ct is a_ckvT_lo):
                               e.matmul(ps_e[0][:, 0:128], ct[:, csl],
                                        wt[:, wsl], start=st, stop=sp),
                               sem="t")
                        first = False
                vd = p.val("t")
                p.wait("vector", "t", vd)
                vsl = slice((B * 4 + tt) * 128, (B * 4 + tt + 1) * 128)
                p.chain("vector", "v", [
                    lambda e, vsl=vsl: e.tensor_copy(out=a_v_hi[:, vsl],
                                                     in_=ps_e[0][:, 0:128]),
                    lambda e, vsl=vsl: e.tensor_sub(a_v_lo[:, vsl],
                                                    ps_e[0][:, 0:128],
                                                    a_v_hi[:, vsl]),
                ])

        p.wait("vector", "v")
        p.chain("vector", "v", [
            lambda e: e.tensor_mul(m_tmp[0:1, 2:3], qn2max[:, :], kn2max[:, :]),
        ])
        p.wait("scalar", "v")
        p.inst("scalar", lambda e: e.activation(
            out=m_tmp[0:1, 3:4], in_=m_tmp[0:1, 2:3], func=AF.Sqrt), sem="s")
        p.wait("vector", "s")
        p.chain("vector", "v", [
            lambda e: e.tensor_scalar(out=m_tmp[0:1, 0:1], in0=m_tmp[0:1, 3:4],
                                      scalar1=-QK_SCALE, scalar2=None,
                                      op0=ALU.mult),
        ])
        p.wait("tensor", "v")
        p.inst("tensor", lambda e: e.matmul(
            ps_x[:, 0:1], ones_row[0:1, :], m_tmp[0:1, 0:1],
            start=True, stop=True), sem="t")
        p.wait("vector", "t")
        p.inst("vector", lambda e: e.tensor_copy(out=neg_m[:, :],
                                                 in_=ps_x[:, 0:1]), sem="v")
        if _DEBUG:
            p.wait("sync", "v")
            p.dma("sync", dbg["d_m"][:, :], m_tmp[:, :], sem="st")
        p.barrier(("tensor", "vector", "scalar"))

        exp_done = {}
        split_done = {}
        avden_done = {}
        fin_done = {}
        rec_done = {}
        flat = []
        for qc in range(16):
            for kc in range(4 * qc + 4):
                flat.append((qc, kc))
        nj = len(flat)

        def emit_scores(j):
            qc, kc = flat[j]
            jj = j % 2
            qsl = ts(qc, 512)
            ksl = ts(kc, 128)
            if j >= 2:
                p.wait("tensor", "s", exp_done[j - 2])
            for i, (kt, qt) in enumerate((
                    (a_kn_hi, a_qn_hi), (a_kn_hi, a_qn_lo),
                    (a_kn_lo, a_qn_hi))):
                p.inst("tensor", lambda e, kt=kt, qt=qt, ksl=ksl, qsl=qsl,
                       st=(i == 0), jj=jj:
                       e.matmul(ps_s[jj][:, :], kt[:, ksl], qt[:, qsl],
                                start=st, stop=False), sem="t")
            p.inst("tensor", lambda e, ksl=ksl, qsl=qsl, jj=jj:
                   e.matmul(ps_s[jj][:, :], a_kr_P[:, ksl], a_qr_A[:, qsl],
                            start=False, stop=False), sem="t")
            return p.inst("tensor", lambda e, ksl=ksl, qsl=qsl, jj=jj:
                          e.matmul(ps_s[jj][:, :], a_kr_P[:, ksl],
                                   a_qr_B[:, qsl],
                                   start=False, stop=True), sem="t")

        def emit_exp_split(j, sc_done):
            qc, kc = flat[j]
            jj = j % 2
            p.wait("scalar", "t", sc_done)
            if j >= 2:
                p.wait("scalar", "v", split_done[j - 2])
            p.inst("scalar", lambda e, jj=jj: e.activation(
                out=a_ehi[jj][:, :], in_=ps_s[jj][:, :], func=AF.Exp,
                bias=neg_m[:, :], scale=QK_SCALE), sem="s")
            exp_done[j] = p.inst("scalar", lambda e, jj=jj: e.activation(
                out=ps_e[jj][:, :], in_=ps_s[jj][:, :], func=AF.Exp,
                bias=neg_m[:, :], scale=QK_SCALE), sem="s")
            p.wait("vector", "s", exp_done[j])
            if kc >= 4 * qc:
                didx = kc - 4 * qc
                p.chain("vector", "v", [
                    lambda e, jj=jj, didx=didx: e.tensor_mul(
                        a_ehi[jj][:, :], a_ehi[jj][:, :],
                        masks[:, ts(didx, 512)]),
                    lambda e, jj=jj, didx=didx: e.tensor_mul(
                        a_em[:, :], ps_e[jj][:, :], masks[:, ts(didx, 512)]),
                    lambda e, jj=jj: e.tensor_sub(a_elo[jj][:, :],
                                                  a_em[:, :],
                                                  a_ehi[jj][:, :]),
                ])
            else:
                p.inst("vector", lambda e, jj=jj: e.tensor_sub(
                    a_elo[jj][:, :], ps_e[jj][:, :], a_ehi[jj][:, :]), sem="v")
            split_done[j] = p.val("v")

        def emit_avden(j):
            qc, kc = flat[j]
            jj = j % 2
            qq = qc % 2
            nkc = 4 * qc + 4
            ksl = ts(kc, 128)
            p.wait("tensor", "v", split_done[j])
            if kc == 0 and qc >= 2:
                p.wait("tensor", "v", fin_done[qc - 2])
            if kc == 0 and qc >= 1:
                p.wait("tensor", "v", rec_done[qc - 1])
            for i, (vt, et) in enumerate((
                    (a_v_hi, a_ehi[jj]), (a_v_lo, a_ehi[jj]),
                    (a_v_hi, a_elo[jj]))):
                p.inst("tensor", lambda e, vt=vt, et=et, ksl=ksl, qq=qq,
                       st=(kc == 0 and i == 0),
                       sp=(kc == nkc - 1 and i == 2):
                       e.matmul(ps_av[qq][:, :], vt[:, ksl], et[:, :],
                                start=st, stop=sp), sem="t")
            avden_done[j] = p.inst(
                "tensor", lambda e, jj=jj, st=(kc == 0), sp=(kc == nkc - 1):
                e.matmul(ps_d[0:1, :], ones_b[:, :], a_ehi[jj][:, :],
                         start=st, stop=sp), sem="t")

        def emit_finalize(qc):
            qq = qc % 2
            p.wait("vector", "t", avden_done[sum(4 * q + 4 for q in range(qc + 1)) - 1])
            p.inst("vector", lambda e: e.reciprocal(recip_sb[0:1, :],
                                                    ps_d[0:1, :]), sem="v")
            rec_done[qc] = p.val("v")
            p.wait("tensor", "v", rec_done[qc])
            bc_mm = p.inst("tensor", lambda e: e.matmul(
                ps_x[:, :], ones_row[0:1, :], recip_sb[0:1, :],
                start=True, stop=True), sem="t")
            p.wait("vector", "t", bc_mm)
            p.chain("vector", "v", [
                lambda e: e.tensor_copy(out=a_bc[:, :], in_=ps_x[:, :]),
                lambda e, qq=qq: e.tensor_mul(a_fin[:, :], ps_av[qq][:, :],
                                              a_bc[:, :]),
            ])
            find = p.val("v")
            p.wait("tensor", "v", find)
            for q4 in range(4):
                p.inst("tensor", lambda e, q4=q4: e.transpose(
                    ps_x[:, ts(q4, 128)], a_fin[:, ts(q4, 128)], identf[:, :]),
                    sem="t")
            trd = p.val("t")
            p.wait("vector", "t", trd)
            p.wait("vector", "a2")
            p.inst("vector", lambda e: e.tensor_copy(out=a_ast[:, :],
                                                     in_=ps_x[:, :]), sem="v")
            fin_done[qc] = p.val("v")
            p.wait("sync", "v", fin_done[qc])
            for q4 in range(4):
                p.dma("sync", a2a_in[qc * 512 + q4 * 128:
                                     qc * 512 + (q4 + 1) * 128,
                                     h * D_V:(h + 1) * D_V],
                      a_ast[:, ts(q4, 128)], sem="a2")

        sc_done = {}
        sc_done[0] = emit_scores(0)
        for j in range(nj):
            emit_exp_split(j, sc_done[j])
            if j + 1 < nj:
                sc_done[j + 1] = emit_scores(j + 1)
            emit_avden(j)
            if j % 3 == 1:
                emit_qz_step()
            qc, kc = flat[j]
            if kc == 4 * qc + 3:
                emit_finalize(qc)

        if _DEBUG and h == 0:
            p.wait("sync", "v")
            p.wait("sync", "t")
            p.dma("sync", dbg["d_q"][0:128, :], a_qn_hi[:, :], sem="st")
            p.dma("sync", dbg["d_q"][128:256, :], a_qn_lo[:, :], sem="st")
            p.dma("sync", dbg["d_q"][256:384, :], a_qr_A[:, :], sem="st")
            p.dma("sync", dbg["d_q"][384:512, :], a_qr_B[:, :], sem="st")
            p.dma("sync", dbg["d_k"][0:128, :], a_kn_hi[:, :], sem="st")
            p.dma("sync", dbg["d_k"][128:256, :], a_kn_lo[:, :], sem="st")
            p.dma("sync", dbg["d_k"][256:384, :], a_kr_P[:, :], sem="st")
            p.dma("sync", dbg["d_v"][0:128, :], a_v_hi[:, :], sem="st")
            p.dma("sync", dbg["d_v"][128:256, :], a_v_lo[:, :], sem="st")
        p.barrier()

    for h in range(HL):
        head_phase(h)
        if stop_after == f"head{h}":
            _finish_early()
            return nc, dbg

    if _DEBUG:
        p.wait("sync", "a2")
        p.dma("sync", dbg["d_attn"][:, :], a2a_in[:, :], sem="st")

    p.wait("gpsimd", "a2")
    p.inst("gpsimd", lambda e: e.collective_compute(
        "AllToAll", ALU.bypass, replica_groups=[list(range(NCORES))],
        ins=[a2a_in[:, :]], outs=[a2a_out[:, :]]), sem="cc")

    a_at_hi = av(0, 128, 16 * 1024, BF16)
    a_at_lo = av(32 * 1024, 128, 16 * 1024, BF16)
    a_xpre = av(64 * 1024, 128, 16 * 1024, F32)
    a_wo_f = [av(128 * 1024, 128, 2048, F32), av(136 * 1024, 128, 2048, F32)]
    a_wo_hi = [av(144 * 1024, 128, 2048, BF16), av(148 * 1024, 128, 2048, BF16)]
    a_wo_lo = [av(152 * 1024, 128, 2048, BF16), av(156 * 1024, 128, 2048, BF16)]
    a_xrs = [av(160 * 1024, 128, 1024, F32), av(164 * 1024, 128, 1024, F32)]
    a_p4st = av(168 * 1024, 128, 512, F32)
    a_x1_hi = av(0, 128, 16 * 1024, BF16)
    a_x1_lo = av(32 * 1024, 128, 16 * 1024, BF16)

    wo_pre = {}
    for oc in range(2):
        b = oc
        p.dma("sync", a_wo_f[b][:, :],
              w_o[:, ts(oc, 128)].rearrange("(a p) c -> p a c", p=128))
        p.dma("sync", a_xrs[b][:, :], x_t[ts(oc, 128), :])
        p.qbrk("sync", "ld")
        p.wait("vector", "ld")
        for k in range(16):
            sl = ts(k, 128)
            p.chain("vector", "v", [
                lambda e, b=b, sl=sl: e.tensor_copy(out=a_wo_hi[b][:, sl],
                                                    in_=a_wo_f[b][:, sl]),
                lambda e, b=b, sl=sl: e.tensor_sub(a_wo_lo[b][:, sl],
                                                   a_wo_f[b][:, sl],
                                                   a_wo_hi[b][:, sl]),
            ])
        wo_pre[oc] = p.val("v")

    p.wait("sync", "cc")
    p.wait("tensor", "cc")
    p.wait("vector", "cc")
    p.wait("gpsimd", "cc")

    for fc in range(16):
        sblk, hl = fc // 2, fc % 2
        for ttg in range(2):
            pst = ps_x if ttg == 0 else ps_av[0]
            p.wait("gpsimd", "t")
            p.dma("gpsimd", a_p4st[:, :],
                  a2a_out[sblk * 1024 + ttg * 512: sblk * 1024 + (ttg + 1) * 512,
                          hl * 128:(hl + 1) * 128]
                  .rearrange("(a p) c -> p a c", p=128), sem="gd")
            p.wait("tensor", "gd")
            p.wait("tensor", "v")
            for q4 in range(4):
                p.inst("tensor", lambda e, q4=q4, pst=pst: e.transpose(
                    pst[:, ts(q4, 128)], a_p4st[:, ts(q4, 128)], identf[:, :]),
                    sem="t")
            trd = p.val("t")
            p.wait("vector", "t", trd)
            dsl = slice(fc * 1024 + ttg * 512, fc * 1024 + (ttg + 1) * 512)
            p.chain("vector", "v", [
                lambda e, dsl=dsl, pst=pst: e.tensor_copy(out=a_at_hi[:, dsl],
                                                          in_=pst[:, :]),
                lambda e, dsl=dsl, pst=pst: e.tensor_sub(a_at_lo[:, dsl],
                                                         pst[:, :],
                                                         a_at_hi[:, dsl]),
            ])
    at_done = p.val("v")
    p.wait("tensor", "v", at_done)

    wo_done = {}
    for oc in range(16):
        b = oc % 2
        if oc >= 2:
            p.wait("sync", "v", wo_done[oc - 2][1])
            p.dma("sync", a_wo_f[b][:, :],
                  w_o[:, ts(oc, 128)].rearrange("(a p) c -> p a c", p=128))
            p.dma("sync", a_xrs[b][:, :], x_t[ts(oc, 128), :])
            p.qbrk("sync", "ld")
            p.wait("vector", "ld")
            p.wait("vector", "t", wo_done[oc - 2][0])
            for k in range(16):
                sl = ts(k, 128)
                p.chain("vector", "v", [
                    lambda e, b=b, sl=sl: e.tensor_copy(out=a_wo_hi[b][:, sl],
                                                        in_=a_wo_f[b][:, sl]),
                    lambda e, b=b, sl=sl: e.tensor_sub(a_wo_lo[b][:, sl],
                                                       a_wo_f[b][:, sl],
                                                       a_wo_hi[b][:, sl]),
                ])
            spd = p.val("v")
        else:
            spd = wo_pre[oc]
        p.wait("tensor", "v", spd)
        if oc >= 1:
            p.wait("tensor", "v", wo_done[oc - 1][1])
        for k in range(16):
            for (wt, at) in ((a_wo_hi[b], a_at_hi), (a_wo_hi[b], a_at_lo),
                             (a_wo_lo[b], a_at_hi)):
                for nt in range(2):
                    asl = slice(k * 1024 + nt * 512, k * 1024 + nt * 512 + 512)
                    p.inst("tensor", lambda e, wt=wt, at=at, k=k, asl=asl,
                           pst=ps_s[nt],
                           st=(k == 0 and wt is a_wo_hi[b] and at is a_at_hi),
                           sp=(k == 15 and wt is a_wo_lo[b]):
                           e.matmul(pst[:, :], wt[:, ts(k, 128)], at[:, asl],
                                    start=st, stop=sp), sem="t")
        mmd = p.val("t")
        p.wait("vector", "t", mmd)
        for nt in range(2):
            dsl = slice(oc * 1024 + nt * 512, oc * 1024 + nt * 512 + 512)
            p.inst("vector", lambda e, nt=nt, b=b, dsl=dsl: e.tensor_add(
                a_xpre[:, dsl], ps_s[nt][:, :], a_xrs[b][:, ts(nt, 512)]),
                sem="v")
        wo_done[oc] = (mmd, p.val("v"))

    def layernorm(src, dst_hi, dst_lo, out_dma=None):
        for k in range(16):
            for nt in range(2):
                sl = slice(k * 1024 + nt * 512, k * 1024 + nt * 512 + 512)
                p.wait("vector", "t")
                p.chain("vector", "v", [
                    lambda e, sl=sl: e.tensor_copy(out=lnhi[:, :],
                                                   in_=src[:, sl]),
                    lambda e, sl=sl: e.tensor_sub(lnlo[:, :], src[:, sl],
                                                  lnhi[:, :]),
                ])
                sd = p.val("v")
                p.wait("tensor", "v", sd)
                for i, t_ in enumerate((lnhi, lnlo)):
                    tgt = ps_d if nt == 0 else ps_e[0]
                    p.inst("tensor", lambda e, t_=t_, tgt=tgt, k=k, i=i:
                           e.matmul(tgt[0:1, :], ones_b[:, :], t_[:, :],
                                    start=(k == 0 and i == 0),
                                    stop=(k == 15 and i == 1)), sem="t")
                p.wait("scalar", "v", sd)
                p.wait("scalar", "t")
                p.inst("scalar", lambda e, sl=sl, nt=nt: e.activation(
                    out=scr_c[:, ts(nt, 512)], in_=src[:, sl],
                    func=AF.Square), sem="s")
                sqd = p.val("s")
                p.wait("vector", "s", sqd)
                p.wait("vector", "t")
                p.chain("vector", "v", [
                    lambda e, nt=nt: e.tensor_copy(out=lnhi[:, :],
                                                   in_=scr_c[:, ts(nt, 512)]),
                    lambda e, nt=nt: e.tensor_sub(lnlo[:, :],
                                                  scr_c[:, ts(nt, 512)],
                                                  lnhi[:, :]),
                ])
                sd2 = p.val("v")
                p.wait("tensor", "v", sd2)
                for i, t_ in enumerate((lnhi, lnlo)):
                    tgt = ps_s[0] if nt == 0 else ps_e[1]
                    p.inst("tensor", lambda e, t_=t_, tgt=tgt, k=k, i=i:
                           e.matmul(tgt[0:1, :], ones_b[:, :], t_[:, :],
                                    start=(k == 0 and i == 0),
                                    stop=(k == 15 and i == 1)), sem="t")
        std = p.val("t")
        p.wait("vector", "t", std)
        p.chain("vector", "v", [
            lambda e: e.tensor_scalar(out=stat1[0:1, 0:512], in0=ps_d[0:1, :],
                                      scalar1=1.0 / D, scalar2=None,
                                      op0=ALU.mult),
            lambda e: e.tensor_scalar(out=stat1[0:1, 512:1024],
                                      in0=ps_e[0][0:1, :], scalar1=1.0 / D,
                                      scalar2=None, op0=ALU.mult),
            lambda e: e.tensor_scalar(out=stat2[0:1, 0:512],
                                      in0=ps_s[0][0:1, :],
                                      scalar1=1.0 / D, scalar2=None,
                                      op0=ALU.mult),
            lambda e: e.tensor_scalar(out=stat2[0:1, 512:1024],
                                      in0=ps_e[1][0:1, :], scalar1=1.0 / D,
                                      scalar2=None, op0=ALU.mult),
            lambda e: e.tensor_mul(alpha[0:1, :], stat1[0:1, :], stat1[0:1, :]),
            lambda e: e.tensor_sub(stat2[0:1, :], stat2[0:1, :], alpha[0:1, :]),
        ])
        p.wait("vector", "v")
        p.chain("vector", "v", [
            lambda e: e.tensor_scalar(out=stat2[0:1, :], in0=stat2[0:1, :],
                                      scalar1=LN_EPS, scalar2=None,
                                      op0=ALU.add),
            lambda e: e.reciprocal(stat2[0:1, :], stat2[0:1, :]),
        ])
        p.wait("scalar", "v")
        p.inst("scalar", lambda e: e.activation(
            out=stat2[0:1, :], in_=stat2[0:1, :], func=AF.Sqrt), sem="s")
        rsd = p.val("s")
        p.wait("tensor", "s", rsd)
        for nt in range(2):
            mtgt = ps_d if nt == 0 else ps_e[0]
            rtgt = ps_x if nt == 0 else ps_e[1]
            p.inst("tensor", lambda e, nt=nt, mtgt=mtgt: e.matmul(
                mtgt[:, :], ones_row[0:1, :], stat1[0:1, ts(nt, 512)],
                start=True, stop=True), sem="t")
            p.inst("tensor", lambda e, nt=nt, rtgt=rtgt: e.matmul(
                rtgt[:, :], ones_row[0:1, :], stat2[0:1, ts(nt, 512)],
                start=True, stop=True), sem="t")
        bcd = p.val("t")
        p.wait("vector", "t", bcd)
        for k in range(16):
            for nt in range(2):
                sl = slice(k * 1024 + nt * 512, k * 1024 + nt * 512 + 512)
                mean_b = ps_d if nt == 0 else ps_e[0]
                rstd_b = ps_x if nt == 0 else ps_e[1]
                p.wait("vector", "v")
                p.wait("vector", "st")
                ops = [
                    lambda e, sl=sl, mean_b=mean_b: e.tensor_sub(
                        scr_b[:, 0:512], src[:, sl], mean_b[:, :]),
                    lambda e, sl=sl, rstd_b=rstd_b: e.tensor_mul(
                        scr_b[:, 512:1024], scr_b[:, 0:512], rstd_b[:, :]),
                ]
                if dst_hi is not None:
                    ops += [
                        lambda e, sl=sl: e.tensor_copy(out=dst_hi[:, sl],
                                                       in_=scr_b[:, 512:1024]),
                        lambda e, sl=sl: e.tensor_sub(dst_lo[:, sl],
                                                      scr_b[:, 512:1024],
                                                      dst_hi[:, sl]),
                    ]
                p.chain("vector", "v", ops)
                if out_dma is not None:
                    od = p.val("v")
                    p.wait("sync", "v", od)
                    p.dma("sync", out_dma(k, nt), scr_b[:, 512:1024], sem="st")

    if stop_after == "wo":
        _finish_early()
        return nc, dbg

    layernorm(a_xpre, a_x1_hi, a_x1_lo)
    if _DEBUG:
        p.wait("sync", "v")
        for k in range(16):
            p.dma("sync", dbg["d_x1"][ts(k, 128), :], a_xpre[:, ts(k, 1024)],
                  sem="st")
    p.barrier()

    if stop_after == "ln1":
        _finish_early()
        return nc, dbg

    a_fu_hi = av(64 * 1024, 128, 6 * 1024, BF16)
    a_fu_lo = av(76 * 1024, 128, 6 * 1024, BF16)
    a_dacc = av(88 * 1024, 128, 16 * 1024, F32)
    a_gw = [av(152 * 1024, 128, 2048, BF16), av(156 * 1024, 128, 2048, BF16)]
    a_uw = [av(160 * 1024, 128, 2048, BF16), av(164 * 1024, 128, 2048, BF16)]
    a_dw = [av(168 * 1024, 128, 768, BF16), av(170 * 1024, 128, 768, BF16)]

    gu_done = {}
    for e8 in range(8):
        for fc6 in range(6):
            fc = e8 * 6 + fc6
            b = fc % 2
            if fc >= 2:
                p.wait("sync", "t", gu_done[fc - 2])
            p.dma("sync", a_gw[b][:, :],
                  wq_gate[:, ts(fc, 128)].rearrange("(a p) c -> p a c", p=128))
            p.dma("sync", a_uw[b][:, :],
                  wq_up[:, ts(fc, 128)].rearrange("(a p) c -> p a c", p=128))
            p.qbrk("sync", "ld")
            p.wait("tensor", "ld")
            p.wait("tensor", "v")
            p.wait("tensor", "s")
            for k in range(16):
                for (psp, wsrc_b) in ((ps_s, a_gw[b]), (ps_e, a_uw[b])):
                    for nt in range(2):
                        xsl = slice(k * 1024 + nt * 512,
                                    k * 1024 + nt * 512 + 512)
                        for xt_ in (a_x1_hi, a_x1_lo):
                            p.inst("tensor", lambda e, wsrc_b=wsrc_b, k=k,
                                   xt_=xt_, xsl=xsl, pst=psp[nt],
                                   st=(k == 0 and xt_ is a_x1_hi),
                                   sp=(k == 15 and xt_ is a_x1_lo):
                                   e.matmul(pst[:, :], wsrc_b[:, ts(k, 128)],
                                            xt_[:, xsl], start=st, stop=sp),
                                   sem="t")
            gu_done[fc] = p.val("t")
            p.wait("scalar", "t", gu_done[fc])
            p.wait("scalar", "v")
            for nt in range(2):
                p.inst("scalar", lambda e, nt=nt, fc=fc: e.activation(
                    out=scr_b[:, ts(nt, 512)], in_=ps_s[nt][:, :],
                    func=AF.Silu, scale=gs_c[:, fc:fc + 1]), sem="s")
            sid = p.val("s")
            p.wait("vector", "s", sid)
            p.wait("vector", "t", gu_done[fc])
            for nt in range(2):
                fsl = slice(fc6 * 1024 + nt * 512, fc6 * 1024 + nt * 512 + 512)
                p.chain("vector", "v", [
                    lambda e, nt=nt, fc=fc: e.scalar_tensor_tensor(
                        out=scr_c[:, ts(nt, 512)], in0=ps_e[nt][:, :],
                        scalar=us_c[:, fc:fc + 1], in1=scr_b[:, ts(nt, 512)],
                        op0=ALU.mult, op1=ALU.mult),
                    lambda e, nt=nt, fsl=fsl: e.tensor_copy(
                        out=a_fu_hi[:, fsl], in_=scr_c[:, ts(nt, 512)]),
                    lambda e, nt=nt, fsl=fsl: e.tensor_sub(
                        a_fu_lo[:, fsl], scr_c[:, ts(nt, 512)],
                        a_fu_hi[:, fsl]),
                ])
        fu_done = p.val("v")
        p.wait("tensor", "v", fu_done)
        dw_done = {}
        for oc in range(16):
            b = oc % 2
            if oc >= 2:
                p.wait("sync", "t", dw_done[oc - 2])
            p.dma("sync", a_dw[b][:, :],
                  wq_down[e8 * 768:(e8 + 1) * 768, ts(oc, 128)]
                  .rearrange("(a p) c -> p a c", p=128))
            p.qbrk("sync", "ld")
            p.wait("tensor", "ld")
            p.wait("tensor", "v")
            for k in range(6):
                for nt in range(2):
                    fsl = slice(k * 1024 + nt * 512, k * 1024 + nt * 512 + 512)
                    for ft in (a_fu_hi, a_fu_lo):
                        p.inst("tensor", lambda e, b=b, k=k, ft=ft, fsl=fsl,
                               pst=ps_av[nt],
                               st=(k == 0 and ft is a_fu_hi),
                               sp=(k == 5 and ft is a_fu_lo):
                               e.matmul(pst[:, :], a_dw[b][:, ts(k, 128)],
                                        ft[:, fsl], start=st, stop=sp),
                               sem="t")
            dw_done[oc] = p.val("t")
            p.wait("vector", "t", dw_done[oc])
            for nt in range(2):
                dsl = slice(oc * 1024 + nt * 512, oc * 1024 + nt * 512 + 512)
                if e8 == 0:
                    p.inst("vector", lambda e, nt=nt, dsl=dsl: e.tensor_copy(
                        out=a_dacc[:, dsl], in_=ps_av[nt][:, :]), sem="v")
                else:
                    p.inst("vector", lambda e, nt=nt, dsl=dsl: e.tensor_add(
                        a_dacc[:, dsl], a_dacc[:, dsl], ps_av[nt][:, :]),
                        sem="v")
        p.barrier(("tensor", "vector", "scalar", "sync"))

    for k in range(16):
        for nt in range(2):
            sl = slice(k * 1024 + nt * 512, k * 1024 + nt * 512 + 512)
            p.chain("vector", "v", [
                lambda e, sl=sl, k=k: e.scalar_tensor_tensor(
                    out=a_dacc[:, sl], in0=a_dacc[:, sl],
                    scalar=ds_c[:, k:k + 1], in1=a_x1_hi[:, sl],
                    op0=ALU.mult, op1=ALU.add),
                lambda e, sl=sl: e.tensor_add(a_dacc[:, sl], a_dacc[:, sl],
                                              a_x1_lo[:, sl]),
            ])
    p.barrier(("vector", "tensor", "scalar"))

    def out_dma(k, nt):
        return out_t[ts(k, 128), nt * 512:(nt + 1) * 512]

    layernorm(a_dacc, None, None, out_dma=out_dma)

    p.wait("gpsimd", "st")
    with allow_ici_all_to_all():
        p.build()
    return nc, dbg



_BUILT = None


def kernel(L0_attn_x, L0_attn_w_qa, L0_attn_q_a_norm_w, L0_attn_w_qb,
           L0_attn_w_kva, L0_attn_kv_a_norm_w, L0_attn_w_kvb, L0_attn_w_o,
           L0_ffn_gate_w, L0_ffn_up_w, L0_ffn_down_w, L0_ffn_gate_scale,
           L0_ffn_up_scale, L0_ffn_down_scale):
    global _BUILT, LAST_DEBUG
    if _BUILT is None:
        _BUILT = build_nc()
    nc, dbg = _BUILT

    x = np.ascontiguousarray(np.asarray(L0_attn_x, dtype=np.float32)[0])
    com = {
        "w_qa": np.ascontiguousarray(L0_attn_w_qa, dtype=np.float32),
        "w_kva": np.ascontiguousarray(L0_attn_w_kva, dtype=np.float32),
        "w_o": np.ascontiguousarray(L0_attn_w_o, dtype=np.float32),
        "gate_w": np.ascontiguousarray(L0_ffn_gate_w, dtype=np.float32),
        "up_w": np.ascontiguousarray(L0_ffn_up_w, dtype=np.float32),
        "down_w": np.ascontiguousarray(L0_ffn_down_w, dtype=np.float32),
        "g_q": np.ascontiguousarray(
            np.asarray(L0_attn_q_a_norm_w, dtype=np.float32).reshape(-1, 128).T),
        "g_kv": np.ascontiguousarray(
            np.asarray(L0_attn_kv_a_norm_w, dtype=np.float32).reshape(-1, 128).T),
        "gate_s": np.ascontiguousarray(
            np.asarray(L0_ffn_gate_scale, dtype=np.float32).reshape(-1, 128).T),
        "up_s": np.ascontiguousarray(
            np.asarray(L0_ffn_up_scale, dtype=np.float32).reshape(-1, 128).T),
        "down_s": np.ascontiguousarray(
            np.asarray(L0_ffn_down_scale, dtype=np.float32).reshape(-1, 128).T),
    }
    wqb = np.asarray(L0_attn_w_qb, dtype=np.float32)
    wkvb = np.asarray(L0_attn_w_kvb, dtype=np.float32)
    in_maps = []
    for c in range(NCORES):
        m = dict(com)
        m["x_t"] = np.ascontiguousarray(x[c * T:(c + 1) * T, :].T)
        m["w_qb_h"] = np.ascontiguousarray(
            wqb[:, c * HL * D_QK:(c + 1) * HL * D_QK])
        m["w_kvb_h"] = np.ascontiguousarray(
            wkvb[:, c * HL * 256:(c + 1) * HL * 256])
        m["tok_base"] = np.full((1, 1), c * T, dtype=np.float32)
        in_maps.append(m)

    res = run_bass_kernel_spmd(nc, in_maps, core_ids=list(range(NCORES)))
    out = np.empty((S, D), dtype=np.float32)
    for c in range(NCORES):
        out[c * T:(c + 1) * T, :] = res.results[c]["out_t"].T
    if _DEBUG:
        LAST_DEBUG = list(res.results)
    return out[None, :, :]


# version 10
# speedup vs baseline: 1.1060x; 1.0246x over previous
import math

import numpy as np

import concourse.bass as bass
import concourse.mybir as mybir
from concourse.bass_utils import run_bass_kernel_spmd
from concourse.aot_env import allow_ici_all_to_all

F32 = mybir.dt.float32
BF16 = mybir.dt.bfloat16
I32 = mybir.dt.int32
F8 = mybir.dt.float8e4
U8 = mybir.dt.uint8
AF = mybir.ActivationFunctionType
ALU = mybir.AluOpType

NCORES = 8
S = 8192
D = 2048
T = S // NCORES
H = 16
HL = H // NCORES
Q_LORA = 768
KV_LORA = 512
D_NOPE = 128
D_ROPE = 64
D_V = 128
D_QK = D_NOPE + D_ROPE
I_FFN = 6144
QK_SCALE = 1.0 / math.sqrt(D_QK)
RMS_EPS = 1e-6
LN_EPS = 1e-5
THETA = 10000.0
AG_COLS = Q_LORA + KV_LORA + D_ROPE

C1 = float(np.float32(2 * np.pi))
C2 = float(2 * np.pi - np.float64(np.float32(2 * np.pi)))

_DEBUG = False
LAST_DEBUG = None
_PCFG = {"mm": True, "evac": True, "split": True, "evac_sq": True, "evac_cp": True, "evac_mm": True}


def ts(i, n):
    return slice(i * n, (i + 1) * n)


class Prog:

    ENGINES = ("sync", "gpsimd", "tensor", "vector", "scalar")

    def __init__(self, nc):
        self.nc = nc
        self.ops = {e: [] for e in self.ENGINES}
        self.sems = {}
        self.cnt = {}

    def sem(self, name):
        if name not in self.sems:
            self.sems[name] = self.nc.alloc_semaphore(name)
            self.cnt[name] = 0
        return self.sems[name]

    def inst(self, eng, make, sem=None, n=1):
        if sem is not None:
            self.sem(sem)
            self.cnt[sem] += n
            self.ops[eng].append(
                lambda e, make=make, s=sem, n=n: make(e).then_inc(self.sems[s], n))
            return self.cnt[sem]
        self.ops[eng].append(lambda e, make=make: make(e))
        return None

    def dma(self, eng, out, in_, sem="ld", slow=False):
        return self.inst(
            eng,
            lambda e, o=out, i=in_, s=slow: e.dma_start(
                out=o, in_=i, allow_slow_non_contiguous=s),
            sem=sem, n=16)

    def wait(self, eng, sem, val=None):
        self.sem(sem)
        v = self.cnt[sem] if val is None else val
        if v > 0:
            self.ops[eng].append(lambda e, s=sem, v=v: e.wait_ge(self.sems[s], v))

    def val(self, sem):
        self.sem(sem)
        return self.cnt[sem]

    def qbrk(self, eng, sem):
        self.wait(eng, sem)

    def chain(self, eng, sem, makes):
        for i, mk in enumerate(makes):
            if i > 0:
                self.wait(eng, sem)
            self.inst(eng, mk, sem=sem)

    def barrier(self, engines=None):
        engines = engines or self.ENGINES
        snap = dict(self.cnt)
        for e in engines:
            for s, v in snap.items():
                if v > 0:
                    self.ops[e].append(
                        lambda eng, s=s, v=v: eng.wait_ge(self.sems[s], v))

    def build(self):
        with self.nc.Block() as block:
            for name in self.ENGINES:
                getattr(block, name)(
                    lambda eng, name=name: [fn(eng) for fn in self.ops[name]])



def build_nc(stop_after=None):
    nc = bass.Bass("TRN2", target_bir_lowering=False, debug=False)
    p = Prog(nc)

    dt_ = nc.dram_tensor
    x_t = dt_("x_t", [D, T], F32, kind="ExternalInput").ap()
    w_qa = dt_("w_qa", [D, Q_LORA], F32, kind="ExternalInput").ap()
    w_qb_h = dt_("w_qb_h", [Q_LORA, HL * D_QK], F32, kind="ExternalInput").ap()
    w_kva = dt_("w_kva", [D, KV_LORA + D_ROPE], F32, kind="ExternalInput").ap()
    w_kvb_h = dt_("w_kvb_h", [KV_LORA, HL * 256], F32, kind="ExternalInput").ap()
    w_o = dt_("w_o", [D, D], F32, kind="ExternalInput").ap()
    gate_w = dt_("gate_w", [D, I_FFN], F32, kind="ExternalInput").ap()
    up_w = dt_("up_w", [D, I_FFN], F32, kind="ExternalInput").ap()
    down_w = dt_("down_w", [I_FFN, D], F32, kind="ExternalInput").ap()
    g_q = dt_("g_q", [128, Q_LORA // 128], F32, kind="ExternalInput").ap()
    g_kv = dt_("g_kv", [128, KV_LORA // 128], F32, kind="ExternalInput").ap()
    gate_s = dt_("gate_s", [128, I_FFN // 128], F32, kind="ExternalInput").ap()
    up_s = dt_("up_s", [128, I_FFN // 128], F32, kind="ExternalInput").ap()
    down_s = dt_("down_s", [128, D // 128], F32, kind="ExternalInput").ap()
    tok_base = dt_("tok_base", [1, 1], F32, kind="ExternalInput").ap()
    out_t = dt_("out_t", [D, T], F32, kind="ExternalOutput").ap()

    ag_in = dt_("ag_in", [T, AG_COLS], F32).ap()
    ag_out = dt_("ag_out", [S, AG_COLS], F32).ap()
    a2a_in = dt_("a2a_in", [S, HL * D_V], F32).ap()
    a2a_out = dt_("a2a_out", [S, HL * D_V], F32).ap()
    wq_gate = dt_("wq_gate", [D, I_FFN], BF16).ap()
    wq_up = dt_("wq_up", [D, I_FFN], BF16).ap()
    wq_down = dt_("wq_down", [I_FFN, D], BF16).ap()
    rope_t = dt_("rope_t", [2, 64, S], F32).ap()

    dbg = {}
    if _DEBUG:
        dbg["d_ag"] = dt_("d_ag", [T, AG_COLS], F32, kind="ExternalOutput").ap()
        dbg["d_q"] = dt_("d_q", [4 * 128, S], BF16, kind="ExternalOutput").ap()
        dbg["d_k"] = dt_("d_k", [3 * 128, S], BF16, kind="ExternalOutput").ap()
        dbg["d_v"] = dt_("d_v", [2 * 128, S], BF16, kind="ExternalOutput").ap()
        dbg["d_attn"] = dt_("d_attn", [S, HL * D_V], F32, kind="ExternalOutput").ap()
        dbg["d_x1"] = dt_("d_x1", [D, T], F32, kind="ExternalOutput").ap()
        dbg["d_m"] = dt_("d_m", [1, 4], F32, kind="ExternalOutput").ap()

    A = nc.alloc_sbuf_tensor
    ident = A("ident", [128, 128], BF16).ap()
    identf = A("identf", [128, 128], F32).ap()
    rot = A("rot", [64, 64], BF16).ap()
    masks = A("masks", [128, 4 * 512], BF16).ap()
    ones_b = A("ones_b", [128, 1], BF16).ap()
    ones_row = A("ones_row", [1, 128], F32).ap()
    ones_f = A("ones_f", [128, 1], F32).ap()
    c_eps_rms = A("c_eps_rms", [1, 1], F32).ap()
    c_eps_ln = A("c_eps_ln", [1, 1], F32).ap()
    neg_m = A("neg_m", [128, 1], F32).ap()
    m_tmp = A("m_tmp", [1, 4], F32).ap()
    qn2max = A("qn2max", [1, 1], F32).ap()
    kn2max = A("kn2max", [1, 1], F32).ap()
    tokb = A("tokb", [1, 1], F32).ap()
    gq_c = A("gq_c", [128, Q_LORA // 128], F32).ap()
    gkv_c = A("gkv_c", [128, KV_LORA // 128], F32).ap()
    gs_c = A("gs_c", [128, I_FFN // 128], F32).ap()
    us_c = A("us_c", [128, I_FFN // 128], F32).ap()
    ds_c = A("ds_c", [128, D // 128], F32).ap()
    invf = A("invf", [64, 1], F32).ap()
    invf_i = A("invf_i", [64, 1], I32).ap()
    alpha = A("alpha", [1, T], F32).ap()
    stat1 = A("stat1", [1, T], F32).ap()
    stat2 = A("stat2", [1, T], F32).ap()
    recip_sb = A("recip_sb", [1, 512], F32).ap()
    scr_a = A("scr_a", [128, 1024], F32).ap()
    scr_b = A("scr_b", [128, 1024], F32).ap()
    scr_c = A("scr_c", [128, 1024], F32).ap()
    lnhi = A("lnhi", [128, 512], BF16).ap()
    lnlo = A("lnlo", [128, 512], BF16).ap()

    ARENA = 172 * 1024
    arena_t = A("arena", [128, ARENA], U8)
    views = {F32: arena_t.bitcast(F32).ap(), BF16: arena_t.bitcast(BF16).ap(),
             I32: arena_t.bitcast(I32).ap(), U8: arena_t.ap()}

    def av(off, rows, cols, dt):
        sz = {F32: 4, BF16: 2, I32: 4, U8: 1}[dt]
        assert off % 4 == 0 and off + cols * sz <= ARENA, (off, cols, dt)
        return views[dt][0:rows, off // sz: off // sz + cols]

    PS = nc.alloc_psum_tensor
    ps_s = [PS(f"ps_s{i}", [128, 512], F32).ap() for i in range(2)]
    ps_e = [PS(f"ps_e{i}", [128, 512], F32).ap() for i in range(2)]
    ps_av = [PS(f"ps_av{i}", [128, 512], F32).ap() for i in range(2)]
    ps_d = PS("ps_d", [128, 512], F32).ap()
    ps_x = PS("ps_x", [128, 512], F32).ap()

    p.chain("gpsimd", "g", [
        lambda e: e.memset(ident[:, :], 0.0),
        lambda e: e.affine_select(out=ident[:, :], in_=ident[:, :],
                                  compare_op=ALU.not_equal, fill=1.0, base=0,
                                  pattern=[[-1, 128]], channel_multiplier=1),
        lambda e: e.memset(identf[:, :], 0.0),
        lambda e: e.affine_select(out=identf[:, :], in_=identf[:, :],
                                  compare_op=ALU.not_equal, fill=1.0, base=0,
                                  pattern=[[-1, 128]], channel_multiplier=1),
    ])
    for i in range(4):
        p.chain("gpsimd", "g", [
            lambda e, i=i: e.memset(masks[:, ts(i, 512)], 1.0),
            lambda e, i=i: e.affine_select(
                out=masks[:, ts(i, 512)], in_=masks[:, ts(i, 512)],
                compare_op=ALU.is_ge, fill=0.0, base=-128 * i,
                pattern=[[1, 512]], channel_multiplier=-1),
        ])
    p.chain("gpsimd", "g", [
        lambda e: e.memset(rot[:, :], 0.0),
        lambda e: e.affine_select(
            out=rot[:, 1:64:2], in_=rot[:, 1:64:2], compare_op=ALU.not_equal,
            fill=1.0, base=0, pattern=[[-2, 32]], channel_multiplier=1),
        lambda e: e.affine_select(
            out=rot[:, 0:64:2], in_=rot[:, 0:64:2], compare_op=ALU.not_equal,
            fill=-1.0, base=-1, pattern=[[-2, 32]], channel_multiplier=1),
        lambda e: e.iota(invf_i[:, :], pattern=[[0, 1]], base=0,
                         channel_multiplier=1),
    ])
    for mk in (lambda e: e.memset(ones_b[:, :], 1.0),
               lambda e: e.memset(ones_row[:, :], 1.0),
               lambda e: e.memset(ones_f[:, :], 1.0),
               lambda e: e.memset(c_eps_rms[:, :], RMS_EPS),
               lambda e: e.memset(c_eps_ln[:, :], LN_EPS)):
        p.inst("vector", mk, sem="v")
    p.dma("sync", gq_c[:, :], g_q[:, :])
    p.dma("sync", gkv_c[:, :], g_kv[:, :])
    p.dma("sync", gs_c[:, :], gate_s[:, :])
    p.dma("sync", us_c[:, :], up_s[:, :])
    p.dma("sync", ds_c[:, :], down_s[:, :])
    p.dma("sync", tokb[:, :], tok_base[:, :])
    p.qbrk("sync", "ld")
    p.wait("vector", "g")
    p.wait("vector", "v")
    p.chain("vector", "v", [
        lambda e: e.tensor_copy(out=invf[:, :], in_=invf_i[:, :]),
        lambda e: e.tensor_scalar(out=scr_a[0:64, 0:1], in0=invf[:, :],
                                  scalar1=32.0, scalar2=-32.0,
                                  op0=ALU.is_ge, op1=ALU.mult),
        lambda e: e.tensor_add(invf[:, :], invf[:, :], scr_a[0:64, 0:1]),
        lambda e: e.tensor_scalar(out=invf[:, :], in0=invf[:, :],
                                  scalar1=-math.log(THETA) / 32.0, scalar2=None,
                                  op0=ALU.mult),
    ])
    p.wait("scalar", "v")
    p.inst("scalar", lambda e: e.activation(out=invf[:, :], in_=invf[:, :],
                                            func=AF.Exp), sem="s")

    def build_tables(pos_row, w, cos_o, sin_o, frq, rrf, rri, ps_bc):
        for i in range(max(1, w // 512)):
            wslc = ts(i, 512) if w >= 512 else slice(0, w)
            n = 512 if w >= 512 else w
            p.wait("tensor", "v")
            p.wait("tensor", "s")
            p.inst("tensor", lambda e, wslc=wslc, n=n: e.matmul(
                ps_bc[0:64, 0:n], ones_row[0:1, 0:64], pos_row[0:1, wslc],
                start=True, stop=True), sem="t")
            p.wait("vector", "t")
            p.inst("vector", lambda e, wslc=wslc, n=n: e.tensor_scalar(
                out=frq[0:64, wslc], in0=ps_bc[0:64, 0:n],
                scalar1=invf[0:64, 0:1], scalar2=None, op0=ALU.mult), sem="v")

        def reduce_and_sin(out_ap):
            p.wait("vector", "v")
            p.chain("vector", "v", [
                lambda e: e.tensor_scalar(out=rrf[0:64, 0:w], in0=frq[0:64, 0:w],
                                          scalar1=1.0 / (2 * math.pi),
                                          scalar2=None, op0=ALU.mult),
                lambda e: e.tensor_copy(out=rri[0:64, 0:w], in_=rrf[0:64, 0:w]),
                lambda e: e.tensor_copy(out=rrf[0:64, 0:w], in_=rri[0:64, 0:w]),
                lambda e: e.scalar_tensor_tensor(
                    out=out_ap, in0=rrf[0:64, 0:w], scalar=-C1,
                    in1=frq[0:64, 0:w], op0=ALU.mult, op1=ALU.add),
                lambda e: e.scalar_tensor_tensor(
                    out=out_ap, in0=rrf[0:64, 0:w], scalar=-C2,
                    in1=out_ap, op0=ALU.mult, op1=ALU.add),
                lambda e: e.tensor_scalar(out=rrf[0:64, 0:w], in0=out_ap,
                                          scalar1=math.pi, scalar2=None,
                                          op0=ALU.is_gt),
                lambda e: e.scalar_tensor_tensor(
                    out=out_ap, in0=rrf[0:64, 0:w], scalar=-C1,
                    in1=out_ap, op0=ALU.mult, op1=ALU.add),
                lambda e: e.scalar_tensor_tensor(
                    out=out_ap, in0=rrf[0:64, 0:w], scalar=-C2,
                    in1=out_ap, op0=ALU.mult, op1=ALU.add),
                lambda e: e.tensor_scalar(out=rrf[0:64, 0:w], in0=out_ap,
                                          scalar1=-math.pi, scalar2=None,
                                          op0=ALU.is_lt),
                lambda e: e.scalar_tensor_tensor(
                    out=out_ap, in0=rrf[0:64, 0:w], scalar=C1,
                    in1=out_ap, op0=ALU.mult, op1=ALU.add),
                lambda e: e.scalar_tensor_tensor(
                    out=out_ap, in0=rrf[0:64, 0:w], scalar=C2,
                    in1=out_ap, op0=ALU.mult, op1=ALU.add),
            ])
            p.wait("scalar", "v")
            p.inst("scalar", lambda e: e.activation(
                out=out_ap, in_=out_ap, func=AF.Sin), sem="s")

        reduce_and_sin(sin_o[0:64, 0:w])
        p.wait("vector", "s")
        p.chain("vector", "v", [
            lambda e: e.tensor_scalar(out=frq[0:64, 0:w], in0=frq[0:64, 0:w],
                                      scalar1=math.pi / 2, scalar2=None,
                                      op0=ALU.add),
        ])
        reduce_and_sin(cos_o[0:64, 0:w])

    a_xstage = [av(0, 128, 1024, F32), av(4096, 128, 1024, F32)]
    a_xhi = av(8 * 1024, 128, 16 * 1024, BF16)
    a_xlo = av(40 * 1024, 128, 16 * 1024, BF16)
    a_wf = [av(72 * 1024, 128, 2048, F32), av(80 * 1024, 128, 2048, F32)]
    a_whi = [av(88 * 1024, 128, 2048, BF16), av(92 * 1024, 128, 2048, BF16)]
    a_wlo = [av(96 * 1024, 128, 2048, BF16), av(100 * 1024, 128, 2048, BF16)]
    a_qan = av(104 * 1024, 128, 6 * 1024, F32)
    a_ckvn = av(128 * 1024, 128, 4 * 1024, F32)
    a_krope = av(144 * 1024, 64, 1024, F32)
    a_pos = av(148 * 1024, 1, 1024, F32)
    a_posi = av(152 * 1024, 1, 1024, I32)
    a_cos = av(156 * 1024, 64, 1024, F32)
    a_sin = av(160 * 1024, 64, 1024, F32)
    a_frq = av(164 * 1024, 64, 1024, F32)
    a_rri = av(168 * 1024, 64, 1024, I32)
    a_krf = av(152 * 1024, 64, 1024, F32)
    a_krhi = av(0, 64, 1024, BF16)
    a_krlo = av(2048, 64, 1024, BF16)

    def _finish_early():
        p.barrier()
        p.dma("sync", out_t[0:128, :], scr_a[:, :], sem="fin")
        p.wait("gpsimd", "fin")
        with allow_ici_all_to_all():
            p.build()

    if stop_after == "consts":
        _finish_early()
        return nc, dbg

    p.wait("gpsimd", "g")
    p.inst("gpsimd", lambda e: e.iota(a_posi[:, :], pattern=[[1, T]], base=0,
                                      channel_multiplier=0), sem="g")
    p.wait("vector", "g")
    p.wait("vector", "ld")
    p.chain("vector", "v", [
        lambda e: e.tensor_copy(out=a_pos[:, :], in_=a_posi[:, :]),
        lambda e: e.tensor_scalar(out=a_pos[:, :], in0=a_pos[:, :],
                                  scalar1=tokb[0:1, 0:1], scalar2=None,
                                  op0=ALU.add),
    ])
    build_tables(a_pos, T, a_cos, a_sin, a_frq, scr_c, a_rri, ps_x)

    if stop_after == "tables":
        _finish_early()
        return nc, dbg

    xsplit_done = {}
    for k in range(16):
        b = k % 2
        if k >= 2:
            p.wait("sync", "v", xsplit_done[k - 2])
        p.dma("sync", a_xstage[b][:, :], x_t[ts(k, 128), :])
        p.qbrk("sync", "ld")
        p.wait("vector", "ld")
        p.chain("vector", "v", [
            lambda e, k=k, b=b: e.tensor_copy(out=a_xhi[:, ts(k, 1024)],
                                              in_=a_xstage[b][:, :]),
            lambda e, k=k, b=b: e.tensor_sub(a_xlo[:, ts(k, 1024)],
                                             a_xstage[b][:, :],
                                             a_xhi[:, ts(k, 1024)]),
        ])
        xsplit_done[k] = p.val("v")

    if stop_after == "xload":
        _finish_early()
        return nc, dbg

    wstate = {"i": 0, "mm": {}, "sp": {}}

    def proj_cols(w_dram, col0, ncols, ps_pair, evac_cb):
        i = wstate["i"]
        b = i % 2
        if i >= 2:
            p.wait("sync", "v", wstate["sp"][i - 2])
        p.dma("sync", a_wf[b][:, 0:16 * ncols],
              w_dram[:, col0:col0 + ncols].rearrange("(a p) c -> p a c", p=128))
        p.qbrk("sync", "ld")
        p.wait("vector", "ld")
        if i >= 2:
            p.wait("vector", "t", wstate["mm"][i - 2])
        if _PCFG["split"]:
            for k in range(16):
                sl = ts(k, ncols)
                p.chain("vector", "v", [
                    lambda e, b=b, sl=sl: e.tensor_copy(out=a_whi[b][:, sl],
                                                        in_=a_wf[b][:, sl]),
                    lambda e, b=b, sl=sl: e.tensor_sub(a_wlo[b][:, sl],
                                                       a_wf[b][:, sl],
                                                       a_whi[b][:, sl]),
                ])
        wstate["sp"][i] = p.val("v")
        p.wait("tensor", "v", wstate["sp"][i])
        if not _PCFG["mm"]:
            wstate["mm"][i] = p.val("t")
            wstate["i"] = i + 1
            return
        for k in range(16):
            sl = ts(k, ncols)
            for (wt, xt_) in ((a_whi[b], a_xhi), (a_whi[b], a_xlo),
                              (a_wlo[b], a_xhi)):
                for nt in range(2):
                    xsl = slice(k * 1024 + nt * 512, k * 1024 + nt * 512 + 512)
                    p.inst("tensor", lambda e, wt=wt, xt_=xt_, sl=sl, xsl=xsl,
                           nt=nt,
                           st=(k == 0 and wt is a_whi[b] and xt_ is a_xhi),
                           sp=(k == 15 and wt is a_wlo[b]),
                           nc_=ncols:
                           e.matmul(ps_pair[nt][0:nc_, :], wt[:, sl],
                                    xt_[:, xsl], start=st, stop=sp), sem="t")
        wstate["mm"][i] = p.val("t")
        wstate["i"] = i + 1
        if _PCFG["evac"]:
            evac_cb()

    def make_rms_evac(dst, cc, nchunks, ps_pair):
        def cb():
            tdone = p.val("t")
            if _PCFG["evac_sq"]:
                p.wait("scalar", "t", tdone)
                for nt in range(2):
                    p.inst("scalar", lambda e, nt=nt: e.activation(
                        out=scr_b[:, ts(nt, 512)], in_=ps_pair[nt][:, :],
                        func=AF.Square), sem="s")
            if _PCFG["evac_cp"]:
                p.wait("vector", "t", tdone)
                p.wait("vector", "s")
                for nt in range(2):
                    dsl = slice(cc * 1024 + nt * 512,
                                cc * 1024 + nt * 512 + 512)
                    p.inst("vector", lambda e, nt=nt, dsl=dsl: e.tensor_copy(
                        out=dst[:, dsl], in_=ps_pair[nt][:, :]), sem="v")
            if _PCFG["evac_sq"] and _PCFG["evac_mm"]:
                sq_done = p.val("s")
                p.wait("tensor", "s", sq_done)
                for nt in range(2):
                    tgt = ps_d if nt == 0 else ps_x
                    p.inst("tensor", lambda e, nt=nt, cc=cc, tgt=tgt: e.matmul(
                        tgt[0:1, :], ones_f[:, :], scr_b[:, ts(nt, 512)],
                        start=(cc == 0), stop=(cc == nchunks - 1)), sem="t")
        return cb

    def alpha_and_scale(dst, nchunks, denom):
        ssd = p.val("t")
        p.wait("vector", "t", ssd)
        for nt in range(2):
            src = ps_d if nt == 0 else ps_x
            p.chain("vector", "v", [
                lambda e, nt=nt, src=src: e.tensor_scalar(
                    out=stat1[0:1, ts(nt, 512)], in0=src[0:1, :],
                    scalar1=1.0 / denom, scalar2=RMS_EPS,
                    op0=ALU.mult, op1=ALU.add),
                lambda e, nt=nt: e.reciprocal(stat2[0:1, ts(nt, 512)],
                                              stat1[0:1, ts(nt, 512)]),
            ])
        p.wait("scalar", "v")
        p.inst("scalar", lambda e: e.activation(
            out=alpha[0:1, :], in_=stat2[0:1, :], func=AF.Sqrt), sem="s")
        ad = p.val("s")
        p.wait("tensor", "s", ad)
        for nt in range(2):
            tgt = ps_d if nt == 0 else ps_x
            p.inst("tensor", lambda e, nt=nt, tgt=tgt: e.matmul(
                tgt[:, :], ones_row[0:1, :], alpha[0:1, ts(nt, 512)],
                start=True, stop=True), sem="t")
        bd = p.val("t")
        p.wait("vector", "t", bd)
        for cc in range(nchunks):
            for nt in range(2):
                sl = slice(cc * 1024 + nt * 512, cc * 1024 + nt * 512 + 512)
                src = ps_d if nt == 0 else ps_x
                p.inst("vector", lambda e, sl=sl, src=src: e.tensor_mul(
                    dst[:, sl], dst[:, sl], src[:, :]), sem="v")

    for cc in range(6):
        pair = ps_s if cc % 2 == 0 else ps_e
        proj_cols(w_qa, cc * 128, 128, pair,
                  make_rms_evac(a_qan, cc, 6, pair))
        if stop_after == f"proj_cc{cc}":
            _finish_early()
            return nc, dbg
    if stop_after == "proj_qa_mm":
        _finish_early()
        return nc, dbg
    alpha_and_scale(a_qan, 6, Q_LORA)
    if stop_after == "proj_qa":
        _finish_early()
        return nc, dbg
    qan_done = p.val("v")

    for cc in range(4):
        pair = ps_s if cc % 2 == 0 else ps_e
        proj_cols(w_kva, cc * 128, 128, pair, make_rms_evac(a_ckvn, cc, 4, pair))
    alpha_and_scale(a_ckvn, 4, KV_LORA)
    ckvn_done = p.val("v")

    def krope_evac():
        tdone = p.val("t")
        p.wait("vector", "t", tdone)
        for nt in range(2):
            p.inst("vector", lambda e, nt=nt: e.tensor_copy(
                out=a_krf[:, ts(nt, 512)], in_=ps_av[nt][0:64, :]), sem="v")
        p.wait("vector", "v")
        p.chain("vector", "v", [
            lambda e: e.tensor_copy(out=a_krhi[:, :], in_=a_krf[:, :]),
            lambda e: e.tensor_sub(a_krlo[:, :], a_krf[:, :], a_krhi[:, :]),
        ])
        sp = p.val("v")
        p.wait("tensor", "v", sp)
        for nt in range(2):
            tgt = ps_s[nt]
            p.inst("tensor", lambda e, nt=nt, tgt=tgt: e.matmul(
                tgt[0:64, :], rot[:, :], a_krhi[:, ts(nt, 512)],
                start=True, stop=False), sem="t")
            p.inst("tensor", lambda e, nt=nt, tgt=tgt: e.matmul(
                tgt[0:64, :], rot[:, :], a_krlo[:, ts(nt, 512)],
                start=False, stop=True), sem="t")
        rd = p.val("t")
        p.wait("vector", "t", rd)
        for nt in range(2):
            sl = ts(nt, 512)
            p.chain("vector", "v", [
                lambda e, sl=sl: e.tensor_mul(scr_a[0:64, sl], a_krf[:, sl],
                                              a_cos[0:64, sl]),
                lambda e, sl=sl, nt=nt: e.tensor_mul(
                    scr_b[0:64, sl], ps_s[nt][0:64, :], a_sin[0:64, sl]),
                lambda e, sl=sl: e.tensor_add(a_krope[:, sl], scr_a[0:64, sl],
                                              scr_b[0:64, sl]),
            ])

    if stop_after == "proj_kva":
        _finish_early()
        return nc, dbg
    proj_cols(w_kva, 512, 64, ps_av, krope_evac)

    if stop_after == "proj":
        _finish_early()
        return nc, dbg

    def store_ag(src_fm, nchunks, col_base, rows=128):
        grp_w = 4 * rows
        for cc in range(nchunks):
            for grp in range(2):
                pre = p.val("v")
                p.wait("tensor", "v", pre)
                for q in range(4):
                    tt = grp * 4 + q
                    p.inst("tensor", lambda e, cc=cc, tt=tt, q=q: e.transpose(
                        ps_x[:, q * rows:(q + 1) * rows],
                        src_fm[0:rows, cc * 1024 + tt * 128:
                               cc * 1024 + (tt + 1) * 128],
                        identf[0:rows, 0:rows]), sem="t")
                trd = p.val("t")
                p.wait("vector", "t", trd)
                p.wait("vector", "ag")
                p.inst("vector", lambda e, grp_w=grp_w: e.tensor_copy(
                    out=scr_c[:, 0:grp_w], in_=ps_x[:, 0:grp_w]), sem="v")
                evd = p.val("v")
                p.wait("sync", "v", evd)
                for q in range(4):
                    tt = grp * 4 + q
                    p.dma("sync", ag_in[ts(tt, 128),
                                        col_base + cc * rows:
                                        col_base + (cc + 1) * rows],
                          scr_c[:, q * rows:(q + 1) * rows], sem="ag")

    store_ag(a_qan, 6, 0)
    store_ag(a_ckvn, 4, Q_LORA)
    store_ag(a_krope, 1, Q_LORA + KV_LORA, rows=64)

    if _DEBUG:
        p.wait("sync", "ag")
        p.dma("sync", dbg["d_ag"][:, :], ag_in[:, :], sem="st")

    if stop_after == "p1":
        _finish_early()
        return nc, dbg

    for cblk in range(S // 512):
        p.wait("gpsimd", "v")
        p.inst("gpsimd", lambda e, cblk=cblk: e.iota(
            a_rri[0:1, 0:512], pattern=[[1, 512]], base=cblk * 512,
            channel_multiplier=0), sem="g")
        p.wait("vector", "g")
        p.wait("vector", "st")
        p.inst("vector", lambda e: e.tensor_copy(
            out=a_pos[0:1, 0:512], in_=a_rri[0:1, 0:512]), sem="v")
        p.wait("vector", "v")
        build_tables(a_pos[0:1, 0:512], 512, a_cos, a_sin, a_frq, scr_c,
                     a_rri, ps_x)
        tb = p.val("s")
        p.wait("sync", "s", tb)
        p.dma("sync", rope_t[1, :, cblk * 512:(cblk + 1) * 512],
              a_sin[0:64, 0:512], sem="st")
        p.dma("sync", rope_t[0, :, cblk * 512:(cblk + 1) * 512],
              a_cos[0:64, 0:512], sem="st")
        p.qbrk("sync", "st")

    p.wait("gpsimd", "ag")
    p.inst("gpsimd", lambda e: e.collective_compute(
        "AllGather", ALU.bypass, replica_groups=[list(range(NCORES))],
        ins=[ag_in[:, :]], outs=[ag_out[:, :]]), sem="cc")

    if stop_after == "ag":
        _finish_early()
        return nc, dbg

    p.barrier()

    if stop_after == "qz":
        _finish_early()
        return nc, dbg

    o = 0
    a_qn_hi = av(o, 128, S, BF16); o += 16 * 1024
    a_qn_lo = av(o, 128, S, BF16); o += 16 * 1024
    a_qr_A = av(o, 128, S, BF16); o += 16 * 1024
    a_qr_B = av(o, 128, S, BF16); o += 16 * 1024
    a_kn_hi = av(o, 128, S, BF16); o += 16 * 1024
    a_kn_lo = av(o, 128, S, BF16); o += 16 * 1024
    a_kr_P = av(o, 128, S, BF16); o += 16 * 1024
    a_v_hi = av(o, 128, S, BF16); o += 16 * 1024
    a_v_lo = av(o, 128, S, BF16); o += 16 * 1024
    O_W = o
    a_wb_hi = av(O_W, 128, 6 * 384, BF16)
    a_wb_lo = av(O_W + 5 * 1024, 128, 6 * 384, BF16)
    a_wvb_hi = av(O_W, 128, 4 * 512, BF16)
    a_wvb_lo = av(O_W + 5 * 1024, 128, 4 * 512, BF16)
    O_T = O_W + 10 * 1024
    a_stg2 = av(O_T, 128, 2 * 768, F32)
    a_qaT_hi = av(O_T + 6 * 1024, 128, 6 * 512, BF16)
    a_qaT_lo = av(O_T + 12 * 1024, 128, 6 * 512, BF16)
    a_tb_cos = av(O_T, 64, 512, F32)
    a_tb_sin = av(O_T + 2 * 1024, 64, 512, F32)
    a_qrh = av(O_T + 4 * 1024, 64, 512, BF16)
    a_kstg4 = av(O_T, 128, 4 * 512, F32)
    a_ckvT_hi = av(O_T + 8 * 1024, 128, 4 * 512, BF16)
    a_ckvT_lo = av(O_T + 12 * 1024, 128, 4 * 512, BF16)
    a_kstg = av(O_T, 128, 64, F32)
    a_krT = av(O_T + 16 * 1024, 64, 512, BF16)


    qz_u = av(169984, 128, 1024, U8)
    qz_b = av(169984 + 1024, 128, 1024, BF16)
    qz_plan = []
    for (wsrc, wdst, ncols) in ((gate_w, wq_gate, I_FFN), (up_w, wq_up, I_FFN),
                                (down_w, wq_down, D)):
        for rch in range(wsrc.shape[0] // 128):
            for cch in range(ncols // 1024):
                qz_plan.append((wsrc, wdst, rch, cch))
    qz_state = {"i": 0, "cast": {}, "out": {}}

    def emit_qz_step():
        i = qz_state["i"]
        if i >= len(qz_plan):
            return
        wsrc, wdst, rch, cch = qz_plan[i]
        if i >= 1:
            p.wait("gpsimd", "v", qz_state["cast"][i - 1])
        p.dma("gpsimd", scr_a[:, :],
              wsrc[ts(rch, 128), cch * 1024:(cch + 1) * 1024], sem="qd")
        p.qbrk("gpsimd", "qd")
        p.wait("vector", "qd")
        if i >= 1:
            p.wait("vector", "qd", qz_state["out"][i - 1])
        p.chain("vector", "v", [
            lambda e: e.tensor_copy(out=qz_u.bitcast(F8)[:, :],
                                    in_=scr_a[:, :]),
            lambda e: e.tensor_copy(out=qz_b[:, :],
                                    in_=qz_u.bitcast(F8)[:, :]),
        ])
        qz_state["cast"][i] = p.val("v")
        p.wait("gpsimd", "v", qz_state["cast"][i])
        qz_state["out"][i] = p.dma(
            "gpsimd", wdst[ts(rch, 128), cch * 1024:(cch + 1) * 1024],
            qz_b[:, :], sem="qd")
        p.qbrk("gpsimd", "qd")
        qz_state["i"] = i + 1

    a_ehi = [av(O_T, 128, 512, BF16), av(O_T + 1024, 128, 512, BF16)]
    a_elo = [av(O_T + 2048, 128, 512, BF16), av(O_T + 3072, 128, 512, BF16)]
    a_bc = av(O_T + 4096, 128, 512, F32)
    a_fin = av(O_T + 6144, 128, 512, F32)
    a_ast = av(O_T + 8192, 128, 512, F32)
    a_em = av(O_T + 10240, 128, 512, F32)

    def load_wqb():
        p.wait("sync", "t")
        for k in range(6):
            p.wait("sync", "v")
            p.dma("sync", scr_a[:, 0:384], w_qb_h[ts(k, 128), :])
            p.wait("vector", "ld")
            p.chain("vector", "v", [
                lambda e, k=k: e.tensor_scalar(
                    out=scr_a[:, 0:384], in0=scr_a[:, 0:384],
                    scalar1=gq_c[:, k:k + 1], scalar2=None, op0=ALU.mult),
                lambda e, k=k: e.tensor_copy(out=a_wb_hi[:, ts(k, 384)],
                                             in_=scr_a[:, 0:384]),
                lambda e, k=k: e.tensor_sub(a_wb_lo[:, ts(k, 384)],
                                            scr_a[:, 0:384],
                                            a_wb_hi[:, ts(k, 384)]),
            ])

    def head_phase(h, wqb_loaded=False):
        if not wqb_loaded:
            load_wqb()
        p.inst("vector", lambda e: e.memset(qn2max[:, :], 0.0), sem="v")
        p.inst("vector", lambda e: e.memset(kn2max[:, :], 0.0), sem="v")
        p.wait("tensor", "v")

        qpipe = {"trd": 0, "qb": 0, "rope_v": 0}
        for B in range(16):
            for hb in range(2):
                p.wait("gpsimd", "t", qpipe["trd"])
                p.wait("gpsimd", "v", qpipe["rope_v"])
                p.dma("gpsimd", a_stg2[:, :],
                      ag_out[B * 512 + hb * 256: B * 512 + (hb + 1) * 256,
                             0:Q_LORA]
                      .rearrange("(a p) c -> p a c", p=128), sem="gd")
                p.qbrk("gpsimd", "gd")
                p.wait("tensor", "gd")
                first_evac = True
                for k in range(6):
                    pst = ps_x if k % 2 == 0 else ps_av[0]
                    p.wait("tensor", "v")
                    for tt in range(2):
                        p.inst("tensor", lambda e, k=k, tt=tt, pst=pst:
                               e.transpose(pst[:, ts(tt, 128)],
                                           a_stg2[:, tt * 768 + k * 128:
                                                  tt * 768 + (k + 1) * 128],
                                           identf[:, :]), sem="t")
                    trd = p.val("t")
                    p.wait("vector", "t", trd)
                    if first_evac:
                        p.wait("vector", "t", qpipe["qb"])
                        first_evac = False
                    dsl = slice(k * 512 + hb * 256, k * 512 + hb * 256 + 256)
                    p.chain("vector", "v", [
                        lambda e, dsl=dsl, pst=pst: e.tensor_copy(
                            out=a_qaT_hi[:, dsl], in_=pst[:, 0:256]),
                        lambda e, dsl=dsl, pst=pst: e.tensor_sub(
                            a_qaT_lo[:, dsl], pst[:, 0:256],
                            a_qaT_hi[:, dsl]),
                    ])
                qpipe["trd"] = trd
            qaT_done = p.val("v")
            p.wait("tensor", "v", qaT_done)
            first_n = True
            first_r = True
            for k in range(6):
                wn = slice(k * 384 + h * 192, k * 384 + h * 192 + 128)
                wr = slice(k * 384 + h * 192 + 128, k * 384 + h * 192 + 192)
                qsl = ts(k, 512)
                for (wt, qt) in ((a_wb_hi, a_qaT_hi), (a_wb_hi, a_qaT_lo),
                                 (a_wb_lo, a_qaT_hi)):
                    p.inst("tensor", lambda e, wt=wt, qt=qt, wn=wn, qsl=qsl,
                           st=first_n, sp=(k == 5 and wt is a_wb_lo):
                           e.matmul(ps_s[0][:, :], wt[:, wn], qt[:, qsl],
                                    start=st, stop=sp), sem="t")
                    first_n = False
                for (wt, qt) in ((a_wb_hi, a_qaT_hi), (a_wb_hi, a_qaT_lo),
                                 (a_wb_lo, a_qaT_hi)):
                    p.inst("tensor", lambda e, wt=wt, qt=qt, wr=wr, qsl=qsl,
                           st=first_r, sp=(k == 5 and wt is a_wb_lo):
                           e.matmul(ps_e[0][0:64, :], wt[:, wr], qt[:, qsl],
                                    start=st, stop=sp), sem="t")
                    first_r = False
            qb_done = p.val("t")
            qpipe["qb"] = qb_done
            p.wait("vector", "t", qb_done)
            p.chain("vector", "v", [
                lambda e, B=B: e.tensor_copy(out=a_qn_hi[:, ts(B, 512)],
                                             in_=ps_s[0][:, :]),
                lambda e, B=B: e.tensor_sub(a_qn_lo[:, ts(B, 512)],
                                            ps_s[0][:, :],
                                            a_qn_hi[:, ts(B, 512)]),
            ])
            p.wait("scalar", "t", qb_done)
            p.wait("scalar", "v")
            p.wait("scalar", "t")
            p.inst("scalar", lambda e: e.activation(
                out=scr_b[:, 0:512], in_=ps_s[0][:, :], func=AF.Square), sem="s")
            p.chain("vector", "v", [
                lambda e: e.tensor_copy(out=scr_a[0:64, 0:512],
                                        in_=ps_e[0][0:64, :]),
                lambda e, B=B: e.tensor_copy(out=a_qrh[:, :],
                                             in_=scr_a[0:64, 0:512]),
                lambda e, B=B: e.tensor_sub(scr_c[0:64, 0:512],
                                            scr_a[0:64, 0:512], a_qrh[:, :]),
                lambda e, B=B: e.tensor_copy(out=a_qr_B[0:64, ts(B, 512)],
                                             in_=scr_c[0:64, 0:512]),
            ])
            rsp = p.val("v")
            p.wait("tensor", "v", rsp)
            p.inst("tensor", lambda e: e.matmul(
                ps_e[1][0:64, :], rot[:, :], a_qrh[:, :],
                start=True, stop=False), sem="t")
            p.inst("tensor", lambda e, B=B: e.matmul(
                ps_e[1][0:64, :], rot[:, :], a_qr_B[0:64, ts(B, 512)],
                start=False, stop=True), sem="t")
            rot_done = p.val("t")
            p.wait("gpsimd", "v")
            p.wait("gpsimd", "t")
            p.dma("gpsimd", a_tb_cos[:, :],
                  rope_t[0, :, B * 512:(B + 1) * 512], sem="gd")
            p.dma("gpsimd", a_tb_sin[:, :],
                  rope_t[1, :, B * 512:(B + 1) * 512], sem="gd")
            p.qbrk("gpsimd", "gd")
            p.wait("vector", "gd")
            p.wait("vector", "t", rot_done)
            p.wait("vector", "s")
            p.chain("vector", "v", [
                lambda e: e.tensor_mul(scr_b[0:64, 512:1024],
                                       scr_a[0:64, 0:512], a_tb_cos[:, :]),
                lambda e: e.tensor_mul(scr_c[0:64, 0:512], ps_e[1][0:64, :],
                                       a_tb_sin[:, :]),
                lambda e: e.tensor_add(scr_a[0:64, 0:512],
                                       scr_b[0:64, 512:1024],
                                       scr_c[0:64, 0:512]),
                lambda e, B=B: e.tensor_copy(out=a_qr_A[0:64, ts(B, 512)],
                                             in_=scr_a[0:64, 0:512]),
                lambda e, B=B: e.tensor_sub(scr_c[0:64, 0:512],
                                            scr_a[0:64, 0:512],
                                            a_qr_A[0:64, ts(B, 512)]),
                lambda e, B=B: e.tensor_copy(out=a_qr_B[0:64, ts(B, 512)],
                                             in_=scr_c[0:64, 0:512]),
            ])
            qpipe["rope_v"] = p.val("v")
            p.wait("scalar", "v")
            p.inst("scalar", lambda e: e.activation(
                out=scr_b[0:64, 512:1024], in_=scr_a[0:64, 0:512],
                func=AF.Square), sem="s")
            sq2 = p.val("s")
            p.wait("tensor", "s", sq2)
            p.inst("tensor", lambda e: e.matmul(
                ps_d[0:1, :], ones_f[:, :], scr_b[:, 0:512],
                start=True, stop=False), sem="t")
            p.inst("tensor", lambda e: e.matmul(
                ps_d[0:1, :], ones_f[0:64, :], scr_b[0:64, 512:1024],
                start=False, stop=True), sem="t")
            nrm = p.val("t")
            p.wait("vector", "t", nrm)
            p.chain("vector", "v", [
                lambda e: e.tensor_reduce(m_tmp[0:1, 0:1], ps_d[0:1, :],
                                          axis=mybir.AxisListType.X,
                                          op=ALU.max),
                lambda e: e.tensor_max(qn2max[:, :], qn2max[:, :],
                                       m_tmp[0:1, 0:1]),
            ])
        qfin = p.val("v")
        p.wait("gpsimd", "v", qfin)
        p.dma("gpsimd", a_qr_A[64:128, :], a_qr_A[0:64, :], sem="gd")
        p.qbrk("gpsimd", "gd")
        p.dma("gpsimd", a_qr_B[64:128, :], a_qr_B[0:64, :], sem="gd")
        p.qbrk("gpsimd", "gd")

        p.wait("sync", "t")
        for k in range(4):
            p.wait("sync", "v")
            p.dma("sync", scr_a[:, 0:512], w_kvb_h[ts(k, 128), :])
            p.wait("vector", "ld")
            p.chain("vector", "v", [
                lambda e, k=k: e.tensor_scalar(
                    out=scr_a[:, 0:512], in0=scr_a[:, 0:512],
                    scalar1=gkv_c[:, k:k + 1], scalar2=None, op0=ALU.mult),
                lambda e, k=k: e.tensor_copy(out=a_wvb_hi[:, ts(k, 512)],
                                             in_=scr_a[:, 0:512]),
                lambda e, k=k: e.tensor_sub(a_wvb_lo[:, ts(k, 512)],
                                            scr_a[:, 0:512],
                                            a_wvb_hi[:, ts(k, 512)]),
            ])
        p.wait("tensor", "v")

        kpipe = {"trd": 0, "mm": 0}
        for B in range(16):
            p.wait("gpsimd", "t", kpipe["trd"])
            p.dma("gpsimd", a_kstg4[:, :],
                  ag_out[B * 512:(B + 1) * 512, Q_LORA:Q_LORA + KV_LORA]
                  .rearrange("(a p) c -> p a c", p=128), sem="gd")
            p.qbrk("gpsimd", "gd")
            p.wait("tensor", "gd")
            first_evac = True
            for k in range(4):
                pst = ps_x if k % 2 == 0 else ps_av[0]
                p.wait("tensor", "v")
                for tt in range(4):
                    p.inst("tensor", lambda e, k=k, tt=tt, pst=pst:
                           e.transpose(pst[:, ts(tt, 128)],
                                       a_kstg4[:, tt * 512 + k * 128:
                                               tt * 512 + (k + 1) * 128],
                                       identf[:, :]), sem="t")
                trd = p.val("t")
                kpipe["trd"] = trd
                p.wait("vector", "t", trd)
                if first_evac:
                    p.wait("vector", "t", kpipe["mm"])
                    first_evac = False
                p.chain("vector", "v", [
                    lambda e, k=k, pst=pst: e.tensor_copy(
                        out=a_ckvT_hi[:, ts(k, 512)], in_=pst[:, :]),
                    lambda e, k=k, pst=pst: e.tensor_sub(
                        a_ckvT_lo[:, ts(k, 512)], pst[:, :],
                        a_ckvT_hi[:, ts(k, 512)]),
                ])
            ckvT_done = p.val("v")
            p.wait("tensor", "v", ckvT_done)
            first = True
            for k in range(4):
                wsl = slice(k * 512 + h * 256, k * 512 + h * 256 + 128)
                csl = ts(k, 512)
                for (wt, ct) in ((a_wvb_hi, a_ckvT_hi), (a_wvb_hi, a_ckvT_lo),
                                 (a_wvb_lo, a_ckvT_hi)):
                    p.inst("tensor", lambda e, wt=wt, ct=ct, wsl=wsl, csl=csl,
                           st=first, sp=(k == 3 and wt is a_wvb_lo):
                           e.matmul(ps_s[0][:, :], wt[:, wsl], ct[:, csl],
                                    start=st, stop=sp), sem="t")
                    first = False
            knd = p.val("t")
            p.wait("vector", "t", knd)
            p.chain("vector", "v", [
                lambda e, B=B: e.tensor_copy(out=a_kn_hi[:, ts(B, 512)],
                                             in_=ps_s[0][:, :]),
                lambda e, B=B: e.tensor_sub(a_kn_lo[:, ts(B, 512)],
                                            ps_s[0][:, :],
                                            a_kn_hi[:, ts(B, 512)]),
            ])
            p.wait("scalar", "t", knd)
            p.wait("scalar", "v")
            p.inst("scalar", lambda e: e.activation(
                out=scr_b[:, 0:512], in_=ps_s[0][:, :], func=AF.Square), sem="s")
            for tt in range(4):
                p.wait("gpsimd", "t")
                p.dma("gpsimd", a_kstg[:, 0:64],
                      ag_out[B * 512 + tt * 128: B * 512 + (tt + 1) * 128,
                             Q_LORA + KV_LORA:AG_COLS], sem="gd")
                p.wait("tensor", "gd")
                p.wait("tensor", "v")
                p.wait("tensor", "s")
                p.inst("tensor", lambda e: e.transpose(
                    ps_e[1][0:64, 0:128], a_kstg[:, 0:64], identf[:, :]),
                    sem="t")
                trd = p.val("t")
                p.wait("vector", "t", trd)
                ksl = slice(B * 512 + tt * 128, B * 512 + (tt + 1) * 128)
                p.chain("vector", "v", [
                    lambda e, ksl=ksl: e.tensor_copy(
                        out=a_kr_P[0:64, ksl], in_=ps_e[1][0:64, 0:128]),
                    lambda e, ksl=ksl: e.tensor_sub(
                        scr_c[0:64, 0:128], ps_e[1][0:64, 0:128],
                        a_kr_P[0:64, ksl]),
                    lambda e, tt=tt: e.tensor_copy(
                        out=a_krT[:, ts(tt, 128)], in_=scr_c[0:64, 0:128]),
                ])
                p.wait("scalar", "t", trd)
                p.wait("scalar", "v")
                p.inst("scalar", lambda e, tt=tt: e.activation(
                    out=scr_b[0:64, 512 + tt * 128: 512 + (tt + 1) * 128],
                    in_=ps_e[1][0:64, 0:128], func=AF.Square), sem="s")
            sqd = p.val("s")
            p.wait("tensor", "s", sqd)
            p.inst("tensor", lambda e: e.matmul(
                ps_d[0:1, :], ones_f[:, :], scr_b[:, 0:512],
                start=True, stop=False), sem="t")
            p.inst("tensor", lambda e: e.matmul(
                ps_d[0:1, :], ones_f[0:64, :], scr_b[0:64, 512:1024],
                start=False, stop=True), sem="t")
            nrm = p.val("t")
            p.wait("vector", "t", nrm)
            p.chain("vector", "v", [
                lambda e: e.tensor_reduce(m_tmp[0:1, 1:2], ps_d[0:1, :],
                                          axis=mybir.AxisListType.X,
                                          op=ALU.max),
                lambda e: e.tensor_max(kn2max[:, :], kn2max[:, :],
                                       m_tmp[0:1, 1:2]),
            ])
            krtd = p.val("v")
            p.wait("gpsimd", "v", krtd)
            p.dma("gpsimd", a_kr_P[64:128, ts(B, 512)], a_krT[:, :], sem="gd")
            for tt in range(4):
                p.wait("tensor", "v")
                first = True
                for k in range(4):
                    csl = slice(k * 512 + tt * 128, k * 512 + (tt + 1) * 128)
                    wsl = slice(k * 512 + h * 256 + 128,
                                k * 512 + h * 256 + 256)
                    for (ct, wt) in ((a_ckvT_hi, a_wvb_hi),
                                     (a_ckvT_hi, a_wvb_lo),
                                     (a_ckvT_lo, a_wvb_hi)):
                        p.inst("tensor", lambda e, ct=ct, wt=wt, csl=csl,
                               wsl=wsl, st=first,
                               sp=(k == 3 and ct is a_ckvT_lo):
                               e.matmul(ps_e[0][:, 0:128], ct[:, csl],
                                        wt[:, wsl], start=st, stop=sp),
                               sem="t")
                        first = False
                vd = p.val("t")
                kpipe["mm"] = vd
                p.wait("vector", "t", vd)
                vsl = slice((B * 4 + tt) * 128, (B * 4 + tt + 1) * 128)
                p.chain("vector", "v", [
                    lambda e, vsl=vsl: e.tensor_copy(out=a_v_hi[:, vsl],
                                                     in_=ps_e[0][:, 0:128]),
                    lambda e, vsl=vsl: e.tensor_sub(a_v_lo[:, vsl],
                                                    ps_e[0][:, 0:128],
                                                    a_v_hi[:, vsl]),
                ])

        p.wait("vector", "v")
        p.chain("vector", "v", [
            lambda e: e.tensor_mul(m_tmp[0:1, 2:3], qn2max[:, :], kn2max[:, :]),
        ])
        p.wait("scalar", "v")
        p.inst("scalar", lambda e: e.activation(
            out=m_tmp[0:1, 3:4], in_=m_tmp[0:1, 2:3], func=AF.Sqrt), sem="s")
        p.wait("vector", "s")
        p.chain("vector", "v", [
            lambda e: e.tensor_scalar(out=m_tmp[0:1, 0:1], in0=m_tmp[0:1, 3:4],
                                      scalar1=-QK_SCALE, scalar2=None,
                                      op0=ALU.mult),
        ])
        p.wait("tensor", "v")
        p.inst("tensor", lambda e: e.matmul(
            ps_x[:, 0:1], ones_row[0:1, :], m_tmp[0:1, 0:1],
            start=True, stop=True), sem="t")
        p.wait("vector", "t")
        p.inst("vector", lambda e: e.tensor_copy(out=neg_m[:, :],
                                                 in_=ps_x[:, 0:1]), sem="v")
        if _DEBUG:
            p.wait("sync", "v")
            p.dma("sync", dbg["d_m"][:, :], m_tmp[:, :], sem="st")
        p.barrier(("tensor", "vector", "scalar"))

        exp_done = {}
        split_done = {}
        avden_done = {}
        fin_done = {}
        rec_done = {}
        flat = []
        for qc in range(16):
            for kc in range(4 * qc + 4):
                flat.append((qc, kc))
        nj = len(flat)

        def emit_scores(j):
            qc, kc = flat[j]
            jj = j % 2
            qsl = ts(qc, 512)
            ksl = ts(kc, 128)
            if j >= 2:
                p.wait("tensor", "s", exp_done[j - 2])
            for i, (kt, qt) in enumerate((
                    (a_kn_hi, a_qn_hi), (a_kn_hi, a_qn_lo),
                    (a_kn_lo, a_qn_hi))):
                p.inst("tensor", lambda e, kt=kt, qt=qt, ksl=ksl, qsl=qsl,
                       st=(i == 0), jj=jj:
                       e.matmul(ps_s[jj][:, :], kt[:, ksl], qt[:, qsl],
                                start=st, stop=False), sem="t")
            p.inst("tensor", lambda e, ksl=ksl, qsl=qsl, jj=jj:
                   e.matmul(ps_s[jj][:, :], a_kr_P[:, ksl], a_qr_A[:, qsl],
                            start=False, stop=False), sem="t")
            return p.inst("tensor", lambda e, ksl=ksl, qsl=qsl, jj=jj:
                          e.matmul(ps_s[jj][:, :], a_kr_P[:, ksl],
                                   a_qr_B[:, qsl],
                                   start=False, stop=True), sem="t")

        def emit_exp_split(j, sc_done):
            qc, kc = flat[j]
            jj = j % 2
            p.wait("scalar", "t", sc_done)
            if j >= 2:
                p.wait("scalar", "v", split_done[j - 2])
            p.inst("scalar", lambda e, jj=jj: e.activation(
                out=a_ehi[jj][:, :], in_=ps_s[jj][:, :], func=AF.Exp,
                bias=neg_m[:, :], scale=QK_SCALE), sem="s")
            exp_done[j] = p.inst("scalar", lambda e, jj=jj: e.activation(
                out=ps_e[jj][:, :], in_=ps_s[jj][:, :], func=AF.Exp,
                bias=neg_m[:, :], scale=QK_SCALE), sem="s")
            p.wait("vector", "s", exp_done[j])
            if kc >= 4 * qc:
                didx = kc - 4 * qc
                p.chain("vector", "v", [
                    lambda e, jj=jj, didx=didx: e.tensor_mul(
                        a_ehi[jj][:, :], a_ehi[jj][:, :],
                        masks[:, ts(didx, 512)]),
                    lambda e, jj=jj, didx=didx: e.tensor_mul(
                        a_em[:, :], ps_e[jj][:, :], masks[:, ts(didx, 512)]),
                    lambda e, jj=jj: e.tensor_sub(a_elo[jj][:, :],
                                                  a_em[:, :],
                                                  a_ehi[jj][:, :]),
                ])
            else:
                p.inst("vector", lambda e, jj=jj: e.tensor_sub(
                    a_elo[jj][:, :], ps_e[jj][:, :], a_ehi[jj][:, :]), sem="v")
            split_done[j] = p.val("v")

        def emit_avden(j):
            qc, kc = flat[j]
            jj = j % 2
            qq = qc % 2
            nkc = 4 * qc + 4
            ksl = ts(kc, 128)
            p.wait("tensor", "v", split_done[j])
            if kc == 0 and qc >= 2:
                p.wait("tensor", "v", fin_done[qc - 2])
            if kc == 0 and qc >= 1:
                p.wait("tensor", "v", rec_done[qc - 1])
            for i, (vt, et) in enumerate((
                    (a_v_hi, a_ehi[jj]), (a_v_lo, a_ehi[jj]),
                    (a_v_hi, a_elo[jj]))):
                p.inst("tensor", lambda e, vt=vt, et=et, ksl=ksl, qq=qq,
                       st=(kc == 0 and i == 0),
                       sp=(kc == nkc - 1 and i == 2):
                       e.matmul(ps_av[qq][:, :], vt[:, ksl], et[:, :],
                                start=st, stop=sp), sem="t")
            avden_done[j] = p.inst(
                "tensor", lambda e, jj=jj, st=(kc == 0), sp=(kc == nkc - 1):
                e.matmul(ps_d[0:1, :], ones_b[:, :], a_ehi[jj][:, :],
                         start=st, stop=sp), sem="t")

        def emit_finalize(qc):
            qq = qc % 2
            p.wait("vector", "t", avden_done[sum(4 * q + 4 for q in range(qc + 1)) - 1])
            p.inst("vector", lambda e: e.reciprocal(recip_sb[0:1, :],
                                                    ps_d[0:1, :]), sem="v")
            rec_done[qc] = p.val("v")
            p.wait("tensor", "v", rec_done[qc])
            bc_mm = p.inst("tensor", lambda e: e.matmul(
                ps_x[:, :], ones_row[0:1, :], recip_sb[0:1, :],
                start=True, stop=True), sem="t")
            p.wait("vector", "t", bc_mm)
            p.chain("vector", "v", [
                lambda e: e.tensor_copy(out=a_bc[:, :], in_=ps_x[:, :]),
                lambda e, qq=qq: e.tensor_mul(a_fin[:, :], ps_av[qq][:, :],
                                              a_bc[:, :]),
            ])
            find = p.val("v")
            p.wait("tensor", "v", find)
            for q4 in range(4):
                p.inst("tensor", lambda e, q4=q4: e.transpose(
                    ps_x[:, ts(q4, 128)], a_fin[:, ts(q4, 128)], identf[:, :]),
                    sem="t")
            trd = p.val("t")
            p.wait("vector", "t", trd)
            p.wait("vector", "a2")
            p.inst("vector", lambda e: e.tensor_copy(out=a_ast[:, :],
                                                     in_=ps_x[:, :]), sem="v")
            fin_done[qc] = p.val("v")
            p.wait("sync", "v", fin_done[qc])
            for q4 in range(4):
                p.dma("sync", a2a_in[qc * 512 + q4 * 128:
                                     qc * 512 + (q4 + 1) * 128,
                                     h * D_V:(h + 1) * D_V],
                      a_ast[:, ts(q4, 128)], sem="a2")

        sc_done = {}
        sc_done[0] = emit_scores(0)
        for j in range(nj):
            emit_exp_split(j, sc_done[j])
            if j + 1 < nj:
                sc_done[j + 1] = emit_scores(j + 1)
            emit_avden(j)
            if j % 3 == 1:
                emit_qz_step()
            qc, kc = flat[j]
            if kc == 4 * qc + 3:
                emit_finalize(qc)

        if _DEBUG and h == 0:
            p.wait("sync", "v")
            p.wait("sync", "t")
            p.dma("sync", dbg["d_q"][0:128, :], a_qn_hi[:, :], sem="st")
            p.dma("sync", dbg["d_q"][128:256, :], a_qn_lo[:, :], sem="st")
            p.dma("sync", dbg["d_q"][256:384, :], a_qr_A[:, :], sem="st")
            p.dma("sync", dbg["d_q"][384:512, :], a_qr_B[:, :], sem="st")
            p.dma("sync", dbg["d_k"][0:128, :], a_kn_hi[:, :], sem="st")
            p.dma("sync", dbg["d_k"][128:256, :], a_kn_lo[:, :], sem="st")
            p.dma("sync", dbg["d_k"][256:384, :], a_kr_P[:, :], sem="st")
            p.dma("sync", dbg["d_v"][0:128, :], a_v_hi[:, :], sem="st")
            p.dma("sync", dbg["d_v"][128:256, :], a_v_lo[:, :], sem="st")
        p.barrier()

    for h in range(HL):
        head_phase(h)
        if stop_after == f"head{h}":
            _finish_early()
            return nc, dbg

    if _DEBUG:
        p.wait("sync", "a2")
        p.dma("sync", dbg["d_attn"][:, :], a2a_in[:, :], sem="st")

    p.wait("gpsimd", "a2")
    p.inst("gpsimd", lambda e: e.collective_compute(
        "AllToAll", ALU.bypass, replica_groups=[list(range(NCORES))],
        ins=[a2a_in[:, :]], outs=[a2a_out[:, :]]), sem="cc")

    a_at_hi = av(0, 128, 16 * 1024, BF16)
    a_at_lo = av(32 * 1024, 128, 16 * 1024, BF16)
    a_xpre = av(64 * 1024, 128, 16 * 1024, F32)
    a_wo_f = [av(128 * 1024, 128, 2048, F32), av(136 * 1024, 128, 2048, F32)]
    a_wo_hi = [av(144 * 1024, 128, 2048, BF16), av(148 * 1024, 128, 2048, BF16)]
    a_wo_lo = [av(152 * 1024, 128, 2048, BF16), av(156 * 1024, 128, 2048, BF16)]
    a_xrs = [av(160 * 1024, 128, 1024, F32), av(164 * 1024, 128, 1024, F32)]
    a_p4st = av(168 * 1024, 128, 512, F32)
    a_x1_hi = av(0, 128, 16 * 1024, BF16)
    a_x1_lo = av(32 * 1024, 128, 16 * 1024, BF16)

    wo_pre = {}
    for oc in range(2):
        b = oc
        p.dma("sync", a_wo_f[b][:, :],
              w_o[:, ts(oc, 128)].rearrange("(a p) c -> p a c", p=128))
        p.dma("sync", a_xrs[b][:, :], x_t[ts(oc, 128), :])
        p.qbrk("sync", "ld")
        p.wait("vector", "ld")
        for k in range(16):
            sl = ts(k, 128)
            p.chain("vector", "v", [
                lambda e, b=b, sl=sl: e.tensor_copy(out=a_wo_hi[b][:, sl],
                                                    in_=a_wo_f[b][:, sl]),
                lambda e, b=b, sl=sl: e.tensor_sub(a_wo_lo[b][:, sl],
                                                   a_wo_f[b][:, sl],
                                                   a_wo_hi[b][:, sl]),
            ])
        wo_pre[oc] = p.val("v")

    p.wait("sync", "cc")
    p.wait("tensor", "cc")
    p.wait("vector", "cc")
    p.wait("gpsimd", "cc")

    for fc in range(16):
        sblk, hl = fc // 2, fc % 2
        for ttg in range(2):
            pst = ps_x if ttg == 0 else ps_av[0]
            p.wait("gpsimd", "t")
            p.dma("gpsimd", a_p4st[:, :],
                  a2a_out[sblk * 1024 + ttg * 512: sblk * 1024 + (ttg + 1) * 512,
                          hl * 128:(hl + 1) * 128]
                  .rearrange("(a p) c -> p a c", p=128), sem="gd")
            p.wait("tensor", "gd")
            p.wait("tensor", "v")
            for q4 in range(4):
                p.inst("tensor", lambda e, q4=q4, pst=pst: e.transpose(
                    pst[:, ts(q4, 128)], a_p4st[:, ts(q4, 128)], identf[:, :]),
                    sem="t")
            trd = p.val("t")
            p.wait("vector", "t", trd)
            dsl = slice(fc * 1024 + ttg * 512, fc * 1024 + (ttg + 1) * 512)
            p.chain("vector", "v", [
                lambda e, dsl=dsl, pst=pst: e.tensor_copy(out=a_at_hi[:, dsl],
                                                          in_=pst[:, :]),
                lambda e, dsl=dsl, pst=pst: e.tensor_sub(a_at_lo[:, dsl],
                                                         pst[:, :],
                                                         a_at_hi[:, dsl]),
            ])
    at_done = p.val("v")
    p.wait("tensor", "v", at_done)

    wo_done = {}
    for oc in range(16):
        b = oc % 2
        if oc >= 2:
            p.wait("sync", "v", wo_done[oc - 2][1])
            p.dma("sync", a_wo_f[b][:, :],
                  w_o[:, ts(oc, 128)].rearrange("(a p) c -> p a c", p=128))
            p.dma("sync", a_xrs[b][:, :], x_t[ts(oc, 128), :])
            p.qbrk("sync", "ld")
            p.wait("vector", "ld")
            p.wait("vector", "t", wo_done[oc - 2][0])
            for k in range(16):
                sl = ts(k, 128)
                p.chain("vector", "v", [
                    lambda e, b=b, sl=sl: e.tensor_copy(out=a_wo_hi[b][:, sl],
                                                        in_=a_wo_f[b][:, sl]),
                    lambda e, b=b, sl=sl: e.tensor_sub(a_wo_lo[b][:, sl],
                                                       a_wo_f[b][:, sl],
                                                       a_wo_hi[b][:, sl]),
                ])
            spd = p.val("v")
        else:
            spd = wo_pre[oc]
        p.wait("tensor", "v", spd)
        if oc >= 1:
            p.wait("tensor", "v", wo_done[oc - 1][1])
        for k in range(16):
            for (wt, at) in ((a_wo_hi[b], a_at_hi), (a_wo_hi[b], a_at_lo),
                             (a_wo_lo[b], a_at_hi)):
                for nt in range(2):
                    asl = slice(k * 1024 + nt * 512, k * 1024 + nt * 512 + 512)
                    p.inst("tensor", lambda e, wt=wt, at=at, k=k, asl=asl,
                           pst=ps_s[nt],
                           st=(k == 0 and wt is a_wo_hi[b] and at is a_at_hi),
                           sp=(k == 15 and wt is a_wo_lo[b]):
                           e.matmul(pst[:, :], wt[:, ts(k, 128)], at[:, asl],
                                    start=st, stop=sp), sem="t")
        mmd = p.val("t")
        p.wait("vector", "t", mmd)
        for nt in range(2):
            dsl = slice(oc * 1024 + nt * 512, oc * 1024 + nt * 512 + 512)
            p.inst("vector", lambda e, nt=nt, b=b, dsl=dsl: e.tensor_add(
                a_xpre[:, dsl], ps_s[nt][:, :], a_xrs[b][:, ts(nt, 512)]),
                sem="v")
        wo_done[oc] = (mmd, p.val("v"))

    def layernorm(src, dst_hi, dst_lo, out_dma=None):
        for k in range(16):
            for nt in range(2):
                sl = slice(k * 1024 + nt * 512, k * 1024 + nt * 512 + 512)
                p.wait("vector", "t")
                p.chain("vector", "v", [
                    lambda e, sl=sl: e.tensor_copy(out=lnhi[:, :],
                                                   in_=src[:, sl]),
                    lambda e, sl=sl: e.tensor_sub(lnlo[:, :], src[:, sl],
                                                  lnhi[:, :]),
                ])
                sd = p.val("v")
                p.wait("tensor", "v", sd)
                for i, t_ in enumerate((lnhi, lnlo)):
                    tgt = ps_d if nt == 0 else ps_e[0]
                    p.inst("tensor", lambda e, t_=t_, tgt=tgt, k=k, i=i:
                           e.matmul(tgt[0:1, :], ones_b[:, :], t_[:, :],
                                    start=(k == 0 and i == 0),
                                    stop=(k == 15 and i == 1)), sem="t")
                p.wait("scalar", "v", sd)
                p.wait("scalar", "t")
                p.inst("scalar", lambda e, sl=sl, nt=nt: e.activation(
                    out=scr_c[:, ts(nt, 512)], in_=src[:, sl],
                    func=AF.Square), sem="s")
                sqd = p.val("s")
                p.wait("vector", "s", sqd)
                p.wait("vector", "t")
                p.chain("vector", "v", [
                    lambda e, nt=nt: e.tensor_copy(out=lnhi[:, :],
                                                   in_=scr_c[:, ts(nt, 512)]),
                    lambda e, nt=nt: e.tensor_sub(lnlo[:, :],
                                                  scr_c[:, ts(nt, 512)],
                                                  lnhi[:, :]),
                ])
                sd2 = p.val("v")
                p.wait("tensor", "v", sd2)
                for i, t_ in enumerate((lnhi, lnlo)):
                    tgt = ps_s[0] if nt == 0 else ps_e[1]
                    p.inst("tensor", lambda e, t_=t_, tgt=tgt, k=k, i=i:
                           e.matmul(tgt[0:1, :], ones_b[:, :], t_[:, :],
                                    start=(k == 0 and i == 0),
                                    stop=(k == 15 and i == 1)), sem="t")
        std = p.val("t")
        p.wait("vector", "t", std)
        p.chain("vector", "v", [
            lambda e: e.tensor_scalar(out=stat1[0:1, 0:512], in0=ps_d[0:1, :],
                                      scalar1=1.0 / D, scalar2=None,
                                      op0=ALU.mult),
            lambda e: e.tensor_scalar(out=stat1[0:1, 512:1024],
                                      in0=ps_e[0][0:1, :], scalar1=1.0 / D,
                                      scalar2=None, op0=ALU.mult),
            lambda e: e.tensor_scalar(out=stat2[0:1, 0:512],
                                      in0=ps_s[0][0:1, :],
                                      scalar1=1.0 / D, scalar2=None,
                                      op0=ALU.mult),
            lambda e: e.tensor_scalar(out=stat2[0:1, 512:1024],
                                      in0=ps_e[1][0:1, :], scalar1=1.0 / D,
                                      scalar2=None, op0=ALU.mult),
            lambda e: e.tensor_mul(alpha[0:1, :], stat1[0:1, :], stat1[0:1, :]),
            lambda e: e.tensor_sub(stat2[0:1, :], stat2[0:1, :], alpha[0:1, :]),
        ])
        p.wait("vector", "v")
        p.chain("vector", "v", [
            lambda e: e.tensor_scalar(out=stat2[0:1, :], in0=stat2[0:1, :],
                                      scalar1=LN_EPS, scalar2=None,
                                      op0=ALU.add),
            lambda e: e.reciprocal(stat2[0:1, :], stat2[0:1, :]),
        ])
        p.wait("scalar", "v")
        p.inst("scalar", lambda e: e.activation(
            out=stat2[0:1, :], in_=stat2[0:1, :], func=AF.Sqrt), sem="s")
        rsd = p.val("s")
        p.wait("tensor", "s", rsd)
        for nt in range(2):
            mtgt = ps_d if nt == 0 else ps_e[0]
            rtgt = ps_x if nt == 0 else ps_e[1]
            p.inst("tensor", lambda e, nt=nt, mtgt=mtgt: e.matmul(
                mtgt[:, :], ones_row[0:1, :], stat1[0:1, ts(nt, 512)],
                start=True, stop=True), sem="t")
            p.inst("tensor", lambda e, nt=nt, rtgt=rtgt: e.matmul(
                rtgt[:, :], ones_row[0:1, :], stat2[0:1, ts(nt, 512)],
                start=True, stop=True), sem="t")
        bcd = p.val("t")
        p.wait("vector", "t", bcd)
        for k in range(16):
            for nt in range(2):
                sl = slice(k * 1024 + nt * 512, k * 1024 + nt * 512 + 512)
                mean_b = ps_d if nt == 0 else ps_e[0]
                rstd_b = ps_x if nt == 0 else ps_e[1]
                p.wait("vector", "v")
                p.wait("vector", "st")
                ops = [
                    lambda e, sl=sl, mean_b=mean_b: e.tensor_sub(
                        scr_b[:, 0:512], src[:, sl], mean_b[:, :]),
                    lambda e, sl=sl, rstd_b=rstd_b: e.tensor_mul(
                        scr_b[:, 512:1024], scr_b[:, 0:512], rstd_b[:, :]),
                ]
                if dst_hi is not None:
                    ops += [
                        lambda e, sl=sl: e.tensor_copy(out=dst_hi[:, sl],
                                                       in_=scr_b[:, 512:1024]),
                        lambda e, sl=sl: e.tensor_sub(dst_lo[:, sl],
                                                      scr_b[:, 512:1024],
                                                      dst_hi[:, sl]),
                    ]
                p.chain("vector", "v", ops)
                if out_dma is not None:
                    od = p.val("v")
                    p.wait("sync", "v", od)
                    p.dma("sync", out_dma(k, nt), scr_b[:, 512:1024], sem="st")

    if stop_after == "wo":
        _finish_early()
        return nc, dbg

    layernorm(a_xpre, a_x1_hi, a_x1_lo)
    if _DEBUG:
        p.wait("sync", "v")
        for k in range(16):
            p.dma("sync", dbg["d_x1"][ts(k, 128), :], a_xpre[:, ts(k, 1024)],
                  sem="st")
    p.barrier()

    if stop_after == "ln1":
        _finish_early()
        return nc, dbg

    a_fu_hi = av(64 * 1024, 128, 6 * 1024, BF16)
    a_fu_lo = av(76 * 1024, 128, 6 * 1024, BF16)
    a_dacc = av(88 * 1024, 128, 16 * 1024, F32)
    a_gw = [av(152 * 1024, 128, 2048, BF16), av(156 * 1024, 128, 2048, BF16)]
    a_uw = [av(160 * 1024, 128, 2048, BF16), av(164 * 1024, 128, 2048, BF16)]
    a_dw = [av(168 * 1024, 128, 768, BF16), av(170 * 1024, 128, 768, BF16)]

    gu_done = {}
    for e8 in range(8):
        for fc6 in range(6):
            fc = e8 * 6 + fc6
            b = fc % 2
            if fc >= 2:
                p.wait("sync", "t", gu_done[fc - 2])
            p.dma("sync", a_gw[b][:, :],
                  wq_gate[:, ts(fc, 128)].rearrange("(a p) c -> p a c", p=128))
            p.dma("sync", a_uw[b][:, :],
                  wq_up[:, ts(fc, 128)].rearrange("(a p) c -> p a c", p=128))
            p.qbrk("sync", "ld")
            p.wait("tensor", "ld")
            p.wait("tensor", "v")
            p.wait("tensor", "s")
            for k in range(16):
                for (psp, wsrc_b) in ((ps_s, a_gw[b]), (ps_e, a_uw[b])):
                    for nt in range(2):
                        xsl = slice(k * 1024 + nt * 512,
                                    k * 1024 + nt * 512 + 512)
                        for xt_ in (a_x1_hi, a_x1_lo):
                            p.inst("tensor", lambda e, wsrc_b=wsrc_b, k=k,
                                   xt_=xt_, xsl=xsl, pst=psp[nt],
                                   st=(k == 0 and xt_ is a_x1_hi),
                                   sp=(k == 15 and xt_ is a_x1_lo):
                                   e.matmul(pst[:, :], wsrc_b[:, ts(k, 128)],
                                            xt_[:, xsl], start=st, stop=sp),
                                   sem="t")
            gu_done[fc] = p.val("t")
            p.wait("scalar", "t", gu_done[fc])
            p.wait("scalar", "v")
            for nt in range(2):
                p.inst("scalar", lambda e, nt=nt, fc=fc: e.activation(
                    out=scr_b[:, ts(nt, 512)], in_=ps_s[nt][:, :],
                    func=AF.Silu, scale=gs_c[:, fc:fc + 1]), sem="s")
            sid = p.val("s")
            p.wait("vector", "s", sid)
            p.wait("vector", "t", gu_done[fc])
            for nt in range(2):
                fsl = slice(fc6 * 1024 + nt * 512, fc6 * 1024 + nt * 512 + 512)
                p.chain("vector", "v", [
                    lambda e, nt=nt, fc=fc: e.scalar_tensor_tensor(
                        out=scr_c[:, ts(nt, 512)], in0=ps_e[nt][:, :],
                        scalar=us_c[:, fc:fc + 1], in1=scr_b[:, ts(nt, 512)],
                        op0=ALU.mult, op1=ALU.mult),
                    lambda e, nt=nt, fsl=fsl: e.tensor_copy(
                        out=a_fu_hi[:, fsl], in_=scr_c[:, ts(nt, 512)]),
                    lambda e, nt=nt, fsl=fsl: e.tensor_sub(
                        a_fu_lo[:, fsl], scr_c[:, ts(nt, 512)],
                        a_fu_hi[:, fsl]),
                ])
        fu_done = p.val("v")
        p.wait("tensor", "v", fu_done)
        dw_done = {}
        for oc in range(16):
            b = oc % 2
            if oc >= 2:
                p.wait("sync", "t", dw_done[oc - 2])
            p.dma("sync", a_dw[b][:, :],
                  wq_down[e8 * 768:(e8 + 1) * 768, ts(oc, 128)]
                  .rearrange("(a p) c -> p a c", p=128))
            p.qbrk("sync", "ld")
            p.wait("tensor", "ld")
            p.wait("tensor", "v")
            for k in range(6):
                for nt in range(2):
                    fsl = slice(k * 1024 + nt * 512, k * 1024 + nt * 512 + 512)
                    for ft in (a_fu_hi, a_fu_lo):
                        p.inst("tensor", lambda e, b=b, k=k, ft=ft, fsl=fsl,
                               pst=ps_av[nt],
                               st=(k == 0 and ft is a_fu_hi),
                               sp=(k == 5 and ft is a_fu_lo):
                               e.matmul(pst[:, :], a_dw[b][:, ts(k, 128)],
                                        ft[:, fsl], start=st, stop=sp),
                               sem="t")
            dw_done[oc] = p.val("t")
            p.wait("vector", "t", dw_done[oc])
            for nt in range(2):
                dsl = slice(oc * 1024 + nt * 512, oc * 1024 + nt * 512 + 512)
                if e8 == 0:
                    p.inst("vector", lambda e, nt=nt, dsl=dsl: e.tensor_copy(
                        out=a_dacc[:, dsl], in_=ps_av[nt][:, :]), sem="v")
                else:
                    p.inst("vector", lambda e, nt=nt, dsl=dsl: e.tensor_add(
                        a_dacc[:, dsl], a_dacc[:, dsl], ps_av[nt][:, :]),
                        sem="v")
        p.barrier(("tensor", "vector", "scalar", "sync"))

    for k in range(16):
        for nt in range(2):
            sl = slice(k * 1024 + nt * 512, k * 1024 + nt * 512 + 512)
            p.chain("vector", "v", [
                lambda e, sl=sl, k=k: e.scalar_tensor_tensor(
                    out=a_dacc[:, sl], in0=a_dacc[:, sl],
                    scalar=ds_c[:, k:k + 1], in1=a_x1_hi[:, sl],
                    op0=ALU.mult, op1=ALU.add),
                lambda e, sl=sl: e.tensor_add(a_dacc[:, sl], a_dacc[:, sl],
                                              a_x1_lo[:, sl]),
            ])
    p.barrier(("vector", "tensor", "scalar"))

    def out_dma(k, nt):
        return out_t[ts(k, 128), nt * 512:(nt + 1) * 512]

    layernorm(a_dacc, None, None, out_dma=out_dma)

    p.wait("gpsimd", "st")
    with allow_ici_all_to_all():
        p.build()
    return nc, dbg



_BUILT = None


def kernel(L0_attn_x, L0_attn_w_qa, L0_attn_q_a_norm_w, L0_attn_w_qb,
           L0_attn_w_kva, L0_attn_kv_a_norm_w, L0_attn_w_kvb, L0_attn_w_o,
           L0_ffn_gate_w, L0_ffn_up_w, L0_ffn_down_w, L0_ffn_gate_scale,
           L0_ffn_up_scale, L0_ffn_down_scale):
    global _BUILT, LAST_DEBUG
    if _BUILT is None:
        _BUILT = build_nc()
    nc, dbg = _BUILT

    x = np.ascontiguousarray(np.asarray(L0_attn_x, dtype=np.float32)[0])
    com = {
        "w_qa": np.ascontiguousarray(L0_attn_w_qa, dtype=np.float32),
        "w_kva": np.ascontiguousarray(L0_attn_w_kva, dtype=np.float32),
        "w_o": np.ascontiguousarray(L0_attn_w_o, dtype=np.float32),
        "gate_w": np.ascontiguousarray(L0_ffn_gate_w, dtype=np.float32),
        "up_w": np.ascontiguousarray(L0_ffn_up_w, dtype=np.float32),
        "down_w": np.ascontiguousarray(L0_ffn_down_w, dtype=np.float32),
        "g_q": np.ascontiguousarray(
            np.asarray(L0_attn_q_a_norm_w, dtype=np.float32).reshape(-1, 128).T),
        "g_kv": np.ascontiguousarray(
            np.asarray(L0_attn_kv_a_norm_w, dtype=np.float32).reshape(-1, 128).T),
        "gate_s": np.ascontiguousarray(
            np.asarray(L0_ffn_gate_scale, dtype=np.float32).reshape(-1, 128).T),
        "up_s": np.ascontiguousarray(
            np.asarray(L0_ffn_up_scale, dtype=np.float32).reshape(-1, 128).T),
        "down_s": np.ascontiguousarray(
            np.asarray(L0_ffn_down_scale, dtype=np.float32).reshape(-1, 128).T),
    }
    wqb = np.asarray(L0_attn_w_qb, dtype=np.float32)
    wkvb = np.asarray(L0_attn_w_kvb, dtype=np.float32)
    in_maps = []
    for c in range(NCORES):
        m = dict(com)
        m["x_t"] = np.ascontiguousarray(x[c * T:(c + 1) * T, :].T)
        m["w_qb_h"] = np.ascontiguousarray(
            wqb[:, c * HL * D_QK:(c + 1) * HL * D_QK])
        m["w_kvb_h"] = np.ascontiguousarray(
            wkvb[:, c * HL * 256:(c + 1) * HL * 256])
        m["tok_base"] = np.full((1, 1), c * T, dtype=np.float32)
        in_maps.append(m)

    res = run_bass_kernel_spmd(nc, in_maps, core_ids=list(range(NCORES)))
    out = np.empty((S, D), dtype=np.float32)
    for c in range(NCORES):
        out[c * T:(c + 1) * T, :] = res.results[c]["out_t"].T
    if _DEBUG:
        LAST_DEBUG = list(res.results)
    return out[None, :, :]
